# version 01
# baseline (speedup 1.0000x reference)
import contextlib
import numpy as np
import ml_dtypes
import concourse.bass as bass
import concourse.mybir as mybir
from concourse.bass_utils import run_bass_kernel_spmd

F32 = mybir.dt.float32
BF16 = mybir.dt.bfloat16
AF = mybir.ActivationFunctionType
ALU = mybir.AluOpType
AX = mybir.AxisListType.X
BFNP = ml_dtypes.bfloat16

B = 2
S = 2048
D = 1024
T = B * S
H = 16
HD = 64
E = 8
FE = 2048
DC = 8
FC = 16
EPS = 1e-5
TAB = 4095
UW = 3968


def mkap(ap, offset, dims):
    a = ap.copy()
    a.offset = offset
    a.ap = type(a.ap)(dims)
    return a


def revslice(tile_ap, col_hi, n):
    dims = tile_ap.ap.to_list()
    return mkap(tile_ap, tile_ap.offset + col_hi, [dims[0], [-1, n]])


class SemBank:
    def __init__(self, nc):
        self.nc = nc
        self.h = {}
        self.c = {}

    def sem(self, n):
        if n not in self.h:
            self.h[n] = self.nc.alloc_semaphore(n)
            self.c[n] = 0
        return self.h[n]

    def inc(self, instr, n, amt=1):
        instr.then_inc(self.sem(n), amt)
        self.c[n] += amt
        return self.c[n]

    def wait(self, eng, n, v):
        if v > 0:
            eng.wait_ge(self.sem(n), v)



class MaybeBlk:
    def __init__(self, blk, en):
        self._blk, self._en = blk, en

    def __getattr__(self, k):
        f = getattr(self._blk, k)
        if k in ("tensor", "vector", "scalar", "gpsimd", "sync"):
            return (lambda fn: f(fn)) if self._en else (lambda fn: None)
        return f


class Serial:

    def __init__(self, sb, blk, enabled=True):
        self.sb = sb
        self.blk = blk
        self.enabled = enabled
        sb.sem("ser")

    def step(self, eng_name, fn, dma=False, extra_waits=()):
        if not self.enabled:
            return
        sb = self.sb
        need = sb.c["ser"]
        inc = 16 if dma else 1

        def body(eng):
            for n, v in extra_waits:
                sb.wait(eng, n, v)
            sb.wait(eng, "ser", need)
            ins = fn(eng)
            ins.then_inc(sb.sem("ser"), inc)
        getattr(self.blk, eng_name)(body)
        sb.c["ser"] += inc

    def multi(self, eng_name, fns, dma=False, extra_waits=()):
        if not self.enabled:
            return
        sb = self.sb
        need = sb.c["ser"]
        inc = 16 if dma else 1

        def body(eng):
            for n, v in extra_waits:
                sb.wait(eng, n, v)
            sb.wait(eng, "ser", need)
            for fn in fns:
                fn(eng).then_inc(sb.sem("ser"), inc)
        getattr(self.blk, eng_name)(body)
        sb.c["ser"] += inc * len(fns)


def build(debug=False, phases=(0, 1, 2, 3, 4, 5, 6)):
    nc = bass.Bass("TRN2", target_bir_lowering=False)

    x_in = nc.dram_tensor("x", [T, D], F32, kind="ExternalInput")
    w_in = nc.dram_tensor("w_in", [D, 3 * D], F32, kind="ExternalInput")
    w_gate = nc.dram_tensor("w_gate", [D, 2 * D], F32, kind="ExternalInput")
    b_gate = nc.dram_tensor("b_gate", [2 * D], F32, kind="ExternalInput")
    amp_in = nc.dram_tensor("amp", [H, 21], F32, kind="ExternalInput")
    shp_in = nc.dram_tensor("shp", [H, 21], F32, kind="ExternalInput")
    off_in = nc.dram_tensor("off", [H, 21], F32, kind="ExternalInput")
    sw1_in = nc.dram_tensor("sw1", [FE, D], F32, kind="ExternalInput")
    sw2_in = nc.dram_tensor("sw2", [D, FE], F32, kind="ExternalInput")
    ew1_in = nc.dram_tensor("ew1", [E, FE, D], F32, kind="ExternalInput")
    ew2_in = nc.dram_tensor("ew2", [E, D, FE], F32, kind="ExternalInput")
    rw_in = nc.dram_tensor("rw", [E, D], F32, kind="ExternalInput")
    gam_in = nc.dram_tensor("gam", [D], F32, kind="ExternalInput")
    bet_in = nc.dram_tensor("bet", [D], F32, kind="ExternalInput")
    ident = nc.dram_tensor("ident", [128, 128], BF16, kind="ExternalInput")
    identf = nc.dram_tensor("identf", [128, 128], F32, kind="ExternalInput")
    ramp = nc.dram_tensor("ramp", [112, TAB], F32, kind="ExternalInput")
    hsum = nc.dram_tensor("hsum", [112, 16], F32, kind="ExternalInput")

    out = nc.dram_tensor("out", [T, D], F32, kind="ExternalOutput")

    ikind = "ExternalOutput" if debug else "Internal"
    tab_s = nc.dram_tensor("tab_s", [H, TAB], F32, kind=ikind)
    k_s = nc.dram_tensor("k_s", [B, H, HD, S], BF16, kind=ikind)
    q_s = nc.dram_tensor("q_s", [B, H, HD, S], BF16, kind=ikind)
    v_s = nc.dram_tensor("v_s", [B, S, D], BF16, kind=ikind)
    h_s = nc.dram_tensor("h_s", [DC, 128, T], BF16, kind=ikind)
    ao_s = nc.dram_tensor("ao_s", [DC, 128, T], BF16, kind=ikind)
    ag_s = nc.dram_tensor("ag_s", [DC, 128, T], BF16, kind=ikind)
    gt_s = nc.dram_tensor("gt_s", [E, T], BF16, kind=ikind)
    eo_s = nc.dram_tensor("eo_s", [DC, 128, T], BF16, kind=ikind)
    w1t_s = nc.dram_tensor("w1t_s", [E, DC, 128, FE], BF16, kind=ikind)
    w2t_s = nc.dram_tensor("w2t_s", [E, FC, 128, D], BF16, kind=ikind)
    st1_s = nc.dram_tensor("st1_s", [DC, 128, FE], BF16, kind=ikind)
    st2_s = nc.dram_tensor("st2_s", [FC, 128, D], BF16, kind=ikind)
    s1_s = nc.dram_tensor("s1_s", [E + 1, 128, FC], F32, kind=ikind)

    id_bf = nc.alloc_sbuf_tensor("id_bf", [128, 128], BF16)
    id_f32 = nc.alloc_sbuf_tensor("id_f32", [128, 128], F32)
    ones_bf = nc.alloc_sbuf_tensor("ones_bf", [128, 1], BF16)
    ones_bfr = nc.alloc_sbuf_tensor("ones_bfr", [1, 128], BF16)
    ones_fr = nc.alloc_sbuf_tensor("ones_fr", [1, 128], F32)
    bg_sb = nc.alloc_sbuf_tensor("bg_sb", [128, 16], F32)
    rwt_sb = nc.alloc_sbuf_tensor("rwt_sb", [128, DC * E], F32)
    gate_fm = nc.alloc_sbuf_tensor("gate_fm", [8, T], BF16)

    pf = [nc.alloc_psum_tensor(f"pf{i}", [128, 512], F32) for i in range(6)]
    pb = [nc.alloc_psum_tensor(f"pb{i}", [128, 1024], BF16) for i in range(2)]

    sb = SemBank(nc)
    for n in ["ser", "ld", "pe", "ac", "dv", "po", "st", "pe2", "dv2", "ac2", "out", "pe3", "dv3"]:
        sb.sem(n)

    with contextlib.ExitStack() as _es:
        row_sb = _es.enter_context(nc.sbuf_tensor("p0_row", [1, D], F32))
        rw_sb = _es.enter_context(nc.sbuf_tensor("p0_rw", [8, D], F32))
        bgt_row = _es.enter_context(nc.sbuf_tensor("p0_bgt", [16, 128], F32))
        rwb_sb = _es.enter_context(nc.sbuf_tensor("p0_rwb", [8, D], BF16))
        prm = _es.enter_context(nc.sbuf_tensor("p0_prm", [112, 9], F32))
        prm2 = _es.enter_context(nc.sbuf_tensor("p0_prm2", [112, 6], F32))
        ramp_sb = _es.enter_context(nc.sbuf_tensor("p0_ramp", [112, TAB], F32))
        hsum_sb = _es.enter_context(nc.sbuf_tensor("p0_hsum", [112, 16], F32))
        d2_sb = _es.enter_context(nc.sbuf_tensor("p0_d2", [112, TAB], F32))
        eg_sb = _es.enter_context(nc.sbuf_tensor("p0_eg", [112, 3 * TAB], F32))
        tab_sb = _es.enter_context(nc.sbuf_tensor("p0_tab", [16, TAB], F32))
        blk = _es.enter_context(nc.Block())
        blk = MaybeBlk(blk, 0 in phases)
        sr = Serial(sb, blk, 0 in phases)
        sr.multi("sync", [
            lambda e: e.dma_start(id_bf[:], ident.ap()),
            lambda e: e.dma_start(id_f32[:], identf.ap()),
            lambda e: e.dma_start(ramp_sb[:], ramp.ap()),
            lambda e: e.dma_start(hsum_sb[:], hsum.ap()),
            lambda e: e.dma_start(bgt_row[:], mkap(b_gate.ap(), 0, [[128, 16], [1, 128]])),
            lambda e: e.dma_start(rw_sb[:], rw_in.ap()),
        ] + [
            (lambda gi, g, src: lambda e: e.dma_start(
                prm[:, gi * 3 + g: gi * 3 + g + 1],
                mkap(src.ap(), g * 7, [[21, 16], [1, 7], [1, 1]])))(gi, g, src)
            for gi, src in enumerate([off_in, shp_in, amp_in]) for g in range(3)
        ], dma=True)
        sr.multi("vector", [
            lambda e: e.memset(ones_bf[:], 1.0),
            lambda e: e.memset(ones_bfr[:], 1.0),
            lambda e: e.memset(ones_fr[:], 1.0),
            lambda e: e.tensor_scalar(prm2[:, 0:3], prm[:, 0:3], -1.0, None, op0=ALU.mult),
        ])
        sr.step("scalar", lambda e: e.activation(out=prm2[:, 3:6], in_=prm[:, 3:6], func=AF.Abs))
        sr.step("vector", lambda e: e.tensor_scalar(prm2[:, 3:6], prm2[:, 3:6], -1.0, None, op0=ALU.mult))

        sr.multi("tensor", [
            (lambda dc: lambda e: e.transpose(pf[3][:, dc * 8:(dc + 1) * 8],
                                              rw_sb[0:8, dc * 128:(dc + 1) * 128], id_f32[0:8, 0:8]))(dc)
            for dc in range(DC)
        ])
        sr.step("vector", lambda e: e.tensor_copy(rwt_sb[:], pf[3][:, 0:DC * E]))
        sr.step("tensor", lambda e: e.transpose(pf[3][:, 0:16], bgt_row[:], id_f32[0:16, 0:16]))
        sr.step("vector", lambda e: e.tensor_copy(bg_sb[:], pf[3][:, 0:16]))
        for g in range(3):
            sr.step("scalar", (lambda g: lambda e: e.activation(
                out=d2_sb[:], in_=ramp_sb[:], func=AF.Square, bias=prm2[:, g:g + 1]))(g))
            sr.step("scalar", (lambda g: lambda e: e.activation(
                out=eg_sb[:, g * TAB:(g + 1) * TAB], in_=d2_sb[:], func=AF.Exp,
                scale=prm2[:, 3 + g:4 + g]))(g))
            sr.step("vector", (lambda g: lambda e: e.tensor_scalar(
                eg_sb[:, g * TAB:(g + 1) * TAB], eg_sb[:, g * TAB:(g + 1) * TAB],
                prm[:, 6 + g:7 + g], None, op0=ALU.mult))(g))
        for c in range(8):
            w = 512 if c < 7 else TAB - 7 * 512
            sr.step("tensor", (lambda c, w: lambda e: [
                e.matmul(pf[c % 2][0:16, 0:w], hsum_sb[:], eg_sb[:, g * TAB + c * 512: g * TAB + c * 512 + w],
                         start=(g == 0), stop=(g == 2)) for g in range(3)][-1])(c, w))
            sr.step("scalar", (lambda c, w: lambda e: e.copy(
                tab_sb[:, c * 512:c * 512 + w], pf[c % 2][0:16, 0:w]))(c, w))
        sr.step("sync", lambda e: e.dma_start(tab_s.ap(), tab_sb[:]), dma=True)

    with contextlib.ExitStack() as _es:
        gam_rep = _es.enter_context(nc.sbuf_tensor("p1_gam", [128, D], F32))
        bet_rep = _es.enter_context(nc.sbuf_tensor("p1_bet", [128, D], F32))
        win_sb = _es.enter_context(nc.sbuf_tensor("p1_win", [128, DC * 3 * D], BF16))
        wstg = _es.enter_context(nc.sbuf_tensor("p1_wstg", [128, 3 * D], F32))
        xt2 = _es.enter_context(nc.sbuf_tensor("p1_xt", [128, 2 * D], F32))
        sq_scr = _es.enter_context(nc.sbuf_tensor("p1_scr", [128, D], BF16))
        xn_t = _es.enter_context(nc.sbuf_tensor("p1_xn", [128, D], BF16))
        xnn = _es.enter_context(nc.sbuf_tensor("p1_xnn", [128, D], F32))
        xn_f = _es.enter_context(nc.sbuf_tensor("p1_xnf", [128, D], F32))
        h1tmp = _es.enter_context(nc.sbuf_tensor("p1_ht", [128, D], F32))
        h_t = _es.enter_context(nc.sbuf_tensor("p1_hb", [128, D], BF16))
        h_f = _es.enter_context(nc.sbuf_tensor("p1_hf", [128, D], F32))
        h_fm32 = _es.enter_context(nc.sbuf_tensor("p1_hfm32", [128, DC * 512], F32))
        tny = _es.enter_context(nc.sbuf_tensor("p1_tiny", [128, 16], F32))
        xn_fm = _es.enter_context(nc.sbuf_tensor("p1_xnfm", [128, DC * 512], BF16))
        h_fm = _es.enter_context(nc.sbuf_tensor("p1_hfm", [128, DC * 512], BF16))
        pstg = _es.enter_context(nc.sbuf_tensor("p1_stg", [128, 2 * 512], BF16))
        vstg = _es.enter_context(nc.sbuf_tensor("p1_vstg", [128, 2 * 512], BF16))
        r8 = _es.enter_context(nc.sbuf_tensor("p1_r8", [128, 8], F32))
        g8 = _es.enter_context(nc.sbuf_tensor("p1_g8", [128, 8], BF16))
        blk = _es.enter_context(nc.Block())
        blk = MaybeBlk(blk, 1 in phases)
        sr = Serial(sb, blk, 1 in phases)

        def _emit_hf32(sr, tt):
            for half in range(2):
                sr.multi("tensor", [
                    (lambda dc, half: lambda e: e.transpose(
                        pf[2 + half][:, (dc % 4) * 128:(dc % 4 + 1) * 128],
                        h_f[:, dc * 128:(dc + 1) * 128], id_f32[:]))(dc, half)
                    for dc in range(half * 4, half * 4 + 4)
                ])
                sr.step("vector", (lambda tt=tt, half=half: lambda e: e.tensor_copy(
                    mkap(h_fm32[:, :], h_fm32[:, :].offset + half * 4 * 512 + tt * 128,
                         [h_fm32[:, :].ap.to_list()[0], [512, 4], [1, 128]]),
                    pf[2 + half][:, 0:512]))(tt, half))

        sr.step("sync", lambda e: e.dma_start(
            xt2[0:1, 0:D], gam_in.ap().rearrange("(one d) -> one d", one=1)), dma=True)
        for c in range(2):
            sr.step("tensor", (lambda c: lambda e: e.matmul(
                pf[c][:, 0:512], ones_fr[:], xt2[0:1, c * 512:(c + 1) * 512], start=True, stop=True))(c))
        sr.multi("vector", [
            (lambda c: lambda e: e.tensor_copy(gam_rep[:, c * 512:(c + 1) * 512], pf[c][:]))(c)
            for c in range(2)
        ])
        sr.step("sync", lambda e: e.dma_start(xt2[0:1, 0:D], bet_in.ap().rearrange("(one d) -> one d", one=1)), dma=True)
        for c in range(2):
            sr.step("tensor", (lambda c: lambda e: e.matmul(
                pf[c][:, 0:512], ones_fr[:], xt2[0:1, c * 512:(c + 1) * 512], start=True, stop=True))(c))
        sr.multi("vector", [
            (lambda c: lambda e: e.tensor_copy(bet_rep[:, c * 512:(c + 1) * 512], pf[c][:]))(c)
            for c in range(2)
        ])
        for dc in range(DC):
            sr.step("sync", (lambda dc: lambda e: e.dma_start(
                wstg[:], x_in.ap() if False else mkap(w_in.ap(), dc * 128 * 3 * D,
                                                      [[3 * D, 128], [1, 3 * D]])))(dc), dma=True)
            sr.step("vector", (lambda dc: lambda e: e.tensor_copy(
                win_sb[:, dc * 3 * D:(dc + 1) * 3 * D], wstg[:]))(dc))

        for grp in range(8):
            for tt in range(4):
                xt = xt2[:, (tt % 2) * D:((tt % 2) + 1) * D]
                r0 = (grp * 4 + tt) * 128
                sr.step("sync", (lambda r0, xt=xt: lambda e: e.dma_start(
                    xt, mkap(x_in.ap(), r0 * D, [[D, 128], [1, D]])))(r0), dma=True)
                sr.step("vector", (lambda xt=xt: lambda e: e.tensor_reduce(
                    tny[:, 0:1], xt, axis=AX, op=ALU.add))(xt))
                sr.step("scalar", (lambda xt=xt: lambda e: e.activation(
                    out=sq_scr[:], in_=xt, func=AF.Square, accum_out=tny[:, 1:2]))(xt))
                sr.multi("vector", [
                    lambda e: e.tensor_scalar(tny[:, 2:3], tny[:, 0:1], 1.0 / D, None, op0=ALU.mult),
                    lambda e: e.tensor_scalar(tny[:, 3:4], tny[:, 1:2], 1.0 / D, None, op0=ALU.mult),
                ])
                sr.step("vector", lambda e: e.tensor_tensor(tny[:, 4:5], tny[:, 2:3], tny[:, 2:3], op=ALU.mult))
                sr.step("vector", lambda e: e.tensor_tensor(tny[:, 5:6], tny[:, 3:4], tny[:, 4:5], op=ALU.subtract))
                sr.step("vector", lambda e: e.tensor_scalar(tny[:, 6:7], tny[:, 5:6], EPS, None, op0=ALU.add))
                sr.step("scalar", lambda e: e.sqrt(tny[:, 7:8], tny[:, 6:7]))
                sr.step("vector", lambda e: e.reciprocal(tny[:, 8:9], tny[:, 7:8]))
                sr.step("vector", (lambda xt=xt: lambda e: e.tensor_scalar(
                    xn_f[:], xt, tny[:, 2:3], tny[:, 8:9], op0=ALU.subtract, op1=ALU.mult))(xt))
                sr.step("vector", lambda e: e.tensor_copy(xn_t[:], xn_f[:]))
                sr.step("vector", lambda e: e.tensor_reduce(tny[:, 0:1], xn_f[:], axis=AX, op=ALU.add))
                sr.step("scalar", lambda e: e.activation(out=sq_scr[:], in_=xn_f[:], func=AF.Square,
                                                         accum_out=tny[:, 1:2]))
                sr.multi("vector", [
                    lambda e: e.tensor_scalar(tny[:, 2:3], tny[:, 0:1], 1.0 / D, None, op0=ALU.mult),
                    lambda e: e.tensor_scalar(tny[:, 3:4], tny[:, 1:2], 1.0 / D, None, op0=ALU.mult),
                ])
                sr.step("vector", lambda e: e.tensor_tensor(tny[:, 4:5], tny[:, 2:3], tny[:, 2:3], op=ALU.mult))
                sr.step("vector", lambda e: e.tensor_tensor(tny[:, 5:6], tny[:, 3:4], tny[:, 4:5], op=ALU.subtract))
                sr.step("vector", lambda e: e.tensor_scalar(tny[:, 6:7], tny[:, 5:6], EPS, None, op0=ALU.add))
                sr.step("scalar", lambda e: e.sqrt(tny[:, 7:8], tny[:, 6:7]))
                sr.step("vector", lambda e: e.reciprocal(tny[:, 8:9], tny[:, 7:8]))
                sr.step("vector", lambda e: e.tensor_scalar(
                    xnn[:], xn_f[:], tny[:, 2:3], tny[:, 8:9], op0=ALU.subtract, op1=ALU.mult))
                sr.step("gpsimd", lambda e: e.tensor_tensor(h1tmp[:], xnn[:], gam_rep[:], op=ALU.mult))
                sr.step("gpsimd", lambda e: e.tensor_tensor(h_f[:], h1tmp[:], bet_rep[:], op=ALU.add))
                sr.step("vector", lambda e: e.tensor_copy(h_t[:], h_f[:]))
                sr.multi("tensor", [
                    (lambda dc: lambda e: e.transpose(
                        pb[0][:, dc * 128:(dc + 1) * 128], xn_t[:, dc * 128:(dc + 1) * 128], id_bf[:]))(dc)
                    for dc in range(DC)
                ])
                sr.step("vector", (lambda tt=tt: lambda e: e.tensor_copy(
                    mkap(xn_fm[:, :], xn_fm[:, :].offset + tt * 128,
                         [xn_fm[:, :].ap.to_list()[0], [512, DC], [1, 128]]),
                    pb[0][:, 0:1024]))(tt))
                sr.multi("tensor", [
                    (lambda dc: lambda e: e.transpose(
                        pb[1][:, dc * 128:(dc + 1) * 128], h_t[:, dc * 128:(dc + 1) * 128], id_bf[:]))(dc)
                    for dc in range(DC)
                ])
                sr.step("vector", (lambda tt=tt: lambda e: e.tensor_copy(
                    mkap(h_fm[:, :], h_fm[:, :].offset + tt * 128, [h_fm[:, :].ap.to_list()[0], [512, DC], [1, 128]]),
                    pb[1][:, 0:1024]))(tt))
                _emit_hf32(sr, tt)
            sr.multi("gpsimd", [
                (lambda dc, grp=grp: lambda e: e.dma_start(
                    mkap(h_s.ap(), dc * 128 * T + grp * 512, [[T, 128], [1, 512]]),
                    h_fm[:, dc * 512:(dc + 1) * 512]))(dc)
                for dc in range(DC)
            ], dma=True)
            for oc in range(24):
                par = oc % 2

                def mmblk(e, oc=oc, par=par):
                    ins = None
                    for dc in range(DC):
                        ins = e.matmul(pf[par][:], win_sb[:, dc * 3 * D + oc * 128: dc * 3 * D + (oc + 1) * 128],
                                       xn_fm[:, dc * 512:(dc + 1) * 512],
                                       start=(dc == 0), stop=(dc == DC - 1))
                    return ins
                sr.step("tensor", mmblk)
                sr.step("scalar", (lambda par: lambda e: e.copy(
                    pstg[:, par * 512:(par + 1) * 512], pf[par][:]))(par))
                if oc < 8:
                    h0 = oc * 2
                    sr.step("gpsimd", (lambda h0, par, grp=grp: lambda e: e.dma_start(
                        mkap(k_s.ap(), ((0 * H + h0) * HD) * S + (grp % 4) * 512 + (grp // 4) * H * HD * S
                             if False else (grp // 4) * H * HD * S + h0 * HD * S + ((grp % 4) * 512),
                             [[S, 128], [1, 512]]),
                        pstg[:, par * 512:(par + 1) * 512]))(h0, par), dma=True)
                elif oc >= 16:
                    h0 = (oc - 16) * 2
                    sr.step("gpsimd", (lambda h0, par, grp=grp: lambda e: e.dma_start(
                        mkap(q_s.ap(), (grp // 4) * H * HD * S + h0 * HD * S + ((grp % 4) * 512),
                             [[S, 128], [1, 512]]),
                        pstg[:, par * 512:(par + 1) * 512]))(h0, par), dma=True)
                else:
                    vc = oc - 8
                    sr.multi("tensor", [
                        (lambda ts, par: lambda e: e.transpose(
                            pb[par][:, ts * 128:(ts + 1) * 128],
                            pstg[:, par * 512 + ts * 128: par * 512 + (ts + 1) * 128], id_bf[:]))(ts, par)
                        for ts in range(4)
                    ])
                    sr.step("vector", (lambda par: lambda e: e.tensor_copy(
                        vstg[:, par * 512:(par + 1) * 512], pb[par][:, 0:512]))(par))
                    sr.multi("gpsimd", [
                        (lambda ts, vc, par, grp=grp: lambda e: e.dma_start(
                            mkap(v_s.ap(), (grp // 4) * S * D + ((grp % 4) * 512 + ts * 128) * D + vc * 128,
                                 [[D, 128], [1, 128]]),
                            vstg[:, par * 512 + ts * 128: par * 512 + (ts + 1) * 128]))(ts, vc, par)
                        for ts in range(4)
                    ], dma=True)
            for tq in range(4):
                def rmm(e, tq=tq):
                    ins = None
                    for dc in range(DC):
                        ins = e.matmul(pf[4][:, 0:8], h_fm32[:, dc * 512 + tq * 128: dc * 512 + (tq + 1) * 128],
                                       rwt_sb[:, dc * 8:(dc + 1) * 8],
                                       start=(dc == 0), stop=(dc == DC - 1))
                    return ins
                sr.step("tensor", rmm)
                sr.step("vector", lambda e: e.tensor_reduce(tny[:, 0:1], pf[4][:, 0:8], axis=AX,
                                                            op=ALU.max, negate=True))
                sr.step("scalar", lambda e: e.activation(out=r8[:], in_=pf[4][:, 0:8], func=AF.Exp,
                                                         bias=tny[:, 0:1]))
                sr.step("vector", lambda e: e.tensor_reduce(tny[:, 1:2], r8[:], axis=AX, op=ALU.add))
                sr.step("vector", lambda e: e.reciprocal(tny[:, 2:3], tny[:, 1:2]))
                sr.step("vector", lambda e: e.tensor_reduce(tny[:, 3:4], r8[:], axis=AX, op=ALU.max))
                sr.step("vector", lambda e: e.tensor_tensor(tny[:, 4:5], tny[:, 3:4], tny[:, 2:3], op=ALU.mult))
                sr.step("vector", lambda e: e.tensor_scalar(g8[:], r8[:], tny[:, 3:4], tny[:, 4:5],
                                                            op0=ALU.is_equal, op1=ALU.mult))
                sr.step("tensor", (lambda tq=tq: lambda e: e.transpose(
                    pb[0][0:8, tq * 128:(tq + 1) * 128], g8[:], id_bf[:]))(tq))
            sr.step("vector", (lambda grp=grp: lambda e: e.tensor_copy(
                gate_fm[0:8, grp * 512:(grp + 1) * 512], pb[0][0:8, 0:512]))(grp))
        sr.step("gpsimd", lambda e: e.dma_start(gt_s.ap(), gate_fm[:]), dma=True)

    with contextlib.ExitStack() as _es:
        wr2 = _es.enter_context(nc.sbuf_tensor("p2_wr", [128, 2 * FE], F32))
        qa = _es.enter_context(nc.sbuf_tensor("p2_a", [128, FE], BF16))
        qb = _es.enter_context(nc.sbuf_tensor("p2_b", [128, FE], BF16))
        qt = _es.enter_context(nc.sbuf_tensor("p2_t", [128, FE], BF16))
        tstg = _es.enter_context(nc.sbuf_tensor("p2_ts", [128, 2 * 1024], BF16))
        qs = _es.enter_context(nc.sbuf_tensor("p2_s", [128, 8], F32))
        s1stg = _es.enter_context(nc.sbuf_tensor("p2_s1", [128, FC], F32))
        blk = _es.enter_context(nc.Block())
        blk = MaybeBlk(blk, 2 in phases)
        sr = Serial(sb, blk, 2 in phases)
        mats = []
        for e_ in range(E):
            mats.append(("w1", ew1_in, e_, FE, D))
        for e_ in range(E):
            mats.append(("w2", ew2_in, e_, D, FE))
        mats.append(("w1", sw1_in, E, FE, D))
        mats.append(("w2", sw2_in, E, D, FE))
        for kind, src, e_, NR, LC in mats:
            ntile = NR // 128
            for rt in range(ntile):
                par = rt % 2
                wr = wr2[:, par * FE: par * FE + LC]
                base = (e_ * NR * LC if e_ < E else 0) + rt * 128 * LC
                sr.step("sync", (lambda base, LC, wr=wr: lambda e: e.dma_start(
                    wr, mkap(src.ap(), base, [[LC, 128], [1, LC]])))(base, LC), dma=True)
                sr.step("vector", (lambda wr=wr: lambda e: e.tensor_reduce(
                    qs[:, 0:1], wr, axis=AX, op=ALU.add, apply_absolute_value=True))(wr))
                sr.step("vector", (lambda LC: lambda e: e.tensor_scalar(
                    qs[:, 1:2], qs[:, 0:1], 1.0 / LC, 1e-5, op0=ALU.mult, op1=ALU.max))(LC))
                sr.multi("vector", [
                    lambda e: e.tensor_scalar(qs[:, 2:3], qs[:, 1:2], -0.5, None, op0=ALU.mult),
                    lambda e: e.tensor_scalar(qs[:, 3:4], qs[:, 1:2], 0.5, None, op0=ALU.mult),
                ])
                sr.step("scalar", (lambda wr=wr, LC=LC: lambda e: e.activation(
                    out=qa[:, 0:LC], in_=wr, func=AF.Sign, bias=qs[:, 2:3]))(wr, LC))
                sr.step("scalar", (lambda wr=wr, LC=LC: lambda e: e.activation(
                    out=qb[:, 0:LC], in_=wr, func=AF.Sign, bias=qs[:, 2:3], scale=-1.0))(wr, LC))
                sr.step("vector", (lambda LC: lambda e: e.tensor_tensor(
                    qt[:, 0:LC], qa[:, 0:LC], qb[:, 0:LC], op=ALU.subtract))(LC))
                if kind == "w1":
                    sr.step("vector", (lambda rt: lambda e: e.tensor_copy(
                        s1stg[:, rt:rt + 1], qs[:, 3:4]))(rt))
                else:
                    sr.step("scalar", (lambda LC: lambda e: e.activation(
                        out=qt[:, 0:LC], in_=qt[:, 0:LC], func=AF.Copy, scale=qs[:, 3:4]))(LC))
                for hh in range(LC // 1024):
                    sr.multi("tensor", [
                        (lambda bkk, hh, par: lambda e: e.transpose(
                            pb[par][:, bkk * 128:(bkk + 1) * 128],
                            qt[:, hh * 1024 + bkk * 128: hh * 1024 + (bkk + 1) * 128], id_bf[:]))(bkk, hh, par)
                        for bkk in range(8)
                    ])
                    sr.step("vector", (lambda par: lambda e: e.tensor_copy(
                        tstg[:, par * 1024:(par + 1) * 1024], pb[par][:]))(par))
                    if kind == "w1":
                        dst = w1t_s if e_ < E else st1_s
                        dbase = (e_ * DC * 128 * FE if e_ < E else 0)
                        sr.multi("gpsimd", [
                            (lambda bkk, hh, rt, par, dbase=dbase, dst=dst: lambda e: e.dma_start(
                                mkap(dst.ap(), dbase + (hh * 8 + bkk) * 128 * FE + rt * 128,
                                     [[FE, 128], [1, 128]]),
                                tstg[:, par * 1024 + bkk * 128: par * 1024 + (bkk + 1) * 128]))(bkk, hh, rt, par)
                            for bkk in range(8)
                        ], dma=True)
                    else:
                        dst = w2t_s if e_ < E else st2_s
                        dbase = (e_ * FC * 128 * D if e_ < E else 0)
                        sr.multi("gpsimd", [
                            (lambda bkk, hh, rt, par, dbase=dbase, dst=dst: lambda e: e.dma_start(
                                mkap(dst.ap(), dbase + (hh * 8 + bkk) * 128 * D + rt * 128,
                                     [[D, 128], [1, 128]]),
                                tstg[:, par * 1024 + bkk * 128: par * 1024 + (bkk + 1) * 128]))(bkk, hh, rt, par)
                            for bkk in range(8)
                        ], dma=True)
            if kind == "w1":
                sr.step("gpsimd", (lambda e_: lambda e: e.dma_start(
                    mkap(s1_s.ap(), e_ * 128 * FC, [[FC, 128], [1, FC]]), s1stg[:]))(e_), dma=True)

    with contextlib.ExitStack() as _es:
        kab = _es.enter_context(nc.sbuf_tensor("p3_k", [128, S], BF16))
        qab = _es.enter_context(nc.sbuf_tensor("p3_q", [128, S], BF16))
        v65 = _es.enter_context(nc.sbuf_tensor("p3_v", [128, 2 * 16 * 65], BF16))
        u_sb = _es.enter_context(nc.sbuf_tensor("p3_u", [128, UW], F32))
        eb = _es.enter_context(nc.sbuf_tensor("p3_eb", [128, 2 * UW], BF16))
        sqt = _es.enter_context(nc.sbuf_tensor("p3_sq", [128, S], BF16))
        exb = _es.enter_context(nc.sbuf_tensor("p3_ex", [128, 4 * 512], BF16))
        pub = _es.enter_context(nc.sbuf_tensor("p3_pu", [128, 4 * 512], BF16))
        nmx = _es.enter_context(nc.sbuf_tensor("p3_nm", [1, 16], F32))
        negm = _es.enter_context(nc.sbuf_tensor("p3_negm", [128, 1], F32))
        rcp = _es.enter_context(nc.sbuf_tensor("p3_rcp", [1, 2 * 512], F32))
        repsb = _es.enter_context(nc.sbuf_tensor("p3_rep", [64, 2 * 512], F32))
        aob = _es.enter_context(nc.sbuf_tensor("p3_ao", [64, 4 * 512], BF16))
        blk = _es.enter_context(nc.Block())
        blk = MaybeBlk(blk, 3 in phases)
        sr = Serial(sb, blk, 3 in phases)
        sr.step("vector", lambda e: e.memset(v65[:], 1.0))
        for pr in range(16):
            b_, j = pr // 8, pr % 8
            sr.multi("sync", [
                (lambda b_, j: lambda e: e.dma_start(
                    kab[:], mkap(k_s.ap(), b_ * H * HD * S + (2 * j) * HD * S, [[S, 128], [1, S]])))(b_, j),
                (lambda b_, j: lambda e: e.dma_start(
                    qab[:], mkap(q_s.ap(), b_ * H * HD * S + (2 * j) * HD * S, [[S, 128], [1, S]])))(b_, j),
                (lambda j: lambda e: e.dma_start(
                    u_sb[:], mkap(tab_s.ap(), (2 * j) * TAB, [[1, 128], [1, UW]])))(j),
            ], dma=True)
            sr.step("scalar", lambda e: e.activation(out=eb[:, 0:UW], in_=u_sb[:], func=AF.Exp))
            sr.step("sync", (lambda j: lambda e: e.dma_start(
                u_sb[:], mkap(tab_s.ap(), (2 * j + 1) * TAB, [[1, 128], [1, UW]])))(j), dma=True)
            sr.step("scalar", lambda e: e.activation(out=eb[:, UW:2 * UW], in_=u_sb[:], func=AF.Exp))
            sr.multi("sync", [
                (lambda b_, j, kc, hh: lambda e: e.dma_start(
                    v65[:, (hh * 16 + kc) * 65:(hh * 16 + kc) * 65 + 64],
                    mkap(v_s.ap(), b_ * S * D + (kc * 128) * D + (2 * j + hh) * 64, [[D, 128], [1, 64]])))(b_, j, kc, hh)
                for hh in range(2) for kc in range(16)
            ], dma=True)
            sr.step("scalar", lambda e: e.activation(out=sqt[:], in_=kab[:], func=AF.Square))
            for c in range(4):
                sr.step("tensor", (lambda c: lambda e: e.matmul(
                    pf[4][0:1, 0:512], ones_bf[:], sqt[:, c * 512:(c + 1) * 512], start=True, stop=True))(c))
                sr.step("vector", (lambda c: lambda e: e.tensor_reduce(
                    nmx[0:1, c:c + 1], pf[4][0:1, 0:512], axis=AX, op=ALU.max))(c))
            sr.step("scalar", lambda e: e.activation(out=sqt[:], in_=qab[:], func=AF.Square))
            for c in range(4):
                sr.step("tensor", (lambda c: lambda e: e.matmul(
                    pf[4][0:1, 0:512], ones_bf[:], sqt[:, c * 512:(c + 1) * 512], start=True, stop=True))(c))
                sr.step("vector", (lambda c: lambda e: e.tensor_reduce(
                    nmx[0:1, 4 + c:5 + c], pf[4][0:1, 0:512], axis=AX, op=ALU.max))(c))
            sr.step("vector", lambda e: e.tensor_reduce(nmx[0:1, 8:9], nmx[0:1, 0:4], axis=AX, op=ALU.max))
            sr.step("vector", lambda e: e.tensor_reduce(nmx[0:1, 9:10], nmx[0:1, 4:8], axis=AX, op=ALU.max))
            sr.step("vector", lambda e: e.tensor_tensor(nmx[0:1, 10:11], nmx[0:1, 8:9], nmx[0:1, 9:10], op=ALU.add))
            sr.step("vector", lambda e: e.tensor_scalar(nmx[0:1, 11:12], nmx[0:1, 10:11], -1.0 / 16.0, None, op0=ALU.mult))
            sr.step("tensor", lambda e: e.matmul(pf[4][:, 0:1], ones_fr[:], nmx[0:1, 11:12], start=True, stop=True))
            sr.step("vector", lambda e: e.tensor_copy(negm[:], pf[4][:, 0:1]))

            base = {n: sb.c[n] for n in ["pe", "ac", "dv", "pe2", "dv2", "st"]}
            serv = sb.c["ser"]
            for qc in range(4):
                for kc in range(16):
                    g = qc * 16 + kc

                    def mm_qk(e, qc=qc, kc=kc, g=g):
                        e.wait_ge(sb.sem("ser"), serv)
                        if g >= 2:
                            e.wait_ge(sb.sem("ac"), base["ac"] + g - 1)
                        if kc == 0:
                            e.wait_ge(sb.sem("dv2"), base["dv2"] + qc)
                        e.matmul(pf[kc % 2][:], kab[0:64, kc * 128:(kc + 1) * 128],
                                 qab[0:64, qc * 512:(qc + 1) * 512], start=True, stop=True,
                                 tile_position=(0, 0))
                        ins = e.matmul(pf[2 + kc % 2][:], kab[64:128, kc * 128:(kc + 1) * 128],
                                       qab[64:128, qc * 512:(qc + 1) * 512], start=True, stop=True,
                                       tile_position=(64, 0))
                        sb.inc(ins, "pe")
                    blk.tensor(mm_qk)

                    def ex_fn(e, qc=qc, kc=kc, g=g):
                        e.wait_ge(sb.sem("pe"), base["pe"] + g + 1)
                        if g >= 2:
                            e.wait_ge(sb.sem("dv"), base["dv"] + g - 1)
                        e.activation(out=exb[:, (kc % 2) * 512:(kc % 2 + 1) * 512],
                                     in_=pf[kc % 2][:], func=AF.Exp, bias=negm[:], scale=0.125)
                        ins = e.activation(out=exb[:, (2 + kc % 2) * 512:(3 + kc % 2) * 512],
                                           in_=pf[2 + kc % 2][:], func=AF.Exp, bias=negm[:], scale=0.125)
                        sb.inc(ins, "ac")
                    blk.scalar(ex_fn)

                    def mul_fn(e, qc=qc, kc=kc, g=g):
                        e.wait_ge(sb.sem("ac"), base["ac"] + g + 1)
                        if g >= 2:
                            e.wait_ge(sb.sem("pe2"), base["pe2"] + g - 1)
                        a_col = 2047 + kc * 128 - qc * 512
                        e.tensor_tensor(pub[:, (kc % 2) * 512:(kc % 2 + 1) * 512],
                                        exb[:, (kc % 2) * 512:(kc % 2 + 1) * 512],
                                        revslice(eb[:, 0:UW], a_col, 512), op=ALU.mult)
                        ins = e.tensor_tensor(pub[:, (2 + kc % 2) * 512:(3 + kc % 2) * 512],
                                              exb[:, (2 + kc % 2) * 512:(3 + kc % 2) * 512],
                                              revslice(mkap(eb[:, :], eb[:, :].offset + UW,
                                                            [eb[:, :].ap.to_list()[0], [1, UW]]),
                                                       a_col, 512), op=ALU.mult)
                        sb.inc(ins, "dv")
                    blk.vector(mul_fn)

                    def mm_av(e, qc=qc, kc=kc, g=g):
                        e.wait_ge(sb.sem("dv"), base["dv"] + g + 1)
                        e.matmul(pf[4][0:65, :], v65[:, kc * 65:kc * 65 + 65],
                                 pub[:, (kc % 2) * 512:(kc % 2 + 1) * 512],
                                 start=(kc == 0), stop=(kc == 15))
                        ins = e.matmul(pf[5][0:65, :], v65[:, (16 + kc) * 65:(16 + kc) * 65 + 65],
                                       pub[:, (2 + kc % 2) * 512:(3 + kc % 2) * 512],
                                       start=(kc == 0), stop=(kc == 15))
                        sb.inc(ins, "pe2")
                    blk.tensor(mm_av)

                def rcp_fn(e, qc=qc):
                    e.wait_ge(sb.sem("pe2"), base["pe2"] + (qc + 1) * 16)
                    e.reciprocal(rcp[0:1, 0:512], pf[4][64:65, :])
                    ins = e.reciprocal(rcp[0:1, 512:1024], pf[5][64:65, :])
                    sb.inc(ins, "dv3")
                blk.vector(rcp_fn)

                def rep_fn(e, qc=qc):
                    e.wait_ge(sb.sem("dv3"), sb.c["dv3"])
                    e.matmul(pf[0][0:64, :], ones_fr[0:1, 0:64], rcp[0:1, 0:512], start=True, stop=True)
                    ins = e.matmul(pf[2][0:64, :], ones_fr[0:1, 0:64], rcp[0:1, 512:1024], start=True, stop=True)
                    sb.inc(ins, "pe3")
                blk.tensor(rep_fn)

                def evac_fn(e, qc=qc):
                    e.wait_ge(sb.sem("pe3"), sb.c["pe3"])
                    if qc >= 2:
                        e.wait_ge(sb.sem("st"), base["st"] + (qc - 1) * 32)
                    e.tensor_copy(repsb[:, 0:512], pf[0][0:64, :])
                    e.tensor_copy(repsb[:, 512:1024], pf[2][0:64, :])
                    e.tensor_tensor(aob[:, (qc % 2) * 1024:(qc % 2) * 1024 + 512],
                                    pf[4][0:64, :], repsb[:, 0:512], op=ALU.mult)
                    ins = e.tensor_tensor(aob[:, (qc % 2) * 1024 + 512:(qc % 2 + 1) * 1024],
                                          pf[5][0:64, :], repsb[:, 512:1024], op=ALU.mult)
                    sb.inc(ins, "dv2")
                blk.vector(evac_fn)

                def st_fn(e, qc=qc, b_=b_, j=j):
                    e.wait_ge(sb.sem("dv2"), sb.c["dv2"])
                    e.dma_start(mkap(ao_s.ap(), j * 128 * T + b_ * S + qc * 512, [[T, 64], [1, 512]]),
                                aob[:, (qc % 2) * 1024:(qc % 2) * 1024 + 512]).then_inc(sb.sem("st"), 16)
                    e.dma_start(mkap(ao_s.ap(), j * 128 * T + 64 * T + b_ * S + qc * 512, [[T, 64], [1, 512]]),
                                aob[:, (qc % 2) * 1024 + 512:(qc % 2 + 1) * 1024]).then_inc(sb.sem("st"), 16)
                    sb.c["st"] += 32
                blk.gpsimd(st_fn)

            def fence(e):
                e.wait_ge(sb.sem("st"), sb.c["st"])
                ins = e.nop()
                return ins
            sr.step("sync", fence)

    with contextlib.ExitStack() as _es:
        wg_sb = _es.enter_context(nc.sbuf_tensor("p4_wg", [128, DC * 2 * D], BF16))
        wstg4 = _es.enter_context(nc.sbuf_tensor("p4_stg", [128, 2 * D], F32))
        aochunk = _es.enter_context(nc.sbuf_tensor("p4_ao", [128, DC * 512], BF16))
        a8 = _es.enter_context(nc.sbuf_tensor("p4_a8", [128, 8 * 512], F32))
        sgb = _es.enter_context(nc.sbuf_tensor("p4_sg", [128, 512], F32))
        agb = _es.enter_context(nc.sbuf_tensor("p4_ag", [128, 2 * 512], BF16))
        blk = _es.enter_context(nc.Block())
        blk = MaybeBlk(blk, 4 in phases)
        sr = Serial(sb, blk, 4 in phases)
        for dc in range(DC):
            sr.step("sync", (lambda dc: lambda e: e.dma_start(
                wstg4[:], mkap(w_gate.ap(), dc * 128 * 2 * D, [[2 * D, 128], [1, 2 * D]])))(dc), dma=True)
            sr.step("vector", (lambda dc: lambda e: e.tensor_copy(
                wg_sb[:, dc * 2 * D:(dc + 1) * 2 * D], wstg4[:]))(dc))
        for tc in range(8):
            sr.multi("sync", [
                (lambda dc, tc: lambda e: e.dma_start(
                    aochunk[:, dc * 512:(dc + 1) * 512],
                    mkap(ao_s.ap(), dc * 128 * T + tc * 512, [[T, 128], [1, 512]])))(dc, tc)
                for dc in range(DC)
            ], dma=True)
            for gc in range(16):
                par = gc % 2

                def mmg(e, gc=gc, par=par):
                    ins = None
                    for dc in range(DC):
                        ins = e.matmul(pf[par][:], wg_sb[:, dc * 2 * D + gc * 128: dc * 2 * D + (gc + 1) * 128],
                                       aochunk[:, dc * 512:(dc + 1) * 512],
                                       start=(dc == 0), stop=(dc == DC - 1))
                    return ins
                sr.step("tensor", mmg)
                if gc < 8:
                    sr.step("vector", (lambda gc, par: lambda e: e.tensor_scalar(
                        a8[:, gc * 512:(gc + 1) * 512], pf[par][:], bg_sb[:, gc:gc + 1], None,
                        op0=ALU.add))(gc, par))
                else:
                    dco = gc - 8
                    sr.step("scalar", (lambda gc, par: lambda e: e.activation(
                        out=sgb[:], in_=pf[par][:], func=AF.Sigmoid, bias=bg_sb[:, gc:gc + 1]))(gc, par))
                    sr.step("vector", (lambda dco: lambda e: e.tensor_tensor(
                        agb[:, (dco % 2) * 512:(dco % 2 + 1) * 512], a8[:, dco * 512:(dco + 1) * 512],
                        sgb[:], op=ALU.mult))(dco))
                    sr.step("gpsimd", (lambda dco, tc: lambda e: e.dma_start(
                        mkap(ag_s.ap(), dco * 128 * T + tc * 512, [[T, 128], [1, 512]]),
                        agb[:, (dco % 2) * 512:(dco % 2 + 1) * 512]))(dco, tc), dma=True)

    with contextlib.ExitStack() as _es:
        eo_acc = _es.enter_context(nc.sbuf_tensor("p5_acc", [128, DC * T], BF16))
        w1b = _es.enter_context(nc.sbuf_tensor("p5_w1", [128, DC * FE], BF16))
        w2b = _es.enter_context(nc.sbuf_tensor("p5_w2", [128, FC * D], BF16))
        hb2 = _es.enter_context(nc.sbuf_tensor("p5_h", [128, 2 * DC * 512], BF16))
        h1g = _es.enter_context(nc.sbuf_tensor("p5_h1g", [128, FC * 512], BF16))
        h1b = _es.enter_context(nc.sbuf_tensor("p5_h1", [128, 2 * 512], BF16))
        grsb = _es.enter_context(nc.sbuf_tensor("p5_gr", [128, 512], BF16))
        grow = _es.enter_context(nc.sbuf_tensor("p5_grow", [1, 512], BF16))
        s1b = _es.enter_context(nc.sbuf_tensor("p5_s1", [128, FC], F32))
        blk = _es.enter_context(nc.Block())
        blk = MaybeBlk(blk, 5 in phases)
        sr = Serial(sb, blk, 5 in phases)
        sr.step("vector", lambda e: e.memset(eo_acc[:], 0.0))
        for e_ in range(E):
            sr.multi("sync", [
                (lambda e_, dc: lambda e: e.dma_start(
                    w1b[:, dc * FE:(dc + 1) * FE],
                    mkap(w1t_s.ap(), e_ * DC * 128 * FE + dc * 128 * FE, [[FE, 128], [1, FE]])))(e_, dc)
                for dc in range(DC)
            ] + [
                (lambda e_, fc: lambda e: e.dma_start(
                    w2b[:, fc * D:(fc + 1) * D],
                    mkap(w2t_s.ap(), e_ * FC * 128 * D + fc * 128 * D, [[D, 128], [1, D]])))(e_, fc)
                for fc in range(FC)
            ] + [
                (lambda e_: lambda e: e.dma_start(
                    s1b[:], mkap(s1_s.ap(), e_ * 128 * FC, [[FC, 128], [1, FC]])))(e_)
            ], dma=True)
            for tc in range(8):
                hb = hb2[:, (tc % 2) * DC * 512:(tc % 2 + 1) * DC * 512]
                sr.multi("sync", [
                    (lambda dc, tc, hb=hb: lambda e: e.dma_start(
                        hb[:, dc * 512:(dc + 1) * 512],
                        mkap(h_s.ap(), dc * 128 * T + tc * 512, [[T, 128], [1, 512]])))(dc, tc)
                    for dc in range(DC)
                ], dma=True)
                sr.step("sync", (lambda e_, tc: lambda e: e.dma_start(
                    grow[:], mkap(gt_s.ap(), e_ * T + tc * 512, [[1, 1], [1, 512]])))(e_, tc), dma=True)
                sr.step("tensor", (lambda: lambda e: e.matmul(
                    pf[4][:], ones_bfr[:], grow[:], start=True, stop=True))())
                sr.step("vector", lambda e: e.tensor_copy(grsb[:], pf[4][:]))
                base = {n: sb.c[n] for n in ["pe", "ac", "po"]}
                serv = sb.c["ser"]
                for fc in range(FC):
                    def m1(e, fc=fc, hb=hb):
                        e.wait_ge(sb.sem("ser"), serv)
                        if fc >= 2:
                            e.wait_ge(sb.sem("ac"), base["ac"] + fc - 1)
                        ins = None
                        for dc in range(DC):
                            ins = e.matmul(pf[fc % 2][:], w1b[:, dc * FE + fc * 128: dc * FE + (fc + 1) * 128],
                                           hb[:, dc * 512:(dc + 1) * 512],
                                           start=(dc == 0), stop=(dc == DC - 1))
                        sb.inc(ins, "pe")
                    blk.tensor(m1)

                    def silu(e, fc=fc):
                        e.wait_ge(sb.sem("pe"), base["pe"] + fc + 1)
                        if fc >= 2:
                            e.wait_ge(sb.sem("po"), base["po"] + fc - 1)
                        ins = e.activation(out=h1b[:, (fc % 2) * 512:(fc % 2 + 1) * 512],
                                           in_=pf[fc % 2][:], func=AF.Silu, scale=s1b[:, fc:fc + 1])
                        sb.inc(ins, "ac")
                    blk.scalar(silu)

                    def gmul(e, fc=fc):
                        e.wait_ge(sb.sem("ac"), base["ac"] + fc + 1)
                        ins = e.tensor_tensor(h1g[:, fc * 512:(fc + 1) * 512],
                                              h1b[:, (fc % 2) * 512:(fc % 2 + 1) * 512], grsb[:], op=ALU.mult)
                        sb.inc(ins, "po")
                    blk.gpsimd(gmul)
                base2 = {n: sb.c[n] for n in ["pe2", "dv2", "po"]}
                for dc2 in range(DC):
                    def m2(e, dc2=dc2):
                        e.wait_ge(sb.sem("po"), base2["po"])
                        if dc2 >= 2:
                            e.wait_ge(sb.sem("dv2"), base2["dv2"] + dc2 - 1)
                        ins = None
                        for fc in range(FC):
                            ins = e.matmul(pf[2 + dc2 % 2][:], w2b[:, fc * D + dc2 * 128: fc * D + (dc2 + 1) * 128],
                                           h1g[:, fc * 512:(fc + 1) * 512],
                                           start=(fc == 0), stop=(fc == FC - 1))
                        sb.inc(ins, "pe2")
                    blk.tensor(m2)

                    def accadd(e, dc2=dc2, tc=tc):
                        e.wait_ge(sb.sem("pe2"), base2["pe2"] + dc2 + 1)
                        sl = eo_acc[:, dc2 * T + tc * 512: dc2 * T + (tc + 1) * 512]
                        ins = e.tensor_tensor(sl, sl, pf[2 + dc2 % 2][:], op=ALU.add)
                        sb.inc(ins, "dv2")
                    blk.vector(accadd)

                def fence(e, n=sb.c["dv2"]):
                    e.wait_ge(sb.sem("dv2"), n)
                    return e.nop()
                sr.step("sync", fence)
        sr.multi("gpsimd", [
            (lambda dc: lambda e: e.dma_start(
                mkap(eo_s.ap(), dc * 128 * T, [[T, 128], [1, T]]), eo_acc[:, dc * T:(dc + 1) * T]))(dc)
            for dc in range(DC)
        ], dma=True)

    with contextlib.ExitStack() as _es:
        sw1b = _es.enter_context(nc.sbuf_tensor("p6_w1", [128, DC * FE], BF16))
        sw2b = _es.enter_context(nc.sbuf_tensor("p6_w2", [128, FC * D], BF16))
        hb6 = _es.enter_context(nc.sbuf_tensor("p6_h", [128, DC * 512], BF16))
        h1s = _es.enter_context(nc.sbuf_tensor("p6_h1", [128, FC * 512], BF16))
        agc = _es.enter_context(nc.sbuf_tensor("p6_ag", [128, DC * 512], BF16))
        eoc = _es.enter_context(nc.sbuf_tensor("p6_eo", [128, DC * 512], BF16))
        x6 = _es.enter_context(nc.sbuf_tensor("p6_x", [128, 4 * D], F32))
        o6 = _es.enter_context(nc.sbuf_tensor("p6_o", [128, 4 * D], F32))
        csum = _es.enter_context(nc.sbuf_tensor("p6_cs", [128, 512], F32))
        csum2 = _es.enter_context(nc.sbuf_tensor("p6_cs2", [128, 512], F32))
        s1b6 = _es.enter_context(nc.sbuf_tensor("p6_s1", [128, FC], F32))
        blk = _es.enter_context(nc.Block())
        blk = MaybeBlk(blk, 6 in phases)
        sr = Serial(sb, blk, 6 in phases)
        sr.multi("sync", [
            (lambda dc: lambda e: e.dma_start(
                sw1b[:, dc * FE:(dc + 1) * FE],
                mkap(st1_s.ap(), dc * 128 * FE, [[FE, 128], [1, FE]])))(dc)
            for dc in range(DC)
        ] + [
            (lambda fc: lambda e: e.dma_start(
                sw2b[:, fc * D:(fc + 1) * D],
                mkap(st2_s.ap(), fc * 128 * D, [[D, 128], [1, D]])))(fc)
            for fc in range(FC)
        ] + [
            lambda e: e.dma_start(s1b6[:], mkap(s1_s.ap(), E * 128 * FC, [[FC, 128], [1, FC]]))
        ], dma=True)
        for tc in range(8):
            sr.multi("sync", [
                (lambda dc, tc: lambda e: e.dma_start(
                    hb6[:, dc * 512:(dc + 1) * 512],
                    mkap(h_s.ap(), dc * 128 * T + tc * 512, [[T, 128], [1, 512]])))(dc, tc)
                for dc in range(DC)
            ] + [
                (lambda dc, tc: lambda e: e.dma_start(
                    agc[:, dc * 512:(dc + 1) * 512],
                    mkap(ag_s.ap(), dc * 128 * T + tc * 512, [[T, 128], [1, 512]])))(dc, tc)
                for dc in range(DC)
            ] + [
                (lambda dc, tc: lambda e: e.dma_start(
                    eoc[:, dc * 512:(dc + 1) * 512],
                    mkap(eo_s.ap(), dc * 128 * T + tc * 512, [[T, 128], [1, 512]])))(dc, tc)
                for dc in range(DC)
            ] + [
                (lambda ts, tc: lambda e: e.dma_start(
                    x6[:, ts * D:(ts + 1) * D],
                    mkap(x_in.ap(), (tc * 512 + ts * 128) * D, [[D, 128], [1, D]])))(ts, tc)
                for ts in range(4)
            ], dma=True)
            base = {n: sb.c[n] for n in ["pe", "ac"]}
            serv = sb.c["ser"]
            for fc in range(FC):
                def m1(e, fc=fc):
                    e.wait_ge(sb.sem("ser"), serv)
                    if fc >= 2:
                        e.wait_ge(sb.sem("ac"), base["ac"] + fc - 1)
                    ins = None
                    for dc in range(DC):
                        ins = e.matmul(pf[fc % 2][:], sw1b[:, dc * FE + fc * 128: dc * FE + (fc + 1) * 128],
                                       hb6[:, dc * 512:(dc + 1) * 512],
                                       start=(dc == 0), stop=(dc == DC - 1))
                    sb.inc(ins, "pe")
                blk.tensor(m1)

                def silu(e, fc=fc):
                    e.wait_ge(sb.sem("pe"), base["pe"] + fc + 1)
                    ins = e.activation(out=h1s[:, fc * 512:(fc + 1) * 512],
                                       in_=pf[fc % 2][:], func=AF.Silu, scale=s1b6[:, fc:fc + 1])
                    sb.inc(ins, "ac")
                blk.scalar(silu)
            base2 = {"pe2": sb.c["pe2"], "dv2": sb.c["dv2"], "po": sb.c["po"],
                     "dv": sb.c["dv"], "pe": sb.c["pe"], "ac": sb.c["ac"]}
            for dc2 in range(DC):
                def m2(e, dc2=dc2):
                    e.wait_ge(sb.sem("ac"), base["ac"] + FC)
                    if dc2 >= 2:
                        e.wait_ge(sb.sem("dv"), base2["dv"] + dc2 - 1)
                    ins = None
                    for fc in range(FC):
                        ins = e.matmul(pf[2 + dc2 % 2][:], sw2b[:, fc * D + dc2 * 128: fc * D + (dc2 + 1) * 128],
                                       h1s[:, fc * 512:(fc + 1) * 512],
                                       start=(fc == 0), stop=(fc == FC - 1))
                    sb.inc(ins, "pe2")
                blk.tensor(m2)

                def comb1(e, dc2=dc2):
                    e.wait_ge(sb.sem("pe2"), base2["pe2"] + dc2 + 1)
                    if dc2 >= 1:
                        e.wait_ge(sb.sem("po"), base2["po"] + dc2)
                    ins = e.tensor_tensor(csum[:], pf[2 + dc2 % 2][:],
                                          eoc[:, dc2 * 512:(dc2 + 1) * 512], op=ALU.add)
                    sb.inc(ins, "dv")
                blk.vector(comb1)

                def comb2(e, dc2=dc2):
                    e.wait_ge(sb.sem("dv"), base2["dv"] + dc2 + 1)
                    if dc2 >= 1:
                        e.wait_ge(sb.sem("pe"), base2["pe"] + dc2)
                    ins = e.tensor_tensor(csum2[:], csum[:], agc[:, dc2 * 512:(dc2 + 1) * 512], op=ALU.add)
                    sb.inc(ins, "po")
                blk.gpsimd(comb2)

                def tpose(e, dc2=dc2):
                    e.wait_ge(sb.sem("po"), base2["po"] + dc2 + 1)
                    if dc2 >= 1:
                        e.wait_ge(sb.sem("dv2"), base2["dv2"] + dc2)
                    ins = None
                    for ts in range(4):
                        ins = e.transpose(pf[5][:, ts * 128:(ts + 1) * 128],
                                          csum2[:, ts * 128:(ts + 1) * 128], id_f32[:])
                    sb.inc(ins, "pe")
                blk.tensor(tpose)

                def oadd(e, dc2=dc2):
                    e.wait_ge(sb.sem("pe"), sb.c["pe"])
                    src = mkap(pf[5][:, :].copy() if False else pf[5][:, :],
                               pf[5][:, :].offset, [pf[5][:, :].ap.to_list()[0], [128, 4], [1, 128]])
                    dst = mkap(o6[:, :], o6[:, :].offset + dc2 * 128,
                               [o6[:, :].ap.to_list()[0], [D, 4], [1, 128]])
                    xsrc = mkap(x6[:, :], x6[:, :].offset + dc2 * 128,
                                [x6[:, :].ap.to_list()[0], [D, 4], [1, 128]])
                    ins = e.tensor_tensor(dst, src, xsrc, op=ALU.add)
                    sb.inc(ins, "dv2")
                blk.vector(oadd)
            def store(e, tc=tc, n=sb.c["dv2"]):
                e.wait_ge(sb.sem("dv2"), n)
                ins = e.dma_start(mkap(out.ap(), tc * 512 * D, [[D, 128], [128 * D, 4], [1, D]]), o6[:])
                ins.then_inc(sb.sem("out"), 16)
                sb.c["out"] += 16
            blk.sync(store)
            sr.step("sync", lambda e: e.nop(),
                    extra_waits=[("dv2", sb.c["dv2"])])

        def final(e):
            e.wait_ge(sb.sem("out"), sb.c["out"])
            return e.nop()
        sr.step("sync", final)

    nc.finalize()
    return nc


_NC_CACHE = {}


def kernel(**inputs):
    x = np.ascontiguousarray(inputs["L0_attn_x"])
    ident_ = np.eye(128, dtype=BFNP)
    rampv = np.tile(np.arange(-(TAB // 2), TAB // 2 + 1, dtype=np.float32), (112, 1))
    hsumv = np.zeros((112, 16), dtype=np.float32)
    for p_ in range(112):
        hsumv[p_, p_ // 7] = 1.0

    if "nc" not in _NC_CACHE:
        _NC_CACHE["nc"] = build(debug=False)
    nc = _NC_CACHE["nc"]

    base = dict(
        w_in=np.ascontiguousarray(inputs["L0_attn_w_in"]),
        w_gate=np.ascontiguousarray(inputs["L0_attn_w_gate"]),
        b_gate=np.ascontiguousarray(inputs["L0_attn_b_gate"]),
        amp=np.ascontiguousarray(inputs["L0_attn_amplitudes"]),
        shp=np.ascontiguousarray(inputs["L0_attn_sharpness"]),
        off=np.ascontiguousarray(inputs["L0_attn_offsets"]),
        sw1=np.ascontiguousarray(inputs["L0_ffn_shared_w1"]),
        sw2=np.ascontiguousarray(inputs["L0_ffn_shared_w2"]),
        ew1=np.ascontiguousarray(inputs["L0_ffn_expert_w1"]),
        ew2=np.ascontiguousarray(inputs["L0_ffn_expert_w2"]),
        rw=np.ascontiguousarray(inputs["L0_ffn_router_w"]),
        gam=np.ascontiguousarray(inputs["L0_ffn_ln_gamma"]),
        bet=np.ascontiguousarray(inputs["L0_ffn_ln_beta"]),
        ident=ident_, identf=np.eye(128, dtype=np.float32), ramp=rampv, hsum=hsumv,
    )
    in_maps = []
    for c in range(8):
        m = dict(base)
        m["x"] = np.ascontiguousarray(x[2 * c:2 * c + 2].reshape(T, D))
        in_maps.append(m)
    res = run_bass_kernel_spmd(nc, in_maps, core_ids=list(range(8)))
    return np.concatenate([r["out"].reshape(2, S, D) for r in res.results], axis=0)


if __name__ == "__main__":
    nc = build(debug=False)
    print("build ok")


# version 03
# speedup vs baseline: 1.2707x; 1.2707x over previous
import contextlib
import numpy as np
import ml_dtypes
import concourse.bass as bass
import concourse.mybir as mybir
from concourse.bass_utils import run_bass_kernel_spmd

F32 = mybir.dt.float32
BF16 = mybir.dt.bfloat16
AF = mybir.ActivationFunctionType
ALU = mybir.AluOpType
AX = mybir.AxisListType.X
BFNP = ml_dtypes.bfloat16

B = 2
S = 2048
D = 1024
T = B * S
H = 16
HD = 64
E = 8
FE = 2048
DC = 8
FC = 16
EPS = 1e-5
TAB = 4095
UW = 3968


def mkap(ap, offset, dims):
    a = ap.copy()
    a.offset = offset
    a.ap = type(a.ap)(dims)
    return a


def revslice(tile_ap, col_hi, n):
    dims = tile_ap.ap.to_list()
    return mkap(tile_ap, tile_ap.offset + col_hi, [dims[0], [-1, n]])


class SemBank:
    def __init__(self, nc):
        self.nc = nc
        self.h = {}
        self.c = {}

    def sem(self, n):
        if n not in self.h:
            self.h[n] = self.nc.alloc_semaphore(n)
            self.c[n] = 0
        return self.h[n]

    def inc(self, instr, n, amt=1):
        instr.then_inc(self.sem(n), amt)
        self.c[n] += amt
        return self.c[n]

    def wait(self, eng, n, v):
        if v > 0:
            eng.wait_ge(self.sem(n), v)



class MaybeBlk:
    def __init__(self, blk, en):
        self._blk, self._en = blk, en

    def __getattr__(self, k):
        f = getattr(self._blk, k)
        if k in ("tensor", "vector", "scalar", "gpsimd", "sync"):
            return (lambda fn: f(fn)) if self._en else (lambda fn: None)
        return f


class Serial:

    def __init__(self, sb, blk, enabled=True):
        self.sb = sb
        self.blk = blk
        self.enabled = enabled
        sb.sem("ser")

    def step(self, eng_name, fn, dma=False, extra_waits=()):
        if not self.enabled:
            return
        sb = self.sb
        need = sb.c["ser"]
        inc = 16 if dma else 1

        def body(eng):
            for n, v in extra_waits:
                sb.wait(eng, n, v)
            sb.wait(eng, "ser", need)
            ins = fn(eng)
            ins.then_inc(sb.sem("ser"), inc)
        getattr(self.blk, eng_name)(body)
        sb.c["ser"] += inc

    def multi(self, eng_name, fns, dma=False, extra_waits=()):
        if not self.enabled:
            return
        sb = self.sb
        need = sb.c["ser"]
        inc = 16 if dma else 1

        def body(eng):
            for n, v in extra_waits:
                sb.wait(eng, n, v)
            sb.wait(eng, "ser", need)
            for fn in fns:
                fn(eng).then_inc(sb.sem("ser"), inc)
        getattr(self.blk, eng_name)(body)
        sb.c["ser"] += inc * len(fns)


def build(debug=False, phases=(0, 1, 2, 3, 4, 5, 6)):
    nc = bass.Bass("TRN2", target_bir_lowering=False)

    x_in = nc.dram_tensor("x", [T, D], F32, kind="ExternalInput")
    w_in = nc.dram_tensor("w_in", [D, 3 * D], F32, kind="ExternalInput")
    w_gate = nc.dram_tensor("w_gate", [D, 2 * D], F32, kind="ExternalInput")
    b_gate = nc.dram_tensor("b_gate", [2 * D], F32, kind="ExternalInput")
    amp_in = nc.dram_tensor("amp", [H, 21], F32, kind="ExternalInput")
    shp_in = nc.dram_tensor("shp", [H, 21], F32, kind="ExternalInput")
    off_in = nc.dram_tensor("off", [H, 21], F32, kind="ExternalInput")
    sw1_in = nc.dram_tensor("sw1", [FE, D], F32, kind="ExternalInput")
    sw2_in = nc.dram_tensor("sw2", [D, FE], F32, kind="ExternalInput")
    ew1_in = nc.dram_tensor("ew1", [E, FE, D], F32, kind="ExternalInput")
    ew2_in = nc.dram_tensor("ew2", [E, D, FE], F32, kind="ExternalInput")
    rw_in = nc.dram_tensor("rw", [E, D], F32, kind="ExternalInput")
    gam_in = nc.dram_tensor("gam", [D], F32, kind="ExternalInput")
    bet_in = nc.dram_tensor("bet", [D], F32, kind="ExternalInput")
    ident = nc.dram_tensor("ident", [128, 128], BF16, kind="ExternalInput")
    identf = nc.dram_tensor("identf", [128, 128], F32, kind="ExternalInput")
    ramp = nc.dram_tensor("ramp", [112, TAB], F32, kind="ExternalInput")
    hsum = nc.dram_tensor("hsum", [112, 16], F32, kind="ExternalInput")

    out = nc.dram_tensor("out", [T, D], F32, kind="ExternalOutput")

    ikind = "ExternalOutput" if debug else "Internal"
    tab_s = nc.dram_tensor("tab_s", [H, TAB], F32, kind=ikind)
    k_s = nc.dram_tensor("k_s", [B, H, HD, S], BF16, kind=ikind)
    q_s = nc.dram_tensor("q_s", [B, H, HD, S], BF16, kind=ikind)
    v_s = nc.dram_tensor("v_s", [B, S, D], BF16, kind=ikind)
    h_s = nc.dram_tensor("h_s", [DC, 128, T], BF16, kind=ikind)
    ao_s = nc.dram_tensor("ao_s", [DC, 128, T], BF16, kind=ikind)
    ag_s = nc.dram_tensor("ag_s", [DC, 128, T], BF16, kind=ikind)
    gt_s = nc.dram_tensor("gt_s", [E, T], BF16, kind=ikind)
    eo_s = nc.dram_tensor("eo_s", [DC, 128, T], BF16, kind=ikind)
    w1t_s = nc.dram_tensor("w1t_s", [E, DC, 128, FE], BF16, kind=ikind)
    w2t_s = nc.dram_tensor("w2t_s", [E, FC, 128, D], BF16, kind=ikind)
    st1_s = nc.dram_tensor("st1_s", [DC, 128, FE], BF16, kind=ikind)
    st2_s = nc.dram_tensor("st2_s", [FC, 128, D], BF16, kind=ikind)
    s1_s = nc.dram_tensor("s1_s", [E + 1, 128, FC], F32, kind=ikind)

    id_bf = nc.alloc_sbuf_tensor("id_bf", [128, 128], BF16)
    id_f32 = nc.alloc_sbuf_tensor("id_f32", [128, 128], F32)
    ones_bf = nc.alloc_sbuf_tensor("ones_bf", [128, 1], BF16)
    ones_bfr = nc.alloc_sbuf_tensor("ones_bfr", [1, 128], BF16)
    ones_fr = nc.alloc_sbuf_tensor("ones_fr", [1, 128], F32)
    bg_sb = nc.alloc_sbuf_tensor("bg_sb", [128, 16], F32)
    rwt_sb = nc.alloc_sbuf_tensor("rwt_sb", [128, DC * E], F32)
    gate_fm = nc.alloc_sbuf_tensor("gate_fm", [8, T], BF16)

    pf = [nc.alloc_psum_tensor(f"pf{i}", [128, 512], F32) for i in range(6)]
    pb = [nc.alloc_psum_tensor(f"pb{i}", [128, 1024], BF16) for i in range(2)]

    sb = SemBank(nc)
    for n in ["ser", "ld", "pe", "ac", "dv", "po", "st", "pe2", "dv2", "ac2", "out", "pe3", "dv3", "fl"]:
        sb.sem(n)

    with contextlib.ExitStack() as _es:
        row_sb = _es.enter_context(nc.sbuf_tensor("p0_row", [1, D], F32))
        rw_sb = _es.enter_context(nc.sbuf_tensor("p0_rw", [8, D], F32))
        bgt_row = _es.enter_context(nc.sbuf_tensor("p0_bgt", [16, 128], F32))
        rwb_sb = _es.enter_context(nc.sbuf_tensor("p0_rwb", [8, D], BF16))
        prm = _es.enter_context(nc.sbuf_tensor("p0_prm", [112, 9], F32))
        prm2 = _es.enter_context(nc.sbuf_tensor("p0_prm2", [112, 6], F32))
        ramp_sb = _es.enter_context(nc.sbuf_tensor("p0_ramp", [112, TAB], F32))
        hsum_sb = _es.enter_context(nc.sbuf_tensor("p0_hsum", [112, 16], F32))
        d2_sb = _es.enter_context(nc.sbuf_tensor("p0_d2", [112, TAB], F32))
        eg_sb = _es.enter_context(nc.sbuf_tensor("p0_eg", [112, 3 * TAB], F32))
        tab_sb = _es.enter_context(nc.sbuf_tensor("p0_tab", [16, TAB], F32))
        blk = _es.enter_context(nc.Block())
        blk = MaybeBlk(blk, 0 in phases)
        sr = Serial(sb, blk, 0 in phases)
        sr.multi("sync", [
            lambda e: e.dma_start(id_bf[:], ident.ap()),
            lambda e: e.dma_start(id_f32[:], identf.ap()),
            lambda e: e.dma_start(ramp_sb[:], ramp.ap()),
            lambda e: e.dma_start(hsum_sb[:], hsum.ap()),
            lambda e: e.dma_start(bgt_row[:], mkap(b_gate.ap(), 0, [[128, 16], [1, 128]])),
            lambda e: e.dma_start(rw_sb[:], rw_in.ap()),
        ] + [
            (lambda gi, g, src: lambda e: e.dma_start(
                prm[:, gi * 3 + g: gi * 3 + g + 1],
                mkap(src.ap(), g * 7, [[21, 16], [1, 7], [1, 1]])))(gi, g, src)
            for gi, src in enumerate([off_in, shp_in, amp_in]) for g in range(3)
        ], dma=True)
        sr.multi("vector", [
            lambda e: e.memset(ones_bf[:], 1.0),
            lambda e: e.memset(ones_bfr[:], 1.0),
            lambda e: e.memset(ones_fr[:], 1.0),
            lambda e: e.tensor_scalar(prm2[:, 0:3], prm[:, 0:3], -1.0, None, op0=ALU.mult),
        ])
        sr.step("scalar", lambda e: e.activation(out=prm2[:, 3:6], in_=prm[:, 3:6], func=AF.Abs))
        sr.step("vector", lambda e: e.tensor_scalar(prm2[:, 3:6], prm2[:, 3:6], -1.0, None, op0=ALU.mult))

        sr.multi("tensor", [
            (lambda dc: lambda e: e.transpose(pf[3][:, dc * 8:(dc + 1) * 8],
                                              rw_sb[0:8, dc * 128:(dc + 1) * 128], id_f32[0:8, 0:8]))(dc)
            for dc in range(DC)
        ])
        sr.step("vector", lambda e: e.tensor_copy(rwt_sb[:], pf[3][:, 0:DC * E]))
        sr.step("tensor", lambda e: e.transpose(pf[3][:, 0:16], bgt_row[:], id_f32[0:16, 0:16]))
        sr.step("vector", lambda e: e.tensor_copy(bg_sb[:], pf[3][:, 0:16]))
        for g in range(3):
            sr.step("scalar", (lambda g: lambda e: e.activation(
                out=d2_sb[:], in_=ramp_sb[:], func=AF.Square, bias=prm2[:, g:g + 1]))(g))
            sr.step("scalar", (lambda g: lambda e: e.activation(
                out=eg_sb[:, g * TAB:(g + 1) * TAB], in_=d2_sb[:], func=AF.Exp,
                scale=prm2[:, 3 + g:4 + g]))(g))
            sr.step("vector", (lambda g: lambda e: e.tensor_scalar(
                eg_sb[:, g * TAB:(g + 1) * TAB], eg_sb[:, g * TAB:(g + 1) * TAB],
                prm[:, 6 + g:7 + g], None, op0=ALU.mult))(g))
        for c in range(8):
            w = 512 if c < 7 else TAB - 7 * 512
            sr.step("tensor", (lambda c, w: lambda e: [
                e.matmul(pf[c % 2][0:16, 0:w], hsum_sb[:], eg_sb[:, g * TAB + c * 512: g * TAB + c * 512 + w],
                         start=(g == 0), stop=(g == 2)) for g in range(3)][-1])(c, w))
            sr.step("scalar", (lambda c, w: lambda e: e.copy(
                tab_sb[:, c * 512:c * 512 + w], pf[c % 2][0:16, 0:w]))(c, w))
        sr.step("sync", lambda e: e.dma_start(tab_s.ap(), tab_sb[:]), dma=True)

    with contextlib.ExitStack() as _es:
        gam_rep = _es.enter_context(nc.sbuf_tensor("p1_gam", [128, D], F32))
        bet_rep = _es.enter_context(nc.sbuf_tensor("p1_bet", [128, D], F32))
        win_sb = _es.enter_context(nc.sbuf_tensor("p1_win", [128, DC * 3 * D], BF16))
        wstg = _es.enter_context(nc.sbuf_tensor("p1_wstg", [128, 3 * D], F32))
        xt2 = _es.enter_context(nc.sbuf_tensor("p1_xt", [128, 2 * D], F32))
        sq_scr = _es.enter_context(nc.sbuf_tensor("p1_scr", [128, D], BF16))
        xn_t = _es.enter_context(nc.sbuf_tensor("p1_xn", [128, D], BF16))
        xnn = _es.enter_context(nc.sbuf_tensor("p1_xnn", [128, D], F32))
        xn_f = _es.enter_context(nc.sbuf_tensor("p1_xnf", [128, D], F32))
        h1tmp = _es.enter_context(nc.sbuf_tensor("p1_ht", [128, D], F32))
        h_t = _es.enter_context(nc.sbuf_tensor("p1_hb", [128, D], BF16))
        h_f = _es.enter_context(nc.sbuf_tensor("p1_hf", [128, D], F32))
        h_fm32 = _es.enter_context(nc.sbuf_tensor("p1_hfm32", [128, DC * 512], F32))
        tny = _es.enter_context(nc.sbuf_tensor("p1_tiny", [128, 16], F32))
        xn_fm = _es.enter_context(nc.sbuf_tensor("p1_xnfm", [128, DC * 512], BF16))
        h_fm = _es.enter_context(nc.sbuf_tensor("p1_hfm", [128, DC * 512], BF16))
        pstg = _es.enter_context(nc.sbuf_tensor("p1_stg", [128, 2 * 512], BF16))
        vstg = _es.enter_context(nc.sbuf_tensor("p1_vstg", [128, 2 * 512], BF16))
        r8 = _es.enter_context(nc.sbuf_tensor("p1_r8", [128, 8], F32))
        g8 = _es.enter_context(nc.sbuf_tensor("p1_g8", [128, 8], BF16))
        blk = _es.enter_context(nc.Block())
        blk = MaybeBlk(blk, 1 in phases)
        sr = Serial(sb, blk, 1 in phases)

        def _emit_hf32(sr, tt):
            for half in range(2):
                sr.multi("tensor", [
                    (lambda dc, half: lambda e: e.transpose(
                        pf[2 + half][:, (dc % 4) * 128:(dc % 4 + 1) * 128],
                        h_f[:, dc * 128:(dc + 1) * 128], id_f32[:]))(dc, half)
                    for dc in range(half * 4, half * 4 + 4)
                ])
                sr.step("vector", (lambda tt=tt, half=half: lambda e: e.tensor_copy(
                    mkap(h_fm32[:, :], h_fm32[:, :].offset + half * 4 * 512 + tt * 128,
                         [h_fm32[:, :].ap.to_list()[0], [512, 4], [1, 128]]),
                    pf[2 + half][:, 0:512]))(tt, half))

        sr.step("sync", lambda e: e.dma_start(
            xt2[0:1, 0:D], gam_in.ap().rearrange("(one d) -> one d", one=1)), dma=True)
        for c in range(2):
            sr.step("tensor", (lambda c: lambda e: e.matmul(
                pf[c][:, 0:512], ones_fr[:], xt2[0:1, c * 512:(c + 1) * 512], start=True, stop=True))(c))
        sr.multi("vector", [
            (lambda c: lambda e: e.tensor_copy(gam_rep[:, c * 512:(c + 1) * 512], pf[c][:]))(c)
            for c in range(2)
        ])
        sr.step("sync", lambda e: e.dma_start(xt2[0:1, 0:D], bet_in.ap().rearrange("(one d) -> one d", one=1)), dma=True)
        for c in range(2):
            sr.step("tensor", (lambda c: lambda e: e.matmul(
                pf[c][:, 0:512], ones_fr[:], xt2[0:1, c * 512:(c + 1) * 512], start=True, stop=True))(c))
        sr.multi("vector", [
            (lambda c: lambda e: e.tensor_copy(bet_rep[:, c * 512:(c + 1) * 512], pf[c][:]))(c)
            for c in range(2)
        ])
        for dc in range(DC):
            sr.step("sync", (lambda dc: lambda e: e.dma_start(
                wstg[:], x_in.ap() if False else mkap(w_in.ap(), dc * 128 * 3 * D,
                                                      [[3 * D, 128], [1, 3 * D]])))(dc), dma=True)
            sr.step("vector", (lambda dc: lambda e: e.tensor_copy(
                win_sb[:, dc * 3 * D:(dc + 1) * 3 * D], wstg[:]))(dc))

        for grp in range(8):
            for tt in range(4):
                xt = xt2[:, (tt % 2) * D:((tt % 2) + 1) * D]
                r0 = (grp * 4 + tt) * 128
                sr.step("sync", (lambda r0, xt=xt: lambda e: e.dma_start(
                    xt, mkap(x_in.ap(), r0 * D, [[D, 128], [1, D]])))(r0), dma=True)
                sr.step("vector", (lambda xt=xt: lambda e: e.tensor_reduce(
                    tny[:, 0:1], xt, axis=AX, op=ALU.add))(xt))
                sr.step("scalar", (lambda xt=xt: lambda e: e.activation(
                    out=sq_scr[:], in_=xt, func=AF.Square, accum_out=tny[:, 1:2]))(xt))
                sr.multi("vector", [
                    lambda e: e.tensor_scalar(tny[:, 2:3], tny[:, 0:1], 1.0 / D, None, op0=ALU.mult),
                    lambda e: e.tensor_scalar(tny[:, 3:4], tny[:, 1:2], 1.0 / D, None, op0=ALU.mult),
                ])
                sr.step("vector", lambda e: e.tensor_tensor(tny[:, 4:5], tny[:, 2:3], tny[:, 2:3], op=ALU.mult))
                sr.step("vector", lambda e: e.tensor_tensor(tny[:, 5:6], tny[:, 3:4], tny[:, 4:5], op=ALU.subtract))
                sr.step("vector", lambda e: e.tensor_scalar(tny[:, 6:7], tny[:, 5:6], EPS, None, op0=ALU.add))
                sr.step("scalar", lambda e: e.sqrt(tny[:, 7:8], tny[:, 6:7]))
                sr.step("vector", lambda e: e.reciprocal(tny[:, 8:9], tny[:, 7:8]))
                sr.step("vector", (lambda xt=xt: lambda e: e.tensor_scalar(
                    xn_f[:], xt, tny[:, 2:3], tny[:, 8:9], op0=ALU.subtract, op1=ALU.mult))(xt))
                sr.step("vector", lambda e: e.tensor_copy(xn_t[:], xn_f[:]))
                sr.step("vector", lambda e: e.tensor_reduce(tny[:, 0:1], xn_f[:], axis=AX, op=ALU.add))
                sr.step("scalar", lambda e: e.activation(out=sq_scr[:], in_=xn_f[:], func=AF.Square,
                                                         accum_out=tny[:, 1:2]))
                sr.multi("vector", [
                    lambda e: e.tensor_scalar(tny[:, 2:3], tny[:, 0:1], 1.0 / D, None, op0=ALU.mult),
                    lambda e: e.tensor_scalar(tny[:, 3:4], tny[:, 1:2], 1.0 / D, None, op0=ALU.mult),
                ])
                sr.step("vector", lambda e: e.tensor_tensor(tny[:, 4:5], tny[:, 2:3], tny[:, 2:3], op=ALU.mult))
                sr.step("vector", lambda e: e.tensor_tensor(tny[:, 5:6], tny[:, 3:4], tny[:, 4:5], op=ALU.subtract))
                sr.step("vector", lambda e: e.tensor_scalar(tny[:, 6:7], tny[:, 5:6], EPS, None, op0=ALU.add))
                sr.step("scalar", lambda e: e.sqrt(tny[:, 7:8], tny[:, 6:7]))
                sr.step("vector", lambda e: e.reciprocal(tny[:, 8:9], tny[:, 7:8]))
                sr.step("vector", lambda e: e.tensor_scalar(
                    xnn[:], xn_f[:], tny[:, 2:3], tny[:, 8:9], op0=ALU.subtract, op1=ALU.mult))
                sr.step("gpsimd", lambda e: e.tensor_tensor(h1tmp[:], xnn[:], gam_rep[:], op=ALU.mult))
                sr.step("gpsimd", lambda e: e.tensor_tensor(h_f[:], h1tmp[:], bet_rep[:], op=ALU.add))
                sr.step("vector", lambda e: e.tensor_copy(h_t[:], h_f[:]))
                sr.multi("tensor", [
                    (lambda dc: lambda e: e.transpose(
                        pb[0][:, dc * 128:(dc + 1) * 128], xn_t[:, dc * 128:(dc + 1) * 128], id_bf[:]))(dc)
                    for dc in range(DC)
                ])
                sr.step("vector", (lambda tt=tt: lambda e: e.tensor_copy(
                    mkap(xn_fm[:, :], xn_fm[:, :].offset + tt * 128,
                         [xn_fm[:, :].ap.to_list()[0], [512, DC], [1, 128]]),
                    pb[0][:, 0:1024]))(tt))
                sr.multi("tensor", [
                    (lambda dc: lambda e: e.transpose(
                        pb[1][:, dc * 128:(dc + 1) * 128], h_t[:, dc * 128:(dc + 1) * 128], id_bf[:]))(dc)
                    for dc in range(DC)
                ])
                sr.step("vector", (lambda tt=tt: lambda e: e.tensor_copy(
                    mkap(h_fm[:, :], h_fm[:, :].offset + tt * 128, [h_fm[:, :].ap.to_list()[0], [512, DC], [1, 128]]),
                    pb[1][:, 0:1024]))(tt))
                _emit_hf32(sr, tt)
            sr.multi("gpsimd", [
                (lambda dc, grp=grp: lambda e: e.dma_start(
                    mkap(h_s.ap(), dc * 128 * T + grp * 512, [[T, 128], [1, 512]]),
                    h_fm[:, dc * 512:(dc + 1) * 512]))(dc)
                for dc in range(DC)
            ], dma=True)
            bp = {n: sb.c[n] for n in ["pe", "ac", "dv", "st", "pe2"]}
            serv1 = sb.c["ser"]
            cons_end = []
            for oc in range(24):
                par = oc % 2

                def MM(e, oc=oc, par=par, grp=grp):
                    e.wait_ge(sb.sem("ser"), serv1)
                    if oc >= 2:
                        e.wait_ge(sb.sem("ac"), bp["ac"] + oc - 1)
                    ins = None
                    for dc in range(DC):
                        ins = e.matmul(pf[par][:], win_sb[:, dc * 3 * D + oc * 128: dc * 3 * D + (oc + 1) * 128],
                                       xn_fm[:, dc * 512:(dc + 1) * 512],
                                       start=(dc == 0), stop=(dc == DC - 1))
                    sb.inc(ins, "pe")
                blk.tensor(MM)

                def EV(e, oc=oc, par=par):
                    e.wait_ge(sb.sem("pe"), bp["pe"] + oc + 1)
                    if oc >= 2:
                        ce = cons_end[oc - 2]
                        e.wait_ge(sb.sem("st"), ce["st"])
                        e.wait_ge(sb.sem("pe2"), ce["pe2"])
                    ins = e.copy(pstg[:, par * 512:(par + 1) * 512], pf[par][:])
                    sb.inc(ins, "ac")
                blk.scalar(EV)

                if oc < 8 or oc >= 16:
                    if oc < 8:
                        dstoff = (grp // 4) * H * HD * S + (oc * 2) * HD * S + ((grp % 4) * 512)
                        dst = k_s
                    else:
                        dstoff = (grp // 4) * H * HD * S + ((oc - 16) * 2) * HD * S + ((grp % 4) * 512)
                        dst = q_s

                    def ST(e, oc=oc, par=par, dst=dst, dstoff=dstoff):
                        e.wait_ge(sb.sem("ac"), bp["ac"] + oc + 1)
                        e.dma_start(mkap(dst.ap(), dstoff, [[S, 128], [1, 512]]),
                                    pstg[:, par * 512:(par + 1) * 512]).then_inc(sb.sem("st"), 16)
                        sb.c["st"] += 16
                    blk.gpsimd(ST)
                else:
                    vc = oc - 8

                    def TP(e, oc=oc, par=par):
                        e.wait_ge(sb.sem("ac"), bp["ac"] + oc + 1)
                        if oc >= 2:
                            e.wait_ge(sb.sem("dv"), cons_end[oc - 2]["dv"])
                        ins = None
                        for ts in range(4):
                            ins = e.transpose(pb[par][:, ts * 128:(ts + 1) * 128],
                                              pstg[:, par * 512 + ts * 128: par * 512 + (ts + 1) * 128], id_bf[:])
                        sb.inc(ins, "pe2")
                    blk.tensor(TP)

                    def VE(e, oc=oc, par=par):
                        e.wait_ge(sb.sem("pe2"), sb.c["pe2"])
                        if oc >= 2:
                            e.wait_ge(sb.sem("st"), cons_end[oc - 2]["st"])
                        ins = e.tensor_copy(vstg[:, par * 512:(par + 1) * 512], pb[par][:, 0:512])
                        sb.inc(ins, "dv")
                    blk.vector(VE)

                    def SV(e, vc=vc, par=par, grp=grp):
                        e.wait_ge(sb.sem("dv"), sb.c["dv"])
                        e.dma_start(mkap(v_s.ap(), (grp // 4) * S * D + (grp % 4) * 512 * D + vc * 128,
                                         [[D, 128], [128 * D, 4], [1, 128]]),
                                    mkap(vstg[:, :], vstg[:, :].offset + par * 512,
                                         [vstg[:, :].ap.to_list()[0], [128, 4], [1, 128]])).then_inc(sb.sem("st"), 16)
                        sb.c["st"] += 16
                    blk.gpsimd(SV)
                cons_end.append({"st": sb.c["st"], "pe2": sb.c["pe2"], "dv": sb.c["dv"]})
            sr.step("sync", lambda e: e.nop(), extra_waits=[
                ("pe", sb.c["pe"]), ("ac", sb.c["ac"]), ("dv", sb.c["dv"]),
                ("st", sb.c["st"]), ("pe2", sb.c["pe2"])])
            for tq in range(4):
                def rmm(e, tq=tq):
                    ins = None
                    for dc in range(DC):
                        ins = e.matmul(pf[4][:, 0:8], h_fm32[:, dc * 512 + tq * 128: dc * 512 + (tq + 1) * 128],
                                       rwt_sb[:, dc * 8:(dc + 1) * 8],
                                       start=(dc == 0), stop=(dc == DC - 1))
                    return ins
                sr.step("tensor", rmm)
                sr.step("vector", lambda e: e.tensor_reduce(tny[:, 0:1], pf[4][:, 0:8], axis=AX,
                                                            op=ALU.max, negate=True))
                sr.step("scalar", lambda e: e.activation(out=r8[:], in_=pf[4][:, 0:8], func=AF.Exp,
                                                         bias=tny[:, 0:1]))
                sr.step("vector", lambda e: e.tensor_reduce(tny[:, 1:2], r8[:], axis=AX, op=ALU.add))
                sr.step("vector", lambda e: e.reciprocal(tny[:, 2:3], tny[:, 1:2]))
                sr.step("vector", lambda e: e.tensor_reduce(tny[:, 3:4], r8[:], axis=AX, op=ALU.max))
                sr.step("vector", lambda e: e.tensor_tensor(tny[:, 4:5], tny[:, 3:4], tny[:, 2:3], op=ALU.mult))
                sr.step("vector", lambda e: e.tensor_scalar(g8[:], r8[:], tny[:, 3:4], tny[:, 4:5],
                                                            op0=ALU.is_equal, op1=ALU.mult))
                sr.step("tensor", (lambda tq=tq: lambda e: e.transpose(
                    pb[0][0:8, tq * 128:(tq + 1) * 128], g8[:], id_bf[:]))(tq))
            sr.step("vector", (lambda grp=grp: lambda e: e.tensor_copy(
                gate_fm[0:8, grp * 512:(grp + 1) * 512], pb[0][0:8, 0:512]))(grp))
        sr.step("gpsimd", lambda e: e.dma_start(gt_s.ap(), gate_fm[:]), dma=True)

    with contextlib.ExitStack() as _es:
        wr2 = _es.enter_context(nc.sbuf_tensor("p2_wr", [128, 2 * FE], F32))
        qa2 = _es.enter_context(nc.sbuf_tensor("p2_a", [128, 2 * FE], BF16))
        qb2 = _es.enter_context(nc.sbuf_tensor("p2_b", [128, 2 * FE], BF16))
        qt2 = _es.enter_context(nc.sbuf_tensor("p2_t", [128, 2 * FE], BF16))
        tstg = _es.enter_context(nc.sbuf_tensor("p2_ts", [128, 2 * 1024], BF16))
        qs = _es.enter_context(nc.sbuf_tensor("p2_s", [128, 8], F32))
        s1stg = _es.enter_context(nc.sbuf_tensor("p2_s1", [128, FC], F32))
        blk = _es.enter_context(nc.Block())
        en2 = 2 in phases
        blk = MaybeBlk(blk, en2)
        mats = []
        for e_ in range(E):
            mats.append(("w1", ew1_in, e_, FE, D))
        for e_ in range(E):
            mats.append(("w2", ew2_in, e_, D, FE))
        mats.append(("w1", sw1_in, E, FE, D))
        mats.append(("w2", sw2_in, E, D, FE))
        if en2:
            b2 = {n: sb.c[n] for n in ["ld", "dv", "ac", "dv2", "pe", "po", "st", "out"]}
            i = 0
            hidx = 0
            ns1 = 0
            tile_pe_end = []
            for kind, srcten, e_, NR, LC in mats:
                ntile = NR // 128
                for rt in range(ntile):
                    par = i % 2
                    wr = wr2[:, par * FE: par * FE + LC]
                    qsl = qs[:, par * 4: par * 4 + 4]
                    base = (e_ * NR * LC if e_ < E else 0) + rt * 128 * LC
                    nhalf = LC // 1024

                    def L(e, wr=wr, base=base, LC=LC, i=i):
                        if i >= 2:
                            e.wait_ge(sb.sem("dv"), b2["dv"] + i - 1)
                            e.wait_ge(sb.sem("ac"), b2["ac"] + i - 1)
                        e.dma_start(wr, mkap(srcten.ap(), base, [[LC, 128], [1, LC]])).then_inc(sb.sem("ld"), 16)
                        sb.c["ld"] += 16
                    blk.sync(L)

                    def R(e, wr=wr, qsl=qsl, LC=LC, i=i, kind=kind, rt=rt, ns1=ns1):
                        e.wait_ge(sb.sem("ld"), b2["ld"] + 16 * (i + 1))
                        if i >= 2:
                            e.wait_ge(sb.sem("dv2"), b2["dv2"] + i - 1)
                        if kind == "w1" and rt == 0:
                            e.wait_ge(sb.sem("out"), b2["out"] + 16 * ns1)
                        ins = e.tensor_reduce(qsl[:, 0:1], wr, axis=AX, op=ALU.add, apply_absolute_value=True)
                        sb.inc(ins, "fl")
                        e.wait_ge(sb.sem("fl"), sb.c["fl"])
                        ins = e.tensor_scalar(qsl[:, 1:2], qsl[:, 0:1], 1.0 / LC, 1e-5, op0=ALU.mult, op1=ALU.max)
                        sb.inc(ins, "fl")
                        e.wait_ge(sb.sem("fl"), sb.c["fl"])
                        e.tensor_scalar(qsl[:, 2:3], qsl[:, 1:2], -0.5, None, op0=ALU.mult)
                        ins = e.tensor_scalar(qsl[:, 3:4], qsl[:, 1:2], 0.5, None, op0=ALU.mult)
                        if kind == "w1":
                            sb.inc(ins, "fl")
                            e.wait_ge(sb.sem("fl"), sb.c["fl"])
                            ins = e.tensor_copy(s1stg[:, rt:rt + 1], qsl[:, 3:4])
                        sb.inc(ins, "dv")
                    blk.vector(R)

                    def A(e, wr=wr, qsl=qsl, LC=LC, i=i, par=par):
                        e.wait_ge(sb.sem("dv"), b2["dv"] + i + 1)
                        e.activation(out=qa2[:, par * FE:par * FE + LC], in_=wr, func=AF.Sign, bias=qsl[:, 2:3])
                        ins = e.activation(out=qb2[:, par * FE:par * FE + LC], in_=wr, func=AF.Sign,
                                           bias=qsl[:, 2:3], scale=-1.0)
                        sb.inc(ins, "ac")
                    blk.scalar(A)

                    def Dst(e, qsl=qsl, LC=LC, i=i, par=par, kind=kind):
                        e.wait_ge(sb.sem("ac"), b2["ac"] + i + 1)
                        if i >= 2:
                            e.wait_ge(sb.sem("pe"), tile_pe_end[i - 2])
                        ins = e.tensor_tensor(qt2[:, par * FE:par * FE + LC], qa2[:, par * FE:par * FE + LC],
                                              qb2[:, par * FE:par * FE + LC], op=ALU.subtract)
                        if kind == "w2":
                            sb.inc(ins, "fl")
                            e.wait_ge(sb.sem("fl"), sb.c["fl"])
                            ins = e.tensor_scalar(qt2[:, par * FE:par * FE + LC],
                                                  qt2[:, par * FE:par * FE + LC], qsl[:, 3:4], None, op0=ALU.mult)
                        sb.inc(ins, "dv2")
                    blk.vector(Dst)

                    for hh in range(nhalf):
                        hpar = hidx % 2

                        def Tst(e, i=i, par=par, hh=hh, hpar=hpar, hidx=hidx):
                            e.wait_ge(sb.sem("dv2"), b2["dv2"] + i + 1)
                            if hidx >= 2:
                                e.wait_ge(sb.sem("po"), b2["po"] + hidx - 1)
                            ins = None
                            for bkk in range(8):
                                ins = e.transpose(pb[hpar][:, bkk * 128:(bkk + 1) * 128],
                                                  qt2[:, par * FE + hh * 1024 + bkk * 128:
                                                      par * FE + hh * 1024 + (bkk + 1) * 128], id_bf[:])
                            sb.inc(ins, "pe")
                        blk.tensor(Tst)

                        def Est(e, hpar=hpar, hidx=hidx):
                            e.wait_ge(sb.sem("pe"), b2["pe"] + hidx + 1)
                            if hidx >= 2:
                                e.wait_ge(sb.sem("st"), b2["st"] + 16 * (hidx - 1))
                            ins = e.tensor_copy(tstg[:, hpar * 1024:(hpar + 1) * 1024], pb[hpar][:])
                            sb.inc(ins, "po")
                        blk.vector(Est)

                        if kind == "w1":
                            dst = w1t_s if e_ < E else st1_s
                            dbase = (e_ * DC * 128 * FE if e_ < E else 0) + hh * 8 * 128 * FE + rt * 128
                            ddims = [[FE, 128], [128 * FE, 8], [1, 128]]
                        else:
                            dst = w2t_s if e_ < E else st2_s
                            dbase = (e_ * FC * 128 * D if e_ < E else 0) + hh * 8 * 128 * D + rt * 128
                            ddims = [[D, 128], [128 * D, 8], [1, 128]]

                        def Sst(e, dst=dst, dbase=dbase, ddims=ddims, hpar=hpar, hidx=hidx):
                            e.wait_ge(sb.sem("po"), b2["po"] + hidx + 1)
                            sap = mkap(tstg[:, :], tstg[:, :].offset + hpar * 1024,
                                       [tstg[:, :].ap.to_list()[0], [128, 8], [1, 128]])
                            e.dma_start(mkap(dst.ap(), dbase, ddims), sap).then_inc(sb.sem("st"), 16)
                            sb.c["st"] += 16
                        blk.gpsimd(Sst)
                        hidx += 1
                    tile_pe_end.append(sb.c["pe"])
                    i += 1
                if kind == "w1":
                    def S1(e, e_=e_, n=sb.c["dv"]):
                        e.wait_ge(sb.sem("dv"), n)
                        e.dma_start(mkap(s1_s.ap(), e_ * 128 * FC, [[FC, 128], [1, FC]]),
                                    s1stg[:]).then_inc(sb.sem("out"), 16)
                        sb.c["out"] += 16
                    blk.sync(S1)
                    ns1 += 1
            def F(e, n=sb.c["st"], n2=sb.c["out"]):
                e.wait_ge(sb.sem("st"), n)
                e.wait_ge(sb.sem("out"), n2)
                e.nop()
            blk.sync(F)

    with contextlib.ExitStack() as _es:
        kab = _es.enter_context(nc.sbuf_tensor("p3_k", [128, S], BF16))
        qab = _es.enter_context(nc.sbuf_tensor("p3_q", [128, S], BF16))
        v65 = _es.enter_context(nc.sbuf_tensor("p3_v", [128, 2 * 16 * 65], BF16))
        u_sb = _es.enter_context(nc.sbuf_tensor("p3_u", [128, UW], F32))
        eb = _es.enter_context(nc.sbuf_tensor("p3_eb", [128, 2 * UW], BF16))
        sqt = _es.enter_context(nc.sbuf_tensor("p3_sq", [128, S], BF16))
        exb = _es.enter_context(nc.sbuf_tensor("p3_ex", [128, 4 * 512], BF16))
        pub = _es.enter_context(nc.sbuf_tensor("p3_pu", [128, 4 * 512], BF16))
        nmx = _es.enter_context(nc.sbuf_tensor("p3_nm", [1, 16], F32))
        negm = _es.enter_context(nc.sbuf_tensor("p3_negm", [128, 1], F32))
        rcp = _es.enter_context(nc.sbuf_tensor("p3_rcp", [1, 2 * 512], F32))
        repsb = _es.enter_context(nc.sbuf_tensor("p3_rep", [64, 2 * 512], F32))
        aob = _es.enter_context(nc.sbuf_tensor("p3_ao", [64, 4 * 512], BF16))
        blk = _es.enter_context(nc.Block())
        blk = MaybeBlk(blk, 3 in phases)
        sr = Serial(sb, blk, 3 in phases)
        sr.step("vector", lambda e: e.memset(v65[:], 1.0))
        for pr in range(16):
            b_, j = pr // 8, pr % 8
            sr.multi("sync", [
                (lambda b_, j: lambda e: e.dma_start(
                    kab[:], mkap(k_s.ap(), b_ * H * HD * S + (2 * j) * HD * S, [[S, 128], [1, S]])))(b_, j),
                (lambda b_, j: lambda e: e.dma_start(
                    qab[:], mkap(q_s.ap(), b_ * H * HD * S + (2 * j) * HD * S, [[S, 128], [1, S]])))(b_, j),
                (lambda j: lambda e: e.dma_start(
                    u_sb[:], mkap(tab_s.ap(), (2 * j) * TAB, [[1, 128], [1, UW]])))(j),
            ], dma=True)
            sr.step("scalar", lambda e: e.activation(out=eb[:, 0:UW], in_=u_sb[:], func=AF.Exp))
            sr.step("sync", (lambda j: lambda e: e.dma_start(
                u_sb[:], mkap(tab_s.ap(), (2 * j + 1) * TAB, [[1, 128], [1, UW]])))(j), dma=True)
            sr.step("scalar", lambda e: e.activation(out=eb[:, UW:2 * UW], in_=u_sb[:], func=AF.Exp))
            sr.multi("sync", [
                (lambda b_, j, kc, hh: lambda e: e.dma_start(
                    v65[:, (hh * 16 + kc) * 65:(hh * 16 + kc) * 65 + 64],
                    mkap(v_s.ap(), b_ * S * D + (kc * 128) * D + (2 * j + hh) * 64, [[D, 128], [1, 64]])))(b_, j, kc, hh)
                for hh in range(2) for kc in range(16)
            ], dma=True)
            sr.step("scalar", lambda e: e.activation(out=sqt[:], in_=kab[:], func=AF.Square))
            for c in range(4):
                sr.step("tensor", (lambda c: lambda e: e.matmul(
                    pf[4][0:1, 0:512], ones_bf[:], sqt[:, c * 512:(c + 1) * 512], start=True, stop=True))(c))
                sr.step("vector", (lambda c: lambda e: e.tensor_reduce(
                    nmx[0:1, c:c + 1], pf[4][0:1, 0:512], axis=AX, op=ALU.max))(c))
            sr.step("scalar", lambda e: e.activation(out=sqt[:], in_=qab[:], func=AF.Square))
            for c in range(4):
                sr.step("tensor", (lambda c: lambda e: e.matmul(
                    pf[4][0:1, 0:512], ones_bf[:], sqt[:, c * 512:(c + 1) * 512], start=True, stop=True))(c))
                sr.step("vector", (lambda c: lambda e: e.tensor_reduce(
                    nmx[0:1, 4 + c:5 + c], pf[4][0:1, 0:512], axis=AX, op=ALU.max))(c))
            sr.step("vector", lambda e: e.tensor_reduce(nmx[0:1, 8:9], nmx[0:1, 0:4], axis=AX, op=ALU.max))
            sr.step("vector", lambda e: e.tensor_reduce(nmx[0:1, 9:10], nmx[0:1, 4:8], axis=AX, op=ALU.max))
            sr.step("vector", lambda e: e.tensor_tensor(nmx[0:1, 10:11], nmx[0:1, 8:9], nmx[0:1, 9:10], op=ALU.add))
            sr.step("vector", lambda e: e.tensor_scalar(nmx[0:1, 11:12], nmx[0:1, 10:11], -1.0 / 16.0, None, op0=ALU.mult))
            sr.step("tensor", lambda e: e.matmul(pf[4][:, 0:1], ones_fr[:], nmx[0:1, 11:12], start=True, stop=True))
            sr.step("vector", lambda e: e.tensor_copy(negm[:], pf[4][:, 0:1]))

            base = {n: sb.c[n] for n in ["pe", "ac", "dv", "pe2", "dv2", "st"]}
            serv = sb.c["ser"]
            for qc in range(4):
                for kc in range(16):
                    g = qc * 16 + kc

                    def mm_qk(e, qc=qc, kc=kc, g=g):
                        e.wait_ge(sb.sem("ser"), serv)
                        if g >= 2:
                            e.wait_ge(sb.sem("ac"), base["ac"] + g - 1)
                        if kc == 0:
                            e.wait_ge(sb.sem("dv2"), base["dv2"] + qc)
                        e.matmul(pf[kc % 2][:], kab[0:64, kc * 128:(kc + 1) * 128],
                                 qab[0:64, qc * 512:(qc + 1) * 512], start=True, stop=True,
                                 tile_position=(0, 0))
                        ins = e.matmul(pf[2 + kc % 2][:], kab[64:128, kc * 128:(kc + 1) * 128],
                                       qab[64:128, qc * 512:(qc + 1) * 512], start=True, stop=True,
                                       tile_position=(64, 0))
                        sb.inc(ins, "pe")
                    blk.tensor(mm_qk)

                    def ex_fn(e, qc=qc, kc=kc, g=g):
                        e.wait_ge(sb.sem("pe"), base["pe"] + g + 1)
                        if g >= 2:
                            e.wait_ge(sb.sem("dv"), base["dv"] + g - 1)
                        e.activation(out=exb[:, (kc % 2) * 512:(kc % 2 + 1) * 512],
                                     in_=pf[kc % 2][:], func=AF.Exp, bias=negm[:], scale=0.125)
                        ins = e.activation(out=exb[:, (2 + kc % 2) * 512:(3 + kc % 2) * 512],
                                           in_=pf[2 + kc % 2][:], func=AF.Exp, bias=negm[:], scale=0.125)
                        sb.inc(ins, "ac")
                    blk.scalar(ex_fn)

                    def mul_fn(e, qc=qc, kc=kc, g=g):
                        e.wait_ge(sb.sem("ac"), base["ac"] + g + 1)
                        if g >= 2:
                            e.wait_ge(sb.sem("pe2"), base["pe2"] + g - 1)
                        a_col = 2047 + kc * 128 - qc * 512
                        e.tensor_tensor(pub[:, (kc % 2) * 512:(kc % 2 + 1) * 512],
                                        exb[:, (kc % 2) * 512:(kc % 2 + 1) * 512],
                                        revslice(eb[:, 0:UW], a_col, 512), op=ALU.mult)
                        ins = e.tensor_tensor(pub[:, (2 + kc % 2) * 512:(3 + kc % 2) * 512],
                                              exb[:, (2 + kc % 2) * 512:(3 + kc % 2) * 512],
                                              revslice(mkap(eb[:, :], eb[:, :].offset + UW,
                                                            [eb[:, :].ap.to_list()[0], [1, UW]]),
                                                       a_col, 512), op=ALU.mult)
                        sb.inc(ins, "dv")
                    blk.vector(mul_fn)

                    def mm_av(e, qc=qc, kc=kc, g=g):
                        e.wait_ge(sb.sem("dv"), base["dv"] + g + 1)
                        e.matmul(pf[4][0:65, :], v65[:, kc * 65:kc * 65 + 65],
                                 pub[:, (kc % 2) * 512:(kc % 2 + 1) * 512],
                                 start=(kc == 0), stop=(kc == 15))
                        ins = e.matmul(pf[5][0:65, :], v65[:, (16 + kc) * 65:(16 + kc) * 65 + 65],
                                       pub[:, (2 + kc % 2) * 512:(3 + kc % 2) * 512],
                                       start=(kc == 0), stop=(kc == 15))
                        sb.inc(ins, "pe2")
                    blk.tensor(mm_av)

                def rcp_fn(e, qc=qc):
                    e.wait_ge(sb.sem("pe2"), base["pe2"] + (qc + 1) * 16)
                    e.reciprocal(rcp[0:1, 0:512], pf[4][64:65, :])
                    ins = e.reciprocal(rcp[0:1, 512:1024], pf[5][64:65, :])
                    sb.inc(ins, "dv3")
                blk.vector(rcp_fn)

                def rep_fn(e, qc=qc):
                    e.wait_ge(sb.sem("dv3"), sb.c["dv3"])
                    e.matmul(pf[0][0:64, :], ones_fr[0:1, 0:64], rcp[0:1, 0:512], start=True, stop=True)
                    ins = e.matmul(pf[2][0:64, :], ones_fr[0:1, 0:64], rcp[0:1, 512:1024], start=True, stop=True)
                    sb.inc(ins, "pe3")
                blk.tensor(rep_fn)

                def evac_fn(e, qc=qc):
                    e.wait_ge(sb.sem("pe3"), sb.c["pe3"])
                    if qc >= 2:
                        e.wait_ge(sb.sem("st"), base["st"] + (qc - 1) * 32)
                    e.tensor_copy(repsb[:, 0:512], pf[0][0:64, :])
                    e.tensor_copy(repsb[:, 512:1024], pf[2][0:64, :])
                    e.tensor_tensor(aob[:, (qc % 2) * 1024:(qc % 2) * 1024 + 512],
                                    pf[4][0:64, :], repsb[:, 0:512], op=ALU.mult)
                    ins = e.tensor_tensor(aob[:, (qc % 2) * 1024 + 512:(qc % 2 + 1) * 1024],
                                          pf[5][0:64, :], repsb[:, 512:1024], op=ALU.mult)
                    sb.inc(ins, "dv2")
                blk.vector(evac_fn)

                def st_fn(e, qc=qc, b_=b_, j=j):
                    e.wait_ge(sb.sem("dv2"), sb.c["dv2"])
                    e.dma_start(mkap(ao_s.ap(), j * 128 * T + b_ * S + qc * 512, [[T, 64], [1, 512]]),
                                aob[:, (qc % 2) * 1024:(qc % 2) * 1024 + 512]).then_inc(sb.sem("st"), 16)
                    e.dma_start(mkap(ao_s.ap(), j * 128 * T + 64 * T + b_ * S + qc * 512, [[T, 64], [1, 512]]),
                                aob[:, (qc % 2) * 1024 + 512:(qc % 2 + 1) * 1024]).then_inc(sb.sem("st"), 16)
                    sb.c["st"] += 32
                blk.gpsimd(st_fn)

            def fence(e):
                e.wait_ge(sb.sem("st"), sb.c["st"])
                ins = e.nop()
                return ins
            sr.step("sync", fence)

    with contextlib.ExitStack() as _es:
        wg_sb = _es.enter_context(nc.sbuf_tensor("p4_wg", [128, DC * 2 * D], BF16))
        wstg4 = _es.enter_context(nc.sbuf_tensor("p4_stg", [128, 2 * D], F32))
        aochunk = _es.enter_context(nc.sbuf_tensor("p4_ao", [128, DC * 512], BF16))
        a8 = _es.enter_context(nc.sbuf_tensor("p4_a8", [128, 8 * 512], F32))
        sgb = _es.enter_context(nc.sbuf_tensor("p4_sg", [128, 512], F32))
        agb = _es.enter_context(nc.sbuf_tensor("p4_ag", [128, 2 * 512], BF16))
        blk = _es.enter_context(nc.Block())
        blk = MaybeBlk(blk, 4 in phases)
        sr = Serial(sb, blk, 4 in phases)
        for dc in range(DC):
            sr.step("sync", (lambda dc: lambda e: e.dma_start(
                wstg4[:], mkap(w_gate.ap(), dc * 128 * 2 * D, [[2 * D, 128], [1, 2 * D]])))(dc), dma=True)
            sr.step("vector", (lambda dc: lambda e: e.tensor_copy(
                wg_sb[:, dc * 2 * D:(dc + 1) * 2 * D], wstg4[:]))(dc))
        for tc in range(8):
            sr.multi("sync", [
                (lambda dc, tc: lambda e: e.dma_start(
                    aochunk[:, dc * 512:(dc + 1) * 512],
                    mkap(ao_s.ap(), dc * 128 * T + tc * 512, [[T, 128], [1, 512]])))(dc, tc)
                for dc in range(DC)
            ], dma=True)
            for gc in range(16):
                par = gc % 2

                def mmg(e, gc=gc, par=par):
                    ins = None
                    for dc in range(DC):
                        ins = e.matmul(pf[par][:], wg_sb[:, dc * 2 * D + gc * 128: dc * 2 * D + (gc + 1) * 128],
                                       aochunk[:, dc * 512:(dc + 1) * 512],
                                       start=(dc == 0), stop=(dc == DC - 1))
                    return ins
                sr.step("tensor", mmg)
                if gc < 8:
                    sr.step("vector", (lambda gc, par: lambda e: e.tensor_scalar(
                        a8[:, gc * 512:(gc + 1) * 512], pf[par][:], bg_sb[:, gc:gc + 1], None,
                        op0=ALU.add))(gc, par))
                else:
                    dco = gc - 8
                    sr.step("scalar", (lambda gc, par: lambda e: e.activation(
                        out=sgb[:], in_=pf[par][:], func=AF.Sigmoid, bias=bg_sb[:, gc:gc + 1]))(gc, par))
                    sr.step("vector", (lambda dco: lambda e: e.tensor_tensor(
                        agb[:, (dco % 2) * 512:(dco % 2 + 1) * 512], a8[:, dco * 512:(dco + 1) * 512],
                        sgb[:], op=ALU.mult))(dco))
                    sr.step("gpsimd", (lambda dco, tc: lambda e: e.dma_start(
                        mkap(ag_s.ap(), dco * 128 * T + tc * 512, [[T, 128], [1, 512]]),
                        agb[:, (dco % 2) * 512:(dco % 2 + 1) * 512]))(dco, tc), dma=True)

    with contextlib.ExitStack() as _es:
        eo_acc = _es.enter_context(nc.sbuf_tensor("p5_acc", [128, DC * T], BF16))
        w1b = _es.enter_context(nc.sbuf_tensor("p5_w1", [128, DC * FE], BF16))
        w2b = _es.enter_context(nc.sbuf_tensor("p5_w2", [128, FC * D], BF16))
        hb2 = _es.enter_context(nc.sbuf_tensor("p5_h", [128, 2 * DC * 512], BF16))
        h1g2 = _es.enter_context(nc.sbuf_tensor("p5_h1g", [128, 2 * FC * 512], BF16))
        h1b = _es.enter_context(nc.sbuf_tensor("p5_h1", [128, 2 * 512], BF16))
        grsb2 = _es.enter_context(nc.sbuf_tensor("p5_gr", [128, 2 * 512], BF16))
        grow2 = _es.enter_context(nc.sbuf_tensor("p5_grow", [1, 2 * 512], BF16))
        s1b = _es.enter_context(nc.sbuf_tensor("p5_s1", [128, FC], F32))
        blk = _es.enter_context(nc.Block())
        en5 = 5 in phases
        blk = MaybeBlk(blk, en5)
        sr = Serial(sb, blk, en5)
        sr.step("vector", lambda e: e.memset(eo_acc[:], 0.0))
        if en5:
            gi = 0
            ld_end = []
            m1_end = []
            silu_end = []
            gmul_end = []
            m2_end = []
            gc_end = []
            for e_ in range(E):
                sr.multi("sync", [
                    (lambda e_, dc: lambda e: e.dma_start(
                        w1b[:, dc * FE:(dc + 1) * FE],
                        mkap(w1t_s.ap(), e_ * DC * 128 * FE + dc * 128 * FE, [[FE, 128], [1, FE]])))(e_, dc)
                    for dc in range(DC)
                ] + [
                    (lambda e_, fc: lambda e: e.dma_start(
                        w2b[:, fc * D:(fc + 1) * D],
                        mkap(w2t_s.ap(), e_ * FC * 128 * D + fc * 128 * D, [[D, 128], [1, D]])))(e_, fc)
                    for fc in range(FC)
                ] + [
                    (lambda e_: lambda e: e.dma_start(
                        s1b[:], mkap(s1_s.ap(), e_ * 128 * FC, [[FC, 128], [1, FC]])))(e_)
                ], dma=True, extra_waits=[("pe", sb.c["pe"]), ("pe2", sb.c["pe2"]),
                                          ("ac", sb.c["ac"])])
                serv5 = sb.c["ser"]
                for tc in range(8):
                    par = gi % 2
                    hb = hb2[:, par * DC * 512:(par + 1) * DC * 512]
                    h1g = h1g2[:, par * FC * 512:(par + 1) * FC * 512]
                    grsb = grsb2[:, par * 512:(par + 1) * 512]
                    grow = grow2[:, par * 512:(par + 1) * 512]

                    def LH(e, e_=e_, tc=tc, hb=hb, grow=grow, gi=gi, serv5=serv5):
                        e.wait_ge(sb.sem("ser"), serv5)
                        if gi >= 2:
                            e.wait_ge(sb.sem("pe"), m1_end[gi - 2])
                        for dc in range(DC):
                            e.dma_start(hb[:, dc * 512:(dc + 1) * 512],
                                        mkap(h_s.ap(), dc * 128 * T + tc * 512,
                                             [[T, 128], [1, 512]])).then_inc(sb.sem("ld"), 16)
                            sb.c["ld"] += 16
                        e.dma_start(grow, mkap(gt_s.ap(), e_ * T + tc * 512,
                                               [[1, 1], [1, 512]])).then_inc(sb.sem("ld"), 16)
                        sb.c["ld"] += 16
                    blk.sync(LH)
                    ld_end.append(sb.c["ld"])

                    def GR(e, grow=grow, gi=gi, serv5=serv5):
                        e.wait_ge(sb.sem("ser"), serv5)
                        e.wait_ge(sb.sem("ld"), ld_end[gi])
                        if gi >= 1:
                            e.wait_ge(sb.sem("dv"), gc_end[gi - 1])
                        ins = e.matmul(pf[4][:], ones_bfr[:], grow, start=True, stop=True)
                        sb.inc(ins, "pe3")
                    blk.tensor(GR)

                    def GC(e, grsb=grsb, gi=gi):
                        e.wait_ge(sb.sem("pe3"), sb.c["pe3"])
                        if gi >= 2:
                            e.wait_ge(sb.sem("po"), gmul_end[gi - 2])
                        ins = e.tensor_copy(grsb, pf[4][:])
                        sb.inc(ins, "dv")
                    blk.vector(GC)
                    gc_end.append(sb.c["dv"])

                    bac = sb.c["ac"]
                    bpo = sb.c["po"]
                    bpe = sb.c["pe"]
                    for fc in range(FC):
                        def m1(e, fc=fc, hb=hb, gi=gi, bac=bac):
                            e.wait_ge(sb.sem("ld"), ld_end[gi])
                            if fc >= 2:
                                e.wait_ge(sb.sem("ac"), bac + fc - 1)
                            ins = None
                            for dc in range(DC):
                                ins = e.matmul(pf[fc % 2][:], w1b[:, dc * FE + fc * 128: dc * FE + (fc + 1) * 128],
                                               hb[:, dc * 512:(dc + 1) * 512],
                                               start=(dc == 0), stop=(dc == DC - 1))
                            sb.inc(ins, "pe")
                        blk.tensor(m1)

                        def silu(e, fc=fc, bpe=bpe, bpo=bpo):
                            e.wait_ge(sb.sem("pe"), bpe + fc + 1)
                            if fc >= 2:
                                e.wait_ge(sb.sem("po"), bpo + fc - 1)
                            ins = e.activation(out=h1b[:, (fc % 2) * 512:(fc % 2 + 1) * 512],
                                               in_=pf[fc % 2][:], func=AF.Silu, scale=s1b[:, fc:fc + 1])
                            sb.inc(ins, "ac")
                        blk.scalar(silu)

                        def gmul(e, fc=fc, h1g=h1g, gi=gi, bac=bac):
                            e.wait_ge(sb.sem("ac"), bac + fc + 1)
                            e.wait_ge(sb.sem("dv"), gc_end[gi])
                            if gi >= 2 and fc == 0:
                                e.wait_ge(sb.sem("pe2"), m2_end[gi - 2])
                            ins = e.tensor_tensor(h1g[:, fc * 512:(fc + 1) * 512],
                                                  h1b[:, (fc % 2) * 512:(fc % 2 + 1) * 512], grsb, op=ALU.mult)
                            sb.inc(ins, "po")
                        blk.gpsimd(gmul)
                    m1_end.append(sb.c["pe"])
                    silu_end.append(sb.c["ac"])
                    gmul_end.append(sb.c["po"])

                    bpe2 = sb.c["pe2"]
                    bdv2 = sb.c["dv2"]
                    for dc2 in range(DC):
                        def m2(e, dc2=dc2, h1g=h1g, gi=gi, bpe2=bpe2, bdv2=bdv2):
                            e.wait_ge(sb.sem("po"), gmul_end[gi])
                            if dc2 >= 2:
                                e.wait_ge(sb.sem("dv2"), bdv2 + dc2 - 1)
                            ins = None
                            for fc in range(FC):
                                ins = e.matmul(pf[2 + dc2 % 2][:], w2b[:, fc * D + dc2 * 128: fc * D + (dc2 + 1) * 128],
                                               h1g[:, fc * 512:(fc + 1) * 512],
                                               start=(fc == 0), stop=(fc == FC - 1))
                            sb.inc(ins, "pe2")
                        blk.tensor(m2)

                        def accadd(e, dc2=dc2, tc=tc, bpe2=bpe2):
                            e.wait_ge(sb.sem("pe2"), bpe2 + dc2 + 1)
                            sl = eo_acc[:, dc2 * T + tc * 512: dc2 * T + (tc + 1) * 512]
                            ins = e.tensor_tensor(sl, sl, pf[2 + dc2 % 2][:], op=ALU.add)
                            sb.inc(ins, "dv2")
                        blk.vector(accadd)
                    m2_end.append(sb.c["pe2"])
                    gi += 1
            sr.step("sync", lambda e: e.nop(), extra_waits=[("dv2", sb.c["dv2"])])
        sr.multi("gpsimd", [
            (lambda dc: lambda e: e.dma_start(
                mkap(eo_s.ap(), dc * 128 * T, [[T, 128], [1, T]]), eo_acc[:, dc * T:(dc + 1) * T]))(dc)
            for dc in range(DC)
        ], dma=True)

    with contextlib.ExitStack() as _es:
        sw1b = _es.enter_context(nc.sbuf_tensor("p6_w1", [128, DC * FE], BF16))
        sw2b = _es.enter_context(nc.sbuf_tensor("p6_w2", [128, FC * D], BF16))
        hb6 = _es.enter_context(nc.sbuf_tensor("p6_h", [128, DC * 512], BF16))
        h1s = _es.enter_context(nc.sbuf_tensor("p6_h1", [128, FC * 512], BF16))
        agc = _es.enter_context(nc.sbuf_tensor("p6_ag", [128, DC * 512], BF16))
        eoc = _es.enter_context(nc.sbuf_tensor("p6_eo", [128, DC * 512], BF16))
        x6 = _es.enter_context(nc.sbuf_tensor("p6_x", [128, 4 * D], F32))
        o6 = _es.enter_context(nc.sbuf_tensor("p6_o", [128, 4 * D], F32))
        csum = _es.enter_context(nc.sbuf_tensor("p6_cs", [128, 512], F32))
        csum2 = _es.enter_context(nc.sbuf_tensor("p6_cs2", [128, 512], F32))
        s1b6 = _es.enter_context(nc.sbuf_tensor("p6_s1", [128, FC], F32))
        blk = _es.enter_context(nc.Block())
        blk = MaybeBlk(blk, 6 in phases)
        sr = Serial(sb, blk, 6 in phases)
        sr.multi("sync", [
            (lambda dc: lambda e: e.dma_start(
                sw1b[:, dc * FE:(dc + 1) * FE],
                mkap(st1_s.ap(), dc * 128 * FE, [[FE, 128], [1, FE]])))(dc)
            for dc in range(DC)
        ] + [
            (lambda fc: lambda e: e.dma_start(
                sw2b[:, fc * D:(fc + 1) * D],
                mkap(st2_s.ap(), fc * 128 * D, [[D, 128], [1, D]])))(fc)
            for fc in range(FC)
        ] + [
            lambda e: e.dma_start(s1b6[:], mkap(s1_s.ap(), E * 128 * FC, [[FC, 128], [1, FC]]))
        ], dma=True)
        for tc in range(8):
            sr.multi("sync", [
                (lambda dc, tc: lambda e: e.dma_start(
                    hb6[:, dc * 512:(dc + 1) * 512],
                    mkap(h_s.ap(), dc * 128 * T + tc * 512, [[T, 128], [1, 512]])))(dc, tc)
                for dc in range(DC)
            ] + [
                (lambda dc, tc: lambda e: e.dma_start(
                    agc[:, dc * 512:(dc + 1) * 512],
                    mkap(ag_s.ap(), dc * 128 * T + tc * 512, [[T, 128], [1, 512]])))(dc, tc)
                for dc in range(DC)
            ] + [
                (lambda dc, tc: lambda e: e.dma_start(
                    eoc[:, dc * 512:(dc + 1) * 512],
                    mkap(eo_s.ap(), dc * 128 * T + tc * 512, [[T, 128], [1, 512]])))(dc, tc)
                for dc in range(DC)
            ] + [
                (lambda ts, tc: lambda e: e.dma_start(
                    x6[:, ts * D:(ts + 1) * D],
                    mkap(x_in.ap(), (tc * 512 + ts * 128) * D, [[D, 128], [1, D]])))(ts, tc)
                for ts in range(4)
            ], dma=True)
            base = {n: sb.c[n] for n in ["pe", "ac"]}
            serv = sb.c["ser"]
            for fc in range(FC):
                def m1(e, fc=fc):
                    e.wait_ge(sb.sem("ser"), serv)
                    if fc >= 2:
                        e.wait_ge(sb.sem("ac"), base["ac"] + fc - 1)
                    ins = None
                    for dc in range(DC):
                        ins = e.matmul(pf[fc % 2][:], sw1b[:, dc * FE + fc * 128: dc * FE + (fc + 1) * 128],
                                       hb6[:, dc * 512:(dc + 1) * 512],
                                       start=(dc == 0), stop=(dc == DC - 1))
                    sb.inc(ins, "pe")
                blk.tensor(m1)

                def silu(e, fc=fc):
                    e.wait_ge(sb.sem("pe"), base["pe"] + fc + 1)
                    ins = e.activation(out=h1s[:, fc * 512:(fc + 1) * 512],
                                       in_=pf[fc % 2][:], func=AF.Silu, scale=s1b6[:, fc:fc + 1])
                    sb.inc(ins, "ac")
                blk.scalar(silu)
            base2 = {"pe2": sb.c["pe2"], "dv2": sb.c["dv2"], "po": sb.c["po"],
                     "dv": sb.c["dv"], "pe": sb.c["pe"], "ac": sb.c["ac"]}
            for dc2 in range(DC):
                def m2(e, dc2=dc2):
                    e.wait_ge(sb.sem("ac"), base["ac"] + FC)
                    if dc2 >= 2:
                        e.wait_ge(sb.sem("dv"), base2["dv"] + dc2 - 1)
                    ins = None
                    for fc in range(FC):
                        ins = e.matmul(pf[2 + dc2 % 2][:], sw2b[:, fc * D + dc2 * 128: fc * D + (dc2 + 1) * 128],
                                       h1s[:, fc * 512:(fc + 1) * 512],
                                       start=(fc == 0), stop=(fc == FC - 1))
                    sb.inc(ins, "pe2")
                blk.tensor(m2)

                def comb1(e, dc2=dc2):
                    e.wait_ge(sb.sem("pe2"), base2["pe2"] + dc2 + 1)
                    if dc2 >= 1:
                        e.wait_ge(sb.sem("po"), base2["po"] + dc2)
                    ins = e.tensor_tensor(csum[:], pf[2 + dc2 % 2][:],
                                          eoc[:, dc2 * 512:(dc2 + 1) * 512], op=ALU.add)
                    sb.inc(ins, "dv")
                blk.vector(comb1)

                def comb2(e, dc2=dc2):
                    e.wait_ge(sb.sem("dv"), base2["dv"] + dc2 + 1)
                    if dc2 >= 1:
                        e.wait_ge(sb.sem("pe"), base2["pe"] + dc2)
                    ins = e.tensor_tensor(csum2[:], csum[:], agc[:, dc2 * 512:(dc2 + 1) * 512], op=ALU.add)
                    sb.inc(ins, "po")
                blk.gpsimd(comb2)

                def tpose(e, dc2=dc2):
                    e.wait_ge(sb.sem("po"), base2["po"] + dc2 + 1)
                    if dc2 >= 1:
                        e.wait_ge(sb.sem("dv2"), base2["dv2"] + dc2)
                    ins = None
                    for ts in range(4):
                        ins = e.transpose(pf[5][:, ts * 128:(ts + 1) * 128],
                                          csum2[:, ts * 128:(ts + 1) * 128], id_f32[:])
                    sb.inc(ins, "pe")
                blk.tensor(tpose)

                def oadd(e, dc2=dc2):
                    e.wait_ge(sb.sem("pe"), sb.c["pe"])
                    src = mkap(pf[5][:, :].copy() if False else pf[5][:, :],
                               pf[5][:, :].offset, [pf[5][:, :].ap.to_list()[0], [128, 4], [1, 128]])
                    dst = mkap(o6[:, :], o6[:, :].offset + dc2 * 128,
                               [o6[:, :].ap.to_list()[0], [D, 4], [1, 128]])
                    xsrc = mkap(x6[:, :], x6[:, :].offset + dc2 * 128,
                                [x6[:, :].ap.to_list()[0], [D, 4], [1, 128]])
                    ins = e.tensor_tensor(dst, src, xsrc, op=ALU.add)
                    sb.inc(ins, "dv2")
                blk.vector(oadd)
            def store(e, tc=tc, n=sb.c["dv2"]):
                e.wait_ge(sb.sem("dv2"), n)
                ins = e.dma_start(mkap(out.ap(), tc * 512 * D, [[D, 128], [128 * D, 4], [1, D]]), o6[:])
                ins.then_inc(sb.sem("out"), 16)
                sb.c["out"] += 16
            blk.sync(store)
            sr.step("sync", lambda e: e.nop(),
                    extra_waits=[("dv2", sb.c["dv2"])])

        def final(e):
            e.wait_ge(sb.sem("out"), sb.c["out"])
            return e.nop()
        sr.step("sync", final)

    nc.finalize()
    return nc


_NC_CACHE = {}


def kernel(**inputs):
    x = np.ascontiguousarray(inputs["L0_attn_x"])
    ident_ = np.eye(128, dtype=BFNP)
    rampv = np.tile(np.arange(-(TAB // 2), TAB // 2 + 1, dtype=np.float32), (112, 1))
    hsumv = np.zeros((112, 16), dtype=np.float32)
    for p_ in range(112):
        hsumv[p_, p_ // 7] = 1.0

    if "nc" not in _NC_CACHE:
        _NC_CACHE["nc"] = build(debug=False)
    nc = _NC_CACHE["nc"]

    base = dict(
        w_in=np.ascontiguousarray(inputs["L0_attn_w_in"]),
        w_gate=np.ascontiguousarray(inputs["L0_attn_w_gate"]),
        b_gate=np.ascontiguousarray(inputs["L0_attn_b_gate"]),
        amp=np.ascontiguousarray(inputs["L0_attn_amplitudes"]),
        shp=np.ascontiguousarray(inputs["L0_attn_sharpness"]),
        off=np.ascontiguousarray(inputs["L0_attn_offsets"]),
        sw1=np.ascontiguousarray(inputs["L0_ffn_shared_w1"]),
        sw2=np.ascontiguousarray(inputs["L0_ffn_shared_w2"]),
        ew1=np.ascontiguousarray(inputs["L0_ffn_expert_w1"]),
        ew2=np.ascontiguousarray(inputs["L0_ffn_expert_w2"]),
        rw=np.ascontiguousarray(inputs["L0_ffn_router_w"]),
        gam=np.ascontiguousarray(inputs["L0_ffn_ln_gamma"]),
        bet=np.ascontiguousarray(inputs["L0_ffn_ln_beta"]),
        ident=ident_, identf=np.eye(128, dtype=np.float32), ramp=rampv, hsum=hsumv,
    )
    in_maps = []
    for c in range(8):
        m = dict(base)
        m["x"] = np.ascontiguousarray(x[2 * c:2 * c + 2].reshape(T, D))
        in_maps.append(m)
    res = run_bass_kernel_spmd(nc, in_maps, core_ids=list(range(8)))
    return np.concatenate([r["out"].reshape(2, S, D) for r in res.results], axis=0)


if __name__ == "__main__":
    nc = build(debug=False)
    print("build ok")


# version 04
# speedup vs baseline: 1.3418x; 1.0560x over previous
import contextlib
import numpy as np
import ml_dtypes
import concourse.bass as bass
import concourse.mybir as mybir
from concourse.bass_utils import run_bass_kernel_spmd

F32 = mybir.dt.float32
BF16 = mybir.dt.bfloat16
AF = mybir.ActivationFunctionType
ALU = mybir.AluOpType
AX = mybir.AxisListType.X
BFNP = ml_dtypes.bfloat16

B = 2
S = 2048
D = 1024
T = B * S
H = 16
HD = 64
E = 8
FE = 2048
DC = 8
FC = 16
EPS = 1e-5
TAB = 4095
UW = 3968


def mkap(ap, offset, dims):
    a = ap.copy()
    a.offset = offset
    a.ap = type(a.ap)(dims)
    return a


def revslice(tile_ap, col_hi, n):
    dims = tile_ap.ap.to_list()
    return mkap(tile_ap, tile_ap.offset + col_hi, [dims[0], [-1, n]])


class SemBank:
    def __init__(self, nc):
        self.nc = nc
        self.h = {}
        self.c = {}

    def sem(self, n):
        if n not in self.h:
            self.h[n] = self.nc.alloc_semaphore(n)
            self.c[n] = 0
        return self.h[n]

    def inc(self, instr, n, amt=1):
        instr.then_inc(self.sem(n), amt)
        self.c[n] += amt
        return self.c[n]

    def wait(self, eng, n, v):
        if v > 0:
            eng.wait_ge(self.sem(n), v)



class MaybeBlk:
    def __init__(self, blk, en):
        self._blk, self._en = blk, en

    def __getattr__(self, k):
        f = getattr(self._blk, k)
        if k in ("tensor", "vector", "scalar", "gpsimd", "sync"):
            return (lambda fn: f(fn)) if self._en else (lambda fn: None)
        return f


class Serial:

    def __init__(self, sb, blk, enabled=True):
        self.sb = sb
        self.blk = blk
        self.enabled = enabled
        sb.sem("ser")

    def step(self, eng_name, fn, dma=False, extra_waits=()):
        if not self.enabled:
            return
        sb = self.sb
        need = sb.c["ser"]
        inc = 16 if dma else 1

        def body(eng):
            for n, v in extra_waits:
                sb.wait(eng, n, v)
            sb.wait(eng, "ser", need)
            ins = fn(eng)
            ins.then_inc(sb.sem("ser"), inc)
        getattr(self.blk, eng_name)(body)
        sb.c["ser"] += inc

    def multi(self, eng_name, fns, dma=False, extra_waits=()):
        if not self.enabled:
            return
        sb = self.sb
        need = sb.c["ser"]
        inc = 16 if dma else 1

        def body(eng):
            for n, v in extra_waits:
                sb.wait(eng, n, v)
            sb.wait(eng, "ser", need)
            for fn in fns:
                fn(eng).then_inc(sb.sem("ser"), inc)
        getattr(self.blk, eng_name)(body)
        sb.c["ser"] += inc * len(fns)


def build(debug=False, phases=(0, 1, 2, 3, 4, 5, 6)):
    nc = bass.Bass("TRN2", target_bir_lowering=False)

    x_in = nc.dram_tensor("x", [T, D], F32, kind="ExternalInput")
    w_in = nc.dram_tensor("w_in", [D, 3 * D], F32, kind="ExternalInput")
    w_gate = nc.dram_tensor("w_gate", [D, 2 * D], F32, kind="ExternalInput")
    b_gate = nc.dram_tensor("b_gate", [2 * D], F32, kind="ExternalInput")
    amp_in = nc.dram_tensor("amp", [H, 21], F32, kind="ExternalInput")
    shp_in = nc.dram_tensor("shp", [H, 21], F32, kind="ExternalInput")
    off_in = nc.dram_tensor("off", [H, 21], F32, kind="ExternalInput")
    sw1_in = nc.dram_tensor("sw1", [FE, D], F32, kind="ExternalInput")
    sw2_in = nc.dram_tensor("sw2", [D, FE], F32, kind="ExternalInput")
    ew1_in = nc.dram_tensor("ew1", [E, FE, D], F32, kind="ExternalInput")
    ew2_in = nc.dram_tensor("ew2", [E, D, FE], F32, kind="ExternalInput")
    rw_in = nc.dram_tensor("rw", [E, D], F32, kind="ExternalInput")
    gam_in = nc.dram_tensor("gam", [D], F32, kind="ExternalInput")
    bet_in = nc.dram_tensor("bet", [D], F32, kind="ExternalInput")
    ident = nc.dram_tensor("ident", [128, 128], BF16, kind="ExternalInput")
    identf = nc.dram_tensor("identf", [128, 128], F32, kind="ExternalInput")
    ramp = nc.dram_tensor("ramp", [112, TAB], F32, kind="ExternalInput")
    hsum = nc.dram_tensor("hsum", [112, 16], F32, kind="ExternalInput")

    out = nc.dram_tensor("out", [T, D], F32, kind="ExternalOutput")

    ikind = "ExternalOutput" if debug else "Internal"
    tab_s = nc.dram_tensor("tab_s", [H, TAB], F32, kind=ikind)
    k_s = nc.dram_tensor("k_s", [B, H, HD, S], BF16, kind=ikind)
    q_s = nc.dram_tensor("q_s", [B, H, HD, S], BF16, kind=ikind)
    v_s = nc.dram_tensor("v_s", [B, S, D], BF16, kind=ikind)
    h_s = nc.dram_tensor("h_s", [DC, 128, T], BF16, kind=ikind)
    ao_s = nc.dram_tensor("ao_s", [DC, 128, T], BF16, kind=ikind)
    ag_s = nc.dram_tensor("ag_s", [DC, 128, T], BF16, kind=ikind)
    gt_s = nc.dram_tensor("gt_s", [E, T], BF16, kind=ikind)
    eo_s = nc.dram_tensor("eo_s", [DC, 128, T], BF16, kind=ikind)
    w1t_s = nc.dram_tensor("w1t_s", [E, DC, 128, FE], BF16, kind=ikind)
    w2t_s = nc.dram_tensor("w2t_s", [E, FC, 128, D], BF16, kind=ikind)
    st1_s = nc.dram_tensor("st1_s", [DC, 128, FE], BF16, kind=ikind)
    st2_s = nc.dram_tensor("st2_s", [FC, 128, D], BF16, kind=ikind)
    s1_s = nc.dram_tensor("s1_s", [E + 1, 128, FC], F32, kind=ikind)

    id_bf = nc.alloc_sbuf_tensor("id_bf", [128, 128], BF16)
    id_f32 = nc.alloc_sbuf_tensor("id_f32", [128, 128], F32)
    ones_bf = nc.alloc_sbuf_tensor("ones_bf", [128, 1], BF16)
    ones_bfr = nc.alloc_sbuf_tensor("ones_bfr", [1, 128], BF16)
    ones_fr = nc.alloc_sbuf_tensor("ones_fr", [1, 128], F32)
    bg_sb = nc.alloc_sbuf_tensor("bg_sb", [128, 16], F32)
    rwt_sb = nc.alloc_sbuf_tensor("rwt_sb", [128, DC * E], F32)
    gate_fm = nc.alloc_sbuf_tensor("gate_fm", [8, T], BF16)

    pf = [nc.alloc_psum_tensor(f"pf{i}", [128, 512], F32) for i in range(6)]
    pb = [nc.alloc_psum_tensor(f"pb{i}", [128, 1024], BF16) for i in range(2)]

    sb = SemBank(nc)
    for n in ["ser", "ld", "pe", "ac", "dv", "po", "st", "pe2", "dv2", "ac2", "out", "pe3", "dv3", "fl", "qld", "qdv", "qpe", "qpo", "qst", "qout"]:
        sb.sem(n)

    with contextlib.ExitStack() as _es:
        row_sb = _es.enter_context(nc.sbuf_tensor("p0_row", [1, D], F32))
        rw_sb = _es.enter_context(nc.sbuf_tensor("p0_rw", [8, D], F32))
        bgt_row = _es.enter_context(nc.sbuf_tensor("p0_bgt", [16, 128], F32))
        rwb_sb = _es.enter_context(nc.sbuf_tensor("p0_rwb", [8, D], BF16))
        prm = _es.enter_context(nc.sbuf_tensor("p0_prm", [112, 9], F32))
        prm2 = _es.enter_context(nc.sbuf_tensor("p0_prm2", [112, 6], F32))
        ramp_sb = _es.enter_context(nc.sbuf_tensor("p0_ramp", [112, TAB], F32))
        hsum_sb = _es.enter_context(nc.sbuf_tensor("p0_hsum", [112, 16], F32))
        d2_sb = _es.enter_context(nc.sbuf_tensor("p0_d2", [112, TAB], F32))
        eg_sb = _es.enter_context(nc.sbuf_tensor("p0_eg", [112, 3 * TAB], F32))
        tab_sb = _es.enter_context(nc.sbuf_tensor("p0_tab", [16, TAB], F32))
        blk = _es.enter_context(nc.Block())
        blk = MaybeBlk(blk, 0 in phases)
        sr = Serial(sb, blk, 0 in phases)
        sr.multi("sync", [
            lambda e: e.dma_start(id_bf[:], ident.ap()),
            lambda e: e.dma_start(id_f32[:], identf.ap()),
            lambda e: e.dma_start(ramp_sb[:], ramp.ap()),
            lambda e: e.dma_start(hsum_sb[:], hsum.ap()),
            lambda e: e.dma_start(bgt_row[:], mkap(b_gate.ap(), 0, [[128, 16], [1, 128]])),
            lambda e: e.dma_start(rw_sb[:], rw_in.ap()),
        ] + [
            (lambda gi, g, src: lambda e: e.dma_start(
                prm[:, gi * 3 + g: gi * 3 + g + 1],
                mkap(src.ap(), g * 7, [[21, 16], [1, 7], [1, 1]])))(gi, g, src)
            for gi, src in enumerate([off_in, shp_in, amp_in]) for g in range(3)
        ], dma=True)
        sr.multi("vector", [
            lambda e: e.memset(ones_bf[:], 1.0),
            lambda e: e.memset(ones_bfr[:], 1.0),
            lambda e: e.memset(ones_fr[:], 1.0),
            lambda e: e.tensor_scalar(prm2[:, 0:3], prm[:, 0:3], -1.0, None, op0=ALU.mult),
        ])
        sr.step("scalar", lambda e: e.activation(out=prm2[:, 3:6], in_=prm[:, 3:6], func=AF.Abs))
        sr.step("vector", lambda e: e.tensor_scalar(prm2[:, 3:6], prm2[:, 3:6], -1.0, None, op0=ALU.mult))

        sr.multi("tensor", [
            (lambda dc: lambda e: e.transpose(pf[3][:, dc * 8:(dc + 1) * 8],
                                              rw_sb[0:8, dc * 128:(dc + 1) * 128], id_f32[0:8, 0:8]))(dc)
            for dc in range(DC)
        ])
        sr.step("vector", lambda e: e.tensor_copy(rwt_sb[:], pf[3][:, 0:DC * E]))
        sr.step("tensor", lambda e: e.transpose(pf[3][:, 0:16], bgt_row[:], id_f32[0:16, 0:16]))
        sr.step("vector", lambda e: e.tensor_copy(bg_sb[:], pf[3][:, 0:16]))
        for g in range(3):
            sr.step("scalar", (lambda g: lambda e: e.activation(
                out=d2_sb[:], in_=ramp_sb[:], func=AF.Square, bias=prm2[:, g:g + 1]))(g))
            sr.step("scalar", (lambda g: lambda e: e.activation(
                out=eg_sb[:, g * TAB:(g + 1) * TAB], in_=d2_sb[:], func=AF.Exp,
                scale=prm2[:, 3 + g:4 + g]))(g))
            sr.step("vector", (lambda g: lambda e: e.tensor_scalar(
                eg_sb[:, g * TAB:(g + 1) * TAB], eg_sb[:, g * TAB:(g + 1) * TAB],
                prm[:, 6 + g:7 + g], None, op0=ALU.mult))(g))
        for c in range(8):
            w = 512 if c < 7 else TAB - 7 * 512
            sr.step("tensor", (lambda c, w: lambda e: [
                e.matmul(pf[c % 2][0:16, 0:w], hsum_sb[:], eg_sb[:, g * TAB + c * 512: g * TAB + c * 512 + w],
                         start=(g == 0), stop=(g == 2)) for g in range(3)][-1])(c, w))
            sr.step("scalar", (lambda c, w: lambda e: e.copy(
                tab_sb[:, c * 512:c * 512 + w], pf[c % 2][0:16, 0:w]))(c, w))
        sr.step("sync", lambda e: e.dma_start(tab_s.ap(), tab_sb[:]), dma=True)

    with contextlib.ExitStack() as _es:
        gam_rep = _es.enter_context(nc.sbuf_tensor("p1_gam", [128, D], F32))
        bet_rep = _es.enter_context(nc.sbuf_tensor("p1_bet", [128, D], F32))
        win_sb = _es.enter_context(nc.sbuf_tensor("p1_win", [128, DC * 3 * D], BF16))
        wstg = _es.enter_context(nc.sbuf_tensor("p1_wstg", [128, 3 * D], F32))
        xt2 = _es.enter_context(nc.sbuf_tensor("p1_xt", [128, 2 * D], F32))
        sq_scr = _es.enter_context(nc.sbuf_tensor("p1_scr", [128, D], BF16))
        xn_t = _es.enter_context(nc.sbuf_tensor("p1_xn", [128, D], BF16))
        xnn = _es.enter_context(nc.sbuf_tensor("p1_xnn", [128, D], F32))
        xn_f = _es.enter_context(nc.sbuf_tensor("p1_xnf", [128, D], F32))
        h1tmp = _es.enter_context(nc.sbuf_tensor("p1_ht", [128, D], F32))
        h_t = _es.enter_context(nc.sbuf_tensor("p1_hb", [128, D], BF16))
        h_f = _es.enter_context(nc.sbuf_tensor("p1_hf", [128, D], F32))
        h_fm32 = _es.enter_context(nc.sbuf_tensor("p1_hfm32", [128, DC * 512], F32))
        tny = _es.enter_context(nc.sbuf_tensor("p1_tiny", [128, 16], F32))
        xn_fm = _es.enter_context(nc.sbuf_tensor("p1_xnfm", [128, DC * 512], BF16))
        h_fm = _es.enter_context(nc.sbuf_tensor("p1_hfm", [128, DC * 512], BF16))
        pstg = _es.enter_context(nc.sbuf_tensor("p1_stg", [128, 2 * 512], BF16))
        vstg = _es.enter_context(nc.sbuf_tensor("p1_vstg", [128, 2 * 512], BF16))
        r8 = _es.enter_context(nc.sbuf_tensor("p1_r8", [128, 8], F32))
        g8 = _es.enter_context(nc.sbuf_tensor("p1_g8", [128, 8], BF16))
        blk = _es.enter_context(nc.Block())
        blk = MaybeBlk(blk, 1 in phases)
        sr = Serial(sb, blk, 1 in phases)

        def _emit_hf32(sr, tt):
            for half in range(2):
                sr.multi("tensor", [
                    (lambda dc, half: lambda e: e.transpose(
                        pf[2 + half][:, (dc % 4) * 128:(dc % 4 + 1) * 128],
                        h_f[:, dc * 128:(dc + 1) * 128], id_f32[:]))(dc, half)
                    for dc in range(half * 4, half * 4 + 4)
                ])
                sr.step("vector", (lambda tt=tt, half=half: lambda e: e.tensor_copy(
                    mkap(h_fm32[:, :], h_fm32[:, :].offset + half * 4 * 512 + tt * 128,
                         [h_fm32[:, :].ap.to_list()[0], [512, 4], [1, 128]]),
                    pf[2 + half][:, 0:512]))(tt, half))

        sr.step("sync", lambda e: e.dma_start(
            xt2[0:1, 0:D], gam_in.ap().rearrange("(one d) -> one d", one=1)), dma=True)
        for c in range(2):
            sr.step("tensor", (lambda c: lambda e: e.matmul(
                pf[c][:, 0:512], ones_fr[:], xt2[0:1, c * 512:(c + 1) * 512], start=True, stop=True))(c))
        sr.multi("vector", [
            (lambda c: lambda e: e.tensor_copy(gam_rep[:, c * 512:(c + 1) * 512], pf[c][:]))(c)
            for c in range(2)
        ])
        sr.step("sync", lambda e: e.dma_start(xt2[0:1, 0:D], bet_in.ap().rearrange("(one d) -> one d", one=1)), dma=True)
        for c in range(2):
            sr.step("tensor", (lambda c: lambda e: e.matmul(
                pf[c][:, 0:512], ones_fr[:], xt2[0:1, c * 512:(c + 1) * 512], start=True, stop=True))(c))
        sr.multi("vector", [
            (lambda c: lambda e: e.tensor_copy(bet_rep[:, c * 512:(c + 1) * 512], pf[c][:]))(c)
            for c in range(2)
        ])
        for dc in range(DC):
            sr.step("sync", (lambda dc: lambda e: e.dma_start(
                wstg[:], x_in.ap() if False else mkap(w_in.ap(), dc * 128 * 3 * D,
                                                      [[3 * D, 128], [1, 3 * D]])))(dc), dma=True)
            sr.step("vector", (lambda dc: lambda e: e.tensor_copy(
                win_sb[:, dc * 3 * D:(dc + 1) * 3 * D], wstg[:]))(dc))

        for grp in range(8):
            for tt in range(4):
                xt = xt2[:, (tt % 2) * D:((tt % 2) + 1) * D]
                r0 = (grp * 4 + tt) * 128
                sr.step("sync", (lambda r0, xt=xt: lambda e: e.dma_start(
                    xt, mkap(x_in.ap(), r0 * D, [[D, 128], [1, D]])))(r0), dma=True)
                sr.step("vector", (lambda xt=xt: lambda e: e.tensor_reduce(
                    tny[:, 0:1], xt, axis=AX, op=ALU.add))(xt))
                sr.step("scalar", (lambda xt=xt: lambda e: e.activation(
                    out=sq_scr[:], in_=xt, func=AF.Square, accum_out=tny[:, 1:2]))(xt))
                sr.multi("vector", [
                    lambda e: e.tensor_scalar(tny[:, 2:3], tny[:, 0:1], 1.0 / D, None, op0=ALU.mult),
                    lambda e: e.tensor_scalar(tny[:, 3:4], tny[:, 1:2], 1.0 / D, None, op0=ALU.mult),
                ])
                sr.step("vector", lambda e: e.tensor_tensor(tny[:, 4:5], tny[:, 2:3], tny[:, 2:3], op=ALU.mult))
                sr.step("vector", lambda e: e.tensor_tensor(tny[:, 5:6], tny[:, 3:4], tny[:, 4:5], op=ALU.subtract))
                sr.step("vector", lambda e: e.tensor_scalar(tny[:, 6:7], tny[:, 5:6], EPS, None, op0=ALU.add))
                sr.step("scalar", lambda e: e.sqrt(tny[:, 7:8], tny[:, 6:7]))
                sr.step("vector", lambda e: e.reciprocal(tny[:, 8:9], tny[:, 7:8]))
                sr.step("vector", (lambda xt=xt: lambda e: e.tensor_scalar(
                    xn_f[:], xt, tny[:, 2:3], tny[:, 8:9], op0=ALU.subtract, op1=ALU.mult))(xt))
                sr.step("vector", lambda e: e.tensor_copy(xn_t[:], xn_f[:]))
                sr.step("vector", lambda e: e.tensor_reduce(tny[:, 0:1], xn_f[:], axis=AX, op=ALU.add))
                sr.step("scalar", lambda e: e.activation(out=sq_scr[:], in_=xn_f[:], func=AF.Square,
                                                         accum_out=tny[:, 1:2]))
                sr.multi("vector", [
                    lambda e: e.tensor_scalar(tny[:, 2:3], tny[:, 0:1], 1.0 / D, None, op0=ALU.mult),
                    lambda e: e.tensor_scalar(tny[:, 3:4], tny[:, 1:2], 1.0 / D, None, op0=ALU.mult),
                ])
                sr.step("vector", lambda e: e.tensor_tensor(tny[:, 4:5], tny[:, 2:3], tny[:, 2:3], op=ALU.mult))
                sr.step("vector", lambda e: e.tensor_tensor(tny[:, 5:6], tny[:, 3:4], tny[:, 4:5], op=ALU.subtract))
                sr.step("vector", lambda e: e.tensor_scalar(tny[:, 6:7], tny[:, 5:6], EPS, None, op0=ALU.add))
                sr.step("scalar", lambda e: e.sqrt(tny[:, 7:8], tny[:, 6:7]))
                sr.step("vector", lambda e: e.reciprocal(tny[:, 8:9], tny[:, 7:8]))
                sr.step("vector", lambda e: e.tensor_scalar(
                    xnn[:], xn_f[:], tny[:, 2:3], tny[:, 8:9], op0=ALU.subtract, op1=ALU.mult))
                sr.step("gpsimd", lambda e: e.tensor_tensor(h1tmp[:], xnn[:], gam_rep[:], op=ALU.mult))
                sr.step("gpsimd", lambda e: e.tensor_tensor(h_f[:], h1tmp[:], bet_rep[:], op=ALU.add))
                sr.step("vector", lambda e: e.tensor_copy(h_t[:], h_f[:]))
                sr.multi("tensor", [
                    (lambda dc: lambda e: e.transpose(
                        pb[0][:, dc * 128:(dc + 1) * 128], xn_t[:, dc * 128:(dc + 1) * 128], id_bf[:]))(dc)
                    for dc in range(DC)
                ])
                sr.step("vector", (lambda tt=tt: lambda e: e.tensor_copy(
                    mkap(xn_fm[:, :], xn_fm[:, :].offset + tt * 128,
                         [xn_fm[:, :].ap.to_list()[0], [512, DC], [1, 128]]),
                    pb[0][:, 0:1024]))(tt))
                sr.multi("tensor", [
                    (lambda dc: lambda e: e.transpose(
                        pb[1][:, dc * 128:(dc + 1) * 128], h_t[:, dc * 128:(dc + 1) * 128], id_bf[:]))(dc)
                    for dc in range(DC)
                ])
                sr.step("vector", (lambda tt=tt: lambda e: e.tensor_copy(
                    mkap(h_fm[:, :], h_fm[:, :].offset + tt * 128, [h_fm[:, :].ap.to_list()[0], [512, DC], [1, 128]]),
                    pb[1][:, 0:1024]))(tt))
                _emit_hf32(sr, tt)
            sr.multi("gpsimd", [
                (lambda dc, grp=grp: lambda e: e.dma_start(
                    mkap(h_s.ap(), dc * 128 * T + grp * 512, [[T, 128], [1, 512]]),
                    h_fm[:, dc * 512:(dc + 1) * 512]))(dc)
                for dc in range(DC)
            ], dma=True)
            bp = {n: sb.c[n] for n in ["pe", "ac", "dv", "st", "pe2"]}
            serv1 = sb.c["ser"]
            cons_end = []
            for oc in range(24):
                par = oc % 2

                def MM(e, oc=oc, par=par, grp=grp):
                    e.wait_ge(sb.sem("ser"), serv1)
                    if oc >= 2:
                        e.wait_ge(sb.sem("ac"), bp["ac"] + oc - 1)
                    ins = None
                    for dc in range(DC):
                        ins = e.matmul(pf[par][:], win_sb[:, dc * 3 * D + oc * 128: dc * 3 * D + (oc + 1) * 128],
                                       xn_fm[:, dc * 512:(dc + 1) * 512],
                                       start=(dc == 0), stop=(dc == DC - 1))
                    sb.inc(ins, "pe")
                blk.tensor(MM)

                def EV(e, oc=oc, par=par):
                    e.wait_ge(sb.sem("pe"), bp["pe"] + oc + 1)
                    if oc >= 2:
                        ce = cons_end[oc - 2]
                        e.wait_ge(sb.sem("st"), ce["st"])
                        e.wait_ge(sb.sem("pe2"), ce["pe2"])
                    ins = e.copy(pstg[:, par * 512:(par + 1) * 512], pf[par][:])
                    sb.inc(ins, "ac")
                blk.scalar(EV)

                if oc < 8 or oc >= 16:
                    if oc < 8:
                        dstoff = (grp // 4) * H * HD * S + (oc * 2) * HD * S + ((grp % 4) * 512)
                        dst = k_s
                    else:
                        dstoff = (grp // 4) * H * HD * S + ((oc - 16) * 2) * HD * S + ((grp % 4) * 512)
                        dst = q_s

                    def ST(e, oc=oc, par=par, dst=dst, dstoff=dstoff):
                        e.wait_ge(sb.sem("ac"), bp["ac"] + oc + 1)
                        e.dma_start(mkap(dst.ap(), dstoff, [[S, 128], [1, 512]]),
                                    pstg[:, par * 512:(par + 1) * 512]).then_inc(sb.sem("st"), 16)
                        sb.c["st"] += 16
                    blk.gpsimd(ST)
                else:
                    vc = oc - 8

                    def TP(e, oc=oc, par=par):
                        e.wait_ge(sb.sem("ac"), bp["ac"] + oc + 1)
                        if oc >= 2:
                            e.wait_ge(sb.sem("dv"), cons_end[oc - 2]["dv"])
                        ins = None
                        for ts in range(4):
                            ins = e.transpose(pb[par][:, ts * 128:(ts + 1) * 128],
                                              pstg[:, par * 512 + ts * 128: par * 512 + (ts + 1) * 128], id_bf[:])
                        sb.inc(ins, "pe2")
                    blk.tensor(TP)

                    def VE(e, oc=oc, par=par):
                        e.wait_ge(sb.sem("pe2"), sb.c["pe2"])
                        if oc >= 2:
                            e.wait_ge(sb.sem("st"), cons_end[oc - 2]["st"])
                        ins = e.tensor_copy(vstg[:, par * 512:(par + 1) * 512], pb[par][:, 0:512])
                        sb.inc(ins, "dv")
                    blk.vector(VE)

                    def SV(e, vc=vc, par=par, grp=grp):
                        e.wait_ge(sb.sem("dv"), sb.c["dv"])
                        e.dma_start(mkap(v_s.ap(), (grp // 4) * S * D + (grp % 4) * 512 * D + vc * 128,
                                         [[D, 128], [128 * D, 4], [1, 128]]),
                                    mkap(vstg[:, :], vstg[:, :].offset + par * 512,
                                         [vstg[:, :].ap.to_list()[0], [128, 4], [1, 128]])).then_inc(sb.sem("st"), 16)
                        sb.c["st"] += 16
                    blk.gpsimd(SV)
                cons_end.append({"st": sb.c["st"], "pe2": sb.c["pe2"], "dv": sb.c["dv"]})
            sr.step("sync", lambda e: e.nop(), extra_waits=[
                ("pe", sb.c["pe"]), ("ac", sb.c["ac"]), ("dv", sb.c["dv"]),
                ("st", sb.c["st"]), ("pe2", sb.c["pe2"])])
            for tq in range(4):
                def rmm(e, tq=tq):
                    ins = None
                    for dc in range(DC):
                        ins = e.matmul(pf[4][:, 0:8], h_fm32[:, dc * 512 + tq * 128: dc * 512 + (tq + 1) * 128],
                                       rwt_sb[:, dc * 8:(dc + 1) * 8],
                                       start=(dc == 0), stop=(dc == DC - 1))
                    return ins
                sr.step("tensor", rmm)
                sr.step("vector", lambda e: e.tensor_reduce(tny[:, 0:1], pf[4][:, 0:8], axis=AX,
                                                            op=ALU.max, negate=True))
                sr.step("scalar", lambda e: e.activation(out=r8[:], in_=pf[4][:, 0:8], func=AF.Exp,
                                                         bias=tny[:, 0:1]))
                sr.step("vector", lambda e: e.tensor_reduce(tny[:, 1:2], r8[:], axis=AX, op=ALU.add))
                sr.step("vector", lambda e: e.reciprocal(tny[:, 2:3], tny[:, 1:2]))
                sr.step("vector", lambda e: e.tensor_reduce(tny[:, 3:4], r8[:], axis=AX, op=ALU.max))
                sr.step("vector", lambda e: e.tensor_tensor(tny[:, 4:5], tny[:, 3:4], tny[:, 2:3], op=ALU.mult))
                sr.step("vector", lambda e: e.tensor_scalar(g8[:], r8[:], tny[:, 3:4], tny[:, 4:5],
                                                            op0=ALU.is_equal, op1=ALU.mult))
                sr.step("tensor", (lambda tq=tq: lambda e: e.transpose(
                    pb[0][0:8, tq * 128:(tq + 1) * 128], g8[:], id_bf[:]))(tq))
            sr.step("vector", (lambda grp=grp: lambda e: e.tensor_copy(
                gate_fm[0:8, grp * 512:(grp + 1) * 512], pb[0][0:8, 0:512]))(grp))
        sr.step("gpsimd", lambda e: e.dma_start(gt_s.ap(), gate_fm[:]), dma=True)

    with contextlib.ExitStack() as _es:
        kab = _es.enter_context(nc.sbuf_tensor("p3_k", [128, S], BF16))
        qab = _es.enter_context(nc.sbuf_tensor("p3_q", [128, S], BF16))
        v65 = _es.enter_context(nc.sbuf_tensor("p3_v", [128, 2 * 16 * 65], BF16))
        u_sb = _es.enter_context(nc.sbuf_tensor("p3_u", [128, UW], F32))
        eb = _es.enter_context(nc.sbuf_tensor("p3_eb", [128, 2 * UW], BF16))
        sqt = _es.enter_context(nc.sbuf_tensor("p3_sq", [128, S], BF16))
        exb = _es.enter_context(nc.sbuf_tensor("p3_ex", [128, 4 * 512], BF16))
        pub = _es.enter_context(nc.sbuf_tensor("p3_pu", [128, 4 * 512], BF16))
        nmx = _es.enter_context(nc.sbuf_tensor("p3_nm", [1, 16], F32))
        negm = _es.enter_context(nc.sbuf_tensor("p3_negm", [128, 1], F32))
        rcp = _es.enter_context(nc.sbuf_tensor("p3_rcp", [1, 2 * 512], F32))
        repsb = _es.enter_context(nc.sbuf_tensor("p3_rep", [64, 2 * 512], F32))
        aob = _es.enter_context(nc.sbuf_tensor("p3_ao", [64, 4 * 512], BF16))
        wr2 = _es.enter_context(nc.sbuf_tensor("p2_wr", [128, 2 * FE], F32))
        qa2 = _es.enter_context(nc.sbuf_tensor("p2_a", [128, 2 * FE], BF16))
        qb2 = _es.enter_context(nc.sbuf_tensor("p2_b", [128, 2 * FE], BF16))
        qt2 = _es.enter_context(nc.sbuf_tensor("p2_t", [128, 2 * FE], BF16))
        tstg = _es.enter_context(nc.sbuf_tensor("p2_ts", [128, 2 * 1024], BF16))
        qs = _es.enter_context(nc.sbuf_tensor("p2_s", [128, 8], F32))
        s1stg = _es.enter_context(nc.sbuf_tensor("p2_s1", [128, FC], F32))
        blk = _es.enter_context(nc.Block())
        blk = MaybeBlk(blk, 3 in phases)
        sr = Serial(sb, blk, 3 in phases)
        sr.step("vector", lambda e: e.memset(v65[:], 1.0))

        en2 = 2 in phases
        blk2 = MaybeBlk(blk._blk if hasattr(blk, "_blk") else blk, en2)
        p2_units = []
        if en2:
            mats = []
            for e_ in range(E):
                mats.append(("w1", ew1_in, e_, FE, D))
            for e_ in range(E):
                mats.append(("w2", ew2_in, e_, D, FE))
            mats.append(("w1", sw1_in, E, FE, D))
            mats.append(("w2", sw2_in, E, D, FE))
            p2_state = {"i": 0, "hidx": 0, "ns1": 0, "tile_pe_end": []}

            def _p2_tile(kind, srcten, e_, NR, LC, rt):
                st_ = p2_state
                i = st_["i"]
                par = i % 2
                wr = wr2[:, par * FE: par * FE + LC]
                qsl = qs[:, par * 4: par * 4 + 4]
                base = (e_ * NR * LC if e_ < E else 0) + rt * 128 * LC
                nhalf = LC // 1024
                ns1 = st_["ns1"]

                def L(e):
                    if i >= 2:
                        e.wait_ge(sb.sem("qdv"), i - 1)
                    if kind == "w1" and rt == 0:
                        e.wait_ge(sb.sem("qout"), 16 * ns1)
                    e.dma_start(wr, mkap(srcten.ap(), base, [[LC, 128], [1, LC]])).then_inc(sb.sem("qld"), 16)
                    sb.c["qld"] += 16
                blk2.sync(L)

                tpe = st_["tile_pe_end"]

                def RAD(e):
                    e.wait_ge(sb.sem("qld"), 16 * (i + 1))
                    if i >= 2:
                        e.wait_ge(sb.sem("qpe"), tpe[i - 2])
                    ins = e.tensor_reduce(qsl[:, 0:1], wr, axis=AX, op=ALU.add, apply_absolute_value=True)
                    sb.inc(ins, "fl")
                    e.wait_ge(sb.sem("fl"), sb.c["fl"])
                    ins = e.tensor_scalar(qsl[:, 1:2], qsl[:, 0:1], 1.0 / LC, 1e-5, op0=ALU.mult, op1=ALU.max)
                    sb.inc(ins, "fl")
                    e.wait_ge(sb.sem("fl"), sb.c["fl"])
                    e.tensor_scalar(qsl[:, 2:3], qsl[:, 1:2], -0.5, None, op0=ALU.mult)
                    ins = e.tensor_scalar(qsl[:, 3:4], qsl[:, 1:2], 0.5, None, op0=ALU.mult)
                    if kind == "w1":
                        ins = e.tensor_copy(s1stg[:, rt:rt + 1], qsl[:, 1:2])
                    sb.inc(ins, "fl")
                    e.wait_ge(sb.sem("fl"), sb.c["fl"])
                    e.tensor_scalar(qa2[:, par * FE:par * FE + LC], wr, qsl[:, 2:3], 0.0,
                                    op0=ALU.add, op1=ALU.is_gt)
                    ins = e.tensor_scalar(qb2[:, par * FE:par * FE + LC], wr, qsl[:, 3:4], 0.0,
                                          op0=ALU.add, op1=ALU.is_lt)
                    sb.inc(ins, "fl")
                    e.wait_ge(sb.sem("fl"), sb.c["fl"])
                    ins = e.tensor_tensor(qt2[:, par * FE:par * FE + LC], qa2[:, par * FE:par * FE + LC],
                                          qb2[:, par * FE:par * FE + LC], op=ALU.subtract)
                    if kind == "w2":
                        sb.inc(ins, "fl")
                        e.wait_ge(sb.sem("fl"), sb.c["fl"])
                        ins = e.tensor_scalar(qt2[:, par * FE:par * FE + LC],
                                              qt2[:, par * FE:par * FE + LC], qsl[:, 1:2], None, op0=ALU.mult)
                    sb.inc(ins, "qdv")
                blk2.vector(RAD)

                for hh in range(nhalf):
                    hidx = st_["hidx"]
                    hpar = hidx % 2

                    def Tst(e, hh=hh, hpar=hpar, hidx=hidx):
                        e.wait_ge(sb.sem("qdv"), i + 1)
                        if hidx >= 2:
                            e.wait_ge(sb.sem("qpo"), hidx - 1)
                        ins = None
                        for bkk in range(8):
                            ins = e.transpose(pb[hpar][:, bkk * 128:(bkk + 1) * 128],
                                              qt2[:, par * FE + hh * 1024 + bkk * 128:
                                                  par * FE + hh * 1024 + (bkk + 1) * 128], id_bf[:])
                        sb.inc(ins, "qpe")
                    blk2.tensor(Tst)

                    def Est(e, hpar=hpar, hidx=hidx):
                        e.wait_ge(sb.sem("qpe"), hidx + 1)
                        if hidx >= 2:
                            e.wait_ge(sb.sem("qst"), 16 * (hidx - 1))
                        ins = e.tensor_copy(tstg[:, hpar * 1024:(hpar + 1) * 1024], pb[hpar][:])
                        sb.inc(ins, "qpo")
                    blk2.vector(Est)

                    if kind == "w1":
                        dst = w1t_s if e_ < E else st1_s
                        dbase = (e_ * DC * 128 * FE if e_ < E else 0) + hh * 8 * 128 * FE + rt * 128
                        ddims = [[FE, 128], [128 * FE, 8], [1, 128]]
                    else:
                        dst = w2t_s if e_ < E else st2_s
                        dbase = (e_ * FC * 128 * D if e_ < E else 0) + hh * 8 * 128 * D + rt * 128
                        ddims = [[D, 128], [128 * D, 8], [1, 128]]

                    def Sst(e, dst=dst, dbase=dbase, ddims=ddims, hpar=hpar, hidx=hidx):
                        e.wait_ge(sb.sem("qpo"), hidx + 1)
                        sap = mkap(tstg[:, :], tstg[:, :].offset + hpar * 1024,
                                   [tstg[:, :].ap.to_list()[0], [128, 8], [1, 128]])
                        e.dma_start(mkap(dst.ap(), dbase, ddims), sap).then_inc(sb.sem("qst"), 16)
                        sb.c["qst"] += 16
                    blk2.gpsimd(Sst)
                    st_["hidx"] += 1
                st_["tile_pe_end"].append(sb.c["qpe"])
                st_["i"] += 1

            def _p2_s1(e_):
                n = sb.c["qdv"]

                def S1(e):
                    e.wait_ge(sb.sem("qdv"), n)
                    e.dma_start(mkap(s1_s.ap(), e_ * 128 * FC, [[FC, 128], [1, FC]]),
                                s1stg[:]).then_inc(sb.sem("qout"), 16)
                    sb.c["qout"] += 16
                blk2.sync(S1)
                p2_state["ns1"] += 1

            for kind, srcten, e_, NR, LC in mats:
                for rt in range(NR // 128):
                    p2_units.append((lambda kind=kind, srcten=srcten, e_=e_, NR=NR, LC=LC, rt=rt:
                                     _p2_tile(kind, srcten, e_, NR, LC, rt)))
                if kind == "w1":
                    p2_units.append((lambda e_=e_: _p2_s1(e_)))
        p2_ptr = [0]

        def drain_p2(n):
            while n > 0 and p2_ptr[0] < len(p2_units):
                p2_units[p2_ptr[0]]()
                p2_ptr[0] += 1
                n -= 1

        for pr in range(16):
            b_, j = pr // 8, pr % 8
            sr.multi("sync", [
                (lambda b_, j: lambda e: e.dma_start(
                    kab[:], mkap(k_s.ap(), b_ * H * HD * S + (2 * j) * HD * S, [[S, 128], [1, S]])))(b_, j),
                (lambda b_, j: lambda e: e.dma_start(
                    qab[:], mkap(q_s.ap(), b_ * H * HD * S + (2 * j) * HD * S, [[S, 128], [1, S]])))(b_, j),
                (lambda j: lambda e: e.dma_start(
                    u_sb[:], mkap(tab_s.ap(), (2 * j) * TAB, [[1, 128], [1, UW]])))(j),
            ], dma=True)
            sr.step("scalar", lambda e: e.activation(out=eb[:, 0:UW], in_=u_sb[:], func=AF.Exp))
            sr.step("sync", (lambda j: lambda e: e.dma_start(
                u_sb[:], mkap(tab_s.ap(), (2 * j + 1) * TAB, [[1, 128], [1, UW]])))(j), dma=True)
            sr.step("scalar", lambda e: e.activation(out=eb[:, UW:2 * UW], in_=u_sb[:], func=AF.Exp))
            sr.multi("sync", [
                (lambda b_, j, kc, hh: lambda e: e.dma_start(
                    v65[:, (hh * 16 + kc) * 65:(hh * 16 + kc) * 65 + 64],
                    mkap(v_s.ap(), b_ * S * D + (kc * 128) * D + (2 * j + hh) * 64, [[D, 128], [1, 64]])))(b_, j, kc, hh)
                for hh in range(2) for kc in range(16)
            ], dma=True)
            sr.step("vector", lambda e: e.tensor_tensor(sqt[:], kab[:], kab[:], op=ALU.mult))
            for c in range(4):
                sr.step("tensor", (lambda c: lambda e: e.matmul(
                    pf[4][0:1, 0:512], ones_bf[:], sqt[:, c * 512:(c + 1) * 512], start=True, stop=True))(c))
                sr.step("vector", (lambda c: lambda e: e.tensor_reduce(
                    nmx[0:1, c:c + 1], pf[4][0:1, 0:512], axis=AX, op=ALU.max))(c))
            sr.step("vector", lambda e: e.tensor_tensor(sqt[:], qab[:], qab[:], op=ALU.mult))
            for c in range(4):
                sr.step("tensor", (lambda c: lambda e: e.matmul(
                    pf[4][0:1, 0:512], ones_bf[:], sqt[:, c * 512:(c + 1) * 512], start=True, stop=True))(c))
                sr.step("vector", (lambda c: lambda e: e.tensor_reduce(
                    nmx[0:1, 4 + c:5 + c], pf[4][0:1, 0:512], axis=AX, op=ALU.max))(c))
            sr.step("vector", lambda e: e.tensor_reduce(nmx[0:1, 8:9], nmx[0:1, 0:4], axis=AX, op=ALU.max))
            sr.step("vector", lambda e: e.tensor_reduce(nmx[0:1, 9:10], nmx[0:1, 4:8], axis=AX, op=ALU.max))
            sr.step("vector", lambda e: e.tensor_tensor(nmx[0:1, 10:11], nmx[0:1, 8:9], nmx[0:1, 9:10], op=ALU.add))
            sr.step("vector", lambda e: e.tensor_scalar(nmx[0:1, 11:12], nmx[0:1, 10:11], -1.0 / 16.0, None, op0=ALU.mult))
            sr.step("tensor", lambda e: e.matmul(pf[4][:, 0:1], ones_fr[:], nmx[0:1, 11:12], start=True, stop=True))
            sr.step("vector", lambda e: e.tensor_copy(negm[:], pf[4][:, 0:1]))

            base = {n: sb.c[n] for n in ["pe", "ac", "dv", "pe2", "dv2", "st"]}
            serv = sb.c["ser"]
            for qc in range(4):
                for kc in range(16):
                    g = qc * 16 + kc

                    def mm_qk(e, qc=qc, kc=kc, g=g):
                        e.wait_ge(sb.sem("ser"), serv)
                        if g >= 2:
                            e.wait_ge(sb.sem("ac"), base["ac"] + g - 1)
                        if kc == 0:
                            e.wait_ge(sb.sem("dv2"), base["dv2"] + qc)
                        e.matmul(pf[kc % 2][:], kab[0:64, kc * 128:(kc + 1) * 128],
                                 qab[0:64, qc * 512:(qc + 1) * 512], start=True, stop=True,
                                 tile_position=(0, 0))
                        ins = e.matmul(pf[2 + kc % 2][:], kab[64:128, kc * 128:(kc + 1) * 128],
                                       qab[64:128, qc * 512:(qc + 1) * 512], start=True, stop=True,
                                       tile_position=(64, 0))
                        sb.inc(ins, "pe")
                    blk.tensor(mm_qk)

                    def ex_fn(e, qc=qc, kc=kc, g=g):
                        e.wait_ge(sb.sem("pe"), base["pe"] + g + 1)
                        if g >= 2:
                            e.wait_ge(sb.sem("dv"), base["dv"] + g - 1)
                        e.activation(out=exb[:, (kc % 2) * 512:(kc % 2 + 1) * 512],
                                     in_=pf[kc % 2][:], func=AF.Exp, bias=negm[:], scale=0.125)
                        ins = e.activation(out=exb[:, (2 + kc % 2) * 512:(3 + kc % 2) * 512],
                                           in_=pf[2 + kc % 2][:], func=AF.Exp, bias=negm[:], scale=0.125)
                        sb.inc(ins, "ac")
                    blk.scalar(ex_fn)

                    def mul_fn(e, qc=qc, kc=kc, g=g):
                        e.wait_ge(sb.sem("ac"), base["ac"] + g + 1)
                        if g >= 2:
                            e.wait_ge(sb.sem("pe2"), base["pe2"] + g - 1)
                        a_col = 2047 + kc * 128 - qc * 512
                        e.tensor_tensor(pub[:, (kc % 2) * 512:(kc % 2 + 1) * 512],
                                        exb[:, (kc % 2) * 512:(kc % 2 + 1) * 512],
                                        revslice(eb[:, 0:UW], a_col, 512), op=ALU.mult)
                        ins = e.tensor_tensor(pub[:, (2 + kc % 2) * 512:(3 + kc % 2) * 512],
                                              exb[:, (2 + kc % 2) * 512:(3 + kc % 2) * 512],
                                              revslice(mkap(eb[:, :], eb[:, :].offset + UW,
                                                            [eb[:, :].ap.to_list()[0], [1, UW]]),
                                                       a_col, 512), op=ALU.mult)
                        sb.inc(ins, "dv")
                    blk.vector(mul_fn)

                    def mm_av(e, qc=qc, kc=kc, g=g):
                        e.wait_ge(sb.sem("dv"), base["dv"] + g + 1)
                        e.matmul(pf[4][0:65, :], v65[:, kc * 65:kc * 65 + 65],
                                 pub[:, (kc % 2) * 512:(kc % 2 + 1) * 512],
                                 start=(kc == 0), stop=(kc == 15))
                        ins = e.matmul(pf[5][0:65, :], v65[:, (16 + kc) * 65:(16 + kc) * 65 + 65],
                                       pub[:, (2 + kc % 2) * 512:(3 + kc % 2) * 512],
                                       start=(kc == 0), stop=(kc == 15))
                        sb.inc(ins, "pe2")
                    blk.tensor(mm_av)

                def rcp_fn(e, qc=qc):
                    e.wait_ge(sb.sem("pe2"), base["pe2"] + (qc + 1) * 16)
                    e.reciprocal(rcp[0:1, 0:512], pf[4][64:65, :])
                    ins = e.reciprocal(rcp[0:1, 512:1024], pf[5][64:65, :])
                    sb.inc(ins, "dv3")
                blk.vector(rcp_fn)

                def rep_fn(e, qc=qc):
                    e.wait_ge(sb.sem("dv3"), sb.c["dv3"])
                    e.matmul(pf[0][0:64, :], ones_fr[0:1, 0:64], rcp[0:1, 0:512], start=True, stop=True)
                    ins = e.matmul(pf[2][0:64, :], ones_fr[0:1, 0:64], rcp[0:1, 512:1024], start=True, stop=True)
                    sb.inc(ins, "pe3")
                blk.tensor(rep_fn)

                def evac_fn(e, qc=qc):
                    e.wait_ge(sb.sem("pe3"), sb.c["pe3"])
                    if qc >= 2:
                        e.wait_ge(sb.sem("st"), base["st"] + (qc - 1) * 32)
                    e.tensor_copy(repsb[:, 0:512], pf[0][0:64, :])
                    e.tensor_copy(repsb[:, 512:1024], pf[2][0:64, :])
                    e.tensor_tensor(aob[:, (qc % 2) * 1024:(qc % 2) * 1024 + 512],
                                    pf[4][0:64, :], repsb[:, 0:512], op=ALU.mult)
                    ins = e.tensor_tensor(aob[:, (qc % 2) * 1024 + 512:(qc % 2 + 1) * 1024],
                                          pf[5][0:64, :], repsb[:, 512:1024], op=ALU.mult)
                    sb.inc(ins, "dv2")
                blk.vector(evac_fn)

                def st_fn(e, qc=qc, b_=b_, j=j):
                    e.wait_ge(sb.sem("dv2"), sb.c["dv2"])
                    e.dma_start(mkap(ao_s.ap(), j * 128 * T + b_ * S + qc * 512, [[T, 64], [1, 512]]),
                                aob[:, (qc % 2) * 1024:(qc % 2) * 1024 + 512]).then_inc(sb.sem("st"), 16)
                    e.dma_start(mkap(ao_s.ap(), j * 128 * T + 64 * T + b_ * S + qc * 512, [[T, 64], [1, 512]]),
                                aob[:, (qc % 2) * 1024 + 512:(qc % 2 + 1) * 1024]).then_inc(sb.sem("st"), 16)
                    sb.c["st"] += 32
                blk.gpsimd(st_fn)
                drain_p2(3)

            def fence(e):
                e.wait_ge(sb.sem("st"), sb.c["st"])
                ins = e.nop()
                return ins
            sr.step("sync", fence)
            drain_p2(2)
        drain_p2(10 ** 9)

        def _p2fence(e, n=None):
            e.wait_ge(sb.sem("qst"), sb.c["qst"])
            e.wait_ge(sb.sem("qout"), sb.c["qout"])
            return e.nop()
        sr.step("sync", _p2fence)

    with contextlib.ExitStack() as _es:
        wg_sb = _es.enter_context(nc.sbuf_tensor("p4_wg", [128, DC * 2 * D], BF16))
        wstg4 = _es.enter_context(nc.sbuf_tensor("p4_stg", [128, 2 * D], F32))
        aochunk = _es.enter_context(nc.sbuf_tensor("p4_ao", [128, DC * 512], BF16))
        a8 = _es.enter_context(nc.sbuf_tensor("p4_a8", [128, 8 * 512], F32))
        sgb = _es.enter_context(nc.sbuf_tensor("p4_sg", [128, 512], F32))
        agb = _es.enter_context(nc.sbuf_tensor("p4_ag", [128, 2 * 512], BF16))
        blk = _es.enter_context(nc.Block())
        blk = MaybeBlk(blk, 4 in phases)
        sr = Serial(sb, blk, 4 in phases)
        for dc in range(DC):
            sr.step("sync", (lambda dc: lambda e: e.dma_start(
                wstg4[:], mkap(w_gate.ap(), dc * 128 * 2 * D, [[2 * D, 128], [1, 2 * D]])))(dc), dma=True)
            sr.step("vector", (lambda dc: lambda e: e.tensor_copy(
                wg_sb[:, dc * 2 * D:(dc + 1) * 2 * D], wstg4[:]))(dc))
        for tc in range(8):
            sr.multi("sync", [
                (lambda dc, tc: lambda e: e.dma_start(
                    aochunk[:, dc * 512:(dc + 1) * 512],
                    mkap(ao_s.ap(), dc * 128 * T + tc * 512, [[T, 128], [1, 512]])))(dc, tc)
                for dc in range(DC)
            ], dma=True)
            for gc in range(16):
                par = gc % 2

                def mmg(e, gc=gc, par=par):
                    ins = None
                    for dc in range(DC):
                        ins = e.matmul(pf[par][:], wg_sb[:, dc * 2 * D + gc * 128: dc * 2 * D + (gc + 1) * 128],
                                       aochunk[:, dc * 512:(dc + 1) * 512],
                                       start=(dc == 0), stop=(dc == DC - 1))
                    return ins
                sr.step("tensor", mmg)
                if gc < 8:
                    sr.step("vector", (lambda gc, par: lambda e: e.tensor_scalar(
                        a8[:, gc * 512:(gc + 1) * 512], pf[par][:], bg_sb[:, gc:gc + 1], None,
                        op0=ALU.add))(gc, par))
                else:
                    dco = gc - 8
                    sr.step("scalar", (lambda gc, par: lambda e: e.activation(
                        out=sgb[:], in_=pf[par][:], func=AF.Sigmoid, bias=bg_sb[:, gc:gc + 1]))(gc, par))
                    sr.step("vector", (lambda dco: lambda e: e.tensor_tensor(
                        agb[:, (dco % 2) * 512:(dco % 2 + 1) * 512], a8[:, dco * 512:(dco + 1) * 512],
                        sgb[:], op=ALU.mult))(dco))
                    sr.step("gpsimd", (lambda dco, tc: lambda e: e.dma_start(
                        mkap(ag_s.ap(), dco * 128 * T + tc * 512, [[T, 128], [1, 512]]),
                        agb[:, (dco % 2) * 512:(dco % 2 + 1) * 512]))(dco, tc), dma=True)

    with contextlib.ExitStack() as _es:
        eo_acc = _es.enter_context(nc.sbuf_tensor("p5_acc", [128, DC * T], BF16))
        w1b = _es.enter_context(nc.sbuf_tensor("p5_w1", [128, DC * FE], BF16))
        w2b = _es.enter_context(nc.sbuf_tensor("p5_w2", [128, FC * D], BF16))
        hb2 = _es.enter_context(nc.sbuf_tensor("p5_h", [128, 2 * DC * 512], BF16))
        h1g2 = _es.enter_context(nc.sbuf_tensor("p5_h1g", [128, 2 * FC * 512], BF16))
        h1b = _es.enter_context(nc.sbuf_tensor("p5_h1", [128, 2 * 512], BF16))
        grsb2 = _es.enter_context(nc.sbuf_tensor("p5_gr", [128, 2 * 512], BF16))
        grow2 = _es.enter_context(nc.sbuf_tensor("p5_grow", [1, 2 * 512], BF16))
        s1b = _es.enter_context(nc.sbuf_tensor("p5_s1", [128, FC], F32))
        blk = _es.enter_context(nc.Block())
        en5 = 5 in phases
        blk = MaybeBlk(blk, en5)
        sr = Serial(sb, blk, en5)
        sr.step("vector", lambda e: e.memset(eo_acc[:], 0.0))
        if en5:
            gi = 0
            ld_end = []
            m1_end = []
            silu_end = []
            gmul_end = []
            m2_end = []
            gc_end = []
            for e_ in range(E):
                sr.multi("sync", [
                    (lambda e_, dc: lambda e: e.dma_start(
                        w1b[:, dc * FE:(dc + 1) * FE],
                        mkap(w1t_s.ap(), e_ * DC * 128 * FE + dc * 128 * FE, [[FE, 128], [1, FE]])))(e_, dc)
                    for dc in range(DC)
                ] + [
                    (lambda e_, fc: lambda e: e.dma_start(
                        w2b[:, fc * D:(fc + 1) * D],
                        mkap(w2t_s.ap(), e_ * FC * 128 * D + fc * 128 * D, [[D, 128], [1, D]])))(e_, fc)
                    for fc in range(FC)
                ] + [
                    (lambda e_: lambda e: e.dma_start(
                        s1b[:], mkap(s1_s.ap(), e_ * 128 * FC, [[FC, 128], [1, FC]])))(e_)
                ], dma=True, extra_waits=[("pe", sb.c["pe"]), ("pe2", sb.c["pe2"]),
                                          ("ac", sb.c["ac"])])
                serv5 = sb.c["ser"]
                for tc in range(8):
                    par = gi % 2
                    hb = hb2[:, par * DC * 512:(par + 1) * DC * 512]
                    h1g = h1g2[:, par * FC * 512:(par + 1) * FC * 512]
                    grsb = grsb2[:, par * 512:(par + 1) * 512]
                    grow = grow2[:, par * 512:(par + 1) * 512]

                    def LH(e, e_=e_, tc=tc, hb=hb, grow=grow, gi=gi, serv5=serv5):
                        e.wait_ge(sb.sem("ser"), serv5)
                        if gi >= 2:
                            e.wait_ge(sb.sem("pe"), m1_end[gi - 2])
                        for dc in range(DC):
                            e.dma_start(hb[:, dc * 512:(dc + 1) * 512],
                                        mkap(h_s.ap(), dc * 128 * T + tc * 512,
                                             [[T, 128], [1, 512]])).then_inc(sb.sem("ld"), 16)
                            sb.c["ld"] += 16
                        e.dma_start(grow, mkap(gt_s.ap(), e_ * T + tc * 512,
                                               [[1, 1], [1, 512]])).then_inc(sb.sem("ld"), 16)
                        sb.c["ld"] += 16
                    blk.sync(LH)
                    ld_end.append(sb.c["ld"])

                    def GR(e, grow=grow, gi=gi, serv5=serv5):
                        e.wait_ge(sb.sem("ser"), serv5)
                        e.wait_ge(sb.sem("ld"), ld_end[gi])
                        if gi >= 1:
                            e.wait_ge(sb.sem("dv"), gc_end[gi - 1])
                        ins = e.matmul(pf[4][:], ones_bfr[:], grow, start=True, stop=True)
                        sb.inc(ins, "pe3")
                    blk.tensor(GR)

                    def GC(e, grsb=grsb, gi=gi):
                        e.wait_ge(sb.sem("pe3"), sb.c["pe3"])
                        if gi >= 2:
                            e.wait_ge(sb.sem("po"), gmul_end[gi - 2])
                        ins = e.tensor_copy(grsb, pf[4][:])
                        sb.inc(ins, "dv")
                    blk.vector(GC)
                    gc_end.append(sb.c["dv"])

                    bac = sb.c["ac"]
                    bpo = sb.c["po"]
                    bpe = sb.c["pe"]
                    for fc in range(FC):
                        def m1(e, fc=fc, hb=hb, gi=gi, bac=bac):
                            e.wait_ge(sb.sem("ld"), ld_end[gi])
                            if fc >= 2:
                                e.wait_ge(sb.sem("ac"), bac + fc - 1)
                            ins = None
                            for dc in range(DC):
                                ins = e.matmul(pf[fc % 2][:], w1b[:, dc * FE + fc * 128: dc * FE + (fc + 1) * 128],
                                               hb[:, dc * 512:(dc + 1) * 512],
                                               start=(dc == 0), stop=(dc == DC - 1))
                            sb.inc(ins, "pe")
                        blk.tensor(m1)

                        def silu(e, fc=fc, bpe=bpe, bpo=bpo):
                            e.wait_ge(sb.sem("pe"), bpe + fc + 1)
                            if fc >= 2:
                                e.wait_ge(sb.sem("po"), bpo + fc - 1)
                            ins = e.activation(out=h1b[:, (fc % 2) * 512:(fc % 2 + 1) * 512],
                                               in_=pf[fc % 2][:], func=AF.Silu, scale=s1b[:, fc:fc + 1])
                            sb.inc(ins, "ac")
                        blk.scalar(silu)

                        def gmul(e, fc=fc, h1g=h1g, gi=gi, bac=bac):
                            e.wait_ge(sb.sem("ac"), bac + fc + 1)
                            e.wait_ge(sb.sem("dv"), gc_end[gi])
                            if gi >= 2 and fc == 0:
                                e.wait_ge(sb.sem("pe2"), m2_end[gi - 2])
                            ins = e.tensor_tensor(h1g[:, fc * 512:(fc + 1) * 512],
                                                  h1b[:, (fc % 2) * 512:(fc % 2 + 1) * 512], grsb, op=ALU.mult)
                            sb.inc(ins, "po")
                        blk.gpsimd(gmul)
                    m1_end.append(sb.c["pe"])
                    silu_end.append(sb.c["ac"])
                    gmul_end.append(sb.c["po"])

                    bpe2 = sb.c["pe2"]
                    bdv2 = sb.c["dv2"]
                    for dc2 in range(DC):
                        def m2(e, dc2=dc2, h1g=h1g, gi=gi, bpe2=bpe2, bdv2=bdv2):
                            e.wait_ge(sb.sem("po"), gmul_end[gi])
                            if dc2 >= 2:
                                e.wait_ge(sb.sem("dv2"), bdv2 + dc2 - 1)
                            ins = None
                            for fc in range(FC):
                                ins = e.matmul(pf[2 + dc2 % 2][:], w2b[:, fc * D + dc2 * 128: fc * D + (dc2 + 1) * 128],
                                               h1g[:, fc * 512:(fc + 1) * 512],
                                               start=(fc == 0), stop=(fc == FC - 1))
                            sb.inc(ins, "pe2")
                        blk.tensor(m2)

                        def accadd(e, dc2=dc2, tc=tc, bpe2=bpe2):
                            e.wait_ge(sb.sem("pe2"), bpe2 + dc2 + 1)
                            sl = eo_acc[:, dc2 * T + tc * 512: dc2 * T + (tc + 1) * 512]
                            ins = e.tensor_tensor(sl, sl, pf[2 + dc2 % 2][:], op=ALU.add)
                            sb.inc(ins, "dv2")
                        blk.vector(accadd)
                    m2_end.append(sb.c["pe2"])
                    gi += 1
            sr.step("sync", lambda e: e.nop(), extra_waits=[("dv2", sb.c["dv2"])])
        sr.multi("gpsimd", [
            (lambda dc: lambda e: e.dma_start(
                mkap(eo_s.ap(), dc * 128 * T, [[T, 128], [1, T]]), eo_acc[:, dc * T:(dc + 1) * T]))(dc)
            for dc in range(DC)
        ], dma=True)

    with contextlib.ExitStack() as _es:
        sw1b = _es.enter_context(nc.sbuf_tensor("p6_w1", [128, DC * FE], BF16))
        sw2b = _es.enter_context(nc.sbuf_tensor("p6_w2", [128, FC * D], BF16))
        hb6 = _es.enter_context(nc.sbuf_tensor("p6_h", [128, DC * 512], BF16))
        h1s = _es.enter_context(nc.sbuf_tensor("p6_h1", [128, FC * 512], BF16))
        agc = _es.enter_context(nc.sbuf_tensor("p6_ag", [128, DC * 512], BF16))
        eoc = _es.enter_context(nc.sbuf_tensor("p6_eo", [128, DC * 512], BF16))
        x6 = _es.enter_context(nc.sbuf_tensor("p6_x", [128, 4 * D], F32))
        o6 = _es.enter_context(nc.sbuf_tensor("p6_o", [128, 4 * D], F32))
        csum = _es.enter_context(nc.sbuf_tensor("p6_cs", [128, 512], F32))
        csum2 = _es.enter_context(nc.sbuf_tensor("p6_cs2", [128, 512], F32))
        s1b6 = _es.enter_context(nc.sbuf_tensor("p6_s1", [128, FC], F32))
        blk = _es.enter_context(nc.Block())
        blk = MaybeBlk(blk, 6 in phases)
        sr = Serial(sb, blk, 6 in phases)
        sr.multi("sync", [
            (lambda dc: lambda e: e.dma_start(
                sw1b[:, dc * FE:(dc + 1) * FE],
                mkap(st1_s.ap(), dc * 128 * FE, [[FE, 128], [1, FE]])))(dc)
            for dc in range(DC)
        ] + [
            (lambda fc: lambda e: e.dma_start(
                sw2b[:, fc * D:(fc + 1) * D],
                mkap(st2_s.ap(), fc * 128 * D, [[D, 128], [1, D]])))(fc)
            for fc in range(FC)
        ] + [
            lambda e: e.dma_start(s1b6[:], mkap(s1_s.ap(), E * 128 * FC, [[FC, 128], [1, FC]]))
        ], dma=True)
        for tc in range(8):
            sr.multi("sync", [
                (lambda dc, tc: lambda e: e.dma_start(
                    hb6[:, dc * 512:(dc + 1) * 512],
                    mkap(h_s.ap(), dc * 128 * T + tc * 512, [[T, 128], [1, 512]])))(dc, tc)
                for dc in range(DC)
            ] + [
                (lambda dc, tc: lambda e: e.dma_start(
                    agc[:, dc * 512:(dc + 1) * 512],
                    mkap(ag_s.ap(), dc * 128 * T + tc * 512, [[T, 128], [1, 512]])))(dc, tc)
                for dc in range(DC)
            ] + [
                (lambda dc, tc: lambda e: e.dma_start(
                    eoc[:, dc * 512:(dc + 1) * 512],
                    mkap(eo_s.ap(), dc * 128 * T + tc * 512, [[T, 128], [1, 512]])))(dc, tc)
                for dc in range(DC)
            ] + [
                (lambda ts, tc: lambda e: e.dma_start(
                    x6[:, ts * D:(ts + 1) * D],
                    mkap(x_in.ap(), (tc * 512 + ts * 128) * D, [[D, 128], [1, D]])))(ts, tc)
                for ts in range(4)
            ], dma=True)
            base = {n: sb.c[n] for n in ["pe", "ac"]}
            serv = sb.c["ser"]
            for fc in range(FC):
                def m1(e, fc=fc):
                    e.wait_ge(sb.sem("ser"), serv)
                    if fc >= 2:
                        e.wait_ge(sb.sem("ac"), base["ac"] + fc - 1)
                    ins = None
                    for dc in range(DC):
                        ins = e.matmul(pf[fc % 2][:], sw1b[:, dc * FE + fc * 128: dc * FE + (fc + 1) * 128],
                                       hb6[:, dc * 512:(dc + 1) * 512],
                                       start=(dc == 0), stop=(dc == DC - 1))
                    sb.inc(ins, "pe")
                blk.tensor(m1)

                def silu(e, fc=fc):
                    e.wait_ge(sb.sem("pe"), base["pe"] + fc + 1)
                    ins = e.activation(out=h1s[:, fc * 512:(fc + 1) * 512],
                                       in_=pf[fc % 2][:], func=AF.Silu, scale=s1b6[:, fc:fc + 1])
                    sb.inc(ins, "ac")
                blk.scalar(silu)
            base2 = {"pe2": sb.c["pe2"], "dv2": sb.c["dv2"], "po": sb.c["po"],
                     "dv": sb.c["dv"], "pe": sb.c["pe"], "ac": sb.c["ac"]}
            for dc2 in range(DC):
                def m2(e, dc2=dc2):
                    e.wait_ge(sb.sem("ac"), base["ac"] + FC)
                    if dc2 >= 2:
                        e.wait_ge(sb.sem("dv"), base2["dv"] + dc2 - 1)
                    ins = None
                    for fc in range(FC):
                        ins = e.matmul(pf[2 + dc2 % 2][:], sw2b[:, fc * D + dc2 * 128: fc * D + (dc2 + 1) * 128],
                                       h1s[:, fc * 512:(fc + 1) * 512],
                                       start=(fc == 0), stop=(fc == FC - 1))
                    sb.inc(ins, "pe2")
                blk.tensor(m2)

                def comb1(e, dc2=dc2):
                    e.wait_ge(sb.sem("pe2"), base2["pe2"] + dc2 + 1)
                    if dc2 >= 1:
                        e.wait_ge(sb.sem("po"), base2["po"] + dc2)
                    ins = e.tensor_tensor(csum[:], pf[2 + dc2 % 2][:],
                                          eoc[:, dc2 * 512:(dc2 + 1) * 512], op=ALU.add)
                    sb.inc(ins, "dv")
                blk.vector(comb1)

                def comb2(e, dc2=dc2):
                    e.wait_ge(sb.sem("dv"), base2["dv"] + dc2 + 1)
                    if dc2 >= 1:
                        e.wait_ge(sb.sem("pe"), base2["pe"] + dc2)
                    ins = e.tensor_tensor(csum2[:], csum[:], agc[:, dc2 * 512:(dc2 + 1) * 512], op=ALU.add)
                    sb.inc(ins, "po")
                blk.gpsimd(comb2)

                def tpose(e, dc2=dc2):
                    e.wait_ge(sb.sem("po"), base2["po"] + dc2 + 1)
                    if dc2 >= 1:
                        e.wait_ge(sb.sem("dv2"), base2["dv2"] + dc2)
                    ins = None
                    for ts in range(4):
                        ins = e.transpose(pf[5][:, ts * 128:(ts + 1) * 128],
                                          csum2[:, ts * 128:(ts + 1) * 128], id_f32[:])
                    sb.inc(ins, "pe")
                blk.tensor(tpose)

                def oadd(e, dc2=dc2):
                    e.wait_ge(sb.sem("pe"), sb.c["pe"])
                    src = mkap(pf[5][:, :].copy() if False else pf[5][:, :],
                               pf[5][:, :].offset, [pf[5][:, :].ap.to_list()[0], [128, 4], [1, 128]])
                    dst = mkap(o6[:, :], o6[:, :].offset + dc2 * 128,
                               [o6[:, :].ap.to_list()[0], [D, 4], [1, 128]])
                    xsrc = mkap(x6[:, :], x6[:, :].offset + dc2 * 128,
                                [x6[:, :].ap.to_list()[0], [D, 4], [1, 128]])
                    ins = e.tensor_tensor(dst, src, xsrc, op=ALU.add)
                    sb.inc(ins, "dv2")
                blk.vector(oadd)
            def store(e, tc=tc, n=sb.c["dv2"]):
                e.wait_ge(sb.sem("dv2"), n)
                ins = e.dma_start(mkap(out.ap(), tc * 512 * D, [[D, 128], [128 * D, 4], [1, D]]), o6[:])
                ins.then_inc(sb.sem("out"), 16)
                sb.c["out"] += 16
            blk.sync(store)
            sr.step("sync", lambda e: e.nop(),
                    extra_waits=[("dv2", sb.c["dv2"])])

        def final(e):
            e.wait_ge(sb.sem("out"), sb.c["out"])
            return e.nop()
        sr.step("sync", final)

    nc.finalize()
    return nc


_NC_CACHE = {}


def kernel(**inputs):
    x = np.ascontiguousarray(inputs["L0_attn_x"])
    ident_ = np.eye(128, dtype=BFNP)
    rampv = np.tile(np.arange(-(TAB // 2), TAB // 2 + 1, dtype=np.float32), (112, 1))
    hsumv = np.zeros((112, 16), dtype=np.float32)
    for p_ in range(112):
        hsumv[p_, p_ // 7] = 1.0

    if "nc" not in _NC_CACHE:
        _NC_CACHE["nc"] = build(debug=False)
    nc = _NC_CACHE["nc"]

    base = dict(
        w_in=np.ascontiguousarray(inputs["L0_attn_w_in"]),
        w_gate=np.ascontiguousarray(inputs["L0_attn_w_gate"]),
        b_gate=np.ascontiguousarray(inputs["L0_attn_b_gate"]),
        amp=np.ascontiguousarray(inputs["L0_attn_amplitudes"]),
        shp=np.ascontiguousarray(inputs["L0_attn_sharpness"]),
        off=np.ascontiguousarray(inputs["L0_attn_offsets"]),
        sw1=np.ascontiguousarray(inputs["L0_ffn_shared_w1"]),
        sw2=np.ascontiguousarray(inputs["L0_ffn_shared_w2"]),
        ew1=np.ascontiguousarray(inputs["L0_ffn_expert_w1"]),
        ew2=np.ascontiguousarray(inputs["L0_ffn_expert_w2"]),
        rw=np.ascontiguousarray(inputs["L0_ffn_router_w"]),
        gam=np.ascontiguousarray(inputs["L0_ffn_ln_gamma"]),
        bet=np.ascontiguousarray(inputs["L0_ffn_ln_beta"]),
        ident=ident_, identf=np.eye(128, dtype=np.float32), ramp=rampv, hsum=hsumv,
    )
    in_maps = []
    for c in range(8):
        m = dict(base)
        m["x"] = np.ascontiguousarray(x[2 * c:2 * c + 2].reshape(T, D))
        in_maps.append(m)
    res = run_bass_kernel_spmd(nc, in_maps, core_ids=list(range(8)))
    return np.concatenate([r["out"].reshape(2, S, D) for r in res.results], axis=0)


if __name__ == "__main__":
    nc = build(debug=False)
    print("build ok")


# version 06
# speedup vs baseline: 1.5794x; 1.1771x over previous
import contextlib
import numpy as np
import ml_dtypes
import concourse.bass as bass
import concourse.mybir as mybir
from concourse.bass_utils import run_bass_kernel_spmd

F32 = mybir.dt.float32
BF16 = mybir.dt.bfloat16
AF = mybir.ActivationFunctionType
ALU = mybir.AluOpType
AX = mybir.AxisListType.X
BFNP = ml_dtypes.bfloat16

B = 2
S = 2048
D = 1024
T = B * S
H = 16
HD = 64
E = 8
FE = 2048
DC = 8
FC = 16
EPS = 1e-5
TAB = 4095
UW = 3968


def mkap(ap, offset, dims):
    a = ap.copy()
    a.offset = offset
    a.ap = type(a.ap)(dims)
    return a


def revslice(tile_ap, col_hi, n):
    dims = tile_ap.ap.to_list()
    return mkap(tile_ap, tile_ap.offset + col_hi, [dims[0], [-1, n]])


class SemBank:
    def __init__(self, nc):
        self.nc = nc
        self.h = {}
        self.c = {}

    def sem(self, n):
        if n not in self.h:
            self.h[n] = self.nc.alloc_semaphore(n)
            self.c[n] = 0
        return self.h[n]

    def inc(self, instr, n, amt=1):
        instr.then_inc(self.sem(n), amt)
        self.c[n] += amt
        return self.c[n]

    def wait(self, eng, n, v):
        if v > 0:
            eng.wait_ge(self.sem(n), v)



class MaybeBlk:
    def __init__(self, blk, en):
        self._blk, self._en = blk, en

    def __getattr__(self, k):
        f = getattr(self._blk, k)
        if k in ("tensor", "vector", "scalar", "gpsimd", "sync"):
            return (lambda fn: f(fn)) if self._en else (lambda fn: None)
        return f


class Serial:

    def __init__(self, sb, blk, enabled=True, name="ser"):
        self.sb = sb
        self.blk = blk
        self.enabled = enabled
        self.name = name
        sb.sem(name)

    def step(self, eng_name, fn, dma=False, extra_waits=()):
        if not self.enabled:
            return
        sb = self.sb
        nm = self.name
        need = sb.c[nm]
        inc = 16 if dma else 1

        def body(eng):
            for n, v in extra_waits:
                sb.wait(eng, n, v)
            sb.wait(eng, nm, need)
            ins = fn(eng)
            ins.then_inc(sb.sem(nm), inc)
        getattr(self.blk, eng_name)(body)
        sb.c[nm] += inc

    def multi(self, eng_name, fns, dma=False, extra_waits=()):
        if not self.enabled:
            return
        sb = self.sb
        nm = self.name
        need = sb.c[nm]
        inc = 16 if dma else 1

        def body(eng):
            for n, v in extra_waits:
                sb.wait(eng, n, v)
            sb.wait(eng, nm, need)
            for fn in fns:
                fn(eng).then_inc(sb.sem(nm), inc)
        getattr(self.blk, eng_name)(body)
        sb.c[nm] += inc * len(fns)


def build(debug=False, phases=(0, 1, 2, 3, 4, 5, 6)):
    nc = bass.Bass("TRN2", target_bir_lowering=False)

    x_in = nc.dram_tensor("x", [T, D], F32, kind="ExternalInput")
    w_in = nc.dram_tensor("w_in", [D, 3 * D], F32, kind="ExternalInput")
    w_gate = nc.dram_tensor("w_gate", [D, 2 * D], F32, kind="ExternalInput")
    b_gate = nc.dram_tensor("b_gate", [2 * D], F32, kind="ExternalInput")
    amp_in = nc.dram_tensor("amp", [H, 21], F32, kind="ExternalInput")
    shp_in = nc.dram_tensor("shp", [H, 21], F32, kind="ExternalInput")
    off_in = nc.dram_tensor("off", [H, 21], F32, kind="ExternalInput")
    sw1_in = nc.dram_tensor("sw1", [FE, D], F32, kind="ExternalInput")
    sw2_in = nc.dram_tensor("sw2", [D, FE], F32, kind="ExternalInput")
    ew1_in = nc.dram_tensor("ew1", [E, FE, D], F32, kind="ExternalInput")
    ew2_in = nc.dram_tensor("ew2", [E, D, FE], F32, kind="ExternalInput")
    rw_in = nc.dram_tensor("rw", [E, D], F32, kind="ExternalInput")
    gam_in = nc.dram_tensor("gam", [D], F32, kind="ExternalInput")
    bet_in = nc.dram_tensor("bet", [D], F32, kind="ExternalInput")
    ident = nc.dram_tensor("ident", [128, 128], BF16, kind="ExternalInput")
    identf = nc.dram_tensor("identf", [128, 128], F32, kind="ExternalInput")
    ramp = nc.dram_tensor("ramp", [112, TAB], F32, kind="ExternalInput")
    hsum = nc.dram_tensor("hsum", [112, 16], F32, kind="ExternalInput")

    out = nc.dram_tensor("out", [T, D], F32, kind="ExternalOutput")

    ikind = "ExternalOutput" if debug else "Internal"
    tab_s = nc.dram_tensor("tab_s", [H, TAB], F32, kind=ikind)
    k_s = nc.dram_tensor("k_s", [B, H, HD, S], BF16, kind=ikind)
    q_s = nc.dram_tensor("q_s", [B, H, HD, S], BF16, kind=ikind)
    v_s = nc.dram_tensor("v_s", [B, S, D], BF16, kind=ikind)
    h_s = nc.dram_tensor("h_s", [DC, 128, T], BF16, kind=ikind)
    ao_s = nc.dram_tensor("ao_s", [DC, 128, T], BF16, kind=ikind)
    ag_s = nc.dram_tensor("ag_s", [DC, 128, T], BF16, kind=ikind)
    gt_s = nc.dram_tensor("gt_s", [E, T], BF16, kind=ikind)
    eo_s = nc.dram_tensor("eo_s", [DC, 128, T], BF16, kind=ikind)
    w1t_s = nc.dram_tensor("w1t_s", [E, DC, 128, FE], BF16, kind=ikind)
    w2t_s = nc.dram_tensor("w2t_s", [E, FC, 128, D], BF16, kind=ikind)
    st1_s = nc.dram_tensor("st1_s", [DC, 128, FE], BF16, kind=ikind)
    st2_s = nc.dram_tensor("st2_s", [FC, 128, D], BF16, kind=ikind)
    s1_s = nc.dram_tensor("s1_s", [E + 1, 128, FC], F32, kind=ikind)

    id_bf = nc.alloc_sbuf_tensor("id_bf", [128, 128], BF16)
    id_f32 = nc.alloc_sbuf_tensor("id_f32", [128, 128], F32)
    ones_bf = nc.alloc_sbuf_tensor("ones_bf", [128, 1], BF16)
    ones_bfr = nc.alloc_sbuf_tensor("ones_bfr", [1, 128], BF16)
    ones_fr = nc.alloc_sbuf_tensor("ones_fr", [1, 128], F32)
    bg_sb = nc.alloc_sbuf_tensor("bg_sb", [128, 16], F32)
    rwt_sb = nc.alloc_sbuf_tensor("rwt_sb", [128, DC * E], F32)
    gate_fm = nc.alloc_sbuf_tensor("gate_fm", [8, T], BF16)


    sb = SemBank(nc)
    for n in ["ser", "ld", "pe", "ac", "dv", "po", "st", "pe2", "dv2", "ac2", "out", "pe3", "dv3", "fl", "qld", "qdv", "qpe", "qpo", "qst", "qout"]:
        sb.sem(n)

    with contextlib.ExitStack() as _es:
        row_sb = _es.enter_context(nc.sbuf_tensor("p0_row", [1, D], F32))
        rw_sb = _es.enter_context(nc.sbuf_tensor("p0_rw", [8, D], F32))
        bgt_row = _es.enter_context(nc.sbuf_tensor("p0_bgt", [16, 128], F32))
        rwb_sb = _es.enter_context(nc.sbuf_tensor("p0_rwb", [8, D], BF16))
        prm = _es.enter_context(nc.sbuf_tensor("p0_prm", [112, 9], F32))
        prm2 = _es.enter_context(nc.sbuf_tensor("p0_prm2", [112, 6], F32))
        ramp_sb = _es.enter_context(nc.sbuf_tensor("p0_ramp", [112, TAB], F32))
        hsum_sb = _es.enter_context(nc.sbuf_tensor("p0_hsum", [112, 16], F32))
        d2_sb = _es.enter_context(nc.sbuf_tensor("p0_d2", [112, TAB], F32))
        eg_sb = _es.enter_context(nc.sbuf_tensor("p0_eg", [112, 3 * TAB], F32))
        tab_sb = _es.enter_context(nc.sbuf_tensor("p0_tab", [16, TAB], F32))
        pf = [_es.enter_context(nc.psum_tensor(f"p0pf{i}", [128, 512], F32)) for i in range(4)]
        blk = _es.enter_context(nc.Block())
        blk = MaybeBlk(blk, 0 in phases)
        sr = Serial(sb, blk, 0 in phases)
        sr.multi("sync", [
            lambda e: e.dma_start(id_bf[:], ident.ap()),
            lambda e: e.dma_start(id_f32[:], identf.ap()),
            lambda e: e.dma_start(ramp_sb[:], ramp.ap()),
            lambda e: e.dma_start(hsum_sb[:], hsum.ap()),
            lambda e: e.dma_start(bgt_row[:], mkap(b_gate.ap(), 0, [[128, 16], [1, 128]])),
            lambda e: e.dma_start(rw_sb[:], rw_in.ap()),
        ] + [
            (lambda gi, g, src: lambda e: e.dma_start(
                prm[:, gi * 3 + g: gi * 3 + g + 1],
                mkap(src.ap(), g * 7, [[21, 16], [1, 7], [1, 1]])))(gi, g, src)
            for gi, src in enumerate([off_in, shp_in, amp_in]) for g in range(3)
        ], dma=True)
        sr.multi("vector", [
            lambda e: e.memset(ones_bf[:], 1.0),
            lambda e: e.memset(ones_bfr[:], 1.0),
            lambda e: e.memset(ones_fr[:], 1.0),
            lambda e: e.tensor_scalar(prm2[:, 0:3], prm[:, 0:3], -1.0, None, op0=ALU.mult),
        ])
        sr.step("scalar", lambda e: e.activation(out=prm2[:, 3:6], in_=prm[:, 3:6], func=AF.Abs))
        sr.step("vector", lambda e: e.tensor_scalar(prm2[:, 3:6], prm2[:, 3:6], -1.0, None, op0=ALU.mult))

        sr.multi("tensor", [
            (lambda dc: lambda e: e.transpose(pf[3][:, dc * 8:(dc + 1) * 8],
                                              rw_sb[0:8, dc * 128:(dc + 1) * 128], id_f32[0:8, 0:8]))(dc)
            for dc in range(DC)
        ])
        sr.step("vector", lambda e: e.tensor_copy(rwt_sb[:], pf[3][:, 0:DC * E]))
        sr.step("tensor", lambda e: e.transpose(pf[3][:, 0:16], bgt_row[:], id_f32[0:16, 0:16]))
        sr.step("vector", lambda e: e.tensor_copy(bg_sb[:], pf[3][:, 0:16]))
        for g in range(3):
            sr.step("scalar", (lambda g: lambda e: e.activation(
                out=d2_sb[:], in_=ramp_sb[:], func=AF.Square, bias=prm2[:, g:g + 1]))(g))
            sr.step("scalar", (lambda g: lambda e: e.activation(
                out=eg_sb[:, g * TAB:(g + 1) * TAB], in_=d2_sb[:], func=AF.Exp,
                scale=prm2[:, 3 + g:4 + g]))(g))
            sr.step("vector", (lambda g: lambda e: e.tensor_scalar(
                eg_sb[:, g * TAB:(g + 1) * TAB], eg_sb[:, g * TAB:(g + 1) * TAB],
                prm[:, 6 + g:7 + g], None, op0=ALU.mult))(g))
        for c in range(8):
            w = 512 if c < 7 else TAB - 7 * 512
            sr.step("tensor", (lambda c, w: lambda e: [
                e.matmul(pf[c % 2][0:16, 0:w], hsum_sb[:], eg_sb[:, g * TAB + c * 512: g * TAB + c * 512 + w],
                         start=(g == 0), stop=(g == 2)) for g in range(3)][-1])(c, w))
            sr.step("scalar", (lambda c, w: lambda e: e.copy(
                tab_sb[:, c * 512:c * 512 + w], pf[c % 2][0:16, 0:w]))(c, w))
        sr.step("sync", lambda e: e.dma_start(tab_s.ap(), tab_sb[:]), dma=True)

    with contextlib.ExitStack() as _es:
        gam_rep = _es.enter_context(nc.sbuf_tensor("p1_gam", [128, D], F32))
        bet_rep = _es.enter_context(nc.sbuf_tensor("p1_bet", [128, D], F32))
        win_sb = _es.enter_context(nc.sbuf_tensor("p1_win", [128, DC * 3 * D], BF16))
        wstg = _es.enter_context(nc.sbuf_tensor("p1_wstg", [128, 3 * D], F32))
        xt2 = _es.enter_context(nc.sbuf_tensor("p1_xt", [128, 2 * D], F32))
        sq_scr = _es.enter_context(nc.sbuf_tensor("p1_scr", [128, D], BF16))
        xn_t = _es.enter_context(nc.sbuf_tensor("p1_xn", [128, D], BF16))
        xnn = _es.enter_context(nc.sbuf_tensor("p1_xnn", [128, D], F32))
        xn_f = _es.enter_context(nc.sbuf_tensor("p1_xnf", [128, D], F32))
        h1tmp = _es.enter_context(nc.sbuf_tensor("p1_ht", [128, D], F32))
        h_t = _es.enter_context(nc.sbuf_tensor("p1_hb", [128, D], BF16))
        h_f = _es.enter_context(nc.sbuf_tensor("p1_hf", [128, D], F32))
        h_fm32 = _es.enter_context(nc.sbuf_tensor("p1_hfm32", [128, DC * 512], F32))
        tny = _es.enter_context(nc.sbuf_tensor("p1_tiny", [128, 16], F32))
        xn_fm = _es.enter_context(nc.sbuf_tensor("p1_xnfm", [128, DC * 512], BF16))
        h_fm = _es.enter_context(nc.sbuf_tensor("p1_hfm", [128, DC * 512], BF16))
        pstg = _es.enter_context(nc.sbuf_tensor("p1_stg", [128, 2 * 512], BF16))
        vstg = _es.enter_context(nc.sbuf_tensor("p1_vstg", [128, 2 * 512], BF16))
        r8 = _es.enter_context(nc.sbuf_tensor("p1_r8", [128, 8], F32))
        g8 = _es.enter_context(nc.sbuf_tensor("p1_g8", [128, 8], BF16))
        wr2 = _es.enter_context(nc.sbuf_tensor("p2_wr", [128, 2 * FE], F32))
        qa2 = _es.enter_context(nc.sbuf_tensor("p2_a", [128, 2 * FE], BF16))
        qb2 = _es.enter_context(nc.sbuf_tensor("p2_b", [128, 2 * FE], BF16))
        qt2 = _es.enter_context(nc.sbuf_tensor("p2_t", [128, 2 * FE], BF16))
        tstg = _es.enter_context(nc.sbuf_tensor("p2_ts", [128, 2 * 1024], BF16))
        qs = _es.enter_context(nc.sbuf_tensor("p2_s", [128, 8], F32))
        s1stg = _es.enter_context(nc.sbuf_tensor("p2_s1", [128, FC], F32))
        pb = [_es.enter_context(nc.psum_tensor(f"pb{i}", [128, 1024], BF16)) for i in range(2)]
        pb2 = _es.enter_context(nc.psum_tensor("pb2", [128, 1024], BF16))
        pfR = _es.enter_context(nc.psum_tensor("pfR", [128, 512], F32))
        pf = [_es.enter_context(nc.psum_tensor(f"p1pf{i}", [128, 512], F32)) for i in range(4)]
        blk = _es.enter_context(nc.Block())
        blk = MaybeBlk(blk, 1 in phases)
        sr = Serial(sb, blk, 1 in phases)

        def _emit_hf32(sr, tt):
            for half in range(2):
                sr.multi("tensor", [
                    (lambda dc, half: lambda e: e.transpose(
                        pf[2 + half][:, (dc % 4) * 128:(dc % 4 + 1) * 128],
                        h_f[:, dc * 128:(dc + 1) * 128], id_f32[:]))(dc, half)
                    for dc in range(half * 4, half * 4 + 4)
                ])
                sr.step("vector", (lambda tt=tt, half=half: lambda e: e.tensor_copy(
                    mkap(h_fm32[:, :], h_fm32[:, :].offset + half * 4 * 512 + tt * 128,
                         [h_fm32[:, :].ap.to_list()[0], [512, 4], [1, 128]]),
                    pf[2 + half][:, 0:512]))(tt, half))

        en2 = 2 in phases
        blk2 = MaybeBlk(blk._blk if hasattr(blk, "_blk") else blk, en2)
        p2_units = []
        if en2:
            mats = []
            for e_ in range(E):
                mats.append(("w1", ew1_in, e_, FE, D))
            for e_ in range(E):
                mats.append(("w2", ew2_in, e_, D, FE))
            mats.append(("w1", sw1_in, E, FE, D))
            mats.append(("w2", sw2_in, E, D, FE))
            p2_state = {"i": 0, "hidx": 0, "ns1": 0, "tile_pe_end": []}

            def _p2_tile(kind, srcten, e_, NR, LC, rt):
                st_ = p2_state
                i = st_["i"]
                par = i % 2
                wr = wr2[:, par * FE: par * FE + LC]
                qsl = qs[:, par * 4: par * 4 + 4]
                base = (e_ * NR * LC if e_ < E else 0) + rt * 128 * LC
                nhalf = LC // 1024
                ns1 = st_["ns1"]

                def L(e):
                    if i >= 2:
                        e.wait_ge(sb.sem("qdv"), i - 1)
                    if kind == "w1" and rt == 0:
                        e.wait_ge(sb.sem("qout"), 16 * ns1)
                    e.dma_start(wr, mkap(srcten.ap(), base, [[LC, 128], [1, LC]])).then_inc(sb.sem("qld"), 16)
                    sb.c["qld"] += 16
                blk2.sync(L)

                tpe = st_["tile_pe_end"]

                def RAD(e):
                    e.wait_ge(sb.sem("qld"), 16 * (i + 1))
                    if i >= 2:
                        e.wait_ge(sb.sem("qpe"), tpe[i - 2])
                    ins = e.tensor_reduce(qsl[:, 0:1], wr, axis=AX, op=ALU.add, apply_absolute_value=True)
                    sb.inc(ins, "fl")
                    e.wait_ge(sb.sem("fl"), sb.c["fl"])
                    ins = e.tensor_scalar(qsl[:, 1:2], qsl[:, 0:1], 1.0 / LC, 1e-5, op0=ALU.mult, op1=ALU.max)
                    sb.inc(ins, "fl")
                    e.wait_ge(sb.sem("fl"), sb.c["fl"])
                    e.tensor_scalar(qsl[:, 2:3], qsl[:, 1:2], -0.5, None, op0=ALU.mult)
                    ins = e.tensor_scalar(qsl[:, 3:4], qsl[:, 1:2], 0.5, None, op0=ALU.mult)
                    if kind == "w1":
                        ins = e.tensor_copy(s1stg[:, rt:rt + 1], qsl[:, 1:2])
                    sb.inc(ins, "fl")
                    e.wait_ge(sb.sem("fl"), sb.c["fl"])
                    e.tensor_scalar(qa2[:, par * FE:par * FE + LC], wr, qsl[:, 2:3], 0.0,
                                    op0=ALU.add, op1=ALU.is_gt)
                    ins = e.tensor_scalar(qb2[:, par * FE:par * FE + LC], wr, qsl[:, 3:4], 0.0,
                                          op0=ALU.add, op1=ALU.is_lt)
                    sb.inc(ins, "fl")
                    e.wait_ge(sb.sem("fl"), sb.c["fl"])
                    ins = e.tensor_tensor(qt2[:, par * FE:par * FE + LC], qa2[:, par * FE:par * FE + LC],
                                          qb2[:, par * FE:par * FE + LC], op=ALU.subtract)
                    if kind == "w2":
                        sb.inc(ins, "fl")
                        e.wait_ge(sb.sem("fl"), sb.c["fl"])
                        ins = e.tensor_scalar(qt2[:, par * FE:par * FE + LC],
                                              qt2[:, par * FE:par * FE + LC], qsl[:, 1:2], None, op0=ALU.mult)
                    sb.inc(ins, "qdv")
                blk2.vector(RAD)

                for hh in range(nhalf):
                    hidx = st_["hidx"]
                    hpar = hidx % 2

                    def Tst(e, hh=hh, hpar=hpar, hidx=hidx):
                        e.wait_ge(sb.sem("qdv"), i + 1)
                        if hidx >= 1:
                            e.wait_ge(sb.sem("qpo"), hidx)
                        ins = None
                        for bkk in range(8):
                            ins = e.transpose(pb2[:, bkk * 128:(bkk + 1) * 128],
                                              qt2[:, par * FE + hh * 1024 + bkk * 128:
                                                  par * FE + hh * 1024 + (bkk + 1) * 128], id_bf[:])
                        sb.inc(ins, "qpe")
                    blk2.tensor(Tst)

                    def Est(e, hpar=hpar, hidx=hidx):
                        e.wait_ge(sb.sem("qpe"), hidx + 1)
                        if hidx >= 2:
                            e.wait_ge(sb.sem("qst"), 16 * (hidx - 1))
                        ins = e.tensor_copy(tstg[:, hpar * 1024:(hpar + 1) * 1024], pb2[:])
                        sb.inc(ins, "qpo")
                    blk2.vector(Est)

                    if kind == "w1":
                        dst = w1t_s if e_ < E else st1_s
                        dbase = (e_ * DC * 128 * FE if e_ < E else 0) + hh * 8 * 128 * FE + rt * 128
                        ddims = [[FE, 128], [128 * FE, 8], [1, 128]]
                    else:
                        dst = w2t_s if e_ < E else st2_s
                        dbase = (e_ * FC * 128 * D if e_ < E else 0) + hh * 8 * 128 * D + rt * 128
                        ddims = [[D, 128], [128 * D, 8], [1, 128]]

                    def Sst(e, dst=dst, dbase=dbase, ddims=ddims, hpar=hpar, hidx=hidx):
                        e.wait_ge(sb.sem("qpo"), hidx + 1)
                        sap = mkap(tstg[:, :], tstg[:, :].offset + hpar * 1024,
                                   [tstg[:, :].ap.to_list()[0], [128, 8], [1, 128]])
                        e.dma_start(mkap(dst.ap(), dbase, ddims), sap).then_inc(sb.sem("qst"), 16)
                        sb.c["qst"] += 16
                    blk2.gpsimd(Sst)
                    st_["hidx"] += 1
                st_["tile_pe_end"].append(sb.c["qpe"])
                st_["i"] += 1

            def _p2_s1(e_):
                n = sb.c["qdv"]

                def S1(e):
                    e.wait_ge(sb.sem("qdv"), n)
                    e.dma_start(mkap(s1_s.ap(), e_ * 128 * FC, [[FC, 128], [1, FC]]),
                                s1stg[:]).then_inc(sb.sem("qout"), 16)
                    sb.c["qout"] += 16
                blk2.sync(S1)
                p2_state["ns1"] += 1

            for kind, srcten, e_, NR, LC in mats:
                for rt in range(NR // 128):
                    p2_units.append((lambda kind=kind, srcten=srcten, e_=e_, NR=NR, LC=LC, rt=rt:
                                     _p2_tile(kind, srcten, e_, NR, LC, rt)))
                if kind == "w1":
                    p2_units.append((lambda e_=e_: _p2_s1(e_)))
        p2_ptr = [0]

        def drain_p2(n):
            while n > 0 and p2_ptr[0] < len(p2_units):
                p2_units[p2_ptr[0]]()
                p2_ptr[0] += 1
                n -= 1



        sr.step("sync", lambda e: e.dma_start(
            xt2[0:1, 0:D], gam_in.ap().rearrange("(one d) -> one d", one=1)), dma=True)
        for c in range(2):
            sr.step("tensor", (lambda c: lambda e: e.matmul(
                pf[c][:, 0:512], ones_fr[:], xt2[0:1, c * 512:(c + 1) * 512], start=True, stop=True))(c))
        sr.multi("vector", [
            (lambda c: lambda e: e.tensor_copy(gam_rep[:, c * 512:(c + 1) * 512], pf[c][:]))(c)
            for c in range(2)
        ])
        sr.step("sync", lambda e: e.dma_start(xt2[0:1, 0:D], bet_in.ap().rearrange("(one d) -> one d", one=1)), dma=True)
        for c in range(2):
            sr.step("tensor", (lambda c: lambda e: e.matmul(
                pf[c][:, 0:512], ones_fr[:], xt2[0:1, c * 512:(c + 1) * 512], start=True, stop=True))(c))
        sr.multi("vector", [
            (lambda c: lambda e: e.tensor_copy(bet_rep[:, c * 512:(c + 1) * 512], pf[c][:]))(c)
            for c in range(2)
        ])
        for dc in range(DC):
            sr.step("sync", (lambda dc: lambda e: e.dma_start(
                wstg[:], x_in.ap() if False else mkap(w_in.ap(), dc * 128 * 3 * D,
                                                      [[3 * D, 128], [1, 3 * D]])))(dc), dma=True)
            sr.step("vector", (lambda dc: lambda e: e.tensor_copy(
                win_sb[:, dc * 3 * D:(dc + 1) * 3 * D], wstg[:]))(dc))

        for grp in range(8):
            for tt in range(4):
                xt = xt2[:, (tt % 2) * D:((tt % 2) + 1) * D]
                r0 = (grp * 4 + tt) * 128
                sr.step("sync", (lambda r0, xt=xt: lambda e: e.dma_start(
                    xt, mkap(x_in.ap(), r0 * D, [[D, 128], [1, D]])))(r0), dma=True)
                sr.step("vector", (lambda xt=xt: lambda e: e.tensor_reduce(
                    tny[:, 0:1], xt, axis=AX, op=ALU.add))(xt))
                sr.step("scalar", (lambda xt=xt: lambda e: e.activation(
                    out=sq_scr[:], in_=xt, func=AF.Square, accum_out=tny[:, 1:2]))(xt))
                sr.multi("vector", [
                    lambda e: e.tensor_scalar(tny[:, 2:3], tny[:, 0:1], 1.0 / D, None, op0=ALU.mult),
                    lambda e: e.tensor_scalar(tny[:, 3:4], tny[:, 1:2], 1.0 / D, None, op0=ALU.mult),
                ])
                sr.step("vector", lambda e: e.tensor_tensor(tny[:, 4:5], tny[:, 2:3], tny[:, 2:3], op=ALU.mult))
                sr.step("vector", lambda e: e.tensor_tensor(tny[:, 5:6], tny[:, 3:4], tny[:, 4:5], op=ALU.subtract))
                sr.step("vector", lambda e: e.tensor_scalar(tny[:, 6:7], tny[:, 5:6], EPS, None, op0=ALU.add))
                sr.step("scalar", lambda e: e.sqrt(tny[:, 7:8], tny[:, 6:7]))
                sr.step("vector", lambda e: e.reciprocal(tny[:, 8:9], tny[:, 7:8]))
                sr.step("vector", (lambda xt=xt: lambda e: e.tensor_scalar(
                    xn_f[:], xt, tny[:, 2:3], tny[:, 8:9], op0=ALU.subtract, op1=ALU.mult))(xt))
                sr.step("vector", lambda e: e.tensor_copy(xn_t[:], xn_f[:]))
                sr.step("vector", lambda e: e.tensor_reduce(tny[:, 0:1], xn_f[:], axis=AX, op=ALU.add))
                sr.step("scalar", lambda e: e.activation(out=sq_scr[:], in_=xn_f[:], func=AF.Square,
                                                         accum_out=tny[:, 1:2]))
                sr.multi("vector", [
                    lambda e: e.tensor_scalar(tny[:, 2:3], tny[:, 0:1], 1.0 / D, None, op0=ALU.mult),
                    lambda e: e.tensor_scalar(tny[:, 3:4], tny[:, 1:2], 1.0 / D, None, op0=ALU.mult),
                ])
                sr.step("vector", lambda e: e.tensor_tensor(tny[:, 4:5], tny[:, 2:3], tny[:, 2:3], op=ALU.mult))
                sr.step("vector", lambda e: e.tensor_tensor(tny[:, 5:6], tny[:, 3:4], tny[:, 4:5], op=ALU.subtract))
                sr.step("vector", lambda e: e.tensor_scalar(tny[:, 6:7], tny[:, 5:6], EPS, None, op0=ALU.add))
                sr.step("scalar", lambda e: e.sqrt(tny[:, 7:8], tny[:, 6:7]))
                sr.step("vector", lambda e: e.reciprocal(tny[:, 8:9], tny[:, 7:8]))
                sr.step("vector", lambda e: e.tensor_scalar(
                    xnn[:], xn_f[:], tny[:, 2:3], tny[:, 8:9], op0=ALU.subtract, op1=ALU.mult))
                sr.step("gpsimd", lambda e: e.tensor_tensor(h1tmp[:], xnn[:], gam_rep[:], op=ALU.mult))
                sr.step("gpsimd", lambda e: e.tensor_tensor(h_f[:], h1tmp[:], bet_rep[:], op=ALU.add))
                sr.step("vector", lambda e: e.tensor_copy(h_t[:], h_f[:]))
                sr.multi("tensor", [
                    (lambda dc: lambda e: e.transpose(
                        pb[0][:, dc * 128:(dc + 1) * 128], xn_t[:, dc * 128:(dc + 1) * 128], id_bf[:]))(dc)
                    for dc in range(DC)
                ])
                sr.step("vector", (lambda tt=tt: lambda e: e.tensor_copy(
                    mkap(xn_fm[:, :], xn_fm[:, :].offset + tt * 128,
                         [xn_fm[:, :].ap.to_list()[0], [512, DC], [1, 128]]),
                    pb[0][:, 0:1024]))(tt))
                sr.multi("tensor", [
                    (lambda dc: lambda e: e.transpose(
                        pb[1][:, dc * 128:(dc + 1) * 128], h_t[:, dc * 128:(dc + 1) * 128], id_bf[:]))(dc)
                    for dc in range(DC)
                ])
                sr.step("vector", (lambda tt=tt: lambda e: e.tensor_copy(
                    mkap(h_fm[:, :], h_fm[:, :].offset + tt * 128, [h_fm[:, :].ap.to_list()[0], [512, DC], [1, 128]]),
                    pb[1][:, 0:1024]))(tt))
                _emit_hf32(sr, tt)
                drain_p2(2)
            sr.multi("gpsimd", [
                (lambda dc, grp=grp: lambda e: e.dma_start(
                    mkap(h_s.ap(), dc * 128 * T + grp * 512, [[T, 128], [1, 512]]),
                    h_fm[:, dc * 512:(dc + 1) * 512]))(dc)
                for dc in range(DC)
            ], dma=True)
            bp = {n: sb.c[n] for n in ["pe", "ac", "dv", "st", "pe2"]}
            serv1 = sb.c["ser"]
            cons_end = []
            for oc in range(24):
                par = oc % 2

                def MM(e, oc=oc, par=par, grp=grp):
                    e.wait_ge(sb.sem("ser"), serv1)
                    if oc >= 2:
                        e.wait_ge(sb.sem("ac"), bp["ac"] + oc - 1)
                    ins = None
                    for dc in range(DC):
                        ins = e.matmul(pf[par][:], win_sb[:, dc * 3 * D + oc * 128: dc * 3 * D + (oc + 1) * 128],
                                       xn_fm[:, dc * 512:(dc + 1) * 512],
                                       start=(dc == 0), stop=(dc == DC - 1))
                    sb.inc(ins, "pe")
                blk.tensor(MM)

                def EV(e, oc=oc, par=par):
                    e.wait_ge(sb.sem("pe"), bp["pe"] + oc + 1)
                    if oc >= 2:
                        ce = cons_end[oc - 2]
                        e.wait_ge(sb.sem("st"), ce["st"])
                        e.wait_ge(sb.sem("pe2"), ce["pe2"])
                    ins = e.copy(pstg[:, par * 512:(par + 1) * 512], pf[par][:])
                    sb.inc(ins, "ac")
                blk.scalar(EV)

                if oc < 8 or oc >= 16:
                    if oc < 8:
                        dstoff = (grp // 4) * H * HD * S + (oc * 2) * HD * S + ((grp % 4) * 512)
                        dst = k_s
                    else:
                        dstoff = (grp // 4) * H * HD * S + ((oc - 16) * 2) * HD * S + ((grp % 4) * 512)
                        dst = q_s

                    def ST(e, oc=oc, par=par, dst=dst, dstoff=dstoff):
                        e.wait_ge(sb.sem("ac"), bp["ac"] + oc + 1)
                        e.dma_start(mkap(dst.ap(), dstoff, [[S, 128], [1, 512]]),
                                    pstg[:, par * 512:(par + 1) * 512]).then_inc(sb.sem("st"), 16)
                        sb.c["st"] += 16
                    blk.gpsimd(ST)
                else:
                    vc = oc - 8

                    def TP(e, oc=oc, par=par):
                        e.wait_ge(sb.sem("ac"), bp["ac"] + oc + 1)
                        if oc >= 2:
                            e.wait_ge(sb.sem("dv"), cons_end[oc - 2]["dv"])
                        ins = None
                        for ts in range(4):
                            ins = e.transpose(pb[par][:, ts * 128:(ts + 1) * 128],
                                              pstg[:, par * 512 + ts * 128: par * 512 + (ts + 1) * 128], id_bf[:])
                        sb.inc(ins, "pe2")
                    blk.tensor(TP)

                    def VE(e, oc=oc, par=par):
                        e.wait_ge(sb.sem("pe2"), sb.c["pe2"])
                        if oc >= 2:
                            e.wait_ge(sb.sem("st"), cons_end[oc - 2]["st"])
                        ins = e.tensor_copy(vstg[:, par * 512:(par + 1) * 512], pb[par][:, 0:512])
                        sb.inc(ins, "dv")
                    blk.vector(VE)

                    def SV(e, vc=vc, par=par, grp=grp):
                        e.wait_ge(sb.sem("dv"), sb.c["dv"])
                        e.dma_start(mkap(v_s.ap(), (grp // 4) * S * D + (grp % 4) * 512 * D + vc * 128,
                                         [[D, 128], [128 * D, 4], [1, 128]]),
                                    mkap(vstg[:, :], vstg[:, :].offset + par * 512,
                                         [vstg[:, :].ap.to_list()[0], [128, 4], [1, 128]])).then_inc(sb.sem("st"), 16)
                        sb.c["st"] += 16
                    blk.gpsimd(SV)
                cons_end.append({"st": sb.c["st"], "pe2": sb.c["pe2"], "dv": sb.c["dv"]})
                drain_p2(1)
            sr.step("sync", lambda e: e.nop(), extra_waits=[
                ("pe", sb.c["pe"]), ("ac", sb.c["ac"]), ("dv", sb.c["dv"]),
                ("st", sb.c["st"]), ("pe2", sb.c["pe2"])])
            for tq in range(4):
                def rmm(e, tq=tq):
                    ins = None
                    for dc in range(DC):
                        ins = e.matmul(pfR[:, 0:8], h_fm32[:, dc * 512 + tq * 128: dc * 512 + (tq + 1) * 128],
                                       rwt_sb[:, dc * 8:(dc + 1) * 8],
                                       start=(dc == 0), stop=(dc == DC - 1))
                    return ins
                sr.step("tensor", rmm)
                sr.step("vector", lambda e: e.tensor_reduce(tny[:, 0:1], pfR[:, 0:8], axis=AX,
                                                            op=ALU.max, negate=True))
                sr.step("scalar", lambda e: e.activation(out=r8[:], in_=pfR[:, 0:8], func=AF.Exp,
                                                         bias=tny[:, 0:1]))
                sr.step("vector", lambda e: e.tensor_reduce(tny[:, 1:2], r8[:], axis=AX, op=ALU.add))
                sr.step("vector", lambda e: e.reciprocal(tny[:, 2:3], tny[:, 1:2]))
                sr.step("vector", lambda e: e.tensor_reduce(tny[:, 3:4], r8[:], axis=AX, op=ALU.max))
                sr.step("vector", lambda e: e.tensor_tensor(tny[:, 4:5], tny[:, 3:4], tny[:, 2:3], op=ALU.mult))
                sr.step("vector", lambda e: e.tensor_scalar(g8[:], r8[:], tny[:, 3:4], tny[:, 4:5],
                                                            op0=ALU.is_equal, op1=ALU.mult))
                sr.step("tensor", (lambda tq=tq: lambda e: e.transpose(
                    pb[0][0:8, tq * 128:(tq + 1) * 128], g8[:], id_bf[:]))(tq))
            sr.step("vector", (lambda grp=grp: lambda e: e.tensor_copy(
                gate_fm[0:8, grp * 512:(grp + 1) * 512], pb[0][0:8, 0:512]))(grp))
        drain_p2(10 ** 9)
        sr.step("gpsimd", lambda e: e.dma_start(gt_s.ap(), gate_fm[:]), dma=True)
        sr.step("sync", lambda e: e.nop(), extra_waits=[("qst", sb.c["qst"]), ("qout", sb.c["qout"])])

    with contextlib.ExitStack() as _es:
        kab = _es.enter_context(nc.sbuf_tensor("p3_k", [128, S], BF16))
        qab = _es.enter_context(nc.sbuf_tensor("p3_q", [128, S], BF16))
        v65 = _es.enter_context(nc.sbuf_tensor("p3_v", [128, 2 * 16 * 65], BF16))
        u_sb = _es.enter_context(nc.sbuf_tensor("p3_u", [128, UW], F32))
        eb = _es.enter_context(nc.sbuf_tensor("p3_eb", [128, 2 * UW], BF16))
        sqt = _es.enter_context(nc.sbuf_tensor("p3_sq", [128, S], BF16))
        exb = _es.enter_context(nc.sbuf_tensor("p3_ex", [128, 2 * 1024], BF16))
        pub = _es.enter_context(nc.sbuf_tensor("p3_pu", [128, 2 * 1024], BF16))
        nmx = _es.enter_context(nc.sbuf_tensor("p3_nm", [1, 16], F32))
        negm = _es.enter_context(nc.sbuf_tensor("p3_ng", [128, 1], F32))
        rcp = _es.enter_context(nc.sbuf_tensor("p3_rc", [1, 2 * 512], F32))
        repsb = _es.enter_context(nc.sbuf_tensor("p3_rp", [64, 2 * 512], F32))
        aob = _es.enter_context(nc.sbuf_tensor("p3_ao", [64, 4 * 512], BF16))
        s01 = [_es.enter_context(nc.psum_tensor(f"p3s{i}", [128, 1024], F32)) for i in range(2)]
        avA = _es.enter_context(nc.psum_tensor("p3avA", [128, 512], F32))
        avB = _es.enter_context(nc.psum_tensor("p3avB", [128, 512], F32))
        blk = _es.enter_context(nc.Block())
        en3 = 3 in phases
        blk = MaybeBlk(blk, en3)
        sr = Serial(sb, blk, en3)
        sr.step("vector", lambda e: e.memset(v65[:], 1.0))
        for pr in range(16):
            b_, j = pr // 8, pr % 8
            sr.multi("sync", [
                (lambda b_, j: lambda e: e.dma_start(
                    kab[:], mkap(k_s.ap(), b_ * H * HD * S + (2 * j) * HD * S, [[S, 128], [1, S]])))(b_, j),
                (lambda b_, j: lambda e: e.dma_start(
                    qab[:], mkap(q_s.ap(), b_ * H * HD * S + (2 * j) * HD * S, [[S, 128], [1, S]])))(b_, j),
                (lambda j: lambda e: e.dma_start(
                    u_sb[:], mkap(tab_s.ap(), (2 * j) * TAB, [[1, 128], [1, UW]])))(j),
            ], dma=True)
            sr.step("scalar", lambda e: e.activation(out=eb[:, 0:UW], in_=u_sb[:], func=AF.Exp))
            sr.step("sync", (lambda j: lambda e: e.dma_start(
                u_sb[:], mkap(tab_s.ap(), (2 * j + 1) * TAB, [[1, 128], [1, UW]])))(j), dma=True)
            sr.step("scalar", lambda e: e.activation(out=eb[:, UW:2 * UW], in_=u_sb[:], func=AF.Exp))
            sr.multi("sync", [
                (lambda b_, j, kc, hh: lambda e: e.dma_start(
                    v65[:, (hh * 16 + kc) * 65:(hh * 16 + kc) * 65 + 64],
                    mkap(v_s.ap(), b_ * S * D + (kc * 128) * D + (2 * j + hh) * 64,
                         [[D, 128], [1, 64]])))(b_, j, kc, hh)
                for hh in range(2) for kc in range(16)
            ], dma=True)
            sr.step("vector", lambda e: e.tensor_tensor(sqt[:], kab[:], kab[:], op=ALU.mult))
            for c in range(4):
                sr.step("tensor", (lambda c: lambda e: e.matmul(
                    avA[0:1, 0:512], ones_bf[:], sqt[:, c * 512:(c + 1) * 512], start=True, stop=True))(c))
                sr.step("vector", (lambda c: lambda e: e.tensor_reduce(
                    nmx[0:1, c:c + 1], avA[0:1, 0:512], axis=AX, op=ALU.max))(c))
            sr.step("vector", lambda e: e.tensor_tensor(sqt[:], qab[:], qab[:], op=ALU.mult))
            for c in range(4):
                sr.step("tensor", (lambda c: lambda e: e.matmul(
                    avA[0:1, 0:512], ones_bf[:], sqt[:, c * 512:(c + 1) * 512], start=True, stop=True))(c))
                sr.step("vector", (lambda c: lambda e: e.tensor_reduce(
                    nmx[0:1, 4 + c:5 + c], avA[0:1, 0:512], axis=AX, op=ALU.max))(c))
            sr.step("vector", lambda e: e.tensor_reduce(nmx[0:1, 8:9], nmx[0:1, 0:4], axis=AX, op=ALU.max))
            sr.step("vector", lambda e: e.tensor_reduce(nmx[0:1, 9:10], nmx[0:1, 4:8], axis=AX, op=ALU.max))
            sr.step("vector", lambda e: e.tensor_tensor(nmx[0:1, 10:11], nmx[0:1, 8:9], nmx[0:1, 9:10], op=ALU.add))
            sr.step("vector", lambda e: e.tensor_scalar(nmx[0:1, 11:12], nmx[0:1, 10:11], -1.0 / 16.0, None, op0=ALU.mult))
            sr.step("tensor", lambda e: e.matmul(avA[:, 0:1], ones_fr[:], nmx[0:1, 11:12], start=True, stop=True))
            sr.step("vector", lambda e: e.tensor_copy(negm[:], avA[:, 0:1]))

            base = {n: sb.c[n] for n in ["pe", "ac", "dv", "pe2", "dv2", "dv3", "pe3", "st"]}
            serv = sb.c["ser"]

            def em_exp(u):
                kc = u % 16

                def ex_fn(e, u=u, kc=kc):
                    e.wait_ge(sb.sem("pe"), base["pe"] + u + 1)
                    if u >= 2:
                        e.wait_ge(sb.sem("dv"), base["dv"] + u - 1)
                    ins = e.activation(out=exb[:, (kc % 2) * 1024:(kc % 2 + 1) * 1024],
                                       in_=s01[kc % 2][:], func=AF.Exp, bias=negm[:], scale=0.125)
                    sb.inc(ins, "ac")
                blk.scalar(ex_fn)

            def em_mul(u):
                qc, kc = divmod(u, 16)

                def mul_fn(e, u=u, qc=qc, kc=kc):
                    e.wait_ge(sb.sem("ac"), base["ac"] + u + 1)
                    if u >= 2:
                        e.wait_ge(sb.sem("pe2"), base["pe2"] + u - 1)
                    a_col = 2047 + kc * 128 - qc * 512
                    src_eb = mkap(eb[:, :], eb[:, :].offset + a_col,
                                  [eb[:, :].ap.to_list()[0], [UW, 2], [-1, 512]])
                    ins = e.tensor_tensor(pub[:, (kc % 2) * 1024:(kc % 2 + 1) * 1024],
                                          exb[:, (kc % 2) * 1024:(kc % 2 + 1) * 1024],
                                          src_eb, op=ALU.mult)
                    sb.inc(ins, "dv")
                blk.vector(mul_fn)

            def em_qk(u):
                qc, kc = divmod(u, 16)

                def mm_qk(e, u=u, qc=qc, kc=kc):
                    e.wait_ge(sb.sem("ser"), serv)
                    if u >= 2:
                        e.wait_ge(sb.sem("ac"), base["ac"] + u - 1)
                    if kc == 0:
                        e.wait_ge(sb.sem("dv2"), base["dv2"] + qc)
                    e.matmul(s01[kc % 2][:, 0:512], kab[0:64, kc * 128:(kc + 1) * 128],
                             qab[0:64, qc * 512:(qc + 1) * 512], start=True, stop=True,
                             tile_position=(0, 0))
                    ins = e.matmul(s01[kc % 2][:, 512:1024], kab[64:128, kc * 128:(kc + 1) * 128],
                                   qab[64:128, qc * 512:(qc + 1) * 512], start=True, stop=True,
                                   tile_position=(64, 0))
                    sb.inc(ins, "pe")
                blk.tensor(mm_qk)

            def em_av(u):
                qc, kc = divmod(u, 16)

                def mm_av(e, u=u, kc=kc):
                    e.wait_ge(sb.sem("dv"), base["dv"] + u + 1)
                    e.matmul(avA[0:65, :], v65[:, kc * 65:kc * 65 + 65],
                             pub[:, (kc % 2) * 1024:(kc % 2) * 1024 + 512],
                             start=(kc == 0), stop=(kc == 15))
                    ins = e.matmul(avB[0:65, :], v65[:, (16 + kc) * 65:(16 + kc) * 65 + 65],
                                   pub[:, (kc % 2) * 1024 + 512:(kc % 2 + 1) * 1024],
                                   start=(kc == 0), stop=(kc == 15))
                    sb.inc(ins, "pe2")
                blk.tensor(mm_av)

            def em_epi(qc):
                def rcp_fn(e, qc=qc):
                    e.wait_ge(sb.sem("pe2"), base["pe2"] + (qc + 1) * 16)
                    e.reciprocal(rcp[0:1, 0:512], avA[64:65, :])
                    ins = e.reciprocal(rcp[0:1, 512:1024], avB[64:65, :])
                    sb.inc(ins, "dv3")
                blk.vector(rcp_fn)

                def rep_fn(e, qc=qc):
                    e.wait_ge(sb.sem("dv3"), sb.c["dv3"])
                    e.matmul(s01[0][0:64, 0:512], ones_fr[0:1, 0:64], rcp[0:1, 0:512], start=True, stop=True)
                    ins = e.matmul(s01[0][0:64, 512:1024], ones_fr[0:1, 0:64], rcp[0:1, 512:1024],
                                   start=True, stop=True)
                    sb.inc(ins, "pe3")
                blk.tensor(rep_fn)

                def evac_fn(e, qc=qc):
                    e.wait_ge(sb.sem("pe3"), sb.c["pe3"])
                    if qc >= 2:
                        e.wait_ge(sb.sem("st"), base["st"] + (qc - 1) * 32)
                    e.tensor_copy(repsb[:, 0:512], s01[0][0:64, 0:512])
                    e.tensor_copy(repsb[:, 512:1024], s01[0][0:64, 512:1024])
                    e.tensor_tensor(aob[:, (qc % 2) * 1024:(qc % 2) * 1024 + 512],
                                    avA[0:64, :], repsb[:, 0:512], op=ALU.mult)
                    ins = e.tensor_tensor(aob[:, (qc % 2) * 1024 + 512:(qc % 2 + 1) * 1024],
                                          avB[0:64, :], repsb[:, 512:1024], op=ALU.mult)
                    sb.inc(ins, "dv2")
                blk.vector(evac_fn)

                def st_fn(e, qc=qc, b_=b_, j=j):
                    e.wait_ge(sb.sem("dv2"), sb.c["dv2"])
                    e.dma_start(mkap(ao_s.ap(), j * 128 * T + b_ * S + qc * 512, [[T, 64], [1, 512]]),
                                aob[:, (qc % 2) * 1024:(qc % 2) * 1024 + 512]).then_inc(sb.sem("st"), 16)
                    e.dma_start(mkap(ao_s.ap(), j * 128 * T + 64 * T + b_ * S + qc * 512, [[T, 64], [1, 512]]),
                                aob[:, (qc % 2) * 1024 + 512:(qc % 2 + 1) * 1024]).then_inc(sb.sem("st"), 16)
                    sb.c["st"] += 32
                blk.gpsimd(st_fn)

            for u in range(64):
                if u == 0:
                    em_qk(u)
                    continue
                em_exp(u - 1)
                em_mul(u - 1)
                if u % 16 == 0:
                    em_av(u - 1)
                    em_epi(u // 16 - 1)
                    em_qk(u)
                else:
                    em_qk(u)
                    em_av(u - 1)
            em_exp(63)
            em_mul(63)
            em_av(63)
            em_epi(3)

            def fence(e, n=None):
                e.wait_ge(sb.sem("st"), sb.c["st"])
                return e.nop()
            sr.step("sync", fence)

    with contextlib.ExitStack() as _es:
        wg_sb = _es.enter_context(nc.sbuf_tensor("p4_wg", [128, DC * 2 * D], BF16))
        wstg4 = _es.enter_context(nc.sbuf_tensor("p4_stg", [128, 2 * D], F32))
        aochunk = _es.enter_context(nc.sbuf_tensor("p4_ao", [128, DC * 512], BF16))
        a8 = _es.enter_context(nc.sbuf_tensor("p4_a8", [128, 8 * 512], F32))
        sgb = _es.enter_context(nc.sbuf_tensor("p4_sg", [128, 512], F32))
        agb = _es.enter_context(nc.sbuf_tensor("p4_ag", [128, 2 * 512], BF16))
        pf = [_es.enter_context(nc.psum_tensor(f"p4pf{i}", [128, 512], F32)) for i in range(2)]
        blk = _es.enter_context(nc.Block())
        blk = MaybeBlk(blk, 4 in phases)
        sr = Serial(sb, blk, 4 in phases)
        for dc in range(DC):
            sr.step("sync", (lambda dc: lambda e: e.dma_start(
                wstg4[:], mkap(w_gate.ap(), dc * 128 * 2 * D, [[2 * D, 128], [1, 2 * D]])))(dc), dma=True)
            sr.step("vector", (lambda dc: lambda e: e.tensor_copy(
                wg_sb[:, dc * 2 * D:(dc + 1) * 2 * D], wstg4[:]))(dc))
        for tc in range(8):
            sr.multi("sync", [
                (lambda dc, tc: lambda e: e.dma_start(
                    aochunk[:, dc * 512:(dc + 1) * 512],
                    mkap(ao_s.ap(), dc * 128 * T + tc * 512, [[T, 128], [1, 512]])))(dc, tc)
                for dc in range(DC)
            ], dma=True)
            for gc in range(16):
                par = gc % 2

                def mmg(e, gc=gc, par=par):
                    ins = None
                    for dc in range(DC):
                        ins = e.matmul(pf[par][:], wg_sb[:, dc * 2 * D + gc * 128: dc * 2 * D + (gc + 1) * 128],
                                       aochunk[:, dc * 512:(dc + 1) * 512],
                                       start=(dc == 0), stop=(dc == DC - 1))
                    return ins
                sr.step("tensor", mmg)
                if gc < 8:
                    sr.step("vector", (lambda gc, par: lambda e: e.tensor_scalar(
                        a8[:, gc * 512:(gc + 1) * 512], pf[par][:], bg_sb[:, gc:gc + 1], None,
                        op0=ALU.add))(gc, par))
                else:
                    dco = gc - 8
                    sr.step("scalar", (lambda gc, par: lambda e: e.activation(
                        out=sgb[:], in_=pf[par][:], func=AF.Sigmoid, bias=bg_sb[:, gc:gc + 1]))(gc, par))
                    sr.step("vector", (lambda dco: lambda e: e.tensor_tensor(
                        agb[:, (dco % 2) * 512:(dco % 2 + 1) * 512], a8[:, dco * 512:(dco + 1) * 512],
                        sgb[:], op=ALU.mult))(dco))
                    sr.step("gpsimd", (lambda dco, tc: lambda e: e.dma_start(
                        mkap(ag_s.ap(), dco * 128 * T + tc * 512, [[T, 128], [1, 512]]),
                        agb[:, (dco % 2) * 512:(dco % 2 + 1) * 512]))(dco, tc), dma=True)

    with contextlib.ExitStack() as _es:
        eo_acc = _es.enter_context(nc.sbuf_tensor("p5_acc", [128, DC * T], BF16))
        w1b = _es.enter_context(nc.sbuf_tensor("p5_w1", [128, DC * FE], BF16))
        w2b = _es.enter_context(nc.sbuf_tensor("p5_w2", [128, FC * D], BF16))
        hb2 = _es.enter_context(nc.sbuf_tensor("p5_h", [128, 2 * DC * 512], BF16))
        h1g2 = _es.enter_context(nc.sbuf_tensor("p5_h1g", [128, 2 * FC * 512], BF16))
        h1b = _es.enter_context(nc.sbuf_tensor("p5_h1", [128, 2 * 512], BF16))
        grsb2 = _es.enter_context(nc.sbuf_tensor("p5_gr", [128, 2 * 512], BF16))
        grow2 = _es.enter_context(nc.sbuf_tensor("p5_grow", [1, 2 * 512], BF16))
        s1b = _es.enter_context(nc.sbuf_tensor("p5_s1", [128, FC], F32))
        pfG = _es.enter_context(nc.psum_tensor("pfG", [128, 512], F32))
        pf = [_es.enter_context(nc.psum_tensor(f"p5pf{i}", [128, 512], F32)) for i in range(4)]
        blk = _es.enter_context(nc.Block())
        en5 = 5 in phases
        blk = MaybeBlk(blk, en5)
        sr = Serial(sb, blk, en5)
        sr.step("vector", lambda e: e.memset(eo_acc[:], 0.0))
        if en5:
            gi = 0
            ld_end = []
            m1_end = []
            silu_end = []
            gmul_end = []
            m2_end = []
            gc_end = []
            for e_ in range(E):
                sr.multi("sync", [
                    (lambda e_, dc: lambda e: e.dma_start(
                        w1b[:, dc * FE:(dc + 1) * FE],
                        mkap(w1t_s.ap(), e_ * DC * 128 * FE + dc * 128 * FE, [[FE, 128], [1, FE]])))(e_, dc)
                    for dc in range(DC)
                ] + [
                    (lambda e_, fc: lambda e: e.dma_start(
                        w2b[:, fc * D:(fc + 1) * D],
                        mkap(w2t_s.ap(), e_ * FC * 128 * D + fc * 128 * D, [[D, 128], [1, D]])))(e_, fc)
                    for fc in range(FC)
                ] + [
                    (lambda e_: lambda e: e.dma_start(
                        s1b[:], mkap(s1_s.ap(), e_ * 128 * FC, [[FC, 128], [1, FC]])))(e_)
                ], dma=True, extra_waits=[("pe", sb.c["pe"]), ("pe2", sb.c["pe2"]),
                                          ("ac", sb.c["ac"])])
                serv5 = sb.c["ser"]
                for tc in range(8):
                    par = gi % 2
                    hb = hb2[:, par * DC * 512:(par + 1) * DC * 512]
                    h1g = h1g2[:, par * FC * 512:(par + 1) * FC * 512]
                    grsb = grsb2[:, par * 512:(par + 1) * 512]
                    grow = grow2[:, par * 512:(par + 1) * 512]

                    def LH(e, e_=e_, tc=tc, hb=hb, grow=grow, gi=gi, serv5=serv5):
                        e.wait_ge(sb.sem("ser"), serv5)
                        if gi >= 2:
                            e.wait_ge(sb.sem("pe"), m1_end[gi - 2])
                        for dc in range(DC):
                            e.dma_start(hb[:, dc * 512:(dc + 1) * 512],
                                        mkap(h_s.ap(), dc * 128 * T + tc * 512,
                                             [[T, 128], [1, 512]])).then_inc(sb.sem("ld"), 16)
                            sb.c["ld"] += 16
                        e.dma_start(grow, mkap(gt_s.ap(), e_ * T + tc * 512,
                                               [[1, 1], [1, 512]])).then_inc(sb.sem("ld"), 16)
                        sb.c["ld"] += 16
                    blk.sync(LH)
                    ld_end.append(sb.c["ld"])

                    def GR(e, grow=grow, gi=gi, serv5=serv5):
                        e.wait_ge(sb.sem("ser"), serv5)
                        e.wait_ge(sb.sem("ld"), ld_end[gi])
                        if gi >= 1:
                            e.wait_ge(sb.sem("dv"), gc_end[gi - 1])
                        ins = e.matmul(pfG[:], ones_bfr[:], grow, start=True, stop=True)
                        sb.inc(ins, "pe3")
                    blk.tensor(GR)

                    def GC(e, grsb=grsb, gi=gi):
                        e.wait_ge(sb.sem("pe3"), sb.c["pe3"])
                        if gi >= 2:
                            e.wait_ge(sb.sem("po"), gmul_end[gi - 2])
                        ins = e.tensor_copy(grsb, pfG[:])
                        sb.inc(ins, "dv")
                    blk.vector(GC)
                    gc_end.append(sb.c["dv"])

                    bac = sb.c["ac"]
                    bpo = sb.c["po"]
                    bpe = sb.c["pe"]
                    for fc in range(FC):
                        def m1(e, fc=fc, hb=hb, gi=gi, bac=bac):
                            e.wait_ge(sb.sem("ld"), ld_end[gi])
                            if fc >= 2:
                                e.wait_ge(sb.sem("ac"), bac + fc - 1)
                            ins = None
                            for dc in range(DC):
                                ins = e.matmul(pf[fc % 2][:], w1b[:, dc * FE + fc * 128: dc * FE + (fc + 1) * 128],
                                               hb[:, dc * 512:(dc + 1) * 512],
                                               start=(dc == 0), stop=(dc == DC - 1))
                            sb.inc(ins, "pe")
                        blk.tensor(m1)

                        def silu(e, fc=fc, bpe=bpe, bpo=bpo):
                            e.wait_ge(sb.sem("pe"), bpe + fc + 1)
                            if fc >= 2:
                                e.wait_ge(sb.sem("po"), bpo + fc - 1)
                            ins = e.activation(out=h1b[:, (fc % 2) * 512:(fc % 2 + 1) * 512],
                                               in_=pf[fc % 2][:], func=AF.Silu, scale=s1b[:, fc:fc + 1])
                            sb.inc(ins, "ac")
                        blk.scalar(silu)

                        def gmul(e, fc=fc, h1g=h1g, gi=gi, bac=bac):
                            e.wait_ge(sb.sem("ac"), bac + fc + 1)
                            e.wait_ge(sb.sem("dv"), gc_end[gi])
                            if gi >= 2 and fc == 0:
                                e.wait_ge(sb.sem("pe2"), m2_end[gi - 2])
                            ins = e.tensor_tensor(h1g[:, fc * 512:(fc + 1) * 512],
                                                  h1b[:, (fc % 2) * 512:(fc % 2 + 1) * 512], grsb, op=ALU.mult)
                            sb.inc(ins, "po")
                        blk.gpsimd(gmul)
                    m1_end.append(sb.c["pe"])
                    silu_end.append(sb.c["ac"])
                    gmul_end.append(sb.c["po"])

                    bpe2 = sb.c["pe2"]
                    bdv2 = sb.c["dv2"]
                    for dc2 in range(DC):
                        def m2(e, dc2=dc2, h1g=h1g, gi=gi, bpe2=bpe2, bdv2=bdv2):
                            e.wait_ge(sb.sem("po"), gmul_end[gi])
                            if dc2 >= 2:
                                e.wait_ge(sb.sem("dv2"), bdv2 + dc2 - 1)
                            ins = None
                            for fc in range(FC):
                                ins = e.matmul(pf[2 + dc2 % 2][:], w2b[:, fc * D + dc2 * 128: fc * D + (dc2 + 1) * 128],
                                               h1g[:, fc * 512:(fc + 1) * 512],
                                               start=(fc == 0), stop=(fc == FC - 1))
                            sb.inc(ins, "pe2")
                        blk.tensor(m2)

                        def accadd(e, dc2=dc2, tc=tc, bpe2=bpe2):
                            e.wait_ge(sb.sem("pe2"), bpe2 + dc2 + 1)
                            sl = eo_acc[:, dc2 * T + tc * 512: dc2 * T + (tc + 1) * 512]
                            ins = e.tensor_tensor(sl, sl, pf[2 + dc2 % 2][:], op=ALU.add)
                            sb.inc(ins, "dv2")
                        blk.vector(accadd)
                    m2_end.append(sb.c["pe2"])
                    gi += 1
            sr.step("sync", lambda e: e.nop(), extra_waits=[("dv2", sb.c["dv2"])])
        sr.multi("gpsimd", [
            (lambda dc: lambda e: e.dma_start(
                mkap(eo_s.ap(), dc * 128 * T, [[T, 128], [1, T]]), eo_acc[:, dc * T:(dc + 1) * T]))(dc)
            for dc in range(DC)
        ], dma=True)

    with contextlib.ExitStack() as _es:
        sw1b = _es.enter_context(nc.sbuf_tensor("p6_w1", [128, DC * FE], BF16))
        sw2b = _es.enter_context(nc.sbuf_tensor("p6_w2", [128, FC * D], BF16))
        hb6 = _es.enter_context(nc.sbuf_tensor("p6_h", [128, DC * 512], BF16))
        h1s = _es.enter_context(nc.sbuf_tensor("p6_h1", [128, FC * 512], BF16))
        agc = _es.enter_context(nc.sbuf_tensor("p6_ag", [128, DC * 512], BF16))
        eoc = _es.enter_context(nc.sbuf_tensor("p6_eo", [128, DC * 512], BF16))
        x6 = _es.enter_context(nc.sbuf_tensor("p6_x", [128, 4 * D], F32))
        o6 = _es.enter_context(nc.sbuf_tensor("p6_o", [128, 4 * D], F32))
        csum = _es.enter_context(nc.sbuf_tensor("p6_cs", [128, 512], F32))
        csum2 = _es.enter_context(nc.sbuf_tensor("p6_cs2", [128, 512], F32))
        s1b6 = _es.enter_context(nc.sbuf_tensor("p6_s1", [128, FC], F32))
        pfT = _es.enter_context(nc.psum_tensor("pfT", [128, 512], F32))
        pf = [_es.enter_context(nc.psum_tensor(f"p6pf{i}", [128, 512], F32)) for i in range(4)]
        blk = _es.enter_context(nc.Block())
        blk = MaybeBlk(blk, 6 in phases)
        sr = Serial(sb, blk, 6 in phases)
        sr.multi("sync", [
            (lambda dc: lambda e: e.dma_start(
                sw1b[:, dc * FE:(dc + 1) * FE],
                mkap(st1_s.ap(), dc * 128 * FE, [[FE, 128], [1, FE]])))(dc)
            for dc in range(DC)
        ] + [
            (lambda fc: lambda e: e.dma_start(
                sw2b[:, fc * D:(fc + 1) * D],
                mkap(st2_s.ap(), fc * 128 * D, [[D, 128], [1, D]])))(fc)
            for fc in range(FC)
        ] + [
            lambda e: e.dma_start(s1b6[:], mkap(s1_s.ap(), E * 128 * FC, [[FC, 128], [1, FC]]))
        ], dma=True)
        for tc in range(8):
            sr.multi("sync", [
                (lambda dc, tc: lambda e: e.dma_start(
                    hb6[:, dc * 512:(dc + 1) * 512],
                    mkap(h_s.ap(), dc * 128 * T + tc * 512, [[T, 128], [1, 512]])))(dc, tc)
                for dc in range(DC)
            ] + [
                (lambda dc, tc: lambda e: e.dma_start(
                    agc[:, dc * 512:(dc + 1) * 512],
                    mkap(ag_s.ap(), dc * 128 * T + tc * 512, [[T, 128], [1, 512]])))(dc, tc)
                for dc in range(DC)
            ] + [
                (lambda dc, tc: lambda e: e.dma_start(
                    eoc[:, dc * 512:(dc + 1) * 512],
                    mkap(eo_s.ap(), dc * 128 * T + tc * 512, [[T, 128], [1, 512]])))(dc, tc)
                for dc in range(DC)
            ] + [
                (lambda ts, tc: lambda e: e.dma_start(
                    x6[:, ts * D:(ts + 1) * D],
                    mkap(x_in.ap(), (tc * 512 + ts * 128) * D, [[D, 128], [1, D]])))(ts, tc)
                for ts in range(4)
            ], dma=True)
            base = {n: sb.c[n] for n in ["pe", "ac"]}
            serv = sb.c["ser"]
            for fc in range(FC):
                def m1(e, fc=fc):
                    e.wait_ge(sb.sem("ser"), serv)
                    if fc >= 2:
                        e.wait_ge(sb.sem("ac"), base["ac"] + fc - 1)
                    ins = None
                    for dc in range(DC):
                        ins = e.matmul(pf[fc % 2][:], sw1b[:, dc * FE + fc * 128: dc * FE + (fc + 1) * 128],
                                       hb6[:, dc * 512:(dc + 1) * 512],
                                       start=(dc == 0), stop=(dc == DC - 1))
                    sb.inc(ins, "pe")
                blk.tensor(m1)

                def silu(e, fc=fc):
                    e.wait_ge(sb.sem("pe"), base["pe"] + fc + 1)
                    ins = e.activation(out=h1s[:, fc * 512:(fc + 1) * 512],
                                       in_=pf[fc % 2][:], func=AF.Silu, scale=s1b6[:, fc:fc + 1])
                    sb.inc(ins, "ac")
                blk.scalar(silu)
            base2 = {"pe2": sb.c["pe2"], "dv2": sb.c["dv2"], "po": sb.c["po"],
                     "dv": sb.c["dv"], "pe": sb.c["pe"], "ac": sb.c["ac"]}
            for dc2 in range(DC):
                def m2(e, dc2=dc2):
                    e.wait_ge(sb.sem("ac"), base["ac"] + FC)
                    if dc2 >= 2:
                        e.wait_ge(sb.sem("dv"), base2["dv"] + dc2 - 1)
                    ins = None
                    for fc in range(FC):
                        ins = e.matmul(pf[2 + dc2 % 2][:], sw2b[:, fc * D + dc2 * 128: fc * D + (dc2 + 1) * 128],
                                       h1s[:, fc * 512:(fc + 1) * 512],
                                       start=(fc == 0), stop=(fc == FC - 1))
                    sb.inc(ins, "pe2")
                blk.tensor(m2)

                def comb1(e, dc2=dc2):
                    e.wait_ge(sb.sem("pe2"), base2["pe2"] + dc2 + 1)
                    if dc2 >= 1:
                        e.wait_ge(sb.sem("po"), base2["po"] + dc2)
                    ins = e.tensor_tensor(csum[:], pf[2 + dc2 % 2][:],
                                          eoc[:, dc2 * 512:(dc2 + 1) * 512], op=ALU.add)
                    sb.inc(ins, "dv")
                blk.vector(comb1)

                def comb2(e, dc2=dc2):
                    e.wait_ge(sb.sem("dv"), base2["dv"] + dc2 + 1)
                    if dc2 >= 1:
                        e.wait_ge(sb.sem("pe"), base2["pe"] + dc2)
                    ins = e.tensor_tensor(csum2[:], csum[:], agc[:, dc2 * 512:(dc2 + 1) * 512], op=ALU.add)
                    sb.inc(ins, "po")
                blk.gpsimd(comb2)

                def tpose(e, dc2=dc2):
                    e.wait_ge(sb.sem("po"), base2["po"] + dc2 + 1)
                    if dc2 >= 1:
                        e.wait_ge(sb.sem("dv2"), base2["dv2"] + dc2)
                    ins = None
                    for ts in range(4):
                        ins = e.transpose(pfT[:, ts * 128:(ts + 1) * 128],
                                          csum2[:, ts * 128:(ts + 1) * 128], id_f32[:])
                    sb.inc(ins, "pe")
                blk.tensor(tpose)

                def oadd(e, dc2=dc2):
                    e.wait_ge(sb.sem("pe"), sb.c["pe"])
                    src = mkap(pfT[:, :], pfT[:, :].offset, [pfT[:, :].ap.to_list()[0], [128, 4], [1, 128]])
                    dst = mkap(o6[:, :], o6[:, :].offset + dc2 * 128,
                               [o6[:, :].ap.to_list()[0], [D, 4], [1, 128]])
                    xsrc = mkap(x6[:, :], x6[:, :].offset + dc2 * 128,
                                [x6[:, :].ap.to_list()[0], [D, 4], [1, 128]])
                    ins = e.tensor_tensor(dst, src, xsrc, op=ALU.add)
                    sb.inc(ins, "dv2")
                blk.vector(oadd)
            def store(e, tc=tc, n=sb.c["dv2"]):
                e.wait_ge(sb.sem("dv2"), n)
                ins = e.dma_start(mkap(out.ap(), tc * 512 * D, [[D, 128], [128 * D, 4], [1, D]]), o6[:])
                ins.then_inc(sb.sem("out"), 16)
                sb.c["out"] += 16
            blk.sync(store)
            sr.step("sync", lambda e: e.nop(),
                    extra_waits=[("dv2", sb.c["dv2"])])

        def final(e):
            e.wait_ge(sb.sem("out"), sb.c["out"])
            return e.nop()
        sr.step("sync", final)

    nc.finalize()
    return nc


_NC_CACHE = {}


def kernel(**inputs):
    x = np.ascontiguousarray(inputs["L0_attn_x"])
    ident_ = np.eye(128, dtype=BFNP)
    rampv = np.tile(np.arange(-(TAB // 2), TAB // 2 + 1, dtype=np.float32), (112, 1))
    hsumv = np.zeros((112, 16), dtype=np.float32)
    for p_ in range(112):
        hsumv[p_, p_ // 7] = 1.0

    if "nc" not in _NC_CACHE:
        _NC_CACHE["nc"] = build(debug=False)
    nc = _NC_CACHE["nc"]

    base = dict(
        w_in=np.ascontiguousarray(inputs["L0_attn_w_in"]),
        w_gate=np.ascontiguousarray(inputs["L0_attn_w_gate"]),
        b_gate=np.ascontiguousarray(inputs["L0_attn_b_gate"]),
        amp=np.ascontiguousarray(inputs["L0_attn_amplitudes"]),
        shp=np.ascontiguousarray(inputs["L0_attn_sharpness"]),
        off=np.ascontiguousarray(inputs["L0_attn_offsets"]),
        sw1=np.ascontiguousarray(inputs["L0_ffn_shared_w1"]),
        sw2=np.ascontiguousarray(inputs["L0_ffn_shared_w2"]),
        ew1=np.ascontiguousarray(inputs["L0_ffn_expert_w1"]),
        ew2=np.ascontiguousarray(inputs["L0_ffn_expert_w2"]),
        rw=np.ascontiguousarray(inputs["L0_ffn_router_w"]),
        gam=np.ascontiguousarray(inputs["L0_ffn_ln_gamma"]),
        bet=np.ascontiguousarray(inputs["L0_ffn_ln_beta"]),
        ident=ident_, identf=np.eye(128, dtype=np.float32), ramp=rampv, hsum=hsumv,
    )
    in_maps = []
    for c in range(8):
        m = dict(base)
        m["x"] = np.ascontiguousarray(x[2 * c:2 * c + 2].reshape(T, D))
        in_maps.append(m)
    res = run_bass_kernel_spmd(nc, in_maps, core_ids=list(range(8)))
    return np.concatenate([r["out"].reshape(2, S, D) for r in res.results], axis=0)


if __name__ == "__main__":
    nc = build(debug=False)
    print("build ok")


# version 07
# speedup vs baseline: 1.6212x; 1.0264x over previous
import contextlib
import numpy as np
import ml_dtypes
import concourse.bass as bass
import concourse.mybir as mybir
from concourse.bass_utils import run_bass_kernel_spmd

F32 = mybir.dt.float32
BF16 = mybir.dt.bfloat16
AF = mybir.ActivationFunctionType
ALU = mybir.AluOpType
AX = mybir.AxisListType.X
BFNP = ml_dtypes.bfloat16

B = 2
S = 2048
D = 1024
T = B * S
H = 16
HD = 64
E = 8
FE = 2048
DC = 8
FC = 16
EPS = 1e-5
TAB = 4095
UW = 3968


def mkap(ap, offset, dims):
    a = ap.copy()
    a.offset = offset
    a.ap = type(a.ap)(dims)
    return a


def revslice(tile_ap, col_hi, n):
    dims = tile_ap.ap.to_list()
    return mkap(tile_ap, tile_ap.offset + col_hi, [dims[0], [-1, n]])


class SemBank:
    def __init__(self, nc):
        self.nc = nc
        self.h = {}
        self.c = {}

    def sem(self, n):
        if n not in self.h:
            self.h[n] = self.nc.alloc_semaphore(n)
            self.c[n] = 0
        return self.h[n]

    def inc(self, instr, n, amt=1):
        instr.then_inc(self.sem(n), amt)
        self.c[n] += amt
        return self.c[n]

    def wait(self, eng, n, v):
        if v > 0:
            eng.wait_ge(self.sem(n), v)



class MaybeBlk:
    def __init__(self, blk, en):
        self._blk, self._en = blk, en

    def __getattr__(self, k):
        f = getattr(self._blk, k)
        if k in ("tensor", "vector", "scalar", "gpsimd", "sync"):
            return (lambda fn: f(fn)) if self._en else (lambda fn: None)
        return f


class Serial:

    def __init__(self, sb, blk, enabled=True, name="ser"):
        self.sb = sb
        self.blk = blk
        self.enabled = enabled
        self.name = name
        sb.sem(name)

    def step(self, eng_name, fn, dma=False, extra_waits=()):
        if not self.enabled:
            return
        sb = self.sb
        nm = self.name
        need = sb.c[nm]
        inc = 16 if dma else 1

        def body(eng):
            for n, v in extra_waits:
                sb.wait(eng, n, v)
            sb.wait(eng, nm, need)
            ins = fn(eng)
            ins.then_inc(sb.sem(nm), inc)
        getattr(self.blk, eng_name)(body)
        sb.c[nm] += inc

    def multi(self, eng_name, fns, dma=False, extra_waits=()):
        if not self.enabled:
            return
        sb = self.sb
        nm = self.name
        need = sb.c[nm]
        inc = 16 if dma else 1

        def body(eng):
            for n, v in extra_waits:
                sb.wait(eng, n, v)
            sb.wait(eng, nm, need)
            for fn in fns:
                fn(eng).then_inc(sb.sem(nm), inc)
        getattr(self.blk, eng_name)(body)
        sb.c[nm] += inc * len(fns)


def build(debug=False, phases=(0, 1, 2, 3, 4, 5, 6)):
    nc = bass.Bass("TRN2", target_bir_lowering=False)

    x_in = nc.dram_tensor("x", [T, D], F32, kind="ExternalInput")
    w_in = nc.dram_tensor("w_in", [D, 3 * D], F32, kind="ExternalInput")
    w_gate = nc.dram_tensor("w_gate", [D, 2 * D], F32, kind="ExternalInput")
    b_gate = nc.dram_tensor("b_gate", [2 * D], F32, kind="ExternalInput")
    amp_in = nc.dram_tensor("amp", [H, 21], F32, kind="ExternalInput")
    shp_in = nc.dram_tensor("shp", [H, 21], F32, kind="ExternalInput")
    off_in = nc.dram_tensor("off", [H, 21], F32, kind="ExternalInput")
    sw1_in = nc.dram_tensor("sw1", [FE, D], F32, kind="ExternalInput")
    sw2_in = nc.dram_tensor("sw2", [D, FE], F32, kind="ExternalInput")
    ew1_in = nc.dram_tensor("ew1", [E, FE, D], F32, kind="ExternalInput")
    ew2_in = nc.dram_tensor("ew2", [E, D, FE], F32, kind="ExternalInput")
    rw_in = nc.dram_tensor("rw", [E, D], F32, kind="ExternalInput")
    gam_in = nc.dram_tensor("gam", [D], F32, kind="ExternalInput")
    bet_in = nc.dram_tensor("bet", [D], F32, kind="ExternalInput")
    ident = nc.dram_tensor("ident", [128, 128], BF16, kind="ExternalInput")
    identf = nc.dram_tensor("identf", [128, 128], F32, kind="ExternalInput")
    ramp = nc.dram_tensor("ramp", [112, TAB], F32, kind="ExternalInput")
    hsum = nc.dram_tensor("hsum", [112, 16], F32, kind="ExternalInput")

    out = nc.dram_tensor("out", [T, D], F32, kind="ExternalOutput")

    ikind = "ExternalOutput" if debug else "Internal"
    tab_s = nc.dram_tensor("tab_s", [H, TAB], F32, kind=ikind)
    k_s = nc.dram_tensor("k_s", [B, H, HD, S], BF16, kind=ikind)
    q_s = nc.dram_tensor("q_s", [B, H, HD, S], BF16, kind=ikind)
    v_s = nc.dram_tensor("v_s", [B, S, D], BF16, kind=ikind)
    h_s = nc.dram_tensor("h_s", [DC, 128, T], BF16, kind=ikind)
    ao_s = nc.dram_tensor("ao_s", [DC, 128, T], BF16, kind=ikind)
    ag_s = nc.dram_tensor("ag_s", [DC, 128, T], BF16, kind=ikind)
    gt_s = nc.dram_tensor("gt_s", [E, T], BF16, kind=ikind)
    eo_s = nc.dram_tensor("eo_s", [DC, 128, T], BF16, kind=ikind)
    w1t_s = nc.dram_tensor("w1t_s", [E, DC, 128, FE], BF16, kind=ikind)
    w2t_s = nc.dram_tensor("w2t_s", [E, FC, 128, D], BF16, kind=ikind)
    st1_s = nc.dram_tensor("st1_s", [DC, 128, FE], BF16, kind=ikind)
    st2_s = nc.dram_tensor("st2_s", [FC, 128, D], BF16, kind=ikind)
    s1_s = nc.dram_tensor("s1_s", [E + 1, 128, FC], F32, kind=ikind)

    id_bf = nc.alloc_sbuf_tensor("id_bf", [128, 128], BF16)
    id_f32 = nc.alloc_sbuf_tensor("id_f32", [128, 128], F32)
    ones_bf = nc.alloc_sbuf_tensor("ones_bf", [128, 1], BF16)
    ones_bfr = nc.alloc_sbuf_tensor("ones_bfr", [1, 128], BF16)
    ones_fr = nc.alloc_sbuf_tensor("ones_fr", [1, 128], F32)
    bg_sb = nc.alloc_sbuf_tensor("bg_sb", [128, 16], F32)
    rwt_sb = nc.alloc_sbuf_tensor("rwt_sb", [128, DC * E], F32)
    gate_fm = nc.alloc_sbuf_tensor("gate_fm", [8, T], BF16)


    sb = SemBank(nc)
    for n in ["ser", "ld", "pe", "ac", "dv", "po", "st", "pe2", "dv2", "ac2", "out", "pe3", "dv3", "fl", "qld", "qdv", "qpe", "qpo", "qst", "qout"]:
        sb.sem(n)

    with contextlib.ExitStack() as _es:
        row_sb = _es.enter_context(nc.sbuf_tensor("p0_row", [1, D], F32))
        rw_sb = _es.enter_context(nc.sbuf_tensor("p0_rw", [8, D], F32))
        bgt_row = _es.enter_context(nc.sbuf_tensor("p0_bgt", [16, 128], F32))
        rwb_sb = _es.enter_context(nc.sbuf_tensor("p0_rwb", [8, D], BF16))
        prm = _es.enter_context(nc.sbuf_tensor("p0_prm", [112, 9], F32))
        prm2 = _es.enter_context(nc.sbuf_tensor("p0_prm2", [112, 6], F32))
        ramp_sb = _es.enter_context(nc.sbuf_tensor("p0_ramp", [112, TAB], F32))
        hsum_sb = _es.enter_context(nc.sbuf_tensor("p0_hsum", [112, 16], F32))
        d2_sb = _es.enter_context(nc.sbuf_tensor("p0_d2", [112, TAB], F32))
        eg_sb = _es.enter_context(nc.sbuf_tensor("p0_eg", [112, 3 * TAB], F32))
        tab_sb = _es.enter_context(nc.sbuf_tensor("p0_tab", [16, TAB], F32))
        pf = [_es.enter_context(nc.psum_tensor(f"p0pf{i}", [128, 512], F32)) for i in range(4)]
        blk = _es.enter_context(nc.Block())
        blk = MaybeBlk(blk, 0 in phases)
        sr = Serial(sb, blk, 0 in phases)
        sr.multi("sync", [
            lambda e: e.dma_start(id_bf[:], ident.ap()),
            lambda e: e.dma_start(id_f32[:], identf.ap()),
            lambda e: e.dma_start(ramp_sb[:], ramp.ap()),
            lambda e: e.dma_start(hsum_sb[:], hsum.ap()),
            lambda e: e.dma_start(bgt_row[:], mkap(b_gate.ap(), 0, [[128, 16], [1, 128]])),
            lambda e: e.dma_start(rw_sb[:], rw_in.ap()),
        ] + [
            (lambda gi, g, src: lambda e: e.dma_start(
                prm[:, gi * 3 + g: gi * 3 + g + 1],
                mkap(src.ap(), g * 7, [[21, 16], [1, 7], [1, 1]])))(gi, g, src)
            for gi, src in enumerate([off_in, shp_in, amp_in]) for g in range(3)
        ], dma=True)
        sr.multi("vector", [
            lambda e: e.memset(ones_bf[:], 1.0),
            lambda e: e.memset(ones_bfr[:], 1.0),
            lambda e: e.memset(ones_fr[:], 1.0),
            lambda e: e.tensor_scalar(prm2[:, 0:3], prm[:, 0:3], -1.0, None, op0=ALU.mult),
        ])
        sr.step("scalar", lambda e: e.activation(out=prm2[:, 3:6], in_=prm[:, 3:6], func=AF.Abs))
        sr.step("vector", lambda e: e.tensor_scalar(prm2[:, 3:6], prm2[:, 3:6], -1.0, None, op0=ALU.mult))

        sr.multi("tensor", [
            (lambda dc: lambda e: e.transpose(pf[3][:, dc * 8:(dc + 1) * 8],
                                              rw_sb[0:8, dc * 128:(dc + 1) * 128], id_f32[0:8, 0:8]))(dc)
            for dc in range(DC)
        ])
        sr.step("vector", lambda e: e.tensor_copy(rwt_sb[:], pf[3][:, 0:DC * E]))
        sr.step("tensor", lambda e: e.transpose(pf[3][:, 0:16], bgt_row[:], id_f32[0:16, 0:16]))
        sr.step("vector", lambda e: e.tensor_copy(bg_sb[:], pf[3][:, 0:16]))
        for g in range(3):
            sr.step("scalar", (lambda g: lambda e: e.activation(
                out=d2_sb[:], in_=ramp_sb[:], func=AF.Square, bias=prm2[:, g:g + 1]))(g))
            sr.step("scalar", (lambda g: lambda e: e.activation(
                out=eg_sb[:, g * TAB:(g + 1) * TAB], in_=d2_sb[:], func=AF.Exp,
                scale=prm2[:, 3 + g:4 + g]))(g))
            sr.step("vector", (lambda g: lambda e: e.tensor_scalar(
                eg_sb[:, g * TAB:(g + 1) * TAB], eg_sb[:, g * TAB:(g + 1) * TAB],
                prm[:, 6 + g:7 + g], None, op0=ALU.mult))(g))
        for c in range(8):
            w = 512 if c < 7 else TAB - 7 * 512
            sr.step("tensor", (lambda c, w: lambda e: [
                e.matmul(pf[c % 2][0:16, 0:w], hsum_sb[:], eg_sb[:, g * TAB + c * 512: g * TAB + c * 512 + w],
                         start=(g == 0), stop=(g == 2)) for g in range(3)][-1])(c, w))
            sr.step("scalar", (lambda c, w: lambda e: e.copy(
                tab_sb[:, c * 512:c * 512 + w], pf[c % 2][0:16, 0:w]))(c, w))
        sr.step("sync", lambda e: e.dma_start(tab_s.ap(), tab_sb[:]), dma=True)

    with contextlib.ExitStack() as _es:
        gam_rep = _es.enter_context(nc.sbuf_tensor("p1_gam", [128, D], F32))
        bet_rep = _es.enter_context(nc.sbuf_tensor("p1_bet", [128, D], F32))
        win_sb = _es.enter_context(nc.sbuf_tensor("p1_win", [128, DC * 3 * D], BF16))
        wstg = _es.enter_context(nc.sbuf_tensor("p1_wstg", [128, 3 * D], F32))
        xt2 = _es.enter_context(nc.sbuf_tensor("p1_xt", [128, 2 * D], F32))
        sq_scr = _es.enter_context(nc.sbuf_tensor("p1_scr", [128, D], BF16))
        xn_t = _es.enter_context(nc.sbuf_tensor("p1_xn", [128, D], BF16))
        xnn = _es.enter_context(nc.sbuf_tensor("p1_xnn", [128, D], F32))
        xn_f = _es.enter_context(nc.sbuf_tensor("p1_xnf", [128, D], F32))
        h1tmp = _es.enter_context(nc.sbuf_tensor("p1_ht", [128, D], F32))
        h_t = _es.enter_context(nc.sbuf_tensor("p1_hb", [128, D], BF16))
        h_f = _es.enter_context(nc.sbuf_tensor("p1_hf", [128, D], F32))
        h_fm32 = _es.enter_context(nc.sbuf_tensor("p1_hfm32", [128, DC * 512], F32))
        tny = _es.enter_context(nc.sbuf_tensor("p1_tiny", [128, 16], F32))
        xn_fm = _es.enter_context(nc.sbuf_tensor("p1_xnfm", [128, DC * 512], BF16))
        h_fm = _es.enter_context(nc.sbuf_tensor("p1_hfm", [128, DC * 512], BF16))
        pstg = _es.enter_context(nc.sbuf_tensor("p1_stg", [128, 2 * 512], BF16))
        vstg = _es.enter_context(nc.sbuf_tensor("p1_vstg", [128, 2 * 512], BF16))
        r8 = _es.enter_context(nc.sbuf_tensor("p1_r8", [128, 8], F32))
        g8 = _es.enter_context(nc.sbuf_tensor("p1_g8", [128, 8], BF16))
        wr2 = _es.enter_context(nc.sbuf_tensor("p2_wr", [128, 2 * FE], F32))
        qa2 = _es.enter_context(nc.sbuf_tensor("p2_a", [128, 2 * FE], BF16))
        qb2 = _es.enter_context(nc.sbuf_tensor("p2_b", [128, 2 * FE], BF16))
        qt2 = _es.enter_context(nc.sbuf_tensor("p2_t", [128, 2 * FE], BF16))
        tstg = _es.enter_context(nc.sbuf_tensor("p2_ts", [128, 2 * 1024], BF16))
        qs = _es.enter_context(nc.sbuf_tensor("p2_s", [128, 8], F32))
        s1stg = _es.enter_context(nc.sbuf_tensor("p2_s1", [128, FC], F32))
        pb = [_es.enter_context(nc.psum_tensor(f"pb{i}", [128, 1024], BF16)) for i in range(2)]
        pb2 = _es.enter_context(nc.psum_tensor("pb2", [128, 1024], BF16))
        pfR = _es.enter_context(nc.psum_tensor("pfR", [128, 512], F32))
        pf = [_es.enter_context(nc.psum_tensor(f"p1pf{i}", [128, 512], F32)) for i in range(4)]
        blk = _es.enter_context(nc.Block())
        blk = MaybeBlk(blk, 1 in phases)
        sr = Serial(sb, blk, 1 in phases)

        def _emit_hf32(sr, tt):
            for half in range(2):
                sr.multi("tensor", [
                    (lambda dc, half: lambda e: e.transpose(
                        pf[2 + half][:, (dc % 4) * 128:(dc % 4 + 1) * 128],
                        h_f[:, dc * 128:(dc + 1) * 128], id_f32[:]))(dc, half)
                    for dc in range(half * 4, half * 4 + 4)
                ])
                sr.step("vector", (lambda tt=tt, half=half: lambda e: e.tensor_copy(
                    mkap(h_fm32[:, :], h_fm32[:, :].offset + half * 4 * 512 + tt * 128,
                         [h_fm32[:, :].ap.to_list()[0], [512, 4], [1, 128]]),
                    pf[2 + half][:, 0:512]))(tt, half))

        en2 = 2 in phases
        blk2 = MaybeBlk(blk._blk if hasattr(blk, "_blk") else blk, en2)
        p2_units = []
        if en2:
            mats = []
            for e_ in range(E):
                mats.append(("w1", ew1_in, e_, FE, D))
            for e_ in range(E):
                mats.append(("w2", ew2_in, e_, D, FE))
            mats.append(("w1", sw1_in, E, FE, D))
            mats.append(("w2", sw2_in, E, D, FE))
            p2_state = {"i": 0, "hidx": 0, "ns1": 0, "tile_pe_end": []}

            def _p2_tile(kind, srcten, e_, NR, LC, rt):
                st_ = p2_state
                i = st_["i"]
                par = i % 2
                wr = wr2[:, par * FE: par * FE + LC]
                qsl = qs[:, par * 4: par * 4 + 4]
                base = (e_ * NR * LC if e_ < E else 0) + rt * 128 * LC
                nhalf = LC // 1024
                ns1 = st_["ns1"]

                def L(e):
                    if i >= 2:
                        e.wait_ge(sb.sem("qdv"), i - 1)
                    if kind == "w1" and rt == 0:
                        e.wait_ge(sb.sem("qout"), 16 * ns1)
                    e.dma_start(wr, mkap(srcten.ap(), base, [[LC, 128], [1, LC]])).then_inc(sb.sem("qld"), 16)
                    sb.c["qld"] += 16
                blk2.sync(L)

                tpe = st_["tile_pe_end"]

                def RAD(e):
                    e.wait_ge(sb.sem("qld"), 16 * (i + 1))
                    if i >= 2:
                        e.wait_ge(sb.sem("qpe"), tpe[i - 2])
                    ins = e.tensor_reduce(qsl[:, 0:1], wr, axis=AX, op=ALU.add, apply_absolute_value=True)
                    sb.inc(ins, "fl")
                    e.wait_ge(sb.sem("fl"), sb.c["fl"])
                    ins = e.tensor_scalar(qsl[:, 1:2], qsl[:, 0:1], 1.0 / LC, 1e-5, op0=ALU.mult, op1=ALU.max)
                    sb.inc(ins, "fl")
                    e.wait_ge(sb.sem("fl"), sb.c["fl"])
                    e.tensor_scalar(qsl[:, 2:3], qsl[:, 1:2], -0.5, None, op0=ALU.mult)
                    ins = e.tensor_scalar(qsl[:, 3:4], qsl[:, 1:2], 0.5, None, op0=ALU.mult)
                    if kind == "w1":
                        ins = e.tensor_copy(s1stg[:, rt:rt + 1], qsl[:, 1:2])
                    sb.inc(ins, "fl")
                    e.wait_ge(sb.sem("fl"), sb.c["fl"])
                    e.tensor_scalar(qa2[:, par * FE:par * FE + LC], wr, qsl[:, 2:3], 0.0,
                                    op0=ALU.add, op1=ALU.is_gt)
                    ins = e.tensor_scalar(qb2[:, par * FE:par * FE + LC], wr, qsl[:, 3:4], 0.0,
                                          op0=ALU.add, op1=ALU.is_lt)
                    sb.inc(ins, "fl")
                    e.wait_ge(sb.sem("fl"), sb.c["fl"])
                    ins = e.tensor_tensor(qt2[:, par * FE:par * FE + LC], qa2[:, par * FE:par * FE + LC],
                                          qb2[:, par * FE:par * FE + LC], op=ALU.subtract)
                    if kind == "w2":
                        sb.inc(ins, "fl")
                        e.wait_ge(sb.sem("fl"), sb.c["fl"])
                        ins = e.tensor_scalar(qt2[:, par * FE:par * FE + LC],
                                              qt2[:, par * FE:par * FE + LC], qsl[:, 1:2], None, op0=ALU.mult)
                    sb.inc(ins, "qdv")
                blk2.vector(RAD)

                for hh in range(nhalf):
                    hidx = st_["hidx"]
                    hpar = hidx % 2

                    def Tst(e, hh=hh, hpar=hpar, hidx=hidx):
                        e.wait_ge(sb.sem("qdv"), i + 1)
                        if hidx >= 1:
                            e.wait_ge(sb.sem("qpo"), hidx)
                        ins = None
                        for bkk in range(8):
                            ins = e.transpose(pb2[:, bkk * 128:(bkk + 1) * 128],
                                              qt2[:, par * FE + hh * 1024 + bkk * 128:
                                                  par * FE + hh * 1024 + (bkk + 1) * 128], id_bf[:])
                        sb.inc(ins, "qpe")
                    blk2.tensor(Tst)

                    def Est(e, hpar=hpar, hidx=hidx):
                        e.wait_ge(sb.sem("qpe"), hidx + 1)
                        if hidx >= 2:
                            e.wait_ge(sb.sem("qst"), 16 * (hidx - 1))
                        ins = e.tensor_copy(tstg[:, hpar * 1024:(hpar + 1) * 1024], pb2[:])
                        sb.inc(ins, "qpo")
                    blk2.vector(Est)

                    if kind == "w1":
                        dst = w1t_s if e_ < E else st1_s
                        dbase = (e_ * DC * 128 * FE if e_ < E else 0) + hh * 8 * 128 * FE + rt * 128
                        ddims = [[FE, 128], [128 * FE, 8], [1, 128]]
                    else:
                        dst = w2t_s if e_ < E else st2_s
                        dbase = (e_ * FC * 128 * D if e_ < E else 0) + hh * 8 * 128 * D + rt * 128
                        ddims = [[D, 128], [128 * D, 8], [1, 128]]

                    def Sst(e, dst=dst, dbase=dbase, ddims=ddims, hpar=hpar, hidx=hidx):
                        e.wait_ge(sb.sem("qpo"), hidx + 1)
                        sap = mkap(tstg[:, :], tstg[:, :].offset + hpar * 1024,
                                   [tstg[:, :].ap.to_list()[0], [128, 8], [1, 128]])
                        e.dma_start(mkap(dst.ap(), dbase, ddims), sap).then_inc(sb.sem("qst"), 16)
                        sb.c["qst"] += 16
                    blk2.gpsimd(Sst)
                    st_["hidx"] += 1
                st_["tile_pe_end"].append(sb.c["qpe"])
                st_["i"] += 1

            def _p2_s1(e_):
                n = sb.c["qdv"]

                def S1(e):
                    e.wait_ge(sb.sem("qdv"), n)
                    e.dma_start(mkap(s1_s.ap(), e_ * 128 * FC, [[FC, 128], [1, FC]]),
                                s1stg[:]).then_inc(sb.sem("qout"), 16)
                    sb.c["qout"] += 16
                blk2.sync(S1)
                p2_state["ns1"] += 1

            for kind, srcten, e_, NR, LC in mats:
                for rt in range(NR // 128):
                    p2_units.append((lambda kind=kind, srcten=srcten, e_=e_, NR=NR, LC=LC, rt=rt:
                                     _p2_tile(kind, srcten, e_, NR, LC, rt)))
                if kind == "w1":
                    p2_units.append((lambda e_=e_: _p2_s1(e_)))
        p2_ptr = [0]

        def drain_p2(n):
            while n > 0 and p2_ptr[0] < len(p2_units):
                p2_units[p2_ptr[0]]()
                p2_ptr[0] += 1
                n -= 1



        sr.step("sync", lambda e: e.dma_start(
            xt2[0:1, 0:D], gam_in.ap().rearrange("(one d) -> one d", one=1)), dma=True)
        for c in range(2):
            sr.step("tensor", (lambda c: lambda e: e.matmul(
                pf[c][:, 0:512], ones_fr[:], xt2[0:1, c * 512:(c + 1) * 512], start=True, stop=True))(c))
        sr.multi("vector", [
            (lambda c: lambda e: e.tensor_copy(gam_rep[:, c * 512:(c + 1) * 512], pf[c][:]))(c)
            for c in range(2)
        ])
        sr.step("sync", lambda e: e.dma_start(xt2[0:1, 0:D], bet_in.ap().rearrange("(one d) -> one d", one=1)), dma=True)
        for c in range(2):
            sr.step("tensor", (lambda c: lambda e: e.matmul(
                pf[c][:, 0:512], ones_fr[:], xt2[0:1, c * 512:(c + 1) * 512], start=True, stop=True))(c))
        sr.multi("vector", [
            (lambda c: lambda e: e.tensor_copy(bet_rep[:, c * 512:(c + 1) * 512], pf[c][:]))(c)
            for c in range(2)
        ])
        for dc in range(DC):
            sr.step("sync", (lambda dc: lambda e: e.dma_start(
                wstg[:], x_in.ap() if False else mkap(w_in.ap(), dc * 128 * 3 * D,
                                                      [[3 * D, 128], [1, 3 * D]])))(dc), dma=True)
            sr.step("vector", (lambda dc: lambda e: e.tensor_copy(
                win_sb[:, dc * 3 * D:(dc + 1) * 3 * D], wstg[:]))(dc))

        for grp in range(8):
            for tt in range(4):
                xt = xt2[:, (tt % 2) * D:((tt % 2) + 1) * D]
                r0 = (grp * 4 + tt) * 128
                sr.step("sync", (lambda r0, xt=xt: lambda e: e.dma_start(
                    xt, mkap(x_in.ap(), r0 * D, [[D, 128], [1, D]])))(r0), dma=True)
                sr.step("vector", (lambda xt=xt: lambda e: e.tensor_reduce(
                    tny[:, 0:1], xt, axis=AX, op=ALU.add))(xt))
                sr.step("scalar", (lambda xt=xt: lambda e: e.activation(
                    out=sq_scr[:], in_=xt, func=AF.Square, accum_out=tny[:, 1:2]))(xt))
                sr.multi("vector", [
                    lambda e: e.tensor_scalar(tny[:, 2:3], tny[:, 0:1], 1.0 / D, None, op0=ALU.mult),
                    lambda e: e.tensor_scalar(tny[:, 3:4], tny[:, 1:2], 1.0 / D, None, op0=ALU.mult),
                ])
                sr.step("vector", lambda e: e.tensor_tensor(tny[:, 4:5], tny[:, 2:3], tny[:, 2:3], op=ALU.mult))
                sr.step("vector", lambda e: e.tensor_tensor(tny[:, 5:6], tny[:, 3:4], tny[:, 4:5], op=ALU.subtract))
                sr.step("vector", lambda e: e.tensor_scalar(tny[:, 6:7], tny[:, 5:6], EPS, None, op0=ALU.add))
                sr.step("scalar", lambda e: e.sqrt(tny[:, 7:8], tny[:, 6:7]))
                sr.step("vector", lambda e: e.reciprocal(tny[:, 8:9], tny[:, 7:8]))
                sr.step("vector", (lambda xt=xt: lambda e: e.tensor_scalar(
                    xn_f[:], xt, tny[:, 2:3], tny[:, 8:9], op0=ALU.subtract, op1=ALU.mult))(xt))
                sr.step("vector", lambda e: e.tensor_copy(xn_t[:], xn_f[:]))
                sr.step("vector", lambda e: e.tensor_reduce(tny[:, 0:1], xn_f[:], axis=AX, op=ALU.add))
                sr.step("scalar", lambda e: e.activation(out=sq_scr[:], in_=xn_f[:], func=AF.Square,
                                                         accum_out=tny[:, 1:2]))
                sr.multi("vector", [
                    lambda e: e.tensor_scalar(tny[:, 2:3], tny[:, 0:1], 1.0 / D, None, op0=ALU.mult),
                    lambda e: e.tensor_scalar(tny[:, 3:4], tny[:, 1:2], 1.0 / D, None, op0=ALU.mult),
                ])
                sr.step("vector", lambda e: e.tensor_tensor(tny[:, 4:5], tny[:, 2:3], tny[:, 2:3], op=ALU.mult))
                sr.step("vector", lambda e: e.tensor_tensor(tny[:, 5:6], tny[:, 3:4], tny[:, 4:5], op=ALU.subtract))
                sr.step("vector", lambda e: e.tensor_scalar(tny[:, 6:7], tny[:, 5:6], EPS, None, op0=ALU.add))
                sr.step("scalar", lambda e: e.sqrt(tny[:, 7:8], tny[:, 6:7]))
                sr.step("vector", lambda e: e.reciprocal(tny[:, 8:9], tny[:, 7:8]))
                sr.step("vector", lambda e: e.tensor_scalar(
                    xnn[:], xn_f[:], tny[:, 2:3], tny[:, 8:9], op0=ALU.subtract, op1=ALU.mult))
                sr.step("gpsimd", lambda e: e.tensor_tensor(h1tmp[:], xnn[:], gam_rep[:], op=ALU.mult))
                sr.step("gpsimd", lambda e: e.tensor_tensor(h_f[:], h1tmp[:], bet_rep[:], op=ALU.add))
                sr.step("vector", lambda e: e.tensor_copy(h_t[:], h_f[:]))
                sr.multi("tensor", [
                    (lambda dc: lambda e: e.transpose(
                        pb[0][:, dc * 128:(dc + 1) * 128], xn_t[:, dc * 128:(dc + 1) * 128], id_bf[:]))(dc)
                    for dc in range(DC)
                ])
                sr.step("vector", (lambda tt=tt: lambda e: e.tensor_copy(
                    mkap(xn_fm[:, :], xn_fm[:, :].offset + tt * 128,
                         [xn_fm[:, :].ap.to_list()[0], [512, DC], [1, 128]]),
                    pb[0][:, 0:1024]))(tt))
                sr.multi("tensor", [
                    (lambda dc: lambda e: e.transpose(
                        pb[1][:, dc * 128:(dc + 1) * 128], h_t[:, dc * 128:(dc + 1) * 128], id_bf[:]))(dc)
                    for dc in range(DC)
                ])
                sr.step("vector", (lambda tt=tt: lambda e: e.tensor_copy(
                    mkap(h_fm[:, :], h_fm[:, :].offset + tt * 128, [h_fm[:, :].ap.to_list()[0], [512, DC], [1, 128]]),
                    pb[1][:, 0:1024]))(tt))
                _emit_hf32(sr, tt)
                drain_p2(2)
            sr.multi("gpsimd", [
                (lambda dc, grp=grp: lambda e: e.dma_start(
                    mkap(h_s.ap(), dc * 128 * T + grp * 512, [[T, 128], [1, 512]]),
                    h_fm[:, dc * 512:(dc + 1) * 512]))(dc)
                for dc in range(DC)
            ], dma=True)
            bp = {n: sb.c[n] for n in ["pe", "ac", "dv", "st", "pe2"]}
            serv1 = sb.c["ser"]
            cons_end = []
            for oc in range(24):
                par = oc % 2

                def MM(e, oc=oc, par=par, grp=grp):
                    e.wait_ge(sb.sem("ser"), serv1)
                    if oc >= 2:
                        e.wait_ge(sb.sem("ac"), bp["ac"] + oc - 1)
                    ins = None
                    for dc in range(DC):
                        ins = e.matmul(pf[par][:], win_sb[:, dc * 3 * D + oc * 128: dc * 3 * D + (oc + 1) * 128],
                                       xn_fm[:, dc * 512:(dc + 1) * 512],
                                       start=(dc == 0), stop=(dc == DC - 1))
                    sb.inc(ins, "pe")
                blk.tensor(MM)

                def EV(e, oc=oc, par=par):
                    e.wait_ge(sb.sem("pe"), bp["pe"] + oc + 1)
                    if oc >= 2:
                        ce = cons_end[oc - 2]
                        e.wait_ge(sb.sem("st"), ce["st"])
                        e.wait_ge(sb.sem("pe2"), ce["pe2"])
                    ins = e.copy(pstg[:, par * 512:(par + 1) * 512], pf[par][:])
                    sb.inc(ins, "ac")
                blk.scalar(EV)

                if oc < 8 or oc >= 16:
                    if oc < 8:
                        dstoff = (grp // 4) * H * HD * S + (oc * 2) * HD * S + ((grp % 4) * 512)
                        dst = k_s
                    else:
                        dstoff = (grp // 4) * H * HD * S + ((oc - 16) * 2) * HD * S + ((grp % 4) * 512)
                        dst = q_s

                    def ST(e, oc=oc, par=par, dst=dst, dstoff=dstoff):
                        e.wait_ge(sb.sem("ac"), bp["ac"] + oc + 1)
                        e.dma_start(mkap(dst.ap(), dstoff, [[S, 128], [1, 512]]),
                                    pstg[:, par * 512:(par + 1) * 512]).then_inc(sb.sem("st"), 16)
                        sb.c["st"] += 16
                    blk.gpsimd(ST)
                else:
                    vc = oc - 8

                    def TP(e, oc=oc, par=par):
                        e.wait_ge(sb.sem("ac"), bp["ac"] + oc + 1)
                        if oc >= 2:
                            e.wait_ge(sb.sem("dv"), cons_end[oc - 2]["dv"])
                        ins = None
                        for ts in range(4):
                            ins = e.transpose(pb[par][:, ts * 128:(ts + 1) * 128],
                                              pstg[:, par * 512 + ts * 128: par * 512 + (ts + 1) * 128], id_bf[:])
                        sb.inc(ins, "pe2")
                    blk.tensor(TP)

                    def VE(e, oc=oc, par=par):
                        e.wait_ge(sb.sem("pe2"), sb.c["pe2"])
                        if oc >= 2:
                            e.wait_ge(sb.sem("st"), cons_end[oc - 2]["st"])
                        ins = e.tensor_copy(vstg[:, par * 512:(par + 1) * 512], pb[par][:, 0:512])
                        sb.inc(ins, "dv")
                    blk.vector(VE)

                    def SV(e, vc=vc, par=par, grp=grp):
                        e.wait_ge(sb.sem("dv"), sb.c["dv"])
                        e.dma_start(mkap(v_s.ap(), (grp // 4) * S * D + (grp % 4) * 512 * D + vc * 128,
                                         [[D, 128], [128 * D, 4], [1, 128]]),
                                    mkap(vstg[:, :], vstg[:, :].offset + par * 512,
                                         [vstg[:, :].ap.to_list()[0], [128, 4], [1, 128]])).then_inc(sb.sem("st"), 16)
                        sb.c["st"] += 16
                    blk.gpsimd(SV)
                cons_end.append({"st": sb.c["st"], "pe2": sb.c["pe2"], "dv": sb.c["dv"]})
                drain_p2(1)
            sr.step("sync", lambda e: e.nop(), extra_waits=[
                ("pe", sb.c["pe"]), ("ac", sb.c["ac"]), ("dv", sb.c["dv"]),
                ("st", sb.c["st"]), ("pe2", sb.c["pe2"])])
            for tq in range(4):
                def rmm(e, tq=tq):
                    ins = None
                    for dc in range(DC):
                        ins = e.matmul(pfR[:, 0:8], h_fm32[:, dc * 512 + tq * 128: dc * 512 + (tq + 1) * 128],
                                       rwt_sb[:, dc * 8:(dc + 1) * 8],
                                       start=(dc == 0), stop=(dc == DC - 1))
                    return ins
                sr.step("tensor", rmm)
                sr.step("vector", lambda e: e.tensor_reduce(tny[:, 0:1], pfR[:, 0:8], axis=AX,
                                                            op=ALU.max, negate=True))
                sr.step("scalar", lambda e: e.activation(out=r8[:], in_=pfR[:, 0:8], func=AF.Exp,
                                                         bias=tny[:, 0:1]))
                sr.step("vector", lambda e: e.tensor_reduce(tny[:, 1:2], r8[:], axis=AX, op=ALU.add))
                sr.step("vector", lambda e: e.reciprocal(tny[:, 2:3], tny[:, 1:2]))
                sr.step("vector", lambda e: e.tensor_reduce(tny[:, 3:4], r8[:], axis=AX, op=ALU.max))
                sr.step("vector", lambda e: e.tensor_tensor(tny[:, 4:5], tny[:, 3:4], tny[:, 2:3], op=ALU.mult))
                sr.step("vector", lambda e: e.tensor_scalar(g8[:], r8[:], tny[:, 3:4], tny[:, 4:5],
                                                            op0=ALU.is_equal, op1=ALU.mult))
                sr.step("tensor", (lambda tq=tq: lambda e: e.transpose(
                    pb[0][0:8, tq * 128:(tq + 1) * 128], g8[:], id_bf[:]))(tq))
            sr.step("vector", (lambda grp=grp: lambda e: e.tensor_copy(
                gate_fm[0:8, grp * 512:(grp + 1) * 512], pb[0][0:8, 0:512]))(grp))
        drain_p2(10 ** 9)
        sr.step("gpsimd", lambda e: e.dma_start(gt_s.ap(), gate_fm[:]), dma=True)
        sr.step("sync", lambda e: e.nop(), extra_waits=[("qst", sb.c["qst"]), ("qout", sb.c["qout"])])

    with contextlib.ExitStack() as _es:
        kab2 = _es.enter_context(nc.sbuf_tensor("p3_k", [128, 2 * S], BF16))
        qab2 = _es.enter_context(nc.sbuf_tensor("p3_q", [128, 2 * S], BF16))
        v652 = _es.enter_context(nc.sbuf_tensor("p3_v", [128, 2 * 16 * 65 * 2], BF16))
        u_sb = _es.enter_context(nc.sbuf_tensor("p3_u", [128, UW], F32))
        eb2 = _es.enter_context(nc.sbuf_tensor("p3_eb", [128, 4 * UW], BF16))
        sqt = _es.enter_context(nc.sbuf_tensor("p3_sq", [128, S], BF16))
        exb = _es.enter_context(nc.sbuf_tensor("p3_ex", [128, 2 * 1024], BF16))
        pub = _es.enter_context(nc.sbuf_tensor("p3_pu", [128, 2 * 1024], BF16))
        nmx = _es.enter_context(nc.sbuf_tensor("p3_nm", [1, 16], F32))
        negm2 = _es.enter_context(nc.sbuf_tensor("p3_ng", [128, 2], F32))
        rcp = _es.enter_context(nc.sbuf_tensor("p3_rc", [1, 2 * 512], F32))
        repsb = _es.enter_context(nc.sbuf_tensor("p3_rp", [64, 2 * 512], F32))
        aob = _es.enter_context(nc.sbuf_tensor("p3_ao", [64, 4 * 512], BF16))
        s01 = [_es.enter_context(nc.psum_tensor(f"p3s{i}", [128, 1024], F32)) for i in range(2)]
        avA = _es.enter_context(nc.psum_tensor("p3avA", [128, 512], F32))
        avB = _es.enter_context(nc.psum_tensor("p3avB", [128, 512], F32))
        pfN = _es.enter_context(nc.psum_tensor("p3pfN", [128, 512], F32))
        blk = _es.enter_context(nc.Block())
        en3 = 3 in phases
        blk = MaybeBlk(blk, en3)
        sr = Serial(sb, blk, en3)
        sr.step("vector", lambda e: e.memset(v652[:], 1.0))
        st_hist = []

        def prelude_gen(pr):
            par = pr % 2
            b_, j = pr // 8, pr % 8
            kab = kab2[:, par * S:(par + 1) * S]
            qab = qab2[:, par * S:(par + 1) * S]
            v65 = v652[:, par * 16 * 65 * 2:(par + 1) * 16 * 65 * 2]
            eb = eb2[:, par * 2 * UW:(par + 1) * 2 * UW]
            xw = []
            if pr >= 2:
                xw = [("st", st_hist[pr - 2])]
            sr.multi("sync", [
                (lambda b_, j, kab=kab: lambda e: e.dma_start(
                    kab, mkap(k_s.ap(), b_ * H * HD * S + (2 * j) * HD * S, [[S, 128], [1, S]])))(b_, j),
                (lambda b_, j, qab=qab: lambda e: e.dma_start(
                    qab, mkap(q_s.ap(), b_ * H * HD * S + (2 * j) * HD * S, [[S, 128], [1, S]])))(b_, j),
                (lambda j: lambda e: e.dma_start(
                    u_sb[:], mkap(tab_s.ap(), (2 * j) * TAB, [[1, 128], [1, UW]])))(j),
            ], dma=True, extra_waits=xw)
            yield
            sr.step("scalar", (lambda eb=eb: lambda e: e.activation(out=eb[:, 0:UW], in_=u_sb[:], func=AF.Exp))())
            yield
            sr.step("sync", (lambda j: lambda e: e.dma_start(
                u_sb[:], mkap(tab_s.ap(), (2 * j + 1) * TAB, [[1, 128], [1, UW]])))(j), dma=True)
            yield
            sr.step("scalar", (lambda eb=eb: lambda e: e.activation(out=eb[:, UW:2 * UW], in_=u_sb[:], func=AF.Exp))())
            yield
            sr.multi("sync", [
                (lambda b_, j, kc, hh, v65=v65: lambda e: e.dma_start(
                    v65[:, (hh * 16 + kc) * 65:(hh * 16 + kc) * 65 + 64],
                    mkap(v_s.ap(), b_ * S * D + (kc * 128) * D + (2 * j + hh) * 64,
                         [[D, 128], [1, 64]])))(b_, j, kc, hh)
                for hh in range(2) for kc in range(16)
            ], dma=True)
            yield
            sr.step("vector", (lambda kab=kab: lambda e: e.tensor_tensor(sqt[:], kab, kab, op=ALU.mult))())
            yield
            for c in range(4):
                sr.step("tensor", (lambda c: lambda e: e.matmul(
                    pfN[0:1, 0:512], ones_bf[:], sqt[:, c * 512:(c + 1) * 512], start=True, stop=True))(c))
                sr.step("vector", (lambda c: lambda e: e.tensor_reduce(
                    nmx[0:1, c:c + 1], pfN[0:1, 0:512], axis=AX, op=ALU.max))(c))
                yield
            sr.step("vector", (lambda qab=qab: lambda e: e.tensor_tensor(sqt[:], qab, qab, op=ALU.mult))())
            yield
            for c in range(4):
                sr.step("tensor", (lambda c: lambda e: e.matmul(
                    pfN[0:1, 0:512], ones_bf[:], sqt[:, c * 512:(c + 1) * 512], start=True, stop=True))(c))
                sr.step("vector", (lambda c: lambda e: e.tensor_reduce(
                    nmx[0:1, 4 + c:5 + c], pfN[0:1, 0:512], axis=AX, op=ALU.max))(c))
                yield
            sr.step("vector", lambda e: e.tensor_reduce(nmx[0:1, 8:9], nmx[0:1, 0:4], axis=AX, op=ALU.max))
            sr.step("vector", lambda e: e.tensor_reduce(nmx[0:1, 9:10], nmx[0:1, 4:8], axis=AX, op=ALU.max))
            sr.step("vector", lambda e: e.tensor_tensor(nmx[0:1, 10:11], nmx[0:1, 8:9], nmx[0:1, 9:10], op=ALU.add))
            sr.step("vector", lambda e: e.tensor_scalar(nmx[0:1, 11:12], nmx[0:1, 10:11], -1.0 / 16.0, None, op0=ALU.mult))
            yield
            sr.step("tensor", lambda e: e.matmul(pfN[:, 0:1], ones_fr[:], nmx[0:1, 11:12], start=True, stop=True))
            sr.step("vector", (lambda par=par: lambda e: e.tensor_copy(negm2[:, par:par + 1], pfN[:, 0:1]))())
            yield

        def run_main(pr, nxt):
            par = pr % 2
            b_, j = pr // 8, pr % 8
            kab = kab2[:, par * S:(par + 1) * S]
            qab = qab2[:, par * S:(par + 1) * S]
            v65 = v652[:, par * 16 * 65 * 2:(par + 1) * 16 * 65 * 2]
            eb = eb2[:, par * 2 * UW:(par + 1) * 2 * UW]
            negm = negm2[:, par:par + 1]
            base = {n: sb.c[n] for n in ["pe", "ac", "dv", "pe2", "dv2", "dv3", "pe3", "st"]}
            serv = sb.c["ser"]

            def em_exp(u):
                kc = u % 16

                def ex_fn(e, u=u, kc=kc):
                    e.wait_ge(sb.sem("pe"), base["pe"] + u + 1)
                    if u >= 2:
                        e.wait_ge(sb.sem("dv"), base["dv"] + u - 1)
                    ins = e.activation(out=exb[:, (kc % 2) * 1024:(kc % 2 + 1) * 1024],
                                       in_=s01[kc % 2][:], func=AF.Exp, bias=negm, scale=0.125)
                    sb.inc(ins, "ac")
                blk.scalar(ex_fn)

            def em_mul(u):
                qc, kc = divmod(u, 16)

                def mul_fn(e, u=u, qc=qc, kc=kc):
                    e.wait_ge(sb.sem("ac"), base["ac"] + u + 1)
                    if u >= 2:
                        e.wait_ge(sb.sem("pe2"), base["pe2"] + u - 1)
                    a_col = 2047 + kc * 128 - qc * 512
                    src_eb = mkap(eb, eb.offset + a_col,
                                  [eb.ap.to_list()[0], [UW, 2], [-1, 512]])
                    ins = e.tensor_tensor(pub[:, (kc % 2) * 1024:(kc % 2 + 1) * 1024],
                                          exb[:, (kc % 2) * 1024:(kc % 2 + 1) * 1024],
                                          src_eb, op=ALU.mult)
                    sb.inc(ins, "dv")
                blk.vector(mul_fn)

            def em_qk(u):
                qc, kc = divmod(u, 16)

                def mm_qk(e, u=u, qc=qc, kc=kc):
                    e.wait_ge(sb.sem("ser"), serv)
                    if u >= 2:
                        e.wait_ge(sb.sem("ac"), base["ac"] + u - 1)
                    else:
                        e.wait_ge(sb.sem("ac"), base["ac"])
                    if kc == 0:
                        e.wait_ge(sb.sem("dv2"), base["dv2"] + qc)
                    e.matmul(s01[kc % 2][:, 0:512], kab[0:64, kc * 128:(kc + 1) * 128],
                             qab[0:64, qc * 512:(qc + 1) * 512], start=True, stop=True,
                             tile_position=(0, 0))
                    ins = e.matmul(s01[kc % 2][:, 512:1024], kab[64:128, kc * 128:(kc + 1) * 128],
                                   qab[64:128, qc * 512:(qc + 1) * 512], start=True, stop=True,
                                   tile_position=(64, 0))
                    sb.inc(ins, "pe")
                blk.tensor(mm_qk)

            def em_av(u):
                qc, kc = divmod(u, 16)

                def mm_av(e, u=u, kc=kc):
                    e.wait_ge(sb.sem("dv"), base["dv"] + u + 1)
                    e.matmul(avA[0:65, :], v65[:, kc * 65:kc * 65 + 65],
                             pub[:, (kc % 2) * 1024:(kc % 2) * 1024 + 512],
                             start=(kc == 0), stop=(kc == 15))
                    ins = e.matmul(avB[0:65, :], v65[:, (16 + kc) * 65:(16 + kc) * 65 + 65],
                                   pub[:, (kc % 2) * 1024 + 512:(kc % 2 + 1) * 1024],
                                   start=(kc == 0), stop=(kc == 15))
                    sb.inc(ins, "pe2")
                blk.tensor(mm_av)

            def em_epi(qc):
                def rcp_fn(e, qc=qc):
                    e.wait_ge(sb.sem("pe2"), base["pe2"] + (qc + 1) * 16)
                    e.reciprocal(rcp[0:1, 0:512], avA[64:65, :])
                    ins = e.reciprocal(rcp[0:1, 512:1024], avB[64:65, :])
                    sb.inc(ins, "dv3")
                blk.vector(rcp_fn)

                def rep_fn(e, qc=qc):
                    e.wait_ge(sb.sem("dv3"), sb.c["dv3"])
                    e.matmul(s01[0][0:64, 0:512], ones_fr[0:1, 0:64], rcp[0:1, 0:512], start=True, stop=True)
                    ins = e.matmul(s01[0][0:64, 512:1024], ones_fr[0:1, 0:64], rcp[0:1, 512:1024],
                                   start=True, stop=True)
                    sb.inc(ins, "pe3")
                blk.tensor(rep_fn)

                def evac_fn(e, qc=qc):
                    e.wait_ge(sb.sem("pe3"), sb.c["pe3"])
                    if qc >= 2:
                        e.wait_ge(sb.sem("st"), base["st"] + (qc - 1) * 32)
                    else:
                        e.wait_ge(sb.sem("st"), base["st"] - 32 * (1 - qc))
                    e.tensor_copy(repsb[:, 0:512], s01[0][0:64, 0:512])
                    e.tensor_copy(repsb[:, 512:1024], s01[0][0:64, 512:1024])
                    e.tensor_tensor(aob[:, (qc % 2) * 1024:(qc % 2) * 1024 + 512],
                                    avA[0:64, :], repsb[:, 0:512], op=ALU.mult)
                    ins = e.tensor_tensor(aob[:, (qc % 2) * 1024 + 512:(qc % 2 + 1) * 1024],
                                          avB[0:64, :], repsb[:, 512:1024], op=ALU.mult)
                    sb.inc(ins, "dv2")
                blk.vector(evac_fn)

                def st_fn(e, qc=qc, b_=b_, j=j):
                    e.wait_ge(sb.sem("dv2"), sb.c["dv2"])
                    e.dma_start(mkap(ao_s.ap(), j * 128 * T + b_ * S + qc * 512, [[T, 64], [1, 512]]),
                                aob[:, (qc % 2) * 1024:(qc % 2) * 1024 + 512]).then_inc(sb.sem("st"), 16)
                    e.dma_start(mkap(ao_s.ap(), j * 128 * T + 64 * T + b_ * S + qc * 512, [[T, 64], [1, 512]]),
                                aob[:, (qc % 2) * 1024 + 512:(qc % 2 + 1) * 1024]).then_inc(sb.sem("st"), 16)
                    sb.c["st"] += 32
                blk.gpsimd(st_fn)

            for u in range(64):
                if u == 0:
                    em_qk(u)
                else:
                    em_exp(u - 1)
                    em_mul(u - 1)
                    if u % 16 == 0:
                        em_av(u - 1)
                        em_epi(u // 16 - 1)
                        em_qk(u)
                    else:
                        em_qk(u)
                        em_av(u - 1)
                if nxt is not None and u % 5 == 1:
                    next(nxt, None)
            em_exp(63)
            em_mul(63)
            em_av(63)
            em_epi(3)
            if nxt is not None:
                for _ in nxt:
                    pass
            st_hist.append(sb.c["st"])

        pg = prelude_gen(0)
        for _ in pg:
            pass
        for pr in range(16):
            nxt = iter(prelude_gen(pr + 1)) if pr + 1 < 16 else None
            run_main(pr, nxt)

        def fence(e):
            e.wait_ge(sb.sem("st"), sb.c["st"])
            return e.nop()
        sr.step("sync", fence)

    with contextlib.ExitStack() as _es:
        wg_sb = _es.enter_context(nc.sbuf_tensor("p4_wg", [128, DC * 2 * D], BF16))
        wstg4 = _es.enter_context(nc.sbuf_tensor("p4_stg", [128, 2 * D], F32))
        aochunk = _es.enter_context(nc.sbuf_tensor("p4_ao", [128, DC * 512], BF16))
        a8 = _es.enter_context(nc.sbuf_tensor("p4_a8", [128, 8 * 512], F32))
        sgb = _es.enter_context(nc.sbuf_tensor("p4_sg", [128, 512], F32))
        agb = _es.enter_context(nc.sbuf_tensor("p4_ag", [128, 2 * 512], BF16))
        pf = [_es.enter_context(nc.psum_tensor(f"p4pf{i}", [128, 512], F32)) for i in range(2)]
        blk = _es.enter_context(nc.Block())
        blk = MaybeBlk(blk, 4 in phases)
        sr = Serial(sb, blk, 4 in phases)
        for dc in range(DC):
            sr.step("sync", (lambda dc: lambda e: e.dma_start(
                wstg4[:], mkap(w_gate.ap(), dc * 128 * 2 * D, [[2 * D, 128], [1, 2 * D]])))(dc), dma=True)
            sr.step("vector", (lambda dc: lambda e: e.tensor_copy(
                wg_sb[:, dc * 2 * D:(dc + 1) * 2 * D], wstg4[:]))(dc))
        for tc in range(8):
            sr.multi("sync", [
                (lambda dc, tc: lambda e: e.dma_start(
                    aochunk[:, dc * 512:(dc + 1) * 512],
                    mkap(ao_s.ap(), dc * 128 * T + tc * 512, [[T, 128], [1, 512]])))(dc, tc)
                for dc in range(DC)
            ], dma=True)
            for gc in range(16):
                par = gc % 2

                def mmg(e, gc=gc, par=par):
                    ins = None
                    for dc in range(DC):
                        ins = e.matmul(pf[par][:], wg_sb[:, dc * 2 * D + gc * 128: dc * 2 * D + (gc + 1) * 128],
                                       aochunk[:, dc * 512:(dc + 1) * 512],
                                       start=(dc == 0), stop=(dc == DC - 1))
                    return ins
                sr.step("tensor", mmg)
                if gc < 8:
                    sr.step("vector", (lambda gc, par: lambda e: e.tensor_scalar(
                        a8[:, gc * 512:(gc + 1) * 512], pf[par][:], bg_sb[:, gc:gc + 1], None,
                        op0=ALU.add))(gc, par))
                else:
                    dco = gc - 8
                    sr.step("scalar", (lambda gc, par: lambda e: e.activation(
                        out=sgb[:], in_=pf[par][:], func=AF.Sigmoid, bias=bg_sb[:, gc:gc + 1]))(gc, par))
                    sr.step("vector", (lambda dco: lambda e: e.tensor_tensor(
                        agb[:, (dco % 2) * 512:(dco % 2 + 1) * 512], a8[:, dco * 512:(dco + 1) * 512],
                        sgb[:], op=ALU.mult))(dco))
                    sr.step("gpsimd", (lambda dco, tc: lambda e: e.dma_start(
                        mkap(ag_s.ap(), dco * 128 * T + tc * 512, [[T, 128], [1, 512]]),
                        agb[:, (dco % 2) * 512:(dco % 2 + 1) * 512]))(dco, tc), dma=True)

    with contextlib.ExitStack() as _es:
        eo_acc = _es.enter_context(nc.sbuf_tensor("p5_acc", [128, DC * T], BF16))
        w1b = _es.enter_context(nc.sbuf_tensor("p5_w1", [128, DC * FE], BF16))
        w2b = _es.enter_context(nc.sbuf_tensor("p5_w2", [128, FC * D], BF16))
        hb2 = _es.enter_context(nc.sbuf_tensor("p5_h", [128, 2 * DC * 512], BF16))
        h1g2 = _es.enter_context(nc.sbuf_tensor("p5_h1g", [128, 2 * FC * 512], BF16))
        h1b = _es.enter_context(nc.sbuf_tensor("p5_h1", [128, 2 * 512], BF16))
        grsb2 = _es.enter_context(nc.sbuf_tensor("p5_gr", [128, 2 * 512], BF16))
        grow2 = _es.enter_context(nc.sbuf_tensor("p5_grow", [1, 2 * 512], BF16))
        s1b = _es.enter_context(nc.sbuf_tensor("p5_s1", [128, FC], F32))
        pfG = _es.enter_context(nc.psum_tensor("pfG", [128, 512], F32))
        pf = [_es.enter_context(nc.psum_tensor(f"p5pf{i}", [128, 512], F32)) for i in range(4)]
        pfM = _es.enter_context(nc.psum_tensor("p5pfM", [128, 512], F32))
        m1b = pf[0:2] + [pfM]
        blk = _es.enter_context(nc.Block())
        en5 = 5 in phases
        blk = MaybeBlk(blk, en5)
        sr = Serial(sb, blk, en5)
        sr.step("vector", lambda e: e.memset(eo_acc[:], 0.0))
        if en5:
            gi = 0
            ld_end = []
            m1_end = []
            silu_end = []
            gmul_end = []
            m2_end = []
            gc_end = []
            for e_ in range(E):
                sr.multi("sync", [
                    (lambda e_, dc: lambda e: e.dma_start(
                        w1b[:, dc * FE:(dc + 1) * FE],
                        mkap(w1t_s.ap(), e_ * DC * 128 * FE + dc * 128 * FE, [[FE, 128], [1, FE]])))(e_, dc)
                    for dc in range(DC)
                ] + [
                    (lambda e_, fc: lambda e: e.dma_start(
                        w2b[:, fc * D:(fc + 1) * D],
                        mkap(w2t_s.ap(), e_ * FC * 128 * D + fc * 128 * D, [[D, 128], [1, D]])))(e_, fc)
                    for fc in range(FC)
                ] + [
                    (lambda e_: lambda e: e.dma_start(
                        s1b[:], mkap(s1_s.ap(), e_ * 128 * FC, [[FC, 128], [1, FC]])))(e_)
                ], dma=True, extra_waits=[("pe", sb.c["pe"]), ("pe2", sb.c["pe2"]),
                                          ("ac", sb.c["ac"])])
                serv5 = sb.c["ser"]
                for tc in range(8):
                    par = gi % 2
                    hb = hb2[:, par * DC * 512:(par + 1) * DC * 512]
                    h1g = h1g2[:, par * FC * 512:(par + 1) * FC * 512]
                    grsb = grsb2[:, par * 512:(par + 1) * 512]
                    grow = grow2[:, par * 512:(par + 1) * 512]

                    def LH(e, e_=e_, tc=tc, hb=hb, grow=grow, gi=gi, serv5=serv5):
                        e.wait_ge(sb.sem("ser"), serv5)
                        if gi >= 2:
                            e.wait_ge(sb.sem("pe"), m1_end[gi - 2])
                        for dc in range(DC):
                            e.dma_start(hb[:, dc * 512:(dc + 1) * 512],
                                        mkap(h_s.ap(), dc * 128 * T + tc * 512,
                                             [[T, 128], [1, 512]])).then_inc(sb.sem("ld"), 16)
                            sb.c["ld"] += 16
                        e.dma_start(grow, mkap(gt_s.ap(), e_ * T + tc * 512,
                                               [[1, 1], [1, 512]])).then_inc(sb.sem("ld"), 16)
                        sb.c["ld"] += 16
                    blk.sync(LH)
                    ld_end.append(sb.c["ld"])

                    def GR(e, grow=grow, gi=gi, serv5=serv5):
                        e.wait_ge(sb.sem("ser"), serv5)
                        e.wait_ge(sb.sem("ld"), ld_end[gi])
                        if gi >= 1:
                            e.wait_ge(sb.sem("dv"), gc_end[gi - 1])
                        ins = e.matmul(pfG[:], ones_bfr[:], grow, start=True, stop=True)
                        sb.inc(ins, "pe3")
                    blk.tensor(GR)

                    def GC(e, grsb=grsb, gi=gi):
                        e.wait_ge(sb.sem("pe3"), sb.c["pe3"])
                        if gi >= 2:
                            e.wait_ge(sb.sem("po"), gmul_end[gi - 2])
                        ins = e.tensor_copy(grsb, pfG[:])
                        sb.inc(ins, "dv")
                    blk.vector(GC)
                    gc_end.append(sb.c["dv"])

                    bac = sb.c["ac"]
                    bpo = sb.c["po"]
                    bpe = sb.c["pe"]
                    for fc in range(FC):
                        def m1(e, fc=fc, hb=hb, gi=gi, bac=bac):
                            e.wait_ge(sb.sem("ld"), ld_end[gi])
                            if fc >= 3:
                                e.wait_ge(sb.sem("ac"), bac + (fc - 3) // 2 + 1)
                            ins = None
                            for dc in range(DC):
                                ins = e.matmul(m1b[fc % 3][:], w1b[:, dc * FE + fc * 128: dc * FE + (fc + 1) * 128],
                                               hb[:, dc * 512:(dc + 1) * 512],
                                               start=(dc == 0), stop=(dc == DC - 1))
                            sb.inc(ins, "pe")
                        blk.tensor(m1)

                        if fc % 2 == 1:
                            def silu(e, fc=fc, bpe=bpe, bpo=bpo):
                                e.wait_ge(sb.sem("pe"), bpe + fc + 1)
                                if fc >= 3:
                                    e.wait_ge(sb.sem("po"), bpo + (fc - 1) // 2)
                                e.activation(out=h1b[:, 0:512], in_=m1b[(fc - 1) % 3][:],
                                             func=AF.Silu, scale=s1b[:, fc - 1:fc])
                                ins = e.activation(out=h1b[:, 512:1024], in_=m1b[fc % 3][:],
                                                   func=AF.Silu, scale=s1b[:, fc:fc + 1])
                                sb.inc(ins, "ac")
                            blk.scalar(silu)

                            def gmul(e, fc=fc, h1g=h1g, gi=gi, bac=bac):
                                e.wait_ge(sb.sem("ac"), bac + (fc + 1) // 2)
                                e.wait_ge(sb.sem("dv"), gc_end[gi])
                                if gi >= 2 and fc == 1:
                                    e.wait_ge(sb.sem("pe2"), m2_end[gi - 2])
                                e.tensor_tensor(h1g[:, (fc - 1) * 512:fc * 512],
                                                h1b[:, 0:512], grsb, op=ALU.mult)
                                ins = e.tensor_tensor(h1g[:, fc * 512:(fc + 1) * 512],
                                                      h1b[:, 512:1024], grsb, op=ALU.mult)
                                sb.inc(ins, "po")
                            blk.gpsimd(gmul)
                    m1_end.append(sb.c["pe"])
                    silu_end.append(sb.c["ac"])
                    gmul_end.append(sb.c["po"])

                    bpe2 = sb.c["pe2"]
                    bdv2 = sb.c["dv2"]
                    for dc2 in range(DC):
                        def m2(e, dc2=dc2, h1g=h1g, gi=gi, bpe2=bpe2, bdv2=bdv2):
                            e.wait_ge(sb.sem("po"), gmul_end[gi])
                            if dc2 >= 2:
                                e.wait_ge(sb.sem("dv2"), bdv2 + dc2 - 1)
                            ins = None
                            for fc in range(FC):
                                ins = e.matmul(pf[2 + dc2 % 2][:], w2b[:, fc * D + dc2 * 128: fc * D + (dc2 + 1) * 128],
                                               h1g[:, fc * 512:(fc + 1) * 512],
                                               start=(fc == 0), stop=(fc == FC - 1))
                            sb.inc(ins, "pe2")
                        blk.tensor(m2)

                        def accadd(e, dc2=dc2, tc=tc, bpe2=bpe2):
                            e.wait_ge(sb.sem("pe2"), bpe2 + dc2 + 1)
                            sl = eo_acc[:, dc2 * T + tc * 512: dc2 * T + (tc + 1) * 512]
                            ins = e.tensor_tensor(sl, sl, pf[2 + dc2 % 2][:], op=ALU.add)
                            sb.inc(ins, "dv2")
                        blk.vector(accadd)
                    m2_end.append(sb.c["pe2"])
                    gi += 1
            sr.step("sync", lambda e: e.nop(), extra_waits=[("dv2", sb.c["dv2"])])
        sr.multi("gpsimd", [
            (lambda dc: lambda e: e.dma_start(
                mkap(eo_s.ap(), dc * 128 * T, [[T, 128], [1, T]]), eo_acc[:, dc * T:(dc + 1) * T]))(dc)
            for dc in range(DC)
        ], dma=True)

    with contextlib.ExitStack() as _es:
        sw1b = _es.enter_context(nc.sbuf_tensor("p6_w1", [128, DC * FE], BF16))
        sw2b = _es.enter_context(nc.sbuf_tensor("p6_w2", [128, FC * D], BF16))
        hb6 = _es.enter_context(nc.sbuf_tensor("p6_h", [128, DC * 512], BF16))
        h1s = _es.enter_context(nc.sbuf_tensor("p6_h1", [128, FC * 512], BF16))
        agc = _es.enter_context(nc.sbuf_tensor("p6_ag", [128, DC * 512], BF16))
        eoc = _es.enter_context(nc.sbuf_tensor("p6_eo", [128, DC * 512], BF16))
        x6 = _es.enter_context(nc.sbuf_tensor("p6_x", [128, 4 * D], F32))
        o6 = _es.enter_context(nc.sbuf_tensor("p6_o", [128, 4 * D], F32))
        csum = _es.enter_context(nc.sbuf_tensor("p6_cs", [128, 512], F32))
        csum2 = _es.enter_context(nc.sbuf_tensor("p6_cs2", [128, 512], F32))
        s1b6 = _es.enter_context(nc.sbuf_tensor("p6_s1", [128, FC], F32))
        pfT = _es.enter_context(nc.psum_tensor("pfT", [128, 512], F32))
        pf = [_es.enter_context(nc.psum_tensor(f"p6pf{i}", [128, 512], F32)) for i in range(4)]
        blk = _es.enter_context(nc.Block())
        blk = MaybeBlk(blk, 6 in phases)
        sr = Serial(sb, blk, 6 in phases)
        sr.multi("sync", [
            (lambda dc: lambda e: e.dma_start(
                sw1b[:, dc * FE:(dc + 1) * FE],
                mkap(st1_s.ap(), dc * 128 * FE, [[FE, 128], [1, FE]])))(dc)
            for dc in range(DC)
        ] + [
            (lambda fc: lambda e: e.dma_start(
                sw2b[:, fc * D:(fc + 1) * D],
                mkap(st2_s.ap(), fc * 128 * D, [[D, 128], [1, D]])))(fc)
            for fc in range(FC)
        ] + [
            lambda e: e.dma_start(s1b6[:], mkap(s1_s.ap(), E * 128 * FC, [[FC, 128], [1, FC]]))
        ], dma=True)
        for tc in range(8):
            sr.multi("sync", [
                (lambda dc, tc: lambda e: e.dma_start(
                    hb6[:, dc * 512:(dc + 1) * 512],
                    mkap(h_s.ap(), dc * 128 * T + tc * 512, [[T, 128], [1, 512]])))(dc, tc)
                for dc in range(DC)
            ] + [
                (lambda dc, tc: lambda e: e.dma_start(
                    agc[:, dc * 512:(dc + 1) * 512],
                    mkap(ag_s.ap(), dc * 128 * T + tc * 512, [[T, 128], [1, 512]])))(dc, tc)
                for dc in range(DC)
            ] + [
                (lambda dc, tc: lambda e: e.dma_start(
                    eoc[:, dc * 512:(dc + 1) * 512],
                    mkap(eo_s.ap(), dc * 128 * T + tc * 512, [[T, 128], [1, 512]])))(dc, tc)
                for dc in range(DC)
            ] + [
                (lambda ts, tc: lambda e: e.dma_start(
                    x6[:, ts * D:(ts + 1) * D],
                    mkap(x_in.ap(), (tc * 512 + ts * 128) * D, [[D, 128], [1, D]])))(ts, tc)
                for ts in range(4)
            ], dma=True)
            base = {n: sb.c[n] for n in ["pe", "ac"]}
            serv = sb.c["ser"]
            for fc in range(FC):
                def m1(e, fc=fc):
                    e.wait_ge(sb.sem("ser"), serv)
                    if fc >= 2:
                        e.wait_ge(sb.sem("ac"), base["ac"] + fc - 1)
                    ins = None
                    for dc in range(DC):
                        ins = e.matmul(pf[fc % 2][:], sw1b[:, dc * FE + fc * 128: dc * FE + (fc + 1) * 128],
                                       hb6[:, dc * 512:(dc + 1) * 512],
                                       start=(dc == 0), stop=(dc == DC - 1))
                    sb.inc(ins, "pe")
                blk.tensor(m1)

                def silu(e, fc=fc):
                    e.wait_ge(sb.sem("pe"), base["pe"] + fc + 1)
                    ins = e.activation(out=h1s[:, fc * 512:(fc + 1) * 512],
                                       in_=pf[fc % 2][:], func=AF.Silu, scale=s1b6[:, fc:fc + 1])
                    sb.inc(ins, "ac")
                blk.scalar(silu)
            base2 = {"pe2": sb.c["pe2"], "dv2": sb.c["dv2"], "po": sb.c["po"],
                     "dv": sb.c["dv"], "pe": sb.c["pe"], "ac": sb.c["ac"]}
            for dc2 in range(DC):
                def m2(e, dc2=dc2):
                    e.wait_ge(sb.sem("ac"), base["ac"] + FC)
                    if dc2 >= 2:
                        e.wait_ge(sb.sem("dv"), base2["dv"] + dc2 - 1)
                    ins = None
                    for fc in range(FC):
                        ins = e.matmul(pf[2 + dc2 % 2][:], sw2b[:, fc * D + dc2 * 128: fc * D + (dc2 + 1) * 128],
                                       h1s[:, fc * 512:(fc + 1) * 512],
                                       start=(fc == 0), stop=(fc == FC - 1))
                    sb.inc(ins, "pe2")
                blk.tensor(m2)

                def comb1(e, dc2=dc2):
                    e.wait_ge(sb.sem("pe2"), base2["pe2"] + dc2 + 1)
                    if dc2 >= 1:
                        e.wait_ge(sb.sem("po"), base2["po"] + dc2)
                    ins = e.tensor_tensor(csum[:], pf[2 + dc2 % 2][:],
                                          eoc[:, dc2 * 512:(dc2 + 1) * 512], op=ALU.add)
                    sb.inc(ins, "dv")
                blk.vector(comb1)

                def comb2(e, dc2=dc2):
                    e.wait_ge(sb.sem("dv"), base2["dv"] + dc2 + 1)
                    if dc2 >= 1:
                        e.wait_ge(sb.sem("pe"), base2["pe"] + dc2)
                    ins = e.tensor_tensor(csum2[:], csum[:], agc[:, dc2 * 512:(dc2 + 1) * 512], op=ALU.add)
                    sb.inc(ins, "po")
                blk.gpsimd(comb2)

                def tpose(e, dc2=dc2):
                    e.wait_ge(sb.sem("po"), base2["po"] + dc2 + 1)
                    if dc2 >= 1:
                        e.wait_ge(sb.sem("dv2"), base2["dv2"] + dc2)
                    ins = None
                    for ts in range(4):
                        ins = e.transpose(pfT[:, ts * 128:(ts + 1) * 128],
                                          csum2[:, ts * 128:(ts + 1) * 128], id_f32[:])
                    sb.inc(ins, "pe")
                blk.tensor(tpose)

                def oadd(e, dc2=dc2):
                    e.wait_ge(sb.sem("pe"), sb.c["pe"])
                    src = mkap(pfT[:, :], pfT[:, :].offset, [pfT[:, :].ap.to_list()[0], [128, 4], [1, 128]])
                    dst = mkap(o6[:, :], o6[:, :].offset + dc2 * 128,
                               [o6[:, :].ap.to_list()[0], [D, 4], [1, 128]])
                    xsrc = mkap(x6[:, :], x6[:, :].offset + dc2 * 128,
                                [x6[:, :].ap.to_list()[0], [D, 4], [1, 128]])
                    ins = e.tensor_tensor(dst, src, xsrc, op=ALU.add)
                    sb.inc(ins, "dv2")
                blk.vector(oadd)
            def store(e, tc=tc, n=sb.c["dv2"]):
                e.wait_ge(sb.sem("dv2"), n)
                ins = e.dma_start(mkap(out.ap(), tc * 512 * D, [[D, 128], [128 * D, 4], [1, D]]), o6[:])
                ins.then_inc(sb.sem("out"), 16)
                sb.c["out"] += 16
            blk.sync(store)
            sr.step("sync", lambda e: e.nop(),
                    extra_waits=[("dv2", sb.c["dv2"])])

        def final(e):
            e.wait_ge(sb.sem("out"), sb.c["out"])
            return e.nop()
        sr.step("sync", final)

    nc.finalize()
    return nc


_NC_CACHE = {}


def kernel(**inputs):
    x = np.ascontiguousarray(inputs["L0_attn_x"])
    ident_ = np.eye(128, dtype=BFNP)
    rampv = np.tile(np.arange(-(TAB // 2), TAB // 2 + 1, dtype=np.float32), (112, 1))
    hsumv = np.zeros((112, 16), dtype=np.float32)
    for p_ in range(112):
        hsumv[p_, p_ // 7] = 1.0

    if "nc" not in _NC_CACHE:
        _NC_CACHE["nc"] = build(debug=False)
    nc = _NC_CACHE["nc"]

    base = dict(
        w_in=np.ascontiguousarray(inputs["L0_attn_w_in"]),
        w_gate=np.ascontiguousarray(inputs["L0_attn_w_gate"]),
        b_gate=np.ascontiguousarray(inputs["L0_attn_b_gate"]),
        amp=np.ascontiguousarray(inputs["L0_attn_amplitudes"]),
        shp=np.ascontiguousarray(inputs["L0_attn_sharpness"]),
        off=np.ascontiguousarray(inputs["L0_attn_offsets"]),
        sw1=np.ascontiguousarray(inputs["L0_ffn_shared_w1"]),
        sw2=np.ascontiguousarray(inputs["L0_ffn_shared_w2"]),
        ew1=np.ascontiguousarray(inputs["L0_ffn_expert_w1"]),
        ew2=np.ascontiguousarray(inputs["L0_ffn_expert_w2"]),
        rw=np.ascontiguousarray(inputs["L0_ffn_router_w"]),
        gam=np.ascontiguousarray(inputs["L0_ffn_ln_gamma"]),
        bet=np.ascontiguousarray(inputs["L0_ffn_ln_beta"]),
        ident=ident_, identf=np.eye(128, dtype=np.float32), ramp=rampv, hsum=hsumv,
    )
    in_maps = []
    for c in range(8):
        m = dict(base)
        m["x"] = np.ascontiguousarray(x[2 * c:2 * c + 2].reshape(T, D))
        in_maps.append(m)
    res = run_bass_kernel_spmd(nc, in_maps, core_ids=list(range(8)))
    return np.concatenate([r["out"].reshape(2, S, D) for r in res.results], axis=0)


if __name__ == "__main__":
    nc = build(debug=False)
    print("build ok")


# version 08
# speedup vs baseline: 1.6214x; 1.0002x over previous
import contextlib
import numpy as np
import ml_dtypes
import concourse.bass as bass
import concourse.mybir as mybir
from concourse.bass_utils import run_bass_kernel_spmd

F32 = mybir.dt.float32
BF16 = mybir.dt.bfloat16
AF = mybir.ActivationFunctionType
ALU = mybir.AluOpType
AX = mybir.AxisListType.X
BFNP = ml_dtypes.bfloat16

B = 2
S = 2048
D = 1024
T = B * S
H = 16
HD = 64
E = 8
FE = 2048
DC = 8
FC = 16
EPS = 1e-5
TAB = 4095
UW = 3968


def mkap(ap, offset, dims):
    a = ap.copy()
    a.offset = offset
    a.ap = type(a.ap)(dims)
    return a


def revslice(tile_ap, col_hi, n):
    dims = tile_ap.ap.to_list()
    return mkap(tile_ap, tile_ap.offset + col_hi, [dims[0], [-1, n]])


class SemBank:
    def __init__(self, nc):
        self.nc = nc
        self.h = {}
        self.c = {}

    def sem(self, n):
        if n not in self.h:
            self.h[n] = self.nc.alloc_semaphore(n)
            self.c[n] = 0
        return self.h[n]

    def inc(self, instr, n, amt=1):
        instr.then_inc(self.sem(n), amt)
        self.c[n] += amt
        return self.c[n]

    def wait(self, eng, n, v):
        if v > 0:
            eng.wait_ge(self.sem(n), v)



class MaybeBlk:
    def __init__(self, blk, en):
        self._blk, self._en = blk, en

    def __getattr__(self, k):
        f = getattr(self._blk, k)
        if k in ("tensor", "vector", "scalar", "gpsimd", "sync"):
            return (lambda fn: f(fn)) if self._en else (lambda fn: None)
        return f


class Serial:

    def __init__(self, sb, blk, enabled=True, name="ser"):
        self.sb = sb
        self.blk = blk
        self.enabled = enabled
        self.name = name
        sb.sem(name)

    def step(self, eng_name, fn, dma=False, extra_waits=()):
        if not self.enabled:
            return
        sb = self.sb
        nm = self.name
        need = sb.c[nm]
        inc = 16 if dma else 1

        def body(eng):
            for n, v in extra_waits:
                sb.wait(eng, n, v)
            sb.wait(eng, nm, need)
            ins = fn(eng)
            ins.then_inc(sb.sem(nm), inc)
        getattr(self.blk, eng_name)(body)
        sb.c[nm] += inc

    def multi(self, eng_name, fns, dma=False, extra_waits=()):
        if not self.enabled:
            return
        sb = self.sb
        nm = self.name
        need = sb.c[nm]
        inc = 16 if dma else 1

        def body(eng):
            for n, v in extra_waits:
                sb.wait(eng, n, v)
            sb.wait(eng, nm, need)
            for fn in fns:
                fn(eng).then_inc(sb.sem(nm), inc)
        getattr(self.blk, eng_name)(body)
        sb.c[nm] += inc * len(fns)


def build(debug=False, phases=(0, 1, 2, 3, 4, 5, 6)):
    nc = bass.Bass("TRN2", target_bir_lowering=False)

    x_in = nc.dram_tensor("x", [T, D], F32, kind="ExternalInput")
    w_in = nc.dram_tensor("w_in", [D, 3 * D], F32, kind="ExternalInput")
    w_gate = nc.dram_tensor("w_gate", [D, 2 * D], F32, kind="ExternalInput")
    b_gate = nc.dram_tensor("b_gate", [2 * D], F32, kind="ExternalInput")
    amp_in = nc.dram_tensor("amp", [H, 21], F32, kind="ExternalInput")
    shp_in = nc.dram_tensor("shp", [H, 21], F32, kind="ExternalInput")
    off_in = nc.dram_tensor("off", [H, 21], F32, kind="ExternalInput")
    sw1_in = nc.dram_tensor("sw1", [FE, D], F32, kind="ExternalInput")
    sw2_in = nc.dram_tensor("sw2", [D, FE], F32, kind="ExternalInput")
    ew1_in = nc.dram_tensor("ew1", [E, FE, D], F32, kind="ExternalInput")
    ew2_in = nc.dram_tensor("ew2", [E, D, FE], F32, kind="ExternalInput")
    rw_in = nc.dram_tensor("rw", [E, D], F32, kind="ExternalInput")
    gam_in = nc.dram_tensor("gam", [D], F32, kind="ExternalInput")
    bet_in = nc.dram_tensor("bet", [D], F32, kind="ExternalInput")
    ident = nc.dram_tensor("ident", [128, 128], BF16, kind="ExternalInput")
    identf = nc.dram_tensor("identf", [128, 128], F32, kind="ExternalInput")
    ramp = nc.dram_tensor("ramp", [112, TAB], F32, kind="ExternalInput")
    hsum = nc.dram_tensor("hsum", [112, 16], F32, kind="ExternalInput")

    out = nc.dram_tensor("out", [T, D], F32, kind="ExternalOutput")

    ikind = "ExternalOutput" if debug else "Internal"
    tab_s = nc.dram_tensor("tab_s", [H, TAB], F32, kind=ikind)
    k_s = nc.dram_tensor("k_s", [B, H, HD, S], BF16, kind=ikind)
    q_s = nc.dram_tensor("q_s", [B, H, HD, S], BF16, kind=ikind)
    v_s = nc.dram_tensor("v_s", [B, S, D], BF16, kind=ikind)
    h_s = nc.dram_tensor("h_s", [DC, 128, T], BF16, kind=ikind)
    ao_s = nc.dram_tensor("ao_s", [DC, 128, T], BF16, kind=ikind)
    ag_s = nc.dram_tensor("ag_s", [DC, 128, T], BF16, kind=ikind)
    gt_s = nc.dram_tensor("gt_s", [E, T], BF16, kind=ikind)
    eo_s = nc.dram_tensor("eo_s", [DC, 128, T], BF16, kind=ikind)
    w1t_s = nc.dram_tensor("w1t_s", [E, DC, 128, FE], BF16, kind=ikind)
    w2t_s = nc.dram_tensor("w2t_s", [E, FC, 128, D], BF16, kind=ikind)
    st1_s = nc.dram_tensor("st1_s", [DC, 128, FE], BF16, kind=ikind)
    st2_s = nc.dram_tensor("st2_s", [FC, 128, D], BF16, kind=ikind)
    s1_s = nc.dram_tensor("s1_s", [E + 1, 128, FC], F32, kind=ikind)

    id_bf = nc.alloc_sbuf_tensor("id_bf", [128, 128], BF16)
    id_f32 = nc.alloc_sbuf_tensor("id_f32", [128, 128], F32)
    ones_bf = nc.alloc_sbuf_tensor("ones_bf", [128, 1], BF16)
    ones_bfr = nc.alloc_sbuf_tensor("ones_bfr", [1, 128], BF16)
    ones_fr = nc.alloc_sbuf_tensor("ones_fr", [1, 128], F32)
    bg_sb = nc.alloc_sbuf_tensor("bg_sb", [128, 16], F32)
    rwt_sb = nc.alloc_sbuf_tensor("rwt_sb", [128, DC * E], F32)
    gate_fm = nc.alloc_sbuf_tensor("gate_fm", [8, T], BF16)


    sb = SemBank(nc)
    for n in ["ser", "ld", "pe", "ac", "dv", "po", "st", "pe2", "dv2", "ac2", "out", "pe3", "dv3", "fl", "qld", "qdv", "qpe", "qpo", "qst", "qout"]:
        sb.sem(n)

    with contextlib.ExitStack() as _es:
        row_sb = _es.enter_context(nc.sbuf_tensor("p0_row", [1, D], F32))
        rw_sb = _es.enter_context(nc.sbuf_tensor("p0_rw", [8, D], F32))
        bgt_row = _es.enter_context(nc.sbuf_tensor("p0_bgt", [16, 128], F32))
        rwb_sb = _es.enter_context(nc.sbuf_tensor("p0_rwb", [8, D], BF16))
        prm = _es.enter_context(nc.sbuf_tensor("p0_prm", [112, 9], F32))
        prm2 = _es.enter_context(nc.sbuf_tensor("p0_prm2", [112, 6], F32))
        ramp_sb = _es.enter_context(nc.sbuf_tensor("p0_ramp", [112, TAB], F32))
        hsum_sb = _es.enter_context(nc.sbuf_tensor("p0_hsum", [112, 16], F32))
        d2_sb = _es.enter_context(nc.sbuf_tensor("p0_d2", [112, TAB], F32))
        eg_sb = _es.enter_context(nc.sbuf_tensor("p0_eg", [112, 3 * TAB], F32))
        tab_sb = _es.enter_context(nc.sbuf_tensor("p0_tab", [16, TAB], F32))
        pf = [_es.enter_context(nc.psum_tensor(f"p0pf{i}", [128, 512], F32)) for i in range(4)]
        blk = _es.enter_context(nc.Block())
        blk = MaybeBlk(blk, 0 in phases)
        sr = Serial(sb, blk, 0 in phases)
        sr.multi("sync", [
            lambda e: e.dma_start(id_bf[:], ident.ap()),
            lambda e: e.dma_start(id_f32[:], identf.ap()),
            lambda e: e.dma_start(ramp_sb[:], ramp.ap()),
            lambda e: e.dma_start(hsum_sb[:], hsum.ap()),
            lambda e: e.dma_start(bgt_row[:], mkap(b_gate.ap(), 0, [[128, 16], [1, 128]])),
            lambda e: e.dma_start(rw_sb[:], rw_in.ap()),
        ] + [
            (lambda gi, g, src: lambda e: e.dma_start(
                prm[:, gi * 3 + g: gi * 3 + g + 1],
                mkap(src.ap(), g * 7, [[21, 16], [1, 7], [1, 1]])))(gi, g, src)
            for gi, src in enumerate([off_in, shp_in, amp_in]) for g in range(3)
        ], dma=True)
        sr.multi("vector", [
            lambda e: e.memset(ones_bf[:], 1.0),
            lambda e: e.memset(ones_bfr[:], 1.0),
            lambda e: e.memset(ones_fr[:], 1.0),
            lambda e: e.tensor_scalar(prm2[:, 0:3], prm[:, 0:3], -1.0, None, op0=ALU.mult),
        ])
        sr.step("scalar", lambda e: e.activation(out=prm2[:, 3:6], in_=prm[:, 3:6], func=AF.Abs))
        sr.step("vector", lambda e: e.tensor_scalar(prm2[:, 3:6], prm2[:, 3:6], -1.0, None, op0=ALU.mult))

        sr.multi("tensor", [
            (lambda dc: lambda e: e.transpose(pf[3][:, dc * 8:(dc + 1) * 8],
                                              rw_sb[0:8, dc * 128:(dc + 1) * 128], id_f32[0:8, 0:8]))(dc)
            for dc in range(DC)
        ])
        sr.step("vector", lambda e: e.tensor_copy(rwt_sb[:], pf[3][:, 0:DC * E]))
        sr.step("tensor", lambda e: e.transpose(pf[3][:, 0:16], bgt_row[:], id_f32[0:16, 0:16]))
        sr.step("vector", lambda e: e.tensor_copy(bg_sb[:], pf[3][:, 0:16]))
        for g in range(3):
            sr.step("scalar", (lambda g: lambda e: e.activation(
                out=d2_sb[:], in_=ramp_sb[:], func=AF.Square, bias=prm2[:, g:g + 1]))(g))
            sr.step("scalar", (lambda g: lambda e: e.activation(
                out=eg_sb[:, g * TAB:(g + 1) * TAB], in_=d2_sb[:], func=AF.Exp,
                scale=prm2[:, 3 + g:4 + g]))(g))
            sr.step("vector", (lambda g: lambda e: e.tensor_scalar(
                eg_sb[:, g * TAB:(g + 1) * TAB], eg_sb[:, g * TAB:(g + 1) * TAB],
                prm[:, 6 + g:7 + g], None, op0=ALU.mult))(g))
        for c in range(8):
            w = 512 if c < 7 else TAB - 7 * 512
            sr.step("tensor", (lambda c, w: lambda e: [
                e.matmul(pf[c % 2][0:16, 0:w], hsum_sb[:], eg_sb[:, g * TAB + c * 512: g * TAB + c * 512 + w],
                         start=(g == 0), stop=(g == 2)) for g in range(3)][-1])(c, w))
            sr.step("scalar", (lambda c, w: lambda e: e.copy(
                tab_sb[:, c * 512:c * 512 + w], pf[c % 2][0:16, 0:w]))(c, w))
        sr.step("sync", lambda e: e.dma_start(tab_s.ap(), tab_sb[:]), dma=True)

    with contextlib.ExitStack() as _es:
        gam_rep = _es.enter_context(nc.sbuf_tensor("p1_gam", [128, D], F32))
        bet_rep = _es.enter_context(nc.sbuf_tensor("p1_bet", [128, D], F32))
        win_sb = _es.enter_context(nc.sbuf_tensor("p1_win", [128, DC * 3 * D], BF16))
        wstg = _es.enter_context(nc.sbuf_tensor("p1_wstg", [128, 3 * D], F32))
        xt2 = _es.enter_context(nc.sbuf_tensor("p1_xt", [128, 2 * D], F32))
        sq_scr = _es.enter_context(nc.sbuf_tensor("p1_scr", [128, D], BF16))
        xn_t = _es.enter_context(nc.sbuf_tensor("p1_xn", [128, D], BF16))
        xnn = _es.enter_context(nc.sbuf_tensor("p1_xnn", [128, D], F32))
        xn_f = _es.enter_context(nc.sbuf_tensor("p1_xnf", [128, D], F32))
        h1tmp = _es.enter_context(nc.sbuf_tensor("p1_ht", [128, D], F32))
        h_t = _es.enter_context(nc.sbuf_tensor("p1_hb", [128, D], BF16))
        h_f = _es.enter_context(nc.sbuf_tensor("p1_hf", [128, D], F32))
        h_fm32 = _es.enter_context(nc.sbuf_tensor("p1_hfm32", [128, DC * 512], F32))
        tny = _es.enter_context(nc.sbuf_tensor("p1_tiny", [128, 16], F32))
        xn_fm = _es.enter_context(nc.sbuf_tensor("p1_xnfm", [128, DC * 512], BF16))
        h_fm = _es.enter_context(nc.sbuf_tensor("p1_hfm", [128, DC * 512], BF16))
        pstg = _es.enter_context(nc.sbuf_tensor("p1_stg", [128, 2 * 512], BF16))
        vstg = _es.enter_context(nc.sbuf_tensor("p1_vstg", [128, 2 * 512], BF16))
        r8 = _es.enter_context(nc.sbuf_tensor("p1_r8", [128, 8], F32))
        g8 = _es.enter_context(nc.sbuf_tensor("p1_g8", [128, 8], BF16))
        wr2 = _es.enter_context(nc.sbuf_tensor("p2_wr", [128, 2 * FE], F32))
        qa2 = _es.enter_context(nc.sbuf_tensor("p2_a", [128, 2 * FE], BF16))
        qb2 = _es.enter_context(nc.sbuf_tensor("p2_b", [128, 2 * FE], BF16))
        qt2 = _es.enter_context(nc.sbuf_tensor("p2_t", [128, 2 * FE], BF16))
        tstg = _es.enter_context(nc.sbuf_tensor("p2_ts", [128, 2 * 1024], BF16))
        qs = _es.enter_context(nc.sbuf_tensor("p2_s", [128, 8], F32))
        s1stg = _es.enter_context(nc.sbuf_tensor("p2_s1", [128, FC], F32))
        pb = [_es.enter_context(nc.psum_tensor(f"pb{i}", [128, 1024], BF16)) for i in range(2)]
        pb2 = _es.enter_context(nc.psum_tensor("pb2", [128, 1024], BF16))
        pfR = _es.enter_context(nc.psum_tensor("pfR", [128, 512], F32))
        pf = [_es.enter_context(nc.psum_tensor(f"p1pf{i}", [128, 512], F32)) for i in range(4)]
        blk = _es.enter_context(nc.Block())
        blk = MaybeBlk(blk, 1 in phases)
        sr = Serial(sb, blk, 1 in phases)

        def _emit_hf32(sr, tt):
            for half in range(2):
                sr.multi("tensor", [
                    (lambda dc, half: lambda e: e.transpose(
                        pf[2 + half][:, (dc % 4) * 128:(dc % 4 + 1) * 128],
                        h_f[:, dc * 128:(dc + 1) * 128], id_f32[:]))(dc, half)
                    for dc in range(half * 4, half * 4 + 4)
                ])
                sr.step("vector", (lambda tt=tt, half=half: lambda e: e.tensor_copy(
                    mkap(h_fm32[:, :], h_fm32[:, :].offset + half * 4 * 512 + tt * 128,
                         [h_fm32[:, :].ap.to_list()[0], [512, 4], [1, 128]]),
                    pf[2 + half][:, 0:512]))(tt, half))

        en2 = 2 in phases
        blk2 = MaybeBlk(blk._blk if hasattr(blk, "_blk") else blk, en2)
        p2_units = []
        if en2:
            mats = []
            for e_ in range(E):
                mats.append(("w1", ew1_in, e_, FE, D))
            for e_ in range(E):
                mats.append(("w2", ew2_in, e_, D, FE))
            mats.append(("w1", sw1_in, E, FE, D))
            mats.append(("w2", sw2_in, E, D, FE))
            p2_state = {"i": 0, "hidx": 0, "ns1": 0, "tile_pe_end": []}

            def _p2_tile(kind, srcten, e_, NR, LC, rt):
                st_ = p2_state
                i = st_["i"]
                par = i % 2
                wr = wr2[:, par * FE: par * FE + LC]
                qsl = qs[:, par * 4: par * 4 + 4]
                base = (e_ * NR * LC if e_ < E else 0) + rt * 128 * LC
                nhalf = LC // 1024
                ns1 = st_["ns1"]

                def L(e):
                    if i >= 2:
                        e.wait_ge(sb.sem("qdv"), i - 1)
                    if kind == "w1" and rt == 0:
                        e.wait_ge(sb.sem("qout"), 16 * ns1)
                    e.dma_start(wr, mkap(srcten.ap(), base, [[LC, 128], [1, LC]])).then_inc(sb.sem("qld"), 16)
                    sb.c["qld"] += 16
                blk2.sync(L)

                tpe = st_["tile_pe_end"]

                def RAD(e):
                    e.wait_ge(sb.sem("qld"), 16 * (i + 1))
                    if i >= 2:
                        e.wait_ge(sb.sem("qpe"), tpe[i - 2])
                    ins = e.tensor_reduce(qsl[:, 0:1], wr, axis=AX, op=ALU.add, apply_absolute_value=True)
                    sb.inc(ins, "fl")
                    e.wait_ge(sb.sem("fl"), sb.c["fl"])
                    ins = e.tensor_scalar(qsl[:, 1:2], qsl[:, 0:1], 1.0 / LC, 1e-5, op0=ALU.mult, op1=ALU.max)
                    sb.inc(ins, "fl")
                    e.wait_ge(sb.sem("fl"), sb.c["fl"])
                    e.tensor_scalar(qsl[:, 2:3], qsl[:, 1:2], -0.5, None, op0=ALU.mult)
                    ins = e.tensor_scalar(qsl[:, 3:4], qsl[:, 1:2], 0.5, None, op0=ALU.mult)
                    if kind == "w1":
                        ins = e.tensor_copy(s1stg[:, rt:rt + 1], qsl[:, 1:2])
                    sb.inc(ins, "fl")
                    e.wait_ge(sb.sem("fl"), sb.c["fl"])
                    e.tensor_scalar(qa2[:, par * FE:par * FE + LC], wr, qsl[:, 2:3], 0.0,
                                    op0=ALU.add, op1=ALU.is_gt)
                    ins = e.tensor_scalar(qb2[:, par * FE:par * FE + LC], wr, qsl[:, 3:4], 0.0,
                                          op0=ALU.add, op1=ALU.is_lt)
                    sb.inc(ins, "fl")
                    e.wait_ge(sb.sem("fl"), sb.c["fl"])
                    ins = e.tensor_tensor(qt2[:, par * FE:par * FE + LC], qa2[:, par * FE:par * FE + LC],
                                          qb2[:, par * FE:par * FE + LC], op=ALU.subtract)
                    if kind == "w2":
                        sb.inc(ins, "fl")
                        e.wait_ge(sb.sem("fl"), sb.c["fl"])
                        ins = e.tensor_scalar(qt2[:, par * FE:par * FE + LC],
                                              qt2[:, par * FE:par * FE + LC], qsl[:, 1:2], None, op0=ALU.mult)
                    sb.inc(ins, "qdv")
                blk2.vector(RAD)

                for hh in range(nhalf):
                    hidx = st_["hidx"]
                    hpar = hidx % 2

                    def Tst(e, hh=hh, hpar=hpar, hidx=hidx):
                        e.wait_ge(sb.sem("qdv"), i + 1)
                        if hidx >= 1:
                            e.wait_ge(sb.sem("qpo"), hidx)
                        ins = None
                        for bkk in range(8):
                            ins = e.transpose(pb2[:, bkk * 128:(bkk + 1) * 128],
                                              qt2[:, par * FE + hh * 1024 + bkk * 128:
                                                  par * FE + hh * 1024 + (bkk + 1) * 128], id_bf[:])
                        sb.inc(ins, "qpe")
                    blk2.tensor(Tst)

                    def Est(e, hpar=hpar, hidx=hidx):
                        e.wait_ge(sb.sem("qpe"), hidx + 1)
                        if hidx >= 2:
                            e.wait_ge(sb.sem("qst"), 16 * (hidx - 1))
                        ins = e.tensor_copy(tstg[:, hpar * 1024:(hpar + 1) * 1024], pb2[:])
                        sb.inc(ins, "qpo")
                    blk2.vector(Est)

                    if kind == "w1":
                        dst = w1t_s if e_ < E else st1_s
                        dbase = (e_ * DC * 128 * FE if e_ < E else 0) + hh * 8 * 128 * FE + rt * 128
                        ddims = [[FE, 128], [128 * FE, 8], [1, 128]]
                    else:
                        dst = w2t_s if e_ < E else st2_s
                        dbase = (e_ * FC * 128 * D if e_ < E else 0) + hh * 8 * 128 * D + rt * 128
                        ddims = [[D, 128], [128 * D, 8], [1, 128]]

                    def Sst(e, dst=dst, dbase=dbase, ddims=ddims, hpar=hpar, hidx=hidx):
                        e.wait_ge(sb.sem("qpo"), hidx + 1)
                        sap = mkap(tstg[:, :], tstg[:, :].offset + hpar * 1024,
                                   [tstg[:, :].ap.to_list()[0], [128, 8], [1, 128]])
                        e.dma_start(mkap(dst.ap(), dbase, ddims), sap).then_inc(sb.sem("qst"), 16)
                        sb.c["qst"] += 16
                    blk2.gpsimd(Sst)
                    st_["hidx"] += 1
                st_["tile_pe_end"].append(sb.c["qpe"])
                st_["i"] += 1

            def _p2_s1(e_):
                n = sb.c["qdv"]

                def S1(e):
                    e.wait_ge(sb.sem("qdv"), n)
                    e.dma_start(mkap(s1_s.ap(), e_ * 128 * FC, [[FC, 128], [1, FC]]),
                                s1stg[:]).then_inc(sb.sem("qout"), 16)
                    sb.c["qout"] += 16
                blk2.sync(S1)
                p2_state["ns1"] += 1

            for kind, srcten, e_, NR, LC in mats:
                for rt in range(NR // 128):
                    p2_units.append((lambda kind=kind, srcten=srcten, e_=e_, NR=NR, LC=LC, rt=rt:
                                     _p2_tile(kind, srcten, e_, NR, LC, rt)))
                if kind == "w1":
                    p2_units.append((lambda e_=e_: _p2_s1(e_)))
        p2_ptr = [0]

        def drain_p2(n):
            while n > 0 and p2_ptr[0] < len(p2_units):
                p2_units[p2_ptr[0]]()
                p2_ptr[0] += 1
                n -= 1



        sr.step("sync", lambda e: e.dma_start(
            xt2[0:1, 0:D], gam_in.ap().rearrange("(one d) -> one d", one=1)), dma=True)
        for c in range(2):
            sr.step("tensor", (lambda c: lambda e: e.matmul(
                pf[c][:, 0:512], ones_fr[:], xt2[0:1, c * 512:(c + 1) * 512], start=True, stop=True))(c))
        sr.multi("vector", [
            (lambda c: lambda e: e.tensor_copy(gam_rep[:, c * 512:(c + 1) * 512], pf[c][:]))(c)
            for c in range(2)
        ])
        sr.step("sync", lambda e: e.dma_start(xt2[0:1, 0:D], bet_in.ap().rearrange("(one d) -> one d", one=1)), dma=True)
        for c in range(2):
            sr.step("tensor", (lambda c: lambda e: e.matmul(
                pf[c][:, 0:512], ones_fr[:], xt2[0:1, c * 512:(c + 1) * 512], start=True, stop=True))(c))
        sr.multi("vector", [
            (lambda c: lambda e: e.tensor_copy(bet_rep[:, c * 512:(c + 1) * 512], pf[c][:]))(c)
            for c in range(2)
        ])
        for dc in range(DC):
            sr.step("sync", (lambda dc: lambda e: e.dma_start(
                wstg[:], x_in.ap() if False else mkap(w_in.ap(), dc * 128 * 3 * D,
                                                      [[3 * D, 128], [1, 3 * D]])))(dc), dma=True)
            sr.step("vector", (lambda dc: lambda e: e.tensor_copy(
                win_sb[:, dc * 3 * D:(dc + 1) * 3 * D], wstg[:]))(dc))

        for grp in range(8):
            for tt in range(4):
                xt = xt2[:, (tt % 2) * D:((tt % 2) + 1) * D]
                r0 = (grp * 4 + tt) * 128
                sr.step("sync", (lambda r0, xt=xt: lambda e: e.dma_start(
                    xt, mkap(x_in.ap(), r0 * D, [[D, 128], [1, D]])))(r0), dma=True)
                sr.step("vector", (lambda xt=xt: lambda e: e.tensor_reduce(
                    tny[:, 0:1], xt, axis=AX, op=ALU.add))(xt))
                sr.step("scalar", (lambda xt=xt: lambda e: e.activation(
                    out=sq_scr[:], in_=xt, func=AF.Square, accum_out=tny[:, 1:2]))(xt))
                sr.multi("vector", [
                    lambda e: e.tensor_scalar(tny[:, 2:3], tny[:, 0:1], 1.0 / D, None, op0=ALU.mult),
                    lambda e: e.tensor_scalar(tny[:, 3:4], tny[:, 1:2], 1.0 / D, None, op0=ALU.mult),
                ])
                sr.step("vector", lambda e: e.tensor_tensor(tny[:, 4:5], tny[:, 2:3], tny[:, 2:3], op=ALU.mult))
                sr.step("vector", lambda e: e.tensor_tensor(tny[:, 5:6], tny[:, 3:4], tny[:, 4:5], op=ALU.subtract))
                sr.step("vector", lambda e: e.tensor_scalar(tny[:, 6:7], tny[:, 5:6], EPS, None, op0=ALU.add))
                sr.step("scalar", lambda e: e.sqrt(tny[:, 7:8], tny[:, 6:7]))
                sr.step("vector", lambda e: e.reciprocal(tny[:, 8:9], tny[:, 7:8]))
                sr.step("vector", (lambda xt=xt: lambda e: e.tensor_scalar(
                    xn_f[:], xt, tny[:, 2:3], tny[:, 8:9], op0=ALU.subtract, op1=ALU.mult))(xt))
                sr.step("vector", lambda e: e.tensor_copy(xn_t[:], xn_f[:]))
                sr.step("vector", lambda e: e.tensor_reduce(tny[:, 0:1], xn_f[:], axis=AX, op=ALU.add))
                sr.step("scalar", lambda e: e.activation(out=sq_scr[:], in_=xn_f[:], func=AF.Square,
                                                         accum_out=tny[:, 1:2]))
                sr.multi("vector", [
                    lambda e: e.tensor_scalar(tny[:, 2:3], tny[:, 0:1], 1.0 / D, None, op0=ALU.mult),
                    lambda e: e.tensor_scalar(tny[:, 3:4], tny[:, 1:2], 1.0 / D, None, op0=ALU.mult),
                ])
                sr.step("vector", lambda e: e.tensor_tensor(tny[:, 4:5], tny[:, 2:3], tny[:, 2:3], op=ALU.mult))
                sr.step("vector", lambda e: e.tensor_tensor(tny[:, 5:6], tny[:, 3:4], tny[:, 4:5], op=ALU.subtract))
                sr.step("vector", lambda e: e.tensor_scalar(tny[:, 6:7], tny[:, 5:6], EPS, None, op0=ALU.add))
                sr.step("scalar", lambda e: e.sqrt(tny[:, 7:8], tny[:, 6:7]))
                sr.step("vector", lambda e: e.reciprocal(tny[:, 8:9], tny[:, 7:8]))
                sr.step("vector", lambda e: e.tensor_scalar(
                    xnn[:], xn_f[:], tny[:, 2:3], tny[:, 8:9], op0=ALU.subtract, op1=ALU.mult))
                sr.step("gpsimd", lambda e: e.tensor_tensor(h1tmp[:], xnn[:], gam_rep[:], op=ALU.mult))
                sr.step("gpsimd", lambda e: e.tensor_tensor(h_f[:], h1tmp[:], bet_rep[:], op=ALU.add))
                sr.step("vector", lambda e: e.tensor_copy(h_t[:], h_f[:]))
                sr.multi("tensor", [
                    (lambda dc: lambda e: e.transpose(
                        pb[0][:, dc * 128:(dc + 1) * 128], xn_t[:, dc * 128:(dc + 1) * 128], id_bf[:]))(dc)
                    for dc in range(DC)
                ])
                sr.step("vector", (lambda tt=tt: lambda e: e.tensor_copy(
                    mkap(xn_fm[:, :], xn_fm[:, :].offset + tt * 128,
                         [xn_fm[:, :].ap.to_list()[0], [512, DC], [1, 128]]),
                    pb[0][:, 0:1024]))(tt))
                sr.multi("tensor", [
                    (lambda dc: lambda e: e.transpose(
                        pb[1][:, dc * 128:(dc + 1) * 128], h_t[:, dc * 128:(dc + 1) * 128], id_bf[:]))(dc)
                    for dc in range(DC)
                ])
                sr.step("vector", (lambda tt=tt: lambda e: e.tensor_copy(
                    mkap(h_fm[:, :], h_fm[:, :].offset + tt * 128, [h_fm[:, :].ap.to_list()[0], [512, DC], [1, 128]]),
                    pb[1][:, 0:1024]))(tt))
                _emit_hf32(sr, tt)
                drain_p2(2)
            sr.multi("gpsimd", [
                (lambda dc, grp=grp: lambda e: e.dma_start(
                    mkap(h_s.ap(), dc * 128 * T + grp * 512, [[T, 128], [1, 512]]),
                    h_fm[:, dc * 512:(dc + 1) * 512]))(dc)
                for dc in range(DC)
            ], dma=True)
            bp = {n: sb.c[n] for n in ["pe", "ac", "dv", "st", "pe2"]}
            serv1 = sb.c["ser"]
            cons_end = []
            for oc in range(24):
                par = oc % 2

                def MM(e, oc=oc, par=par, grp=grp):
                    e.wait_ge(sb.sem("ser"), serv1)
                    if oc >= 2:
                        e.wait_ge(sb.sem("ac"), bp["ac"] + oc - 1)
                    ins = None
                    for dc in range(DC):
                        ins = e.matmul(pf[par][:], win_sb[:, dc * 3 * D + oc * 128: dc * 3 * D + (oc + 1) * 128],
                                       xn_fm[:, dc * 512:(dc + 1) * 512],
                                       start=(dc == 0), stop=(dc == DC - 1))
                    sb.inc(ins, "pe")
                blk.tensor(MM)

                def EV(e, oc=oc, par=par):
                    e.wait_ge(sb.sem("pe"), bp["pe"] + oc + 1)
                    if oc >= 2:
                        ce = cons_end[oc - 2]
                        e.wait_ge(sb.sem("st"), ce["st"])
                        e.wait_ge(sb.sem("pe2"), ce["pe2"])
                    ins = e.copy(pstg[:, par * 512:(par + 1) * 512], pf[par][:])
                    sb.inc(ins, "ac")
                blk.scalar(EV)

                if oc < 8 or oc >= 16:
                    if oc < 8:
                        dstoff = (grp // 4) * H * HD * S + (oc * 2) * HD * S + ((grp % 4) * 512)
                        dst = k_s
                    else:
                        dstoff = (grp // 4) * H * HD * S + ((oc - 16) * 2) * HD * S + ((grp % 4) * 512)
                        dst = q_s

                    def ST(e, oc=oc, par=par, dst=dst, dstoff=dstoff):
                        e.wait_ge(sb.sem("ac"), bp["ac"] + oc + 1)
                        e.dma_start(mkap(dst.ap(), dstoff, [[S, 128], [1, 512]]),
                                    pstg[:, par * 512:(par + 1) * 512]).then_inc(sb.sem("st"), 16)
                        sb.c["st"] += 16
                    blk.gpsimd(ST)
                else:
                    vc = oc - 8

                    def TP(e, oc=oc, par=par):
                        e.wait_ge(sb.sem("ac"), bp["ac"] + oc + 1)
                        if oc >= 2:
                            e.wait_ge(sb.sem("dv"), cons_end[oc - 2]["dv"])
                        ins = None
                        for ts in range(4):
                            ins = e.transpose(pb[par][:, ts * 128:(ts + 1) * 128],
                                              pstg[:, par * 512 + ts * 128: par * 512 + (ts + 1) * 128], id_bf[:])
                        sb.inc(ins, "pe2")
                    blk.tensor(TP)

                    def VE(e, oc=oc, par=par):
                        e.wait_ge(sb.sem("pe2"), sb.c["pe2"])
                        if oc >= 2:
                            e.wait_ge(sb.sem("st"), cons_end[oc - 2]["st"])
                        ins = e.tensor_copy(vstg[:, par * 512:(par + 1) * 512], pb[par][:, 0:512])
                        sb.inc(ins, "dv")
                    blk.vector(VE)

                    def SV(e, vc=vc, par=par, grp=grp):
                        e.wait_ge(sb.sem("dv"), sb.c["dv"])
                        e.dma_start(mkap(v_s.ap(), (grp // 4) * S * D + (grp % 4) * 512 * D + vc * 128,
                                         [[D, 128], [128 * D, 4], [1, 128]]),
                                    mkap(vstg[:, :], vstg[:, :].offset + par * 512,
                                         [vstg[:, :].ap.to_list()[0], [128, 4], [1, 128]])).then_inc(sb.sem("st"), 16)
                        sb.c["st"] += 16
                    blk.gpsimd(SV)
                cons_end.append({"st": sb.c["st"], "pe2": sb.c["pe2"], "dv": sb.c["dv"]})
                drain_p2(1)
            sr.step("sync", lambda e: e.nop(), extra_waits=[
                ("pe", sb.c["pe"]), ("ac", sb.c["ac"]), ("dv", sb.c["dv"]),
                ("st", sb.c["st"]), ("pe2", sb.c["pe2"])])
            for tq in range(4):
                def rmm(e, tq=tq):
                    ins = None
                    for dc in range(DC):
                        ins = e.matmul(pfR[:, 0:8], h_fm32[:, dc * 512 + tq * 128: dc * 512 + (tq + 1) * 128],
                                       rwt_sb[:, dc * 8:(dc + 1) * 8],
                                       start=(dc == 0), stop=(dc == DC - 1))
                    return ins
                sr.step("tensor", rmm)
                sr.step("vector", lambda e: e.tensor_reduce(tny[:, 0:1], pfR[:, 0:8], axis=AX,
                                                            op=ALU.max, negate=True))
                sr.step("scalar", lambda e: e.activation(out=r8[:], in_=pfR[:, 0:8], func=AF.Exp,
                                                         bias=tny[:, 0:1]))
                sr.step("vector", lambda e: e.tensor_reduce(tny[:, 1:2], r8[:], axis=AX, op=ALU.add))
                sr.step("vector", lambda e: e.reciprocal(tny[:, 2:3], tny[:, 1:2]))
                sr.step("vector", lambda e: e.tensor_reduce(tny[:, 3:4], r8[:], axis=AX, op=ALU.max))
                sr.step("vector", lambda e: e.tensor_tensor(tny[:, 4:5], tny[:, 3:4], tny[:, 2:3], op=ALU.mult))
                sr.step("vector", lambda e: e.tensor_scalar(g8[:], r8[:], tny[:, 3:4], tny[:, 4:5],
                                                            op0=ALU.is_equal, op1=ALU.mult))
                sr.step("tensor", (lambda tq=tq: lambda e: e.transpose(
                    pb[0][0:8, tq * 128:(tq + 1) * 128], g8[:], id_bf[:]))(tq))
            sr.step("vector", (lambda grp=grp: lambda e: e.tensor_copy(
                gate_fm[0:8, grp * 512:(grp + 1) * 512], pb[0][0:8, 0:512]))(grp))
        drain_p2(10 ** 9)
        sr.step("gpsimd", lambda e: e.dma_start(gt_s.ap(), gate_fm[:]), dma=True)
        sr.step("sync", lambda e: e.nop(), extra_waits=[("qst", sb.c["qst"]), ("qout", sb.c["qout"])])

    with contextlib.ExitStack() as _es:
        kab2 = _es.enter_context(nc.sbuf_tensor("p3_k", [128, 2 * S], BF16))
        qab2 = _es.enter_context(nc.sbuf_tensor("p3_q", [128, 2 * S], BF16))
        v652 = _es.enter_context(nc.sbuf_tensor("p3_v", [128, 2 * 16 * 65 * 2], BF16))
        u_sb = _es.enter_context(nc.sbuf_tensor("p3_u", [128, UW], F32))
        eb2 = _es.enter_context(nc.sbuf_tensor("p3_eb", [128, 4 * UW], BF16))
        sqt = _es.enter_context(nc.sbuf_tensor("p3_sq", [128, S], BF16))
        exb = _es.enter_context(nc.sbuf_tensor("p3_ex", [128, 2 * 1024], BF16))
        pub = _es.enter_context(nc.sbuf_tensor("p3_pu", [128, 2 * 1024], BF16))
        nmx = _es.enter_context(nc.sbuf_tensor("p3_nm", [1, 16], F32))
        negm2 = _es.enter_context(nc.sbuf_tensor("p3_ng", [128, 2], F32))
        rcp = _es.enter_context(nc.sbuf_tensor("p3_rc", [1, 2 * 512], F32))
        repsb = _es.enter_context(nc.sbuf_tensor("p3_rp", [64, 2 * 512], F32))
        aob = _es.enter_context(nc.sbuf_tensor("p3_ao", [64, 4 * 512], BF16))
        s01 = [_es.enter_context(nc.psum_tensor(f"p3s{i}", [128, 1024], F32)) for i in range(2)]
        avA = _es.enter_context(nc.psum_tensor("p3avA", [128, 512], F32))
        avB = _es.enter_context(nc.psum_tensor("p3avB", [128, 512], F32))
        pfN = _es.enter_context(nc.psum_tensor("p3pfN", [128, 512], F32))
        blk = _es.enter_context(nc.Block())
        en3 = 3 in phases
        blk = MaybeBlk(blk, en3)
        sr = Serial(sb, blk, en3)
        sr.step("vector", lambda e: e.memset(v652[:], 1.0))
        st_hist = []

        def prelude_gen(pr):
            par = pr % 2
            b_, j = pr // 8, pr % 8
            kab = kab2[:, par * S:(par + 1) * S]
            qab = qab2[:, par * S:(par + 1) * S]
            v65 = v652[:, par * 16 * 65 * 2:(par + 1) * 16 * 65 * 2]
            eb = eb2[:, par * 2 * UW:(par + 1) * 2 * UW]
            xw = []
            if pr >= 2:
                xw = [("st", st_hist[pr - 2])]
            sr.multi("sync", [
                (lambda b_, j, kab=kab: lambda e: e.dma_start(
                    kab, mkap(k_s.ap(), b_ * H * HD * S + (2 * j) * HD * S, [[S, 128], [1, S]])))(b_, j),
                (lambda b_, j, qab=qab: lambda e: e.dma_start(
                    qab, mkap(q_s.ap(), b_ * H * HD * S + (2 * j) * HD * S, [[S, 128], [1, S]])))(b_, j),
                (lambda j: lambda e: e.dma_start(
                    u_sb[:], mkap(tab_s.ap(), (2 * j) * TAB, [[1, 128], [1, UW]])))(j),
            ], dma=True, extra_waits=xw)
            yield
            sr.step("scalar", (lambda eb=eb: lambda e: e.activation(out=eb[:, 0:UW], in_=u_sb[:], func=AF.Exp))())
            yield
            sr.step("sync", (lambda j: lambda e: e.dma_start(
                u_sb[:], mkap(tab_s.ap(), (2 * j + 1) * TAB, [[1, 128], [1, UW]])))(j), dma=True)
            yield
            sr.step("scalar", (lambda eb=eb: lambda e: e.activation(out=eb[:, UW:2 * UW], in_=u_sb[:], func=AF.Exp))())
            yield
            sr.multi("sync", [
                (lambda b_, j, kc, hh, v65=v65: lambda e: e.dma_start(
                    v65[:, (hh * 16 + kc) * 65:(hh * 16 + kc) * 65 + 64],
                    mkap(v_s.ap(), b_ * S * D + (kc * 128) * D + (2 * j + hh) * 64,
                         [[D, 128], [1, 64]])))(b_, j, kc, hh)
                for hh in range(2) for kc in range(16)
            ], dma=True)
            yield
            sr.step("vector", (lambda kab=kab: lambda e: e.tensor_tensor(sqt[:], kab, kab, op=ALU.mult))())
            yield
            for c in range(4):
                sr.step("tensor", (lambda c: lambda e: e.matmul(
                    pfN[0:1, 0:512], ones_bf[:], sqt[:, c * 512:(c + 1) * 512], start=True, stop=True))(c))
                sr.step("vector", (lambda c: lambda e: e.tensor_reduce(
                    nmx[0:1, c:c + 1], pfN[0:1, 0:512], axis=AX, op=ALU.max))(c))
                yield
            sr.step("vector", (lambda qab=qab: lambda e: e.tensor_tensor(sqt[:], qab, qab, op=ALU.mult))())
            yield
            for c in range(4):
                sr.step("tensor", (lambda c: lambda e: e.matmul(
                    pfN[0:1, 0:512], ones_bf[:], sqt[:, c * 512:(c + 1) * 512], start=True, stop=True))(c))
                sr.step("vector", (lambda c: lambda e: e.tensor_reduce(
                    nmx[0:1, 4 + c:5 + c], pfN[0:1, 0:512], axis=AX, op=ALU.max))(c))
                yield
            sr.step("vector", lambda e: e.tensor_reduce(nmx[0:1, 8:9], nmx[0:1, 0:4], axis=AX, op=ALU.max))
            sr.step("vector", lambda e: e.tensor_reduce(nmx[0:1, 9:10], nmx[0:1, 4:8], axis=AX, op=ALU.max))
            sr.step("vector", lambda e: e.tensor_tensor(nmx[0:1, 10:11], nmx[0:1, 8:9], nmx[0:1, 9:10], op=ALU.add))
            sr.step("vector", lambda e: e.tensor_scalar(nmx[0:1, 11:12], nmx[0:1, 10:11], -1.0 / 16.0, None, op0=ALU.mult))
            yield
            sr.step("tensor", lambda e: e.matmul(pfN[:, 0:1], ones_fr[:], nmx[0:1, 11:12], start=True, stop=True))
            sr.step("vector", (lambda par=par: lambda e: e.tensor_copy(negm2[:, par:par + 1], pfN[:, 0:1]))())
            yield

        def run_main(pr, nxt):
            par = pr % 2
            b_, j = pr // 8, pr % 8
            kab = kab2[:, par * S:(par + 1) * S]
            qab = qab2[:, par * S:(par + 1) * S]
            v65 = v652[:, par * 16 * 65 * 2:(par + 1) * 16 * 65 * 2]
            eb = eb2[:, par * 2 * UW:(par + 1) * 2 * UW]
            negm = negm2[:, par:par + 1]
            base = {n: sb.c[n] for n in ["pe", "ac", "dv", "pe2", "dv2", "dv3", "pe3", "st"]}
            serv = sb.c["ser"]

            def em_exp(u):
                kc = u % 16

                def ex_fn(e, u=u, kc=kc):
                    e.wait_ge(sb.sem("pe"), base["pe"] + u + 1)
                    if u >= 2:
                        e.wait_ge(sb.sem("dv"), base["dv"] + u - 1)
                    ins = e.activation(out=exb[:, (kc % 2) * 1024:(kc % 2 + 1) * 1024],
                                       in_=s01[kc % 2][:], func=AF.Exp, bias=negm, scale=0.125)
                    sb.inc(ins, "ac")
                blk.scalar(ex_fn)

            def em_mul(u):
                qc, kc = divmod(u, 16)

                def mul_fn(e, u=u, qc=qc, kc=kc):
                    e.wait_ge(sb.sem("ac"), base["ac"] + u + 1)
                    if u >= 2:
                        e.wait_ge(sb.sem("pe2"), base["pe2"] + u - 1)
                    a_col = 2047 + kc * 128 - qc * 512
                    src_eb = mkap(eb, eb.offset + a_col,
                                  [eb.ap.to_list()[0], [UW, 2], [-1, 512]])
                    ins = e.tensor_tensor(pub[:, (kc % 2) * 1024:(kc % 2 + 1) * 1024],
                                          exb[:, (kc % 2) * 1024:(kc % 2 + 1) * 1024],
                                          src_eb, op=ALU.mult)
                    sb.inc(ins, "dv")
                blk.vector(mul_fn)

            def em_qk(u):
                qc, kc = divmod(u, 16)

                def mm_qk(e, u=u, qc=qc, kc=kc):
                    e.wait_ge(sb.sem("ser"), serv)
                    if u >= 2:
                        e.wait_ge(sb.sem("ac"), base["ac"] + u - 1)
                    else:
                        e.wait_ge(sb.sem("ac"), base["ac"])
                    if kc == 0:
                        e.wait_ge(sb.sem("dv2"), base["dv2"] + qc)
                    e.matmul(s01[kc % 2][:, 0:512], kab[0:64, kc * 128:(kc + 1) * 128],
                             qab[0:64, qc * 512:(qc + 1) * 512], start=True, stop=True,
                             tile_position=(0, 0))
                    ins = e.matmul(s01[kc % 2][:, 512:1024], kab[64:128, kc * 128:(kc + 1) * 128],
                                   qab[64:128, qc * 512:(qc + 1) * 512], start=True, stop=True,
                                   tile_position=(64, 0))
                    sb.inc(ins, "pe")
                blk.tensor(mm_qk)

            def em_av(u):
                qc, kc = divmod(u, 16)

                def mm_av(e, u=u, kc=kc):
                    e.wait_ge(sb.sem("dv"), base["dv"] + u + 1)
                    e.matmul(avA[0:65, :], v65[:, kc * 65:kc * 65 + 65],
                             pub[:, (kc % 2) * 1024:(kc % 2) * 1024 + 512],
                             start=(kc == 0), stop=(kc == 15))
                    ins = e.matmul(avB[0:65, :], v65[:, (16 + kc) * 65:(16 + kc) * 65 + 65],
                                   pub[:, (kc % 2) * 1024 + 512:(kc % 2 + 1) * 1024],
                                   start=(kc == 0), stop=(kc == 15))
                    sb.inc(ins, "pe2")
                blk.tensor(mm_av)

            def em_epi(qc):
                def rcp_fn(e, qc=qc):
                    e.wait_ge(sb.sem("pe2"), base["pe2"] + (qc + 1) * 16)
                    e.reciprocal(rcp[0:1, 0:512], avA[64:65, :])
                    ins = e.reciprocal(rcp[0:1, 512:1024], avB[64:65, :])
                    sb.inc(ins, "dv3")
                blk.vector(rcp_fn)

                def rep_fn(e, qc=qc):
                    e.wait_ge(sb.sem("dv3"), sb.c["dv3"])
                    e.matmul(s01[0][0:64, 0:512], ones_fr[0:1, 0:64], rcp[0:1, 0:512], start=True, stop=True)
                    ins = e.matmul(s01[0][0:64, 512:1024], ones_fr[0:1, 0:64], rcp[0:1, 512:1024],
                                   start=True, stop=True)
                    sb.inc(ins, "pe3")
                blk.tensor(rep_fn)

                def evac_fn(e, qc=qc):
                    e.wait_ge(sb.sem("pe3"), sb.c["pe3"])
                    if qc >= 2:
                        e.wait_ge(sb.sem("st"), base["st"] + (qc - 1) * 32)
                    else:
                        e.wait_ge(sb.sem("st"), base["st"] - 32 * (1 - qc))
                    e.tensor_copy(repsb[:, 0:512], s01[0][0:64, 0:512])
                    e.tensor_copy(repsb[:, 512:1024], s01[0][0:64, 512:1024])
                    e.tensor_tensor(aob[:, (qc % 2) * 1024:(qc % 2) * 1024 + 512],
                                    avA[0:64, :], repsb[:, 0:512], op=ALU.mult)
                    ins = e.tensor_tensor(aob[:, (qc % 2) * 1024 + 512:(qc % 2 + 1) * 1024],
                                          avB[0:64, :], repsb[:, 512:1024], op=ALU.mult)
                    sb.inc(ins, "dv2")
                blk.vector(evac_fn)

                def st_fn(e, qc=qc, b_=b_, j=j):
                    e.wait_ge(sb.sem("dv2"), sb.c["dv2"])
                    e.dma_start(mkap(ao_s.ap(), j * 128 * T + b_ * S + qc * 512, [[T, 64], [1, 512]]),
                                aob[:, (qc % 2) * 1024:(qc % 2) * 1024 + 512]).then_inc(sb.sem("st"), 16)
                    e.dma_start(mkap(ao_s.ap(), j * 128 * T + 64 * T + b_ * S + qc * 512, [[T, 64], [1, 512]]),
                                aob[:, (qc % 2) * 1024 + 512:(qc % 2 + 1) * 1024]).then_inc(sb.sem("st"), 16)
                    sb.c["st"] += 32
                blk.gpsimd(st_fn)

            for u in range(64):
                if u == 0:
                    em_qk(u)
                else:
                    em_exp(u - 1)
                    em_mul(u - 1)
                    if u % 16 == 0:
                        em_av(u - 1)
                        em_epi(u // 16 - 1)
                        em_qk(u)
                    else:
                        em_qk(u)
                        em_av(u - 1)
                if nxt is not None and u % 5 == 1:
                    next(nxt, None)
            em_exp(63)
            em_mul(63)
            em_av(63)
            em_epi(3)
            if nxt is not None:
                for _ in nxt:
                    pass
            st_hist.append(sb.c["st"])

        pg = prelude_gen(0)
        for _ in pg:
            pass
        for pr in range(16):
            nxt = iter(prelude_gen(pr + 1)) if pr + 1 < 16 else None
            run_main(pr, nxt)

        def fence(e):
            e.wait_ge(sb.sem("st"), sb.c["st"])
            return e.nop()
        sr.step("sync", fence)

    with contextlib.ExitStack() as _es:
        wg_sb = _es.enter_context(nc.sbuf_tensor("p4_wg", [128, DC * 2 * D], BF16))
        wstg4 = _es.enter_context(nc.sbuf_tensor("p4_stg", [128, 2 * D], F32))
        aochunk = _es.enter_context(nc.sbuf_tensor("p4_ao", [128, DC * 512], BF16))
        a8 = _es.enter_context(nc.sbuf_tensor("p4_a8", [128, 8 * 512], F32))
        sgb = _es.enter_context(nc.sbuf_tensor("p4_sg", [128, 512], F32))
        agb = _es.enter_context(nc.sbuf_tensor("p4_ag", [128, 2 * 512], BF16))
        pf = [_es.enter_context(nc.psum_tensor(f"p4pf{i}", [128, 512], F32)) for i in range(2)]
        blk = _es.enter_context(nc.Block())
        blk = MaybeBlk(blk, 4 in phases)
        sr = Serial(sb, blk, 4 in phases)
        for dc in range(DC):
            sr.step("sync", (lambda dc: lambda e: e.dma_start(
                wstg4[:], mkap(w_gate.ap(), dc * 128 * 2 * D, [[2 * D, 128], [1, 2 * D]])))(dc), dma=True)
            sr.step("vector", (lambda dc: lambda e: e.tensor_copy(
                wg_sb[:, dc * 2 * D:(dc + 1) * 2 * D], wstg4[:]))(dc))
        for tc in range(8):
            sr.multi("sync", [
                (lambda dc, tc: lambda e: e.dma_start(
                    aochunk[:, dc * 512:(dc + 1) * 512],
                    mkap(ao_s.ap(), dc * 128 * T + tc * 512, [[T, 128], [1, 512]])))(dc, tc)
                for dc in range(DC)
            ], dma=True)
            for gc in range(16):
                par = gc % 2

                def mmg(e, gc=gc, par=par):
                    ins = None
                    for dc in range(DC):
                        ins = e.matmul(pf[par][:], wg_sb[:, dc * 2 * D + gc * 128: dc * 2 * D + (gc + 1) * 128],
                                       aochunk[:, dc * 512:(dc + 1) * 512],
                                       start=(dc == 0), stop=(dc == DC - 1))
                    return ins
                sr.step("tensor", mmg)
                if gc < 8:
                    sr.step("vector", (lambda gc, par: lambda e: e.tensor_scalar(
                        a8[:, gc * 512:(gc + 1) * 512], pf[par][:], bg_sb[:, gc:gc + 1], None,
                        op0=ALU.add))(gc, par))
                else:
                    dco = gc - 8
                    sr.step("scalar", (lambda gc, par: lambda e: e.activation(
                        out=sgb[:], in_=pf[par][:], func=AF.Sigmoid, bias=bg_sb[:, gc:gc + 1]))(gc, par))
                    sr.step("vector", (lambda dco: lambda e: e.tensor_tensor(
                        agb[:, (dco % 2) * 512:(dco % 2 + 1) * 512], a8[:, dco * 512:(dco + 1) * 512],
                        sgb[:], op=ALU.mult))(dco))
                    sr.step("gpsimd", (lambda dco, tc: lambda e: e.dma_start(
                        mkap(ag_s.ap(), dco * 128 * T + tc * 512, [[T, 128], [1, 512]]),
                        agb[:, (dco % 2) * 512:(dco % 2 + 1) * 512]))(dco, tc), dma=True)

    with contextlib.ExitStack() as _es:
        eo_acc = _es.enter_context(nc.sbuf_tensor("p5_acc", [128, DC * T], BF16))
        w1b = _es.enter_context(nc.sbuf_tensor("p5_w1", [128, DC * FE], BF16))
        w2b = _es.enter_context(nc.sbuf_tensor("p5_w2", [128, FC * D], BF16))
        hb2 = _es.enter_context(nc.sbuf_tensor("p5_h", [128, 2 * DC * 512], BF16))
        h1g2 = _es.enter_context(nc.sbuf_tensor("p5_h1g", [128, 2 * FC * 512], BF16))
        h1b = _es.enter_context(nc.sbuf_tensor("p5_h1", [128, 2 * 512], BF16))
        grsb2 = _es.enter_context(nc.sbuf_tensor("p5_gr", [128, 2 * 512], BF16))
        grow2 = _es.enter_context(nc.sbuf_tensor("p5_grow", [1, 2 * 512], BF16))
        s1b = _es.enter_context(nc.sbuf_tensor("p5_s1", [128, FC], F32))
        pfG = _es.enter_context(nc.psum_tensor("pfG", [128, 512], F32))
        pf = [_es.enter_context(nc.psum_tensor(f"p5pf{i}", [128, 512], F32)) for i in range(4)]
        pfM = _es.enter_context(nc.psum_tensor("p5pfM", [128, 512], F32))
        m1b = pf[0:2] + [pfM]
        blk = _es.enter_context(nc.Block())
        en5 = 5 in phases
        blk = MaybeBlk(blk, en5)
        sr = Serial(sb, blk, en5)
        sr.step("vector", lambda e: e.memset(eo_acc[:], 0.0))
        if en5:
            gi = 0
            ld_end = []
            m1_end = []
            silu_end = []
            gmul_end = []
            m2_end = []
            gc_end = []
            for e_ in range(E):
                sr.multi("sync", [
                    (lambda e_, dc: lambda e: e.dma_start(
                        w1b[:, dc * FE:(dc + 1) * FE],
                        mkap(w1t_s.ap(), e_ * DC * 128 * FE + dc * 128 * FE, [[FE, 128], [1, FE]])))(e_, dc)
                    for dc in range(DC)
                ] + [
                    (lambda e_, fc: lambda e: e.dma_start(
                        w2b[:, fc * D:(fc + 1) * D],
                        mkap(w2t_s.ap(), e_ * FC * 128 * D + fc * 128 * D, [[D, 128], [1, D]])))(e_, fc)
                    for fc in range(FC)
                ] + [
                    (lambda e_: lambda e: e.dma_start(
                        s1b[:], mkap(s1_s.ap(), e_ * 128 * FC, [[FC, 128], [1, FC]])))(e_)
                ], dma=True, extra_waits=[("pe", sb.c["pe"]), ("pe2", sb.c["pe2"]),
                                          ("ac", sb.c["ac"])])
                serv5 = sb.c["ser"]
                for tc in range(8):
                    par = gi % 2
                    hb = hb2[:, par * DC * 512:(par + 1) * DC * 512]
                    h1g = h1g2[:, par * FC * 512:(par + 1) * FC * 512]
                    grsb = grsb2[:, par * 512:(par + 1) * 512]
                    grow = grow2[:, par * 512:(par + 1) * 512]

                    def LH(e, e_=e_, tc=tc, hb=hb, grow=grow, gi=gi, serv5=serv5):
                        e.wait_ge(sb.sem("ser"), serv5)
                        if gi >= 2:
                            e.wait_ge(sb.sem("pe"), m1_end[gi - 2])
                        for dc in range(DC):
                            e.dma_start(hb[:, dc * 512:(dc + 1) * 512],
                                        mkap(h_s.ap(), dc * 128 * T + tc * 512,
                                             [[T, 128], [1, 512]])).then_inc(sb.sem("ld"), 16)
                            sb.c["ld"] += 16
                        e.dma_start(grow, mkap(gt_s.ap(), e_ * T + tc * 512,
                                               [[1, 1], [1, 512]])).then_inc(sb.sem("ld"), 16)
                        sb.c["ld"] += 16
                    blk.sync(LH)
                    ld_end.append(sb.c["ld"])

                    def GR(e, grow=grow, gi=gi, serv5=serv5):
                        e.wait_ge(sb.sem("ser"), serv5)
                        e.wait_ge(sb.sem("ld"), ld_end[gi])
                        if gi >= 1:
                            e.wait_ge(sb.sem("dv"), gc_end[gi - 1])
                        ins = e.matmul(pfG[:], ones_bfr[:], grow, start=True, stop=True)
                        sb.inc(ins, "pe3")
                    blk.tensor(GR)

                    def GC(e, grsb=grsb, gi=gi):
                        e.wait_ge(sb.sem("pe3"), sb.c["pe3"])
                        if gi >= 2:
                            e.wait_ge(sb.sem("po"), gmul_end[gi - 2])
                        ins = e.tensor_copy(grsb, pfG[:])
                        sb.inc(ins, "dv")
                    blk.vector(GC)
                    gc_end.append(sb.c["dv"])

                    bac = sb.c["ac"]
                    bpo = sb.c["po"]
                    bpe = sb.c["pe"]
                    for fc in range(FC):
                        def m1(e, fc=fc, hb=hb, gi=gi, bac=bac):
                            e.wait_ge(sb.sem("ld"), ld_end[gi])
                            if fc >= 3:
                                e.wait_ge(sb.sem("ac"), bac + (fc - 3) // 2 + 1)
                            ins = None
                            for dc in range(DC):
                                ins = e.matmul(m1b[fc % 3][:], w1b[:, dc * FE + fc * 128: dc * FE + (fc + 1) * 128],
                                               hb[:, dc * 512:(dc + 1) * 512],
                                               start=(dc == 0), stop=(dc == DC - 1))
                            sb.inc(ins, "pe")
                        blk.tensor(m1)

                        if fc % 2 == 1:
                            def silu(e, fc=fc, bpe=bpe, bpo=bpo):
                                e.wait_ge(sb.sem("pe"), bpe + fc + 1)
                                if fc >= 3:
                                    e.wait_ge(sb.sem("po"), bpo + (fc - 1) // 2)
                                e.activation(out=h1b[:, 0:512], in_=m1b[(fc - 1) % 3][:],
                                             func=AF.Silu, scale=s1b[:, fc - 1:fc])
                                ins = e.activation(out=h1b[:, 512:1024], in_=m1b[fc % 3][:],
                                                   func=AF.Silu, scale=s1b[:, fc:fc + 1])
                                sb.inc(ins, "ac")
                            blk.scalar(silu)

                            def gmul(e, fc=fc, h1g=h1g, gi=gi, bac=bac):
                                e.wait_ge(sb.sem("ac"), bac + (fc + 1) // 2)
                                e.wait_ge(sb.sem("dv"), gc_end[gi])
                                if gi >= 2 and fc == 1:
                                    e.wait_ge(sb.sem("pe2"), m2_end[gi - 2])
                                e.tensor_tensor(h1g[:, (fc - 1) * 512:fc * 512],
                                                h1b[:, 0:512], grsb, op=ALU.mult)
                                ins = e.tensor_tensor(h1g[:, fc * 512:(fc + 1) * 512],
                                                      h1b[:, 512:1024], grsb, op=ALU.mult)
                                sb.inc(ins, "po")
                            blk.gpsimd(gmul)
                    m1_end.append(sb.c["pe"])
                    silu_end.append(sb.c["ac"])
                    gmul_end.append(sb.c["po"])

                    bpe2 = sb.c["pe2"]
                    bdv2 = sb.c["dv2"]
                    for dc2 in range(DC):
                        def m2(e, dc2=dc2, h1g=h1g, gi=gi, bpe2=bpe2, bdv2=bdv2):
                            e.wait_ge(sb.sem("po"), gmul_end[gi])
                            if dc2 >= 2:
                                e.wait_ge(sb.sem("dv2"), bdv2 + dc2 - 1)
                            ins = None
                            for fc in range(FC):
                                ins = e.matmul(pf[2 + dc2 % 2][:], w2b[:, fc * D + dc2 * 128: fc * D + (dc2 + 1) * 128],
                                               h1g[:, fc * 512:(fc + 1) * 512],
                                               start=(fc == 0), stop=(fc == FC - 1))
                            sb.inc(ins, "pe2")
                        blk.tensor(m2)

                        def accadd(e, dc2=dc2, tc=tc, bpe2=bpe2):
                            e.wait_ge(sb.sem("pe2"), bpe2 + dc2 + 1)
                            sl = eo_acc[:, dc2 * T + tc * 512: dc2 * T + (tc + 1) * 512]
                            ins = e.tensor_tensor(sl, sl, pf[2 + dc2 % 2][:], op=ALU.add)
                            sb.inc(ins, "dv2")
                        blk.vector(accadd)
                    m2_end.append(sb.c["pe2"])
                    gi += 1
            sr.step("sync", lambda e: e.nop(), extra_waits=[("dv2", sb.c["dv2"])])
        sr.multi("gpsimd", [
            (lambda dc: lambda e: e.dma_start(
                mkap(eo_s.ap(), dc * 128 * T, [[T, 128], [1, T]]), eo_acc[:, dc * T:(dc + 1) * T]))(dc)
            for dc in range(DC)
        ], dma=True)

    with contextlib.ExitStack() as _es:
        sw1b = _es.enter_context(nc.sbuf_tensor("p6_w1", [128, DC * FE], BF16))
        sw2b = _es.enter_context(nc.sbuf_tensor("p6_w2", [128, FC * D], BF16))
        hb62 = _es.enter_context(nc.sbuf_tensor("p6_h", [128, 2 * DC * 512], BF16))
        h1s = _es.enter_context(nc.sbuf_tensor("p6_h1", [128, FC * 512], BF16))
        agc2 = _es.enter_context(nc.sbuf_tensor("p6_ag", [128, 2 * DC * 512], BF16))
        eoc2 = _es.enter_context(nc.sbuf_tensor("p6_eo", [128, 2 * DC * 512], BF16))
        x62 = _es.enter_context(nc.sbuf_tensor("p6_x", [128, 2 * 4 * D], F32))
        o6 = _es.enter_context(nc.sbuf_tensor("p6_o", [128, 4 * D], F32))
        csum = _es.enter_context(nc.sbuf_tensor("p6_cs", [128, 512], F32))
        csum2 = _es.enter_context(nc.sbuf_tensor("p6_cs2", [128, 512], F32))
        s1b6 = _es.enter_context(nc.sbuf_tensor("p6_s1", [128, FC], F32))
        pfT = _es.enter_context(nc.psum_tensor("pfT", [128, 512], F32))
        pf = [_es.enter_context(nc.psum_tensor(f"p6pf{i}", [128, 512], F32)) for i in range(4)]
        blk = _es.enter_context(nc.Block())
        en6 = 6 in phases
        blk = MaybeBlk(blk, en6)
        sr = Serial(sb, blk, en6)
        sr.multi("sync", [
            (lambda dc: lambda e: e.dma_start(
                sw1b[:, dc * FE:(dc + 1) * FE],
                mkap(st1_s.ap(), dc * 128 * FE, [[FE, 128], [1, FE]])))(dc)
            for dc in range(DC)
        ] + [
            (lambda fc: lambda e: e.dma_start(
                sw2b[:, fc * D:(fc + 1) * D],
                mkap(st2_s.ap(), fc * 128 * D, [[D, 128], [1, D]])))(fc)
            for fc in range(FC)
        ] + [
            lambda e: e.dma_start(s1b6[:], mkap(s1_s.ap(), E * 128 * FC, [[FC, 128], [1, FC]]))
        ], dma=True)
        if en6:
            serv6 = sb.c["ser"]
            b6 = {"ld": sb.c["ld"]}
            ld6_end = []
            m16_end = []
            silu6_end = []
            m26_end = []
            comb16_end = []
            comb26_end = []
            oadd6_end = []
            out_base = sb.c["out"]
            for tc in range(8):
                par = tc % 2
                hb6 = hb62[:, par * DC * 512:(par + 1) * DC * 512]
                agc = agc2[:, par * DC * 512:(par + 1) * DC * 512]
                eoc = eoc2[:, par * DC * 512:(par + 1) * DC * 512]
                x6 = x62[:, par * 4 * D:(par + 1) * 4 * D]

                def LD(e, tc=tc, hb6=hb6, agc=agc, eoc=eoc, x6=x6):
                    if tc >= 2:
                        e.wait_ge(sb.sem("pe"), m16_end[tc - 2])
                        e.wait_ge(sb.sem("po"), comb26_end[tc - 2])
                        e.wait_ge(sb.sem("dv"), comb16_end[tc - 2])
                        e.wait_ge(sb.sem("dv2"), oadd6_end[tc - 2])
                    for dc in range(DC):
                        e.dma_start(hb6[:, dc * 512:(dc + 1) * 512],
                                    mkap(h_s.ap(), dc * 128 * T + tc * 512, [[T, 128], [1, 512]])).then_inc(sb.sem("ld"), 16)
                        e.dma_start(agc[:, dc * 512:(dc + 1) * 512],
                                    mkap(ag_s.ap(), dc * 128 * T + tc * 512, [[T, 128], [1, 512]])).then_inc(sb.sem("ld"), 16)
                        e.dma_start(eoc[:, dc * 512:(dc + 1) * 512],
                                    mkap(eo_s.ap(), dc * 128 * T + tc * 512, [[T, 128], [1, 512]])).then_inc(sb.sem("ld"), 16)
                        sb.c["ld"] += 48
                    for ts in range(4):
                        e.dma_start(x6[:, ts * D:(ts + 1) * D],
                                    mkap(x_in.ap(), (tc * 512 + ts * 128) * D, [[D, 128], [1, D]])).then_inc(sb.sem("ld"), 16)
                        sb.c["ld"] += 16
                blk.sync(LD)
                ld6_end.append(sb.c["ld"])

                base = {n: sb.c[n] for n in ["pe", "ac"]}
                for fc in range(FC):
                    def m1(e, fc=fc, tc=tc, hb6=hb6):
                        if fc == 0 and tc == 0:
                            e.wait_ge(sb.sem("ser"), serv6)
                        e.wait_ge(sb.sem("ld"), ld6_end[tc])
                        if fc >= 2:
                            e.wait_ge(sb.sem("ac"), base["ac"] + fc - 1)
                        elif tc >= 1:
                            e.wait_ge(sb.sem("ac"), silu6_end[tc - 1])
                        ins = None
                        for dc in range(DC):
                            ins = e.matmul(pf[fc % 2][:], sw1b[:, dc * FE + fc * 128: dc * FE + (fc + 1) * 128],
                                           hb6[:, dc * 512:(dc + 1) * 512],
                                           start=(dc == 0), stop=(dc == DC - 1))
                        sb.inc(ins, "pe")
                    blk.tensor(m1)

                    def silu(e, fc=fc, tc=tc):
                        e.wait_ge(sb.sem("pe"), base["pe"] + fc + 1)
                        if fc == 0 and tc >= 1:
                            e.wait_ge(sb.sem("pe2"), m26_end[tc - 1])
                        ins = e.activation(out=h1s[:, fc * 512:(fc + 1) * 512],
                                           in_=pf[fc % 2][:], func=AF.Silu, scale=s1b6[:, fc:fc + 1])
                        sb.inc(ins, "ac")
                    blk.scalar(silu)
                m16_end.append(sb.c["pe"])
                silu6_end.append(sb.c["ac"])

                base2 = {"pe2": sb.c["pe2"], "dv2": sb.c["dv2"], "po": sb.c["po"],
                         "dv": sb.c["dv"], "pe": sb.c["pe"], "ac": sb.c["ac"]}
                for dc2 in range(DC):
                    def m2(e, dc2=dc2, tc=tc):
                        e.wait_ge(sb.sem("ac"), base["ac"] + FC)
                        if dc2 >= 2:
                            e.wait_ge(sb.sem("dv"), base2["dv"] + dc2 - 1)
                        elif tc >= 1:
                            e.wait_ge(sb.sem("dv"), comb16_end[tc - 1])
                        ins = None
                        for fc in range(FC):
                            ins = e.matmul(pf[2 + dc2 % 2][:], sw2b[:, fc * D + dc2 * 128: fc * D + (dc2 + 1) * 128],
                                           h1s[:, fc * 512:(fc + 1) * 512],
                                           start=(fc == 0), stop=(fc == FC - 1))
                        sb.inc(ins, "pe2")
                    blk.tensor(m2)

                    def comb1(e, dc2=dc2, tc=tc, eoc=eoc):
                        e.wait_ge(sb.sem("pe2"), base2["pe2"] + dc2 + 1)
                        e.wait_ge(sb.sem("po"), base2["po"] if dc2 == 0 else base2["po"] + dc2)
                        ins = e.tensor_tensor(csum[:], pf[2 + dc2 % 2][:],
                                              eoc[:, dc2 * 512:(dc2 + 1) * 512], op=ALU.add)
                        sb.inc(ins, "dv")
                    blk.vector(comb1)

                    def comb2(e, dc2=dc2, agc=agc):
                        e.wait_ge(sb.sem("dv"), base2["dv"] + dc2 + 1)
                        e.wait_ge(sb.sem("pe"), base2["pe"] if dc2 == 0 else base2["pe"] + dc2)
                        ins = e.tensor_tensor(csum2[:], csum[:], agc[:, dc2 * 512:(dc2 + 1) * 512], op=ALU.add)
                        sb.inc(ins, "po")
                    blk.gpsimd(comb2)

                    def tpose(e, dc2=dc2):
                        e.wait_ge(sb.sem("po"), base2["po"] + dc2 + 1)
                        e.wait_ge(sb.sem("dv2"), base2["dv2"] if dc2 == 0 else base2["dv2"] + dc2)
                        ins = None
                        for ts in range(4):
                            ins = e.transpose(pfT[:, ts * 128:(ts + 1) * 128],
                                              csum2[:, ts * 128:(ts + 1) * 128], id_f32[:])
                        sb.inc(ins, "pe")
                    blk.tensor(tpose)

                    def oadd(e, dc2=dc2, tc=tc, x6=x6):
                        e.wait_ge(sb.sem("pe"), sb.c["pe"])
                        if dc2 == 0:
                            e.wait_ge(sb.sem("out"), out_base + 16 * tc)
                        psrc = mkap(pfT[:, :], pfT[:, :].offset, [pfT[:, :].ap.to_list()[0], [128, 4], [1, 128]])
                        dst = mkap(o6[:, :], o6[:, :].offset + dc2 * 128,
                                   [o6[:, :].ap.to_list()[0], [D, 4], [1, 128]])
                        xsrc = mkap(x6, x6.offset + dc2 * 128,
                                    [x6.ap.to_list()[0], [D, 4], [1, 128]])
                        ins = e.tensor_tensor(dst, psrc, xsrc, op=ALU.add)
                        sb.inc(ins, "dv2")
                    blk.vector(oadd)
                m26_end.append(sb.c["pe2"])
                comb16_end.append(sb.c["dv"])
                comb26_end.append(sb.c["po"])
                oadd6_end.append(sb.c["dv2"])

                def store(e, tc=tc, n=sb.c["dv2"]):
                    e.wait_ge(sb.sem("dv2"), n)
                    ins = e.dma_start(mkap(out.ap(), tc * 512 * D, [[D, 128], [128 * D, 4], [1, D]]), o6[:])
                    ins.then_inc(sb.sem("out"), 16)
                    sb.c["out"] += 16
                blk.sync(store)

            def final(e):
                e.wait_ge(sb.sem("out"), sb.c["out"])
                return e.nop()
            sr.step("sync", final)

    nc.finalize()
    return nc


_NC_CACHE = {}


def kernel(**inputs):
    x = np.ascontiguousarray(inputs["L0_attn_x"])
    ident_ = np.eye(128, dtype=BFNP)
    rampv = np.tile(np.arange(-(TAB // 2), TAB // 2 + 1, dtype=np.float32), (112, 1))
    hsumv = np.zeros((112, 16), dtype=np.float32)
    for p_ in range(112):
        hsumv[p_, p_ // 7] = 1.0

    if "nc" not in _NC_CACHE:
        _NC_CACHE["nc"] = build(debug=False)
    nc = _NC_CACHE["nc"]

    base = dict(
        w_in=np.ascontiguousarray(inputs["L0_attn_w_in"]),
        w_gate=np.ascontiguousarray(inputs["L0_attn_w_gate"]),
        b_gate=np.ascontiguousarray(inputs["L0_attn_b_gate"]),
        amp=np.ascontiguousarray(inputs["L0_attn_amplitudes"]),
        shp=np.ascontiguousarray(inputs["L0_attn_sharpness"]),
        off=np.ascontiguousarray(inputs["L0_attn_offsets"]),
        sw1=np.ascontiguousarray(inputs["L0_ffn_shared_w1"]),
        sw2=np.ascontiguousarray(inputs["L0_ffn_shared_w2"]),
        ew1=np.ascontiguousarray(inputs["L0_ffn_expert_w1"]),
        ew2=np.ascontiguousarray(inputs["L0_ffn_expert_w2"]),
        rw=np.ascontiguousarray(inputs["L0_ffn_router_w"]),
        gam=np.ascontiguousarray(inputs["L0_ffn_ln_gamma"]),
        bet=np.ascontiguousarray(inputs["L0_ffn_ln_beta"]),
        ident=ident_, identf=np.eye(128, dtype=np.float32), ramp=rampv, hsum=hsumv,
    )
    in_maps = []
    for c in range(8):
        m = dict(base)
        m["x"] = np.ascontiguousarray(x[2 * c:2 * c + 2].reshape(T, D))
        in_maps.append(m)
    res = run_bass_kernel_spmd(nc, in_maps, core_ids=list(range(8)))
    return np.concatenate([r["out"].reshape(2, S, D) for r in res.results], axis=0)


if __name__ == "__main__":
    nc = build(debug=False)
    print("build ok")
